# version 10
# baseline (speedup 1.0000x reference)
import contextlib
import numpy as np
import ml_dtypes
import concourse.bass as bass
import concourse.mybir as mybir
from concourse.bass_utils import run_bass_kernel_spmd

F32 = mybir.dt.float32
BF16 = mybir.dt.bfloat16
I32 = mybir.dt.int32
AF = mybir.ActivationFunctionType
ALU = mybir.AluOpType
AX = mybir.AxisListType
bf16 = ml_dtypes.bfloat16

T = 2048
D = 1024
NTT = T // 128
NDC = D // 128

ALL_PARTS = ["l0aa", "l0ffn1", "l0ab", "l0ffn2", "l1aa", "l1ffn1", "l1ab", "l1ffn2"]
_LAST_EM = None


class E:

    ENGS = ("vector", "scalar", "tensor", "gpsimd", "sync")
    NDMA = 16

    def __init__(self, nc):
        self.nc = nc
        self.ops = {e: [] for e in self.ENGS}
        self.cnt = {e: 0 for e in self.ENGS}
        self.dma_uses = [0] * self.NDMA
        self.dma_next = 0
        self.last_writer = {}
        self.readers = {}

    @staticmethod
    def _aps(x):
        return [a for a in x if type(a).__name__ == "AP"]

    SAME_ENGINE_WAITS = True

    def _dep_wait(self, waits, dep, eng=None):
        kind = dep[0]
        if kind == "E":
            _, deng, dcnt = dep
            if deng == eng and not self.SAME_ENGINE_WAITS:
                return
            key = ("E", deng)
            if waits.get(key, 0) < dcnt:
                waits[key] = dcnt
        else:
            _, slot, val = dep
            key = ("D", slot)
            if waits.get(key, 0) < val:
                waits[key] = val

    def mark(self, label):
        self.marks = getattr(self, "marks", [])
        self.marks.append((label, dict(self.cnt), sum(self.dma_uses)))

    def barrier(self):
        self._barrier_tokens = (
            [("E", e, c) for e, c in self.cnt.items() if c > 0] +
            [("D", s, 16 * u) for s, u in enumerate(self.dma_uses) if u > 0])
        self._barrier_done = set()
        self.last_writer = {}
        self.readers = {}

    def _desc(self, eng, fn, args, kwargs, dma=False):
        aps = self._aps(args) + self._aps(list(kwargs.values()))
        if dma:
            writes = [kwargs["out"]] if "out" in kwargs else [args[0]]
            reads = [a for a in aps if a is not writes[0]]
        else:
            writes = aps[:1]
            if kwargs.get("accum_out") is not None:
                writes = writes + [kwargs["accum_out"]]
            reads = [a for a in aps if all(a is not w for w in writes)]
        waits = {}
        if getattr(self, "_barrier_tokens", None) and eng not in self._barrier_done:
            for dep in self._barrier_tokens:
                self._dep_wait(waits, dep, None)
            self._barrier_done.add(eng)
        rnames = [a.name for a in reads]
        wnames = [a.name for a in writes]
        for t in rnames:
            w = self.last_writer.get(t)
            if w is not None:
                self._dep_wait(waits, w, eng)
        for t in wnames:
            w = self.last_writer.get(t)
            if w is not None:
                self._dep_wait(waits, w, eng)
            for r in self.readers.get(t, ()):
                self._dep_wait(waits, r, eng)
        if dma:
            slot = self.dma_next
            self.dma_next = (self.dma_next + 1) % self.NDMA
            if self.dma_uses[slot] > 0:
                key = ("D", slot)
                if waits.get(key, 0) < 16 * self.dma_uses[slot]:
                    waits[key] = 16 * self.dma_uses[slot]
            self.dma_uses[slot] += 1
            tok = ("D", slot, 16 * self.dma_uses[slot])
            inc = ("D", slot)
        else:
            self.cnt[eng] += 1
            tok = ("E", eng, self.cnt[eng])
            inc = ("E", eng)
        for t in rnames:
            self.readers.setdefault(t, []).append(tok)
        for t in wnames:
            self.last_writer[t] = tok
            self.readers[t] = []
        self.ops[eng].append((waits, fn, inc))

    def v(self, meth, *a, **k):
        self._desc("vector", lambda e, a=a, k=k, m=meth: getattr(e, m)(*a, **k), a, k)

    def a(self, meth, *a, **k):
        self._desc("scalar", lambda e, a=a, k=k, m=meth: getattr(e, m)(*a, **k), a, k)

    def t(self, meth, *a, **k):
        self._desc("tensor", lambda e, a=a, k=k, m=meth: getattr(e, m)(*a, **k), a, k)

    def g(self, meth, *a, **k):
        self._desc("gpsimd", lambda e, a=a, k=k, m=meth: getattr(e, m)(*a, **k), a, k)

    def dma(self, out, in_, eng=None):
        if eng is None:
            eng = ("sync", "gpsimd")[self.dma_next % 2]
        self._desc(eng, lambda e, o=out, i=in_: e.dma_start(out=o, in_=i),
                   (), {"out": out, "in_": in_}, dma=True)

    def dma_t(self, out, in_, eng=None):
        if eng is None:
            eng = "sync"
        self._desc(eng, lambda e, o=out, i=in_: e.dma_start(out=o, in_=i, transpose=True),
                   (), {"out": out, "in_": in_}, dma=True)

    def mm(self, out, lhsT, rhs, start, stop, tile_position=None):
        self._desc("tensor",
                   lambda e, o=out, l=lhsT, r=rhs, s1=start, s2=stop, tp=tile_position:
                   e.matmul(o, l, r, start=s1, stop=s2, tile_position=tp),
                   (out, lhsT, rhs), {})

    def replay(self, block, sems):
        esem, dsem = sems
        total_e = dict(self.cnt)
        total_d = list(self.dma_uses)

        for name in self.ENGS:
            ops = self.ops[name]

            def section(eng, ops=ops, name=name):
                for (waits, fn, inc) in ops:
                    for (key, val) in sorted(waits.items(), key=str):
                        if key[0] == "E":
                            eng.wait_ge(esem[key[1]], val)
                        else:
                            eng.wait_ge(dsem[key[1]], val)
                    inst = fn(eng)
                    if inc[0] == "E":
                        inst.then_inc(esem[inc[1]], 1)
                    else:
                        inst.then_inc(dsem[inc[1]], 16)
                if name == "sync":
                    for e2, c in total_e.items():
                        if c > 0:
                            eng.wait_ge(esem[e2], c)
                    for s, u in enumerate(total_d):
                        if u > 0:
                            eng.wait_ge(dsem[s], 16 * u)

            getattr(block, name)(section)


def prep_inputs(inp):
    p = {}
    c = np.ascontiguousarray
    p["c_ident128"] = np.eye(128, dtype=bf16)
    p["c_ident32f"] = np.eye(32, dtype=np.float32)
    p["c_ident128f"] = np.eye(128, dtype=np.float32)
    p["c_ones_row_f"] = np.ones((1, 128), dtype=np.float32)
    p["c_ones_row_b"] = np.ones((1, 128), dtype=bf16)
    p["c_triu"] = np.triu(np.ones((128, 128), dtype=np.float32)).astype(bf16)
    mn = np.zeros((128, 128), dtype=np.float32)
    mn[np.tril_indices(128, -1)] = -1e30
    p["c_maskneg"] = mn

    W = np.asarray(inp["L0_attn_a_W_qkv"]).reshape(16, 3, 64, D)
    p["aa_wqT"] = c(W[:, 0].reshape(D, D).T)
    p["aa_wkT"] = c(W[:, 1].reshape(D, D).T)
    p["aa_wvT"] = c(W[:, 2].reshape(D, D).T)

    p["ab_wqkv"] = c(np.asarray(inp["L0_attn_b_w_qkv"]))
    p["ab_wout"] = c(np.asarray(inp["L0_attn_b_w_out"]))
    p["ab_bout"] = c(np.asarray(inp["L0_attn_b_b_out"]).reshape(1, D))
    p["ab_ls"] = c(np.asarray(inp["L0_attn_b_logit_scale"]).reshape(16, 1))

    p["f0_w1T"] = c(np.asarray(inp["L0_ffn_w1"]).T)
    p["f0_w2T"] = c(np.asarray(inp["L0_ffn_w2"]).T)
    p["f0_convw"] = c(np.asarray(inp["L0_ffn_conv_w"]).reshape(4096, 3))
    p["f0_convb"] = c(np.asarray(inp["L0_ffn_conv_b"]).reshape(4096, 1))

    p["sa_winT"] = c(np.asarray(inp["L1_attn_a_W_in"]).T)
    p["sa_convw"] = c(np.asarray(inp["L1_attn_a_conv_w"]))
    p["sa_convb"] = c(np.asarray(inp["L1_attn_a_conv_b"]).reshape(2304, 1))
    p["sa_dtb"] = c(np.asarray(inp["L1_attn_a_dt_bias"]).reshape(32, 1))
    p["sa_alog"] = c(np.asarray(inp["L1_attn_a_A_log"]).reshape(32, 1))
    p["sa_d"] = c(np.asarray(inp["L1_attn_a_D"]).reshape(32, 1))
    p["sa_normw"] = c(np.asarray(inp["L1_attn_a_norm_w"]).reshape(1, 2048))
    p["sa_woutT"] = c(np.asarray(inp["L1_attn_a_W_out"]).T)

    p["db_gatew"] = c(np.asarray(inp["L1_attn_b_gate_w"]).reshape(D, 7))
    p["db_gateb"] = c(np.asarray(inp["L1_attn_b_gate_b"]).reshape(D, 1))
    p["db_candw"] = c(np.asarray(inp["L1_attn_b_cand_w"]).reshape(D, 7))
    p["db_candb"] = c(np.asarray(inp["L1_attn_b_cand_b"]).reshape(D, 1))
    p["db_logdec"] = c(np.asarray(inp["L1_attn_b_log_decay"]).reshape(8, 128).T)
    p["db_woutT"] = c(np.asarray(inp["L1_attn_b_out_w"]).T)
    p["db_bout"] = c(np.asarray(inp["L1_attn_b_out_b"]).reshape(1, D))
    p["db_normw"] = c(np.asarray(inp["L1_attn_b_norm_w"]).reshape(1, D))

    p["f1_w1T"] = c(np.asarray(inp["L1_ffn_W1"]).T)
    p["f1_b1"] = c(np.asarray(inp["L1_ffn_b1"]).reshape(1, 4096))
    p["f1_w2T"] = c(np.asarray(inp["L1_ffn_W2"]).T)
    p["f1_b2"] = c(np.asarray(inp["L1_ffn_b2"]).reshape(1, D))
    return p


PART_INPUTS = {
    "l0aa": ["aa_wqT", "aa_wkT", "aa_wvT"],
    "l0ab": ["ab_wqkv", "ab_wout", "ab_bout", "ab_ls"],
    "l0ffn1": ["f0_w1T", "f0_w2T", "f0_convw", "f0_convb"],
    "l0ffn2": ["f0_w1T", "f0_w2T", "f0_convw", "f0_convb"],
    "l1aa": ["sa_winT", "sa_convw", "sa_convb", "sa_dtb", "sa_alog", "sa_d",
             "sa_normw", "sa_woutT"],
    "l1ab": ["db_gatew", "db_gateb", "db_candw", "db_candb", "db_logdec",
             "db_woutT", "db_bout", "db_normw"],
    "l1ffn1": ["f1_w1T", "f1_b1", "f1_w2T", "f1_b2"],
    "l1ffn2": ["f1_w1T", "f1_b1", "f1_w2T", "f1_b2"],
}
CONST_INPUTS = ["c_ident128", "c_ident32f", "c_ident128f", "c_ones_row_f",
                "c_ones_row_b", "c_triu", "c_maskneg"]


class NCProxy:

    def __init__(self, nc):
        self._nc = nc
        self._used = {}

    def _uniq(self, name):
        n = self._used.get(name, 0)
        self._used[name] = n + 1
        return name if n == 0 else f"{name}__{n}"

    def sbuf_tensor(self, name, *a, **k):
        return self._nc.sbuf_tensor(self._uniq(name), *a, **k)

    def psum_tensor(self, name, *a, **k):
        return self._nc.psum_tensor(self._uniq(name), *a, **k)

    def dram_tensor(self, name, *a, **k):
        return self._nc.dram_tensor(self._uniq(name), *a, **k)

    def __getattr__(self, a):
        return getattr(self._nc, a)


class Prog:

    def __init__(self, nc, em, es, dins):
        self.nc = nc
        self.em = em
        self.es = es
        self.din = dins
        self.sb = {}
        self.uid = 0

    def sbuf(self, name, shape, dtype):
        t = self.es.enter_context(self.nc.sbuf_tensor(name, shape, dtype))
        return t

    def psum(self, name, shape, dtype):
        return self.es.enter_context(self.nc.psum_tensor(name, shape, dtype))

    @contextlib.contextmanager
    def scope(self):
        with contextlib.ExitStack() as es:
            yield es
        self.em.barrier()


def setup_shared(pr):
    em = pr.em
    pr.xs = pr.sbuf("xs", [128, NTT * D], F32)[:]
    pr.ident = pr.sbuf("ident", [128, 128], BF16)[:]
    pr.ident32f = pr.sbuf("ident32f", [32, 32], F32)[:]
    pr.ident128f = pr.sbuf("ident128f", [128, 128], F32)[:]
    pr.ones_row_f = pr.sbuf("ones_row_f", [1, 128], F32)[:]
    pr.ones_row_b = pr.sbuf("ones_row_b", [1, 128], BF16)[:]
    pr.triu = pr.sbuf("triu", [128, 128], BF16)[:]
    pr.maskneg = pr.sbuf("maskneg", [128, 128], F32)[:]
    pr.col1 = pr.sbuf("col1", [128, 1], F32)[:]
    pr.col2 = pr.sbuf("col2", [128, 1], F32)[:]
    pr.col3 = pr.sbuf("col3", [128, 1], F32)[:]
    em.dma(pr.ident[:], pr.din["c_ident128"])
    em.dma(pr.ident32f[:], pr.din["c_ident32f"])
    em.dma(pr.ident128f[:], pr.din["c_ident128f"])
    em.dma(pr.ones_row_f[:], pr.din["c_ones_row_f"])
    em.dma(pr.ones_row_b[:], pr.din["c_ones_row_b"])
    em.dma(pr.triu[:], pr.din["c_triu"])
    em.dma(pr.maskneg[:], pr.din["c_maskneg"])


def load_weight_bf16(pr, name, dram_name, rows, cols, tag=None):
    em = pr.em
    nch = rows // 128
    wb = pr.sbuf(name, [128, nch * cols], BF16)
    src = pr.din[dram_name].rearrange("(c p) f -> c p f", p=128)
    with pr.scope() as es_w:
        stg = es_w.enter_context(pr.nc.sbuf_tensor(name + "_stg", [128, cols], F32))
        for ci in range(nch):
            em.dma(stg[:], src[ci])
            em.v("tensor_copy", wb[:, ci * cols:(ci + 1) * cols], stg[:])
    return wb


def emit_ln_tile(pr, tt, xn_tile, ps_tr, dump, to_xnT=True, xn_f32=None):
    em = pr.em
    xst = pr.xs[:, tt * D:(tt + 1) * D]
    em.v("reduce_sum", pr.col1, xst, AX.X)
    em.v("tensor_scalar", pr.col1, pr.col1, -1.0 / D, None, ALU.mult)
    em.a("activation", dump[:, 0:D], xst, AF.Square, bias=pr.col1, scale=1.0,
         accum_out=pr.col2)
    em.v("tensor_scalar", pr.col2, pr.col2, 1.0 / D, 1e-5, ALU.mult, ALU.add)
    em.a("activation", pr.col3, pr.col2, AF.Ln, bias=0.0, scale=1.0)
    em.a("activation", pr.col3, pr.col3, AF.Exp, bias=0.0, scale=-0.5)
    em.v("tensor_scalar", xn_tile[:], xst, pr.col1, pr.col3, ALU.add, ALU.mult)
    if xn_f32 is not None:
        em.v("tensor_scalar", xn_f32[:], xst, pr.col1, pr.col3, ALU.add, ALU.mult)
    if to_xnT:
        for dc in range(NDC):
            em.t("transpose", ps_tr[:, dc * 128:(dc + 1) * 128],
                 xn_tile[:, dc * 128:(dc + 1) * 128], pr.ident[:])
        dst = pr.xnT[:].rearrange("p (dc t) -> p dc t", t=T)[:, :, tt * 128:(tt + 1) * 128]
        src = ps_tr[:].rearrange("p (dc t) -> p dc t", t=128)
        em.v("tensor_copy", dst, src)


def emit_ln_all(pr, xn_f32_buf=None, stats=None):
    with pr.scope() as es2:
        xn_tile = es2.enter_context(pr.nc.sbuf_tensor("ln_xnt", [128, D], BF16))
        dump = es2.enter_context(pr.nc.sbuf_tensor("ln_dump", [128, D], F32))
        ps_tr = es2.enter_context(pr.nc.psum_tensor("ln_pstr", [128, NDC * 128], BF16))
        for tt in range(NTT):
            xf = None if xn_f32_buf is None else xn_f32_buf[:, tt * D:(tt + 1) * D]
            emit_ln_tile(pr, tt, xn_tile, ps_tr, dump, xn_f32=xf)
            if stats is not None:
                pr.em.v("tensor_copy", stats[0][:, tt:tt + 1], pr.col1)
                pr.em.v("tensor_copy", stats[1][:, tt:tt + 1], pr.col3)


def emit_attention_core(pr, qT, kT, vbuf, evac_fn, scale_imm=None,
                        scale_ap16=None, bias_ap16=None):
    em = pr.em
    with pr.scope() as es2:
        nc = pr.nc
        ps_s = [es2.enter_context(nc.psum_tensor(f"att_s{i}", [128, 1024], F32))
                for i in range(2)]
        ps_av = [es2.enter_context(nc.psum_tensor(f"att_av{i}", [128, 1024], F32))
                 for i in range(2)]
        ebuf = [es2.enter_context(nc.sbuf_tensor(f"att_e{i}", [128, 1024], BF16))
                for i in range(2)]
        rden = es2.enter_context(nc.sbuf_tensor("att_rd", [128, 1], F32))

        def avoff(j):
            return (j // 4) * 512 + (j % 4) * 65

        for h in range(16):
            bp = 64 * (h % 2)
            cb = (h // 2) * T
            for qh in range(2):
                av = ps_av[(h * 2 + qh) % 2]

                def front(kc, h=h, bp=bp, cb=cb, qh=qh):
                    pb = kc % 2
                    for qs in range(2):
                        em.mm(ps_s[pb][:, qs * 512:(qs + 1) * 512],
                              kT[bp:bp + 64, cb + kc * 128:cb + (kc + 1) * 128],
                              qT[bp:bp + 64, cb + qh * 1024 + qs * 512:cb + qh * 1024 + (qs + 1) * 512],
                              start=True, stop=True, tile_position=(bp, 0))
                    if scale_imm is not None:
                        em.a("activation", ebuf[pb][:], ps_s[pb][:],
                             AF.Exp, bias=0.0, scale=scale_imm)
                    else:
                        em.a("activation", ebuf[pb][:], ps_s[pb][:],
                             AF.Exp, bias=bias_ap16[:, h:h + 1], scale=scale_ap16[:, h:h + 1])

                def back(kc, h=h, av=av):
                    pb = kc % 2
                    for j in range(8):
                        em.mm(av[:, avoff(j):avoff(j) + 65],
                              ebuf[pb][:, j * 128:(j + 1) * 128],
                              vbuf[:, kc * 1040 + h * 65:kc * 1040 + (h + 1) * 65],
                              start=(kc == 0 and j % 4 == 0),
                              stop=(kc == NTT - 1 and j % 4 == 3))

                for kc in range(NTT + 1):
                    if kc < NTT:
                        front(kc)
                    if kc > 0:
                        back(kc - 1)
                for j in range(8):
                    em.v("reciprocal", rden[:], av[:, avoff(j) + 64:avoff(j) + 65])
                    evac_fn(h, qh * 8 + j, av[:, avoff(j):avoff(j) + 64], rden)


def emit_l0_attn_a(pr):
    em = pr.em
    nc = pr.nc
    with pr.scope() as es2:
        qT = es2.enter_context(nc.sbuf_tensor("aa_qT", [128, NDC * T], BF16))
        kT = es2.enter_context(nc.sbuf_tensor("aa_kT", [128, NDC * T], BF16))
        vbuf = es2.enter_context(nc.sbuf_tensor("aa_v", [128, NTT * D], BF16))
        with pr.scope() as es3:
            pr.xnT = es3.enter_context(nc.sbuf_tensor("aa_xnT", [128, NDC * T], BF16))[:]
            emit_ln_all(pr)
            ps_qk = es3.enter_context(nc.psum_tensor("aa_psqk", [128, 2048], F32))
            ps_v = es3.enter_context(nc.psum_tensor("aa_psv", [128, 1024], F32))
            with pr.scope() as es4:
                wstg = es4.enter_context(nc.sbuf_tensor("aa_wstg", [128, 128], F32))
                wsl = es4.enter_context(nc.sbuf_tensor("aa_wsl", [128, 128], BF16))
                for side, wname in ((qT, "aa_wqT"), (kT, "aa_wkT")):
                    wsrc = pr.din[wname].rearrange("(dc p) f -> dc p f", p=128)
                    for fc in range(NDC):
                        for dc in range(NDC):
                            em.dma(wstg[:], wsrc[dc, :, fc * 128:(fc + 1) * 128])
                            em.v("tensor_copy", wsl[:], wstg[:])
                            for qs in range(4):
                                em.mm(ps_qk[:, qs * 512:(qs + 1) * 512],
                                      wsl[:],
                                      pr.xnT[:, dc * T + qs * 512:dc * T + (qs + 1) * 512],
                                      start=(dc == 0), stop=(dc == NDC - 1))
                        em.v("tensor_copy", side[:, fc * T:(fc + 1) * T], ps_qk[:])
            with pr.scope() as es4:
                wvh = es4.enter_context(nc.sbuf_tensor("aa_wvh", [128, NDC * 512], BF16))
                wstg2 = es4.enter_context(nc.sbuf_tensor("aa_wstg2", [128, 512], F32))
                wsrc = pr.din["aa_wvT"].rearrange("(dc p) f -> dc p f", p=128)
                for vs in range(2):
                    for dc in range(NDC):
                        em.dma(wstg2[:], wsrc[dc, :, vs * 512:(vs + 1) * 512])
                        em.v("tensor_copy", wvh[:, dc * 512:(dc + 1) * 512], wstg2[:])
                    for tt in range(NTT):
                        for dc in range(NDC):
                            em.mm(ps_v[:, vs * 512:(vs + 1) * 512],
                                  pr.xnT[:, dc * T + tt * 128:dc * T + (tt + 1) * 128],
                                  wvh[:, dc * 512:(dc + 1) * 512],
                                  start=(dc == 0), stop=(dc == NDC - 1))
                        em.v("tensor_copy",
                             vbuf[:, tt * D + vs * 512:tt * D + (vs + 1) * 512],
                             ps_v[:, vs * 512:(vs + 1) * 512])

        with pr.scope() as es3:
            NSL = 2
            ps_s = es3.enter_context(nc.psum_tensor("aa_pss", [128, 2048], F32))
            ps_av = [es3.enter_context(nc.psum_tensor(f"aa_psav{j}", [128, 512], F32))
                     for j in range(NSL)]
            ebuf = [es3.enter_context(nc.sbuf_tensor(f"aa_e{j}", [128, 2048], BF16))
                    for j in range(NSL)]
            eT = [es3.enter_context(nc.sbuf_tensor(f"aa_eT{j}", [128, 2048], BF16))
                  for j in range(NSL)]
            nmx = [es3.enter_context(nc.sbuf_tensor(f"aa_nmx{j}", [128, 1], F32))
                   for j in range(NSL)]
            den = [es3.enter_context(nc.sbuf_tensor(f"aa_den{j}", [128, 1], F32))
                   for j in range(NSL)]
            rden = [es3.enter_context(nc.sbuf_tensor(f"aa_rden{j}", [128, 1], F32))
                    for j in range(NSL)]
            iters = [(h, qt) for h in range(16) for qt in range(NTT)]

            def front(i):
                h, qt = iters[i]
                j = i % NSL
                bp = 64 * (h % 2)
                cb = (h // 2) * T
                for ks in range(4):
                    em.mm(ps_s[:, ks * 512:(ks + 1) * 512],
                          qT[bp:bp + 64, cb + qt * 128:cb + (qt + 1) * 128],
                          kT[bp:bp + 64, cb + ks * 512:cb + (ks + 1) * 512],
                          start=True, stop=True, tile_position=(bp, 0))
                em.v("reduce_max", nmx[j][:], ps_s[:], AX.X, negate=True)
                em.v("tensor_scalar", nmx[j][:], nmx[j][:], 0.125, None, ALU.mult)
                em.a("activation", ebuf[j][:], ps_s[:], AF.Exp, bias=nmx[j][:],
                     scale=0.125, accum_out=den[j][:])
                em.v("reciprocal", rden[j][:], den[j][:])
                em.dma_t(eT[j][:].rearrange("p (c q) -> p c q", q=128),
                         ebuf[j][:].rearrange("p (c k) -> p c k", k=128),
                         eng="sync")

            def back(i):
                h, qt = iters[i]
                j = i % NSL
                for kc in range(NTT):
                    em.mm(ps_av[j][:, 0:64], eT[j][:, kc * 128:(kc + 1) * 128],
                          vbuf[:, kc * D + h * 64:kc * D + (h + 1) * 64],
                          start=(kc == 0), stop=(kc == NTT - 1))
                xsl = pr.xs[:, qt * D + h * 64:qt * D + (h + 1) * 64]
                em.v("scalar_tensor_tensor", xsl, ps_av[j][:, 0:64], rden[j][:], xsl,
                     ALU.mult, ALU.add)

            for i in range(len(iters) + 1):
                if i < len(iters):
                    front(i)
                if i > 0:
                    back(i - 1)


def emit_l0_attn_b(pr):
    em = pr.em
    nc = pr.nc
    with pr.scope() as es2:
        pr.xnT = es2.enter_context(nc.sbuf_tensor("ab_xnT", [128, NDC * T], BF16))[:]
        emit_ln_all(pr)
        scb = es2.enter_context(nc.sbuf_tensor("ab_scb", [128, 16], F32))
        nscb = es2.enter_context(nc.sbuf_tensor("ab_nscb", [128, 16], F32))
        ao = pr.xnT

        with pr.scope() as es3:
            lsc = es3.enter_context(nc.sbuf_tensor("ab_lsc", [16, 1], F32))
            lsr = es3.enter_context(nc.sbuf_tensor("ab_lsr", [1, 16], F32))
            ps16 = es3.enter_context(nc.psum_tensor("ab_ps16", [128, 512], F32))
            em.dma(lsc[:], pr.din["ab_ls"])
            em.v("tensor_scalar", lsc[:], lsc[:], float(np.log(100.0)), None, ALU.min)
            em.a("activation", lsc[:], lsc[:], AF.Exp, bias=0.0, scale=1.0)
            em.t("transpose", ps16[0:1, 0:16], lsc[:], pr.ident32f[0:16, 0:16])
            em.v("tensor_copy", lsr[:], ps16[0:1, 0:16])
            em.mm(ps16[:, 0:16], pr.ones_row_f[:], lsr[:], start=True, stop=True)
            em.v("tensor_copy", scb[:], ps16[:, 0:16])
            em.v("tensor_scalar", nscb[:], scb[:], -1.0, None, ALU.mult)

        with pr.scope() as esA:
            qT = esA.enter_context(nc.sbuf_tensor("ab_qT", [128, NDC * T], BF16))
            kT = esA.enter_context(nc.sbuf_tensor("ab_kT", [128, NDC * T], BF16))
            vbuf = esA.enter_context(nc.sbuf_tensor("ab_v", [128, NTT * 1040], BF16))

            with pr.scope() as es3:
                TB = 3
                ps_q3 = [es3.enter_context(nc.psum_tensor(f"ab_psq{i}", [128, 1024], F32))
                         for i in range(TB)]
                ps_tr = es3.enter_context(nc.psum_tensor("ab_pstr", [128, 1024], BF16))
                qn = es3.enter_context(nc.sbuf_tensor("ab_qn", [128, D], BF16))
                ssq = es3.enter_context(nc.sbuf_tensor("ab_ssq", [128, 16], F32))
                dump = es3.enter_context(nc.sbuf_tensor("ab_dump", [128, D], F32))
                wstg = es3.enter_context(nc.sbuf_tensor("ab_wstg", [128, 512], F32))
                wslb = es3.enter_context(nc.sbuf_tensor("ab_wslb", [128, 512], BF16))
                ones_ap = vbuf[:].rearrange("p (t h w) -> p t h w", h=16, w=65)[:, :, :, 64:65]
                em.v("memset", ones_ap, 1.0)
                wsrc = pr.din["ab_wqkv"].rearrange("(dc p) f -> dc p f", p=128)
                for part, dstT in ((0, qT), (1, kT), (2, None)):
                    for t0 in range(0, NTT, TB):
                        tts = list(range(t0, min(t0 + TB, NTT)))
                        for qs in range(2):
                            for dc in range(NDC):
                                em.dma(wstg[:], wsrc[dc, :, part * 1024 + qs * 512:
                                                       part * 1024 + (qs + 1) * 512])
                                em.v("tensor_copy", wslb[:], wstg[:])
                                for i, tt in enumerate(tts):
                                    em.mm(ps_q3[i][:, qs * 512:(qs + 1) * 512],
                                          pr.xnT[:, dc * T + tt * 128:dc * T + (tt + 1) * 128],
                                          wslb[:],
                                          start=(dc == 0), stop=(dc == NDC - 1))
                        for i, tt in enumerate(tts):
                            ps_q = ps_q3[i]
                            emit_ab_qkv_post(pr, part, tt, ps_q, ps_tr, qn, ssq, dump,
                                             qT, kT, vbuf)

            _ab_attention_and_outproj(pr, qT, kT, vbuf, scb, nscb)
        _ab_outproj(pr)


def emit_ab_qkv_post(pr, part, tt, ps_q, ps_tr, qn, ssq, dump, qT, kT, vbuf):
    em = pr.em
    dstT = (qT, kT, None)[part]
    if part < 2:
        em.a("activation", dump[:, 0:D], ps_q[:], AF.Square, bias=0.0, scale=1.0)
        em.v("tensor_reduce",
             ssq[:].rearrange("p (h one) -> p h one", one=1),
             dump[:, 0:D].rearrange("p (h w) -> p h w", w=64),
             AX.X, ALU.add)
        em.v("tensor_scalar", ssq[:], ssq[:], 1e-24, None, ALU.max)
        em.a("activation", ssq[:], ssq[:], AF.Ln, bias=0.0, scale=1.0)
        em.a("activation", ssq[:], ssq[:], AF.Exp, bias=0.0, scale=-0.5)
        em.v("tensor_tensor", qn[:].rearrange("p (h w) -> p h w", w=64),
             ps_q[:].rearrange("p (h w) -> p h w", w=64),
             ssq[:].unsqueeze(2).broadcast_to([128, 16, 64]), ALU.mult)
        for dc in range(NDC):
            em.t("transpose", ps_tr[:, dc * 128:(dc + 1) * 128],
                 qn[:, dc * 128:(dc + 1) * 128], pr.ident[:])
        dst = dstT[:].rearrange("p (dc t) -> p dc t", t=T)[:, :, tt * 128:(tt + 1) * 128]
        em.v("tensor_copy", dst, ps_tr[:].rearrange("p (dc t) -> p dc t", t=128))
    else:
        dst = vbuf[:, tt * 1040:(tt + 1) * 1040].rearrange(
            "p (h w) -> p h w", w=65)[:, :, 0:64]
        em.v("tensor_copy", dst, ps_q[:].rearrange("p (h w) -> p h w", w=64))


def _ab_attention_and_outproj(pr, qT, kT, vbuf, scb, nscb):
    em = pr.em
    nc = pr.nc
    ao = pr.xnT

    def evac(h, qt, ps_block, rden):
        em.v("tensor_scalar", ao[:, qt * D + h * 64:qt * D + (h + 1) * 64],
             ps_block, rden[:], None, ALU.mult)

    emit_attention_core(pr, qT, kT, vbuf, evac, scale_ap16=scb, bias_ap16=nscb)


def _ab_outproj(pr):
    em = pr.em
    nc = pr.nc
    ao = pr.xnT
    with pr.scope() as es3:
        aT = es3.enter_context(nc.sbuf_tensor("ab_aT", [128, NDC * T], BF16))
        ps_tr = es3.enter_context(nc.psum_tensor("ab_pstr2", [128, 1024], BF16))
        ps_o = es3.enter_context(nc.psum_tensor("ab_pso", [128, 1024], F32))
        bout = es3.enter_context(nc.sbuf_tensor("ab_boutb", [1, D], BF16))
        boutf = es3.enter_context(nc.sbuf_tensor("ab_boutf", [1, D], F32))
        pr.es, saved_es = es3, pr.es
        wo = load_weight_bf16(pr, "ab_wob", "ab_wout", D, D)
        pr.es = saved_es
        em.dma(boutf[:], pr.din["ab_bout"])
        em.v("tensor_copy", bout[:], boutf[:])
        for tt in range(NTT):
            for dc in range(NDC):
                em.t("transpose", ps_tr[:, dc * 128:(dc + 1) * 128],
                     ao[:, tt * D + dc * 128:tt * D + (dc + 1) * 128], pr.ident[:])
            em.v("tensor_copy",
                 aT[:].rearrange("p (dc t) -> p dc t", t=T)[:, :, tt * 128:(tt + 1) * 128],
                 ps_tr[:].rearrange("p (dc t) -> p dc t", t=128))
        for tt in range(NTT):
            for os_ in range(2):
                for dc in range(NDC):
                    em.mm(ps_o[:, os_ * 512:(os_ + 1) * 512],
                          aT[:, dc * T + tt * 128:dc * T + (tt + 1) * 128],
                          wo[:, dc * D + os_ * 512:dc * D + (os_ + 1) * 512],
                          start=(dc == 0), stop=False)
                em.mm(ps_o[:, os_ * 512:(os_ + 1) * 512], pr.ones_row_b[:],
                      bout[:, os_ * 512:(os_ + 1) * 512], start=False, stop=True)
            em.v("tensor_tensor", pr.xs[:, tt * D:(tt + 1) * D], ps_o[:],
                 pr.xs[:, tt * D:(tt + 1) * D], ALU.add)



def bcast_1x1_to_col(pr, src11, dst_col, ps512):
    em = pr.em
    em.mm(ps512[:, 0:1], pr.ones_row_f, src11, start=True, stop=True)
    em.v("tensor_copy", dst_col, ps512[:, 0:1])


def bcast_row_to_tile(pr, src_row, dst, ps512, width):
    em = pr.em
    for off in range(0, width, 512):
        w = min(512, width - off)
        em.mm(ps512[:, 0:w], pr.ones_row_f, src_row[:, off:off + w],
              start=True, stop=True)
        em.v("tensor_copy", dst[:, off:off + w], ps512[:, 0:w])


def emit_neground2(pr, v, c_i32, d, r, out_bf, width):
    em = pr.em
    em.v("tensor_copy", c_i32[:, 0:width], v)
    em.v("tensor_sub", d[:, 0:width], v, c_i32[:, 0:width])
    em.v("scalar_tensor_tensor", r[:, 0:width], d[:, 0:width], 0.5,
         c_i32[:, 0:width], ALU.is_gt, ALU.add)
    em.v("scalar_tensor_tensor", out_bf, d[:, 0:width], -0.5, r[:, 0:width],
         ALU.is_lt, ALU.subtract)


def emit_mean_abs(pr, dram_name, rows, cols, out11, stg, abscols, ps512, onescol):
    em = pr.em
    nch = rows // 128
    src = pr.din[dram_name].rearrange("(c p) f -> c p f", p=128)
    for ci in range(nch):
        em.dma(stg[:, 0:cols], src[ci])
        em.a("activation", stg[:, 0:cols], stg[:, 0:cols], AF.Abs, bias=0.0,
             scale=1.0, accum_out=abscols[:, ci:ci + 1])
    em.v("reduce_sum", pr.col1, abscols[:, 0:nch], AX.X)
    em.mm(ps512[0:1, 0:1], pr.col1, onescol, start=True, stop=True)
    em.v("tensor_scalar", out11, ps512[0:1, 0:1], 1.0 / (rows * cols), None,
         ALU.mult)


def emit_l0_ffn(pr):
    em = pr.em
    nc = pr.nc
    H = 4096
    NHC = H // 128
    gdram = nc.dram_tensor(f"f0_g_{pr.uid}", [H, T], BF16, kind="Internal").ap()
    r2dram = nc.dram_tensor(f"f0_r2_{pr.uid}", [H, D], BF16, kind="Internal").ap()
    pr.uid += 1
    with pr.scope() as es2:
        sb = lambda n, s, d: es2.enter_context(nc.sbuf_tensor(n, s, d))
        ps = lambda n, s, d: es2.enter_context(nc.psum_tensor(n, s, d))
        q1T = sb("f0_q1T", [128, NDC * T], BF16)[:]
        pr.xnT = q1T
        onescol = sb("f0_ones", [128, 1], F32)
        g1 = sb("f0_g1", [1, 1], F32)
        g2 = sb("f0_g2", [1, 1], F32)
        t11 = sb("f0_t11", [1, 1], F32)
        thr1p = sb("f0_thr1p", [128, 1], F32)
        thr1n = sb("f0_thr1n", [128, 1], F32)
        thr2p = sb("f0_thr2p", [128, 1], F32)
        thr2n = sb("f0_thr2n", [128, 1], F32)
        geff1 = sb("f0_geff1", [1, 1], F32)
        geff2 = sb("f0_geff2", [1, 1], F32)
        comb1 = sb("f0_comb1", [128, T], F32)
        s2b = sb("f0_s2b", [128, T], F32)
        fin2 = sb("f0_fin2", [128, T], F32)
        rm2cols = sb("f0_rm2c", [128, NTT * NHC], F32)
        ps512 = ps("f0_ps512", [128, 512], F32)

        em.v("memset", onescol[:], 1.0)
        with pr.scope() as esG:
            stgG = esG.enter_context(nc.sbuf_tensor("f0_stgG", [128, 4096], F32))
            abscols = esG.enter_context(nc.sbuf_tensor("f0_absc", [128, 32], F32))
            emit_mean_abs(pr, "f0_w1T", D, H, g1[:], stgG, abscols, ps512, onescol[:])
            emit_mean_abs(pr, "f0_w2T", H, D, g2[:], stgG, abscols, ps512, onescol[:])
        for g, thrp, thrn, geff in ((g1, thr1p, thr1n, geff1),
                                    (g2, thr2p, thr2n, geff2)):
            em.v("tensor_scalar", g[:], g[:], 1e-5, None, ALU.max)
            em.v("reciprocal", t11[:], g[:])
            em.v("reciprocal", geff[:], t11[:])
            em.v("tensor_scalar", t11[:], g[:], 0.5, None, ALU.mult)
            bcast_1x1_to_col(pr, t11[:], thrp[:], ps512)
            em.v("tensor_scalar", thrn[:], thrp[:], -1.0, None, ALU.mult)

        with pr.scope() as es3:
            xn = es3.enter_context(nc.sbuf_tensor("f0_xn", [128, D], F32))
            vq = es3.enter_context(nc.sbuf_tensor("f0_vq", [128, D], F32))
            dq = es3.enter_context(nc.sbuf_tensor("f0_dq", [128, D], F32))
            rq = es3.enter_context(nc.sbuf_tensor("f0_rq", [128, D], F32))
            ci32 = es3.enter_context(nc.sbuf_tensor("f0_ci", [128, D], I32))
            qb = es3.enter_context(nc.sbuf_tensor("f0_qb", [128, D], BF16))
            row1 = es3.enter_context(nc.sbuf_tensor("f0_row1", [1, T], F32))
            ps_tr = es3.enter_context(nc.psum_tensor("f0_pstr", [128, NDC * 128], BF16))
            ps_th = es3.enter_context(nc.psum_tensor("f0_psth", [128, 512], F32))
            for tt in range(NTT):
                emit_ln_tile_f32(pr, tt, xn, dq)
                em.v("reduce_max", pr.col1, xn[:], AX.X, apply_absolute_value=True)
                em.v("tensor_scalar", pr.col1, pr.col1, 1e-5, None, ALU.max)
                em.v("reciprocal", pr.col2, pr.col1)
                em.v("tensor_scalar", pr.col2, pr.col2, 127.0, None, ALU.mult)
                em.v("tensor_scalar", vq[:], xn[:], pr.col2, None, ALU.mult)
                emit_neground2(pr, vq[:], ci32, dq, rq, qb[:], D)
                for dc in range(NDC):
                    em.t("transpose", ps_tr[:, dc * 128:(dc + 1) * 128],
                         qb[:, dc * 128:(dc + 1) * 128], pr.ident[:])
                em.v("tensor_copy",
                     q1T[:].rearrange("p (dc t) -> p dc t", t=T)[:, :, tt * 128:(tt + 1) * 128],
                     ps_tr[:].rearrange("p (dc t) -> p dc t", t=128))
                em.v("tensor_scalar", pr.col1, pr.col1, 1.0 / 127.0, None, ALU.mult)
                em.t("transpose", ps_th[0:1, 0:128], pr.col1, pr.ident128f)
                em.v("tensor_copy", row1[:, tt * 128:(tt + 1) * 128], ps_th[0:1, 0:128])
            em.v("tensor_scalar", row1[:], row1[:], geff1[:], None, ALU.mult)
            bcast_row_to_tile(pr, row1, comb1, ps512, T)

        with pr.scope() as es3:
            stg = es3.enter_context(nc.sbuf_tensor("f0_stgA", [128, 4096], F32))
            r1sl = es3.enter_context(nc.sbuf_tensor("f0_r1sl", [128, NDC * 128], BF16))
            r2sl = es3.enter_context(nc.sbuf_tensor("f0_r2sl", [128, D], BF16))
            hs = es3.enter_context(nc.sbuf_tensor("f0_hs", [128, T], F32))
            acc = es3.enter_context(nc.sbuf_tensor("f0_acc", [128, T], F32))
            gch = es3.enter_context(nc.sbuf_tensor("f0_gch", [128, T], BF16))
            cw = es3.enter_context(nc.sbuf_tensor("f0_cw", [128, 3], F32))
            cb = es3.enter_context(nc.sbuf_tensor("f0_cb", [128, 1], F32))
            junk = es3.enter_context(nc.sbuf_tensor("f0_junk", [128, 128], BF16))
            ps_h = es3.enter_context(nc.psum_tensor("f0_psh", [128, 2048], F32))
            ps_t2 = es3.enter_context(nc.psum_tensor("f0_pst2", [128, 2048], BF16))
            w1src = pr.din["f0_w1T"].rearrange("(dc p) f -> dc p f", p=128)
            w2src = pr.din["f0_w2T"].rearrange("(hc p) f -> hc p f", p=128)
            cwsrc = pr.din["f0_convw"]
            cbsrc = pr.din["f0_convb"]
            for hc in range(NHC):
                for dc in range(NDC):
                    em.dma(stg[:, dc * 128:(dc + 1) * 128],
                           w1src[dc, :, hc * 128:(hc + 1) * 128])
                em.v("tensor_scalar", stg[:, 1024:2048], stg[:, 0:1024], thr1p[:],
                     None, ALU.is_gt)
                em.v("scalar_tensor_tensor", r1sl[:], stg[:, 0:1024], thr1n[:],
                     stg[:, 1024:2048], ALU.is_lt, ALU.subtract)
                em.dma(stg[:, 0:1024], w2src[hc])
                em.v("tensor_scalar", stg[:, 1024:2048], stg[:, 0:1024], thr2p[:],
                     None, ALU.is_gt)
                em.v("scalar_tensor_tensor", r2sl[:], stg[:, 0:1024], thr2n[:],
                     stg[:, 1024:2048], ALU.is_lt, ALU.subtract)
                em.dma(r2dram.rearrange("(hc p) f -> hc p f", p=128)[hc], r2sl[:])
                for ts4 in range(4):
                    for dc in range(NDC):
                        em.mm(ps_h[:, ts4 * 512:(ts4 + 1) * 512],
                              r1sl[:, dc * 128:(dc + 1) * 128],
                              q1T[:, dc * T + ts4 * 512:dc * T + (ts4 + 1) * 512],
                              start=(dc == 0), stop=(dc == NDC - 1))
                em.v("tensor_tensor", hs[:], ps_h[:], comb1[:], ALU.mult)
                em.dma(cw[:], cwsrc[hc * 128:(hc + 1) * 128, :])
                em.dma(cb[:], cbsrc[hc * 128:(hc + 1) * 128, :])
                em.v("tensor_scalar", acc[:, 1:T], hs[:, 0:T - 1], cw[:, 0:1],
                     cb[:], ALU.mult, ALU.add)
                em.v("tensor_copy", acc[:, 0:1], cb[:])
                em.v("scalar_tensor_tensor", acc[:], hs[:], cw[:, 1:2], acc[:],
                     ALU.mult, ALU.add)
                em.v("scalar_tensor_tensor", acc[:, 0:T - 1], hs[:, 1:T], cw[:, 2:3],
                     acc[:, 0:T - 1], ALU.mult, ALU.add)
                em.a("activation", gch[:], acc[:], AF.Silu, bias=0.0, scale=1.0)
                em.dma(gdram.rearrange("(hc p) f -> hc p f", p=128)[hc], gch[:])
                for i in range(NTT):
                    em.t("transpose", ps_t2[:, (i % 8) * 128 + (i // 8) * 1024:
                                             (i % 8) * 128 + (i // 8) * 1024 + 128],
                         gch[:, i * 128:(i + 1) * 128], pr.ident[:])
                for i in range(NTT):
                    em.v("reduce_max", rm2cols[:, i * NHC + hc:i * NHC + hc + 1],
                         ps_t2[:, (i % 8) * 128 + (i // 8) * 1024:
                                  (i % 8) * 128 + (i // 8) * 1024 + 128],
                         AX.X, apply_absolute_value=True)

        with pr.scope() as es3:
            ps_th = es3.enter_context(nc.psum_tensor("f0_psth2", [128, 512], F32))
            row2 = es3.enter_context(nc.sbuf_tensor("f0_row2", [1, T], F32))
            row3 = es3.enter_context(nc.sbuf_tensor("f0_row3", [1, T], F32))
            for tt in range(NTT):
                em.v("reduce_max", pr.col1, rm2cols[:, tt * NHC:(tt + 1) * NHC], AX.X)
                em.v("tensor_scalar", pr.col1, pr.col1, 1e-5, None, ALU.max)
                em.v("reciprocal", pr.col2, pr.col1)
                em.v("tensor_scalar", pr.col2, pr.col2, 127.0, None, ALU.mult)
                em.v("tensor_scalar", pr.col1, pr.col1, 1.0 / 127.0, None, ALU.mult)
                em.t("transpose", ps_th[0:1, 0:128], pr.col1, pr.ident128f)
                em.v("tensor_copy", row2[:, tt * 128:(tt + 1) * 128], ps_th[0:1, 0:128])
                em.t("transpose", ps_th[0:1, 128:256], pr.col2, pr.ident128f)
                em.v("tensor_copy", row3[:, tt * 128:(tt + 1) * 128], ps_th[0:1, 128:256])
            em.v("tensor_scalar", row2[:], row2[:], geff2[:], None, ALU.mult)
            bcast_row_to_tile(pr, row3, s2b, ps512, T)
            bcast_row_to_tile(pr, row2, fin2, ps512, T)

        with pr.scope() as es3:
            gsl = es3.enter_context(nc.sbuf_tensor("f0_gsl", [128, 512], BF16))
            vq = es3.enter_context(nc.sbuf_tensor("f0_vq2", [128, 512], F32))
            dq = es3.enter_context(nc.sbuf_tensor("f0_dq2", [128, 512], F32))
            rq = es3.enter_context(nc.sbuf_tensor("f0_rq2", [128, 512], F32))
            ci32 = es3.enter_context(nc.sbuf_tensor("f0_ci2", [128, 512], I32))
            q2buf = es3.enter_context(nc.sbuf_tensor("f0_q2b", [128, 32 * 512], BF16))
            r2b = es3.enter_context(nc.sbuf_tensor("f0_r2b", [128, D], BF16))
            fout = es3.enter_context(nc.sbuf_tensor("f0_fout", [128, NDC * 512], BF16))
            ps_o = es3.enter_context(nc.psum_tensor("f0_pso", [128, 2048], F32))
            ps_tb = es3.enter_context(nc.psum_tensor("f0_pstb", [128, 1024], BF16))
            gsrc = gdram.rearrange("(hc p) f -> hc p f", p=128)
            r2src = r2dram.rearrange("(hc p) f -> hc p f", p=128)
            for tb in range(4):
                for hc in range(NHC):
                    em.dma(gsl[:], gsrc[hc, :, tb * 512:(tb + 1) * 512])
                    em.v("tensor_tensor", vq[:], gsl[:],
                         s2b[:, tb * 512:(tb + 1) * 512], ALU.mult)
                    emit_neground2(pr, vq[:], ci32, dq, rq,
                                   q2buf[:, hc * 512:(hc + 1) * 512], 512)
                for grp in range(2):
                    for hc in range(NHC):
                        em.dma(r2b[:], r2src[hc])
                        for dmi in range(4):
                            dmc = grp * 4 + dmi
                            em.mm(ps_o[:, dmi * 512:(dmi + 1) * 512],
                                  r2b[:, dmc * 128:(dmc + 1) * 128],
                                  q2buf[:, hc * 512:(hc + 1) * 512],
                                  start=(hc == 0), stop=(hc == NHC - 1))
                    for dmi in range(4):
                        dmc = grp * 4 + dmi
                        em.v("tensor_tensor", fout[:, dmc * 512:(dmc + 1) * 512],
                             ps_o[:, dmi * 512:(dmi + 1) * 512],
                             fin2[:, tb * 512:(tb + 1) * 512], ALU.mult)
                for dmc in range(NDC):
                    for i in range(4):
                        em.t("transpose", ps_tb[:, (i % 4) * 256:(i % 4) * 256 + 128],
                             fout[:, dmc * 512 + i * 128:dmc * 512 + (i + 1) * 128],
                             pr.ident[:])
                    for i in range(4):
                        tt = tb * 4 + i
                        em.v("tensor_tensor",
                             pr.xs[:, tt * D + dmc * 128:tt * D + (dmc + 1) * 128],
                             ps_tb[:, (i % 4) * 256:(i % 4) * 256 + 128],
                             pr.xs[:, tt * D + dmc * 128:tt * D + (dmc + 1) * 128],
                             ALU.add)


def emit_ln_tile_f32(pr, tt, xn_f32, dump):
    em = pr.em
    xst = pr.xs[:, tt * D:(tt + 1) * D]
    em.v("reduce_sum", pr.col1, xst, AX.X)
    em.v("tensor_scalar", pr.col1, pr.col1, -1.0 / D, None, ALU.mult)
    em.a("activation", dump[:, 0:D], xst, AF.Square, bias=pr.col1, scale=1.0,
         accum_out=pr.col2)
    em.v("tensor_scalar", pr.col2, pr.col2, 1.0 / D, 1e-5, ALU.mult, ALU.add)
    em.a("activation", pr.col3, pr.col2, AF.Ln, bias=0.0, scale=1.0)
    em.a("activation", pr.col3, pr.col3, AF.Exp, bias=0.0, scale=-0.5)
    em.v("tensor_scalar", xn_f32[:], xst, pr.col1, pr.col3, ALU.add, ALU.mult)



def emit_l1_ffn(pr):
    em = pr.em
    nc = pr.nc
    H = 4096
    with pr.scope() as es2:
        sb = lambda n, s, d: es2.enter_context(nc.sbuf_tensor(n, s, d))
        pr.xnT = sb("f1_xnT", [128, NDC * T], BF16)[:]
        emit_ln_all(pr)
        onescol = sb("f1_ones", [128, 1], F32)
        g1 = sb("f1_g1", [1, 1], F32)
        g2 = sb("f1_g2", [1, 1], F32)
        t11 = sb("f1_t11", [1, 1], F32)
        thr1p = sb("f1_thr1p", [128, 1], F32)
        thr1n = sb("f1_thr1n", [128, 1], F32)
        thr2p = sb("f1_thr2p", [128, 1], F32)
        thr2n = sb("f1_thr2n", [128, 1], F32)
        em.v("memset", onescol[:], 1.0)
        with pr.scope() as esG:
            ps512 = esG.enter_context(nc.psum_tensor("f1_ps512", [128, 512], F32))
            stgG = esG.enter_context(nc.sbuf_tensor("f1_stgG", [128, 4096], F32))
            abscols = esG.enter_context(nc.sbuf_tensor("f1_absc", [128, 32], F32))
            emit_mean_abs(pr, "f1_w1T", D, H, g1[:], stgG, abscols, ps512, onescol[:])
            emit_mean_abs(pr, "f1_w2T", H, D, g2[:], stgG, abscols, ps512, onescol[:])
            for g, thrp, thrn in ((g1, thr1p, thr1n), (g2, thr2p, thr2n)):
                em.v("tensor_scalar", t11[:], g[:], 1e-5, 0.5, ALU.add, ALU.mult)
                bcast_1x1_to_col(pr, t11[:], thrp[:], ps512)
                em.v("tensor_scalar", thrn[:], thrp[:], -1.0, None, ALU.mult)

        w1src = pr.din["f1_w1T"].rearrange("(dc p) f -> dc p f", p=128)
        w2src = pr.din["f1_w2T"].rearrange("(hc p) f -> hc p f", p=128)
        with pr.scope() as es3:
            r1h = es3.enter_context(nc.sbuf_tensor("f1_r1h", [128, NDC * 2048], BF16))
            r2h = es3.enter_context(nc.sbuf_tensor("f1_r2h", [128, 16 * D], BF16))
            stgq = es3.enter_context(nc.sbuf_tensor("f1_stgq", [128, 2048], F32))
            b1rf = es3.enter_context(nc.sbuf_tensor("f1_b1rf", [1, 2048], F32))
            b1rb = es3.enter_context(nc.sbuf_tensor("f1_b1rb", [1, 2048], BF16))
            b2rf = es3.enter_context(nc.sbuf_tensor("f1_b2rf", [1, D], F32))
            b2rb = es3.enter_context(nc.sbuf_tensor("f1_b2rb", [1, D], BF16))
            hneg = es3.enter_context(nc.sbuf_tensor("f1_hneg", [128, 2048], BF16))
            hT = es3.enter_context(nc.sbuf_tensor("f1_hT", [128, 2048], BF16))
            ps_h = es3.enter_context(nc.psum_tensor("f1_psh", [128, 2048], F32))
            ps_tr = es3.enter_context(nc.psum_tensor("f1_pstr", [128, 1024], BF16))
            ps_o = es3.enter_context(nc.psum_tensor("f1_pso", [128, 1024], F32))
            em.dma(b2rf[:], pr.din["f1_b2"])
            em.v("tensor_copy", b2rb[:], b2rf[:])
            for half in range(2):
                h0 = half * 2048
                for dc in range(NDC):
                    for piece in range(2):
                        sl = stgq[:, 0:1024]
                        em.dma(sl, w1src[dc, :, h0 + piece * 1024:h0 + (piece + 1) * 1024])
                        em.v("tensor_scalar", stgq[:, 1024:2048], sl, thr1p[:],
                             None, ALU.is_gt)
                        em.v("scalar_tensor_tensor",
                             r1h[:, dc * 2048 + piece * 1024:dc * 2048 + (piece + 1) * 1024],
                             sl, thr1n[:], stgq[:, 1024:2048], ALU.is_lt, ALU.subtract)
                for hc in range(16):
                    sl = stgq[:, 0:1024]
                    em.dma(sl, w2src[half * 16 + hc])
                    em.v("tensor_scalar", stgq[:, 1024:2048], sl, thr2p[:],
                         None, ALU.is_gt)
                    em.v("scalar_tensor_tensor", r2h[:, hc * D:(hc + 1) * D],
                         sl, thr2n[:], stgq[:, 1024:2048], ALU.is_lt, ALU.subtract)
                em.dma(b1rf[:], pr.din["f1_b1"][:, h0:h0 + 2048])
                em.v("tensor_scalar", b1rb[:], b1rf[:], -1.0, None, ALU.mult)
                for tt in range(NTT):
                    for hs4 in range(4):
                        for dc in range(NDC):
                            em.mm(ps_h[:, hs4 * 512:(hs4 + 1) * 512],
                                  pr.xnT[:, dc * T + tt * 128:dc * T + (tt + 1) * 128],
                                  r1h[:, dc * 2048 + hs4 * 512:dc * 2048 + (hs4 + 1) * 512],
                                  start=(dc == 0), stop=False)
                        em.mm(ps_h[:, hs4 * 512:(hs4 + 1) * 512], pr.ones_row_b,
                              b1rb[:, hs4 * 512:(hs4 + 1) * 512], start=False, stop=True)
                    em.v("tensor_scalar", hneg[:], ps_h[:], 0.0, None, ALU.min)
                    for grp8 in range(2):
                        for i8 in range(8):
                            i = grp8 * 8 + i8
                            em.t("transpose", ps_tr[:, i8 * 128:(i8 + 1) * 128],
                                 hneg[:, i * 128:(i + 1) * 128], pr.ident[:])
                        em.v("tensor_copy", hT[:, grp8 * 1024:(grp8 + 1) * 1024], ps_tr[:])
                    for os_ in range(2):
                        for hc in range(16):
                            em.mm(ps_o[:, os_ * 512:(os_ + 1) * 512],
                                  hT[:, hc * 128:(hc + 1) * 128],
                                  r2h[:, hc * D + os_ * 512:hc * D + (os_ + 1) * 512],
                                  start=(hc == 0), stop=(half == 1 and hc == 15))
                        if half == 0:
                            em.mm(ps_o[:, os_ * 512:(os_ + 1) * 512], pr.ones_row_b,
                                  b2rb[:, os_ * 512:(os_ + 1) * 512], start=False,
                                  stop=True)
                    em.v("tensor_tensor", pr.xs[:, tt * D:(tt + 1) * D], ps_o[:],
                         pr.xs[:, tt * D:(tt + 1) * D], ALU.add)



import os
_KPHASE = int(os.environ.get("KPHASE", "9"))


def emit_l1_attn_b(pr):
    em = pr.em
    nc = pr.nc
    with pr.scope() as es2:
        sb = lambda n, s, d: es2.enter_context(nc.sbuf_tensor(n, s, d))
        pr.xnT = sb("db_xnT", [128, NDC * T], BF16)[:]
        negm16 = sb("db_negm", [128, 16], F32)
        rstd16 = sb("db_rstd", [128, 16], F32)
        emit_ln_all(pr, stats=(negm16, rstd16))
        gated = sb("db_gated", [128, NDC * T], BF16)
        dec8 = sb("db_dec8", [128, 8], F32)
        nwb = sb("db_nwb", [128, D], F32)
        em.dma(dec8[:], pr.din["db_logdec"])
        em.a("activation", dec8[:], dec8[:], AF.Exp, bias=0.0, scale=1.0)
        with pr.scope() as es3:
            nwrow = es3.enter_context(nc.sbuf_tensor("db_nwrow", [1, D], F32))
            ps512 = es3.enter_context(nc.psum_tensor("db_ps512", [128, 512], F32))
            em.dma(nwrow[:], pr.din["db_normw"])
            bcast_row_to_tile(pr, nwrow, nwb, ps512, D)
        if _KPHASE < 1:
            return
        with pr.scope() as es3:
            acc1 = es3.enter_context(nc.sbuf_tensor("db_acc1", [128, T], F32))
            acc2 = es3.enter_context(nc.sbuf_tensor("db_acc2", [128, T], F32))
            gs = es3.enter_context(nc.sbuf_tensor("db_gs", [128, T], BF16))
            cs = es3.enter_context(nc.sbuf_tensor("db_cs", [128, T], BF16))
            cw7 = es3.enter_context(nc.sbuf_tensor("db_cw7", [128, 7], F32))
            cb1 = es3.enter_context(nc.sbuf_tensor("db_cb1", [128, 1], F32))
            decb = es3.enter_context(nc.sbuf_tensor("db_decb", [128, T], F32))
            for c in range(NDC):
                xc = pr.xnT[:, c * T:(c + 1) * T]
                for (wname, bname, acc, outb, fn) in (
                        ("db_gatew", "db_gateb", acc1, gs, AF.Sigmoid),
                        ("db_candw", "db_candb", acc2, cs, AF.Tanh)):
                    em.dma(cw7[:], pr.din[wname][c * 128:(c + 1) * 128, :])
                    em.dma(cb1[:], pr.din[bname][c * 128:(c + 1) * 128, :])
                    em.v("tensor_scalar", acc[:], xc, cw7[:, 3:4], cb1[:],
                         ALU.mult, ALU.add)
                    for k in (0, 1, 2, 4, 5, 6):
                        sh = k - 3
                        if sh < 0:
                            em.v("scalar_tensor_tensor", acc[:, -sh:T],
                                 xc[:, 0:T + sh], cw7[:, k:k + 1],
                                 acc[:, -sh:T], ALU.mult, ALU.add)
                        else:
                            em.v("scalar_tensor_tensor", acc[:, 0:T - sh],
                                 xc[:, sh:T], cw7[:, k:k + 1],
                                 acc[:, 0:T - sh], ALU.mult, ALU.add)
                    em.a("activation", outb[:], acc[:], fn, bias=0.0, scale=1.0)
                gc = gated[:, c * T:(c + 1) * T]
                em.v("tensor_tensor", gc, gs[:], cs[:], ALU.mult)
                if _KPHASE >= 2:
                    em.v("tensor_scalar", decb[:], acc1[:], 0.0, dec8[:, c:c + 1],
                         ALU.mult, ALU.add)
                    em.v("tensor_tensor_scan", gc, decb[:], gc, 0.0, ALU.mult, ALU.add)
        if _KPHASE < 3:
            return
        with pr.scope() as es3:
            pr.es, saved_es = es3, pr.es
            wo = load_weight_bf16(pr, "db_wob", "db_woutT", D, D)
            pr.es = saved_es
            boutf = es3.enter_context(nc.sbuf_tensor("db_boutf", [1, D], F32))
            boutb = es3.enter_context(nc.sbuf_tensor("db_boutb", [1, D], BF16))
            yb = es3.enter_context(nc.sbuf_tensor("db_yb", [128, D], F32))
            u = es3.enter_context(nc.sbuf_tensor("db_u", [128, D], F32))
            dump = es3.enter_context(nc.sbuf_tensor("db_dump", [128, D], F32))
            ps_o = es3.enter_context(nc.psum_tensor("db_pso", [128, 1024], F32))
            em.dma(boutf[:], pr.din["db_bout"])
            em.v("tensor_copy", boutb[:], boutf[:])
            for tt in range(NTT):
                for os_ in range(2):
                    for c in range(NDC):
                        em.mm(ps_o[:, os_ * 512:(os_ + 1) * 512],
                              gated[:, c * T + tt * 128:c * T + (tt + 1) * 128],
                              wo[:, c * D + os_ * 512:c * D + (os_ + 1) * 512],
                              start=(c == 0), stop=False)
                    em.mm(ps_o[:, os_ * 512:(os_ + 1) * 512], pr.ones_row_b,
                          boutb[:, os_ * 512:(os_ + 1) * 512], start=False, stop=True)
                em.v("tensor_scalar", u[:], pr.xs[:, tt * D:(tt + 1) * D],
                     negm16[:, tt:tt + 1], rstd16[:, tt:tt + 1], ALU.add, ALU.mult)
                em.v("tensor_tensor", yb[:], ps_o[:], u[:], ALU.add)
                em.a("activation", dump[:], yb[:], AF.Square, bias=0.0, scale=1.0,
                     accum_out=pr.col2)
                em.v("tensor_scalar", pr.col2, pr.col2, 1.0 / D, 1.1920929e-07,
                     ALU.mult, ALU.add)
                em.a("activation", pr.col3, pr.col2, AF.Ln, bias=0.0, scale=1.0)
                em.a("activation", pr.col3, pr.col3, AF.Exp, bias=0.0, scale=-0.5)
                em.v("tensor_tensor", u[:], yb[:], nwb[:], ALU.mult)
                em.v("scalar_tensor_tensor", pr.xs[:, tt * D:(tt + 1) * D],
                     u[:], pr.col3, pr.xs[:, tt * D:(tt + 1) * D],
                     ALU.mult, ALU.add)



def emit_l1_attn_a(pr):
    em = pr.em
    nc = pr.nc
    DI = 2048
    NH = 32
    PP = 64
    zdram = nc.dram_tensor(f"sa_z_{pr.uid}", [T, DI], BF16, kind="Internal").ap()
    ydram = nc.dram_tensor(f"sa_y_{pr.uid}", [T, DI], BF16, kind="Internal").ap()
    xdram = nc.dram_tensor(f"sa_x_{pr.uid}", [DI, T], BF16, kind="Internal").ap()
    lscr = nc.dram_tensor(f"sa_l_{pr.uid}", [NH, 128], F32, kind="Internal").ap()
    pr.uid += 1
    with pr.scope() as es2:
        sb = lambda n, s, d: es2.enter_context(nc.sbuf_tensor(n, s, d))
        bc_sb = sb("sa_bc", [128, 2 * T], BF16)
        lam = sb("sa_lam", [32, T], F32)
        U = sb("sa_U", [32, T], F32)
        acol = sb("sa_acol", [32, 1], F32)
        dtb = sb("sa_dtbc", [32, 1], F32)
        ones32 = sb("sa_ones32", [32, 128], F32)
        em.v("memset", ones32[:], 1.0)
        em.dma(dtb[:], pr.din["sa_dtb"])
        em.dma(acol[:], pr.din["sa_alog"])
        em.a("activation", acol[:], acol[:], AF.Exp, bias=0.0, scale=1.0)
        em.v("tensor_scalar", acol[:], acol[:], -1.0, None, ALU.mult)

        with pr.scope() as es3:
            sb3 = lambda n, s, d: es3.enter_context(nc.sbuf_tensor(n, s, d))
            xbc = sb3("sa_xbc", [128, 16 * T], BF16)
            dtR = sb3("sa_dt", [32, T], F32)
            with pr.scope() as es4:
                xnT = es4.enter_context(nc.sbuf_tensor("sa_xnT", [128, NDC * T], BF16))
                pr.xnT = xnT[:]
                emit_ln_all(pr)
                psA = es4.enter_context(nc.psum_tensor("sa_psA", [128, 2048], F32))
                wstg = es4.enter_context(nc.sbuf_tensor("sa_wstg", [128, 512], F32))
                wslb = es4.enter_context(nc.sbuf_tensor("sa_wslb", [128, 512], BF16))
                zb = es4.enter_context(nc.sbuf_tensor("sa_zb", [128, 1024], BF16))
                wsrc = pr.din["sa_winT"].rearrange("(dc p) f -> dc p f", p=128)
                for t0 in range(0, NTT, 2):
                    for zh in range(2):
                        for zs in range(2):
                            for dc in range(NDC):
                                em.dma(wstg[:], wsrc[dc, :, zh * 1024 + zs * 512:
                                                       zh * 1024 + (zs + 1) * 512])
                                em.v("tensor_copy", wslb[:], wstg[:])
                                for i in range(2):
                                    em.mm(psA[:, i * 1024 + zs * 512:
                                              i * 1024 + (zs + 1) * 512],
                                          pr.xnT[:, dc * T + (t0 + i) * 128:
                                                 dc * T + (t0 + i + 1) * 128],
                                          wslb[:], start=(dc == 0), stop=(dc == NDC - 1))
                        for i in range(2):
                            em.v("tensor_copy", zb[:], psA[:, i * 1024:(i + 1) * 1024])
                            em.dma(zdram.rearrange("(tt p) f -> tt p f", p=128)
                                   [t0 + i, :, zh * 1024:(zh + 1) * 1024], zb[:])
                for fc in range(18):
                    for dc in range(NDC):
                        em.dma(wstg[:, 0:128], wsrc[dc, :, DI + fc * 128:DI + (fc + 1) * 128])
                        em.v("tensor_copy", wslb[:, 0:128], wstg[:, 0:128])
                        for ts4 in range(4):
                            em.mm(psA[:, ts4 * 512:(ts4 + 1) * 512],
                                  wslb[:, 0:128],
                                  pr.xnT[:, dc * T + ts4 * 512:dc * T + (ts4 + 1) * 512],
                                  start=(dc == 0), stop=(dc == NDC - 1))
                    dst = (xbc[:, fc * T:(fc + 1) * T] if fc < 16
                           else bc_sb[:, (fc - 16) * T:(fc - 15) * T])
                    em.v("tensor_copy", dst, psA[:])
                for dc in range(NDC):
                    em.dma(wstg[:, 0:32], wsrc[dc, :, DI + 2304:DI + 2304 + 32])
                    em.v("tensor_copy", wslb[:, 0:32], wstg[:, 0:32])
                    for ts4 in range(4):
                        em.mm(psA[0:32, ts4 * 512:(ts4 + 1) * 512],
                              wslb[:, 0:32],
                              pr.xnT[:, dc * T + ts4 * 512:dc * T + (ts4 + 1) * 512],
                              start=(dc == 0), stop=(dc == NDC - 1))
                em.v("tensor_scalar", dtR[:], psA[0:32, :], dtb[:], None, ALU.add)

            with pr.scope() as es4:
                accC = es4.enter_context(nc.sbuf_tensor("sa_accC", [128, T], F32))
                cw4 = es4.enter_context(nc.sbuf_tensor("sa_cw4", [128, 4], F32))
                cb1 = es4.enter_context(nc.sbuf_tensor("sa_cb1", [128, 1], F32))
                for fc in range(18):
                    xc = (xbc[:, fc * T:(fc + 1) * T] if fc < 16
                          else bc_sb[:, (fc - 16) * T:(fc - 15) * T])
                    em.dma(cw4[:], pr.din["sa_convw"][fc * 128:(fc + 1) * 128, :])
                    em.dma(cb1[:], pr.din["sa_convb"][fc * 128:(fc + 1) * 128, :])
                    em.v("tensor_scalar", accC[:], xc, cw4[:, 3:4], cb1[:],
                         ALU.mult, ALU.add)
                    for k in (0, 1, 2):
                        sh = 3 - k
                        em.v("scalar_tensor_tensor", accC[:, sh:T], xc[:, 0:T - sh],
                             cw4[:, k:k + 1], accC[:, sh:T], ALU.mult, ALU.add)
                    if fc < 16:
                        em.a("activation", xc, accC[:], AF.Silu, bias=0.0, scale=1.0)
                        em.dma(xdram.rearrange("(fc p) t -> fc p t", p=128)[fc], xc)
                    else:
                        dst = bc_sb[:, (fc - 16) * T:(fc - 15) * T]
                        em.a("activation", dst, accC[:], AF.Silu, bias=0.0, scale=1.0)

            with pr.scope() as es4:
                t32 = es4.enter_context(nc.sbuf_tensor("sa_t32", [32, T], F32))
                em.a("activation", t32[:], dtR[:], AF.Abs, bias=0.0, scale=1.0)
                em.a("activation", t32[:], t32[:], AF.Exp, bias=0.0, scale=-1.0)
                em.a("activation", t32[:], t32[:], AF.Ln, bias=1.0, scale=1.0)
                em.v("scalar_tensor_tensor", dtR[:], dtR[:], 0.0, t32[:],
                     ALU.max, ALU.add)
                em.v("tensor_scalar", t32[:], dtR[:], 1e-38, None, ALU.max)
                em.a("activation", U[:], t32[:], AF.Ln, bias=0.0, scale=1.0)
                em.v("tensor_scalar", t32[:], dtR[:], acol[:], None, ALU.mult)
                for c in range(NTT):
                    em.v("tensor_tensor_scan", lam[:, c * 128:(c + 1) * 128],
                         ones32[:], t32[:, c * 128:(c + 1) * 128], 0.0,
                         ALU.mult, ALU.add)
                em.v("tensor_sub", U[:], U[:], lam[:])

        with pr.scope() as es3:
            sb3 = lambda n, s, d: es3.enter_context(nc.sbuf_tensor(n, s, d))
            H = sb3("sa_H", [128, DI], F32)
            Hb = sb3("sa_Hb", [128, DI], BF16)
            Mw = sb3("sa_Mw", [128, NH * 128], BF16)
            lamM = sb3("sa_lamM", [128, NH * 128], F32)
            Xstage = sb3("sa_Xstage", [128, DI], BF16)
            Xtok = sb3("sa_Xtok", [128, DI], BF16)
            Xw = sb3("sa_Xw", [128, DI], BF16)
            Btok = sb3("sa_Btok", [128, 128], BF16)
            Gb = sb3("sa_Gb", [128, 128], BF16)
            Utok = sb3("sa_Utok", [128, 32], F32)
            Ltok = sb3("sa_Ltok", [128, 32], F32)
            lend = sb3("sa_lend", [128, 32], F32)
            wS = sb3("sa_wS", [128, 32], F32)
            gam = sb3("sa_gam", [128, 32], F32)
            expl = sb3("sa_expl", [128, 32], F32)
            Db32 = sb3("sa_Db32", [128, 32], F32)
            dcol = sb3("sa_dcol", [32, 1], F32)
            ycbf = sb3("sa_ycbf", [128, DI], BF16)
            lrow = sb3("sa_lrow", [1, 2048], F32)
            r11 = sb3("sa_r11", [1, 32], F32)
            psS = es3.enter_context(nc.psum_tensor("sa_psYS", [128, 2048], F32))
            psY = psS
            psT = es3.enter_context(nc.psum_tensor("sa_psT", [128, 2048], BF16))
            psF = es3.enter_context(nc.psum_tensor("sa_psF", [128, 512], F32))
            em.v("memset", H[:], 0.0)
            em.v("memset", Hb[:], 0.0)
            em.dma(dcol[:], pr.din["sa_d"])
            em.t("transpose", psF[0:1, 0:32], dcol[:], pr.ident32f)
            em.v("tensor_copy", r11[:], psF[0:1, 0:32])
            em.mm(psF[:, 32:64], pr.ones_row_f, r11[:], start=True, stop=True)
            em.v("tensor_copy", Db32[:], psF[:, 32:64])
            lam3 = lamM[:].rearrange("p (h t) -> p h t", t=128)
            Mw3 = Mw[:].rearrange("p (h t) -> p h t", t=128)
            for c in range(NTT):
                cs_ = slice(c * 128, (c + 1) * 128)
                em.t("transpose", psF[:, 32:64], U[:, cs_], pr.ident32f)
                em.v("tensor_copy", Utok[:], psF[:, 32:64])
                em.t("transpose", psF[:, 64:96], lam[:, cs_], pr.ident32f)
                em.v("tensor_copy", Ltok[:], psF[:, 64:96])
                em.t("transpose", psF[0:1, 96:128],
                     lam[:, c * 128 + 127:c * 128 + 128], pr.ident32f)
                em.v("tensor_copy", r11[:], psF[0:1, 96:128])
                em.mm(psF[:, 128:160], pr.ones_row_f, r11[:], start=True, stop=True)
                em.v("tensor_copy", lend[:], psF[:, 128:160])
                em.v("tensor_tensor", wS[:], Utok[:], lend[:], ALU.add)
                em.a("activation", wS[:], wS[:], AF.Exp, bias=0.0, scale=1.0)
                em.a("activation", gam[:], lend[:], AF.Exp, bias=0.0, scale=1.0)
                em.a("activation", expl[:], Ltok[:], AF.Exp, bias=0.0, scale=1.0)
                em.dma(lscr, lam[:, cs_])
                for half in range(2):
                    em.dma(lrow[:], lscr.rearrange("p t -> (p t)").rearrange(
                        "(one f) -> one f", one=1)[:, half * 2048:(half + 1) * 2048])
                    for q4 in range(4):
                        em.mm(psS[:, q4 * 512:(q4 + 1) * 512], pr.ones_row_f,
                              lrow[:, q4 * 512:(q4 + 1) * 512], start=True, stop=True)
                    em.v("tensor_copy", lamM[:, half * 2048:(half + 1) * 2048], psS[:])
                em.v("tensor_tensor", lam3,
                     pr.maskneg.unsqueeze(1).broadcast_to([128, NH, 128]),
                     lam3, ALU.add)
                em.v("tensor_tensor", lam3,
                     Utok[:].unsqueeze(2).broadcast_to([128, NH, 128]),
                     lam3, ALU.add)
                em.a("activation", Mw[:], lamM[:], AF.Exp, bias=0.0, scale=1.0)
                em.mm(psF[:, 0:128], bc_sb[:, cs_],
                      bc_sb[:, T + c * 128:T + (c + 1) * 128], start=True, stop=True)
                em.v("tensor_copy", Gb[:], psF[:, 0:128])
                em.v("tensor_tensor", Mw3, Gb[:].unsqueeze(1).broadcast_to([128, NH, 128]),
                     Mw3, ALU.mult)
                em.dma(Xstage[:].rearrange("p (i t) -> p i t", t=128),
                       xdram.rearrange("(i p) t -> p i t", p=128)[:, :, cs_])
                for i in range(16):
                    em.t("transpose", psT[:, i * 128:(i + 1) * 128],
                         Xstage[:, i * 128:(i + 1) * 128], pr.ident[:])
                em.v("tensor_copy", Xtok[:], psT[:])
                em.t("transpose", psT[:, 0:128], bc_sb[:, cs_], pr.ident[:])
                em.v("tensor_copy", Btok[:], psT[:, 0:128])
                em.v("tensor_tensor", Xw[:].rearrange("p (h w) -> p h w", w=PP),
                     Xtok[:].rearrange("p (h w) -> p h w", w=PP),
                     wS[:].unsqueeze(2).broadcast_to([128, NH, PP]), ALU.mult)
                XDv = lamM[:, 2048:4096].rearrange("p (h w) -> p h w", w=PP)
                em.v("tensor_tensor", XDv,
                     Xtok[:].rearrange("p (h w) -> p h w", w=PP),
                     Db32[:].unsqueeze(2).broadcast_to([128, NH, PP]), ALU.mult)
                for hh in range(2):
                    for hi in range(16):
                        h = hh * 16 + hi
                        em.mm(psY[:, hi * PP:(hi + 1) * PP],
                              Mw[:, h * 128:(h + 1) * 128],
                              Xtok[:, h * PP:(h + 1) * PP],
                              start=(hi % 8 == 0), stop=(hi % 8 == 7))
                        em.mm(psY[:, 1024 + hi * PP:1024 + (hi + 1) * PP],
                              bc_sb[:, T + c * 128:T + (c + 1) * 128],
                              Hb[:, h * PP:(h + 1) * PP],
                              start=(hi % 8 == 0), stop=(hi % 8 == 7))
                    ytmp = lamM[:, 0:1024]
                    em.v("tensor_tensor", ytmp.rearrange("p (h w) -> p h w", w=PP),
                         psY[:, 1024:2048].rearrange("p (h w) -> p h w", w=PP),
                         expl[:, hh * 16:(hh + 1) * 16].unsqueeze(2)
                         .broadcast_to([128, 16, PP]), ALU.mult)
                    em.v("tensor_tensor", ytmp, psY[:, 0:1024], ytmp, ALU.add)
                    em.v("tensor_tensor", ycbf[:, hh * 1024:(hh + 1) * 1024], ytmp,
                         lamM[:, 2048 + hh * 1024:2048 + (hh + 1) * 1024], ALU.add)
                em.dma(ydram.rearrange("(c p) f -> c p f", p=128)[c], ycbf[:])
                for h in range(NH):
                    em.mm(psS[:, h * PP:(h + 1) * PP], Btok[:],
                          Xw[:, h * PP:(h + 1) * PP],
                          start=(h % 8 == 0), stop=(h % 8 == 7))
                em.v("tensor_tensor", H[:].rearrange("p (h w) -> p h w", w=PP),
                     H[:].rearrange("p (h w) -> p h w", w=PP),
                     gam[:].unsqueeze(2).broadcast_to([128, NH, PP]), ALU.mult)
                em.v("tensor_tensor", H[:], psS[:], H[:], ALU.add)
                em.v("tensor_copy", Hb[:], H[:])

        with pr.scope() as es3:
            sb3 = lambda n, s, d: es3.enter_context(nc.sbuf_tensor(n, s, d))
            nw2b = sb3("sa_nw2b", [128, DI], F32)
            ytile = sb3("sa_ytile", [128, DI], BF16)
            ztile = sb3("sa_ztile", [128, DI], BF16)
            zs_ = sb3("sa_zs", [128, DI], F32)
            yz = sb3("sa_yz", [128, DI], F32)
            uu = sb3("sa_uu", [128, DI], F32)
            ynb = sb3("sa_ynb", [128, DI], BF16)
            ynT = sb3("sa_ynT", [128, DI], BF16)
            dump = sb3("sa_dump", [128, DI], F32)
            nwrow = sb3("sa_nwrow", [1, DI], F32)
            psO = es3.enter_context(nc.psum_tensor("sa_psO", [128, 1024], F32))
            psT2 = es3.enter_context(nc.psum_tensor("sa_psT2", [128, 2048], BF16))
            ps512b = es3.enter_context(nc.psum_tensor("sa_ps512b", [128, 512], F32))
            pr.es, saved_es = es3, pr.es
            wo = load_weight_bf16(pr, "sa_wob", "sa_woutT", DI, D)
            pr.es = saved_es
            em.dma(nwrow[:], pr.din["sa_normw"])
            bcast_row_to_tile(pr, nwrow, nw2b, ps512b, DI)
            for tt in range(NTT):
                em.dma(ytile[:], ydram.rearrange("(c p) f -> c p f", p=128)[tt])
                em.dma(ztile[:], zdram.rearrange("(c p) f -> c p f", p=128)[tt])
                em.a("activation", zs_[:], ztile[:], AF.Silu, bias=0.0, scale=1.0)
                em.v("tensor_tensor", yz[:], ytile[:], zs_[:], ALU.mult)
                em.a("activation", dump[:], yz[:], AF.Square, bias=0.0, scale=1.0,
                     accum_out=pr.col2)
                em.v("tensor_scalar", pr.col2, pr.col2, 1.0 / DI, 1e-5,
                     ALU.mult, ALU.add)
                em.a("activation", pr.col3, pr.col2, AF.Ln, bias=0.0, scale=1.0)
                em.a("activation", pr.col3, pr.col3, AF.Exp, bias=0.0, scale=-0.5)
                em.v("tensor_tensor", uu[:], yz[:], nw2b[:], ALU.mult)
                em.v("tensor_scalar", ynb[:], uu[:], pr.col3, None, ALU.mult)
                for i in range(16):
                    em.t("transpose", psT2[:, i * 128:(i + 1) * 128],
                         ynb[:, i * 128:(i + 1) * 128], pr.ident[:])
                em.v("tensor_copy", ynT[:], psT2[:])
                for os_ in range(2):
                    for ic in range(16):
                        em.mm(psO[:, os_ * 512:(os_ + 1) * 512],
                              ynT[:, ic * 128:(ic + 1) * 128],
                              wo[:, ic * D + os_ * 512:ic * D + (os_ + 1) * 512],
                              start=(ic == 0), stop=(ic == 15))
                em.v("tensor_tensor", pr.xs[:, tt * D:(tt + 1) * D], psO[:],
                     pr.xs[:, tt * D:(tt + 1) * D], ALU.add)


EMITTERS = {
    "l0aa": emit_l0_attn_a,
    "l0ab": emit_l0_attn_b,
    "l0ffn1": emit_l0_ffn,
    "l0ffn2": emit_l0_ffn,
    "l1ffn1": emit_l1_ffn,
    "l1ffn2": emit_l1_ffn,
    "l1ab": emit_l1_attn_b,
    "l1aa": emit_l1_attn_a,
}


def build_program(parts):
    nc = bass.Bass("TRN2", target_bir_lowering=False)
    x_in = nc.dram_tensor("x_in", [T, D], F32, kind="ExternalInput").ap()
    x_out = nc.dram_tensor("x_out", [T, D], F32, kind="ExternalOutput").ap()
    names = set(CONST_INPUTS)
    for p_ in parts:
        names.update(PART_INPUTS[p_])
    dins = {}
    for n in sorted(names):
        shp = PREP_SHAPES[n]
        dt = BF16 if n in ("c_ident128", "c_ones_row_b", "c_triu") else F32
        dins[n] = nc.dram_tensor(n, list(shp), dt, kind="ExternalInput").ap()

    em = E(nc)
    global _LAST_EM
    _LAST_EM = em
    with contextlib.ExitStack() as es:
        pr = Prog(NCProxy(nc), em, es, dins)
        setup_shared(pr)
        em.dma(pr.xs[:].rearrange("p (tt d) -> p tt d", d=D),
               x_in.rearrange("(tt p) d -> p tt d", p=128))
        for p_ in parts:
            em.mark(p_)
            EMITTERS[p_](pr)
        em.mark("end")
        em.dma(x_out.rearrange("(tt p) d -> p tt d", p=128),
               pr.xs[:].rearrange("p (tt d) -> p tt d", d=D))
        esem = {e: es.enter_context(nc.semaphore(f"SE_{e}")) for e in E.ENGS}
        dsem = [es.enter_context(nc.semaphore(f"SD_{i}")) for i in range(E.NDMA)]
        block = es.enter_context(nc.Block())
        em.replay(block, (esem, dsem))
    return nc, sorted(names)


PREP_SHAPES = {
    "c_ident128": (128, 128), "c_ident32f": (32, 32), "c_ident128f": (128, 128),
    "c_ones_row_f": (1, 128), "c_maskneg": (128, 128),
    "c_ones_row_b": (1, 128), "c_triu": (128, 128),
    "aa_wqT": (D, D), "aa_wkT": (D, D), "aa_wvT": (D, D),
    "ab_wqkv": (D, 3072), "ab_wout": (D, D), "ab_bout": (1, D), "ab_ls": (16, 1),
    "f0_w1T": (D, 4096), "f0_w2T": (4096, D), "f0_convw": (4096, 3), "f0_convb": (4096, 1),
    "sa_winT": (D, 4384), "sa_convw": (2304, 4), "sa_convb": (2304, 1),
    "sa_dtb": (32, 1), "sa_alog": (32, 1), "sa_d": (32, 1), "sa_normw": (1, 2048),
    "sa_woutT": (2048, D),
    "db_gatew": (D, 7), "db_gateb": (D, 1), "db_candw": (D, 7), "db_candb": (D, 1),
    "db_logdec": (128, 8), "db_woutT": (D, D), "db_bout": (1, D), "db_normw": (1, D),
    "f1_w1T": (D, 4096), "f1_b1": (1, 4096), "f1_w2T": (4096, D), "f1_b2": (1, D),
}


def run_parts(x_full, prepped, parts, sim=False):
    nc, names = build_program(parts)
    in_maps = []
    ncores = 1 if sim else 8
    for b in range(ncores):
        m = {"x_in": np.ascontiguousarray(x_full[b])}
        for n in names:
            arr = prepped[n]
            want_bf = n in ("c_ident128", "c_ones_row_b", "c_triu")
            m[n] = np.ascontiguousarray(arr.astype(bf16 if want_bf else np.float32))
        in_maps.append(m)
    if sim:
        from concourse.bass_interp import MultiCoreSim
        nc.finalize()
        s = MultiCoreSim(nc, num_cores=1, num_workers=1, trace=False)
        for k, v in in_maps[0].items():
            s.cores[0].tensor(k)[:] = v
        s.simulate()
        return np.asarray(s.cores[0].tensor("x_out"))[None]
    res = run_bass_kernel_spmd(nc, in_maps, core_ids=list(range(8)))
    return np.stack([r["x_out"] for r in res.results])


def kernel(**inputs):
    x = np.ascontiguousarray(np.asarray(inputs["L0_attn_a_z"], dtype=np.float32))
    prepped = prep_inputs(inputs)
    return run_parts(x, prepped, ALL_PARTS)


# version 15
# speedup vs baseline: 1.0804x; 1.0804x over previous
import contextlib
import numpy as np
import ml_dtypes
import concourse.bass as bass
import concourse.mybir as mybir
from concourse.bass_utils import run_bass_kernel_spmd

F32 = mybir.dt.float32
BF16 = mybir.dt.bfloat16
I32 = mybir.dt.int32
AF = mybir.ActivationFunctionType
ALU = mybir.AluOpType
AX = mybir.AxisListType
bf16 = ml_dtypes.bfloat16

T = 2048
D = 1024
NTT = T // 128
NDC = D // 128

ALL_PARTS = ["l0aa", "l0ffn1", "l0ab", "l0ffn2", "l1aa", "l1ffn1", "l1ab", "l1ffn2"]
_LAST_EM = None


class E:

    ENGS = ("vector", "scalar", "tensor", "gpsimd", "sync")
    NDMA = 16

    def __init__(self, nc):
        self.nc = nc
        self.ops = {e: [] for e in self.ENGS}
        self.cnt = {e: 0 for e in self.ENGS}
        self.dma_uses = [0] * self.NDMA
        self.dma_next = 0
        self.last_writer = {}
        self.readers = {}

    @staticmethod
    def _aps(x):
        return [a for a in x if type(a).__name__ == "AP"]

    SAME_ENGINE_WAITS = True

    def _dep_wait(self, waits, dep, eng=None):
        kind = dep[0]
        if kind == "E":
            _, deng, dcnt = dep
            if deng == eng and not self.SAME_ENGINE_WAITS:
                return
            key = ("E", deng)
            if waits.get(key, 0) < dcnt:
                waits[key] = dcnt
        else:
            _, slot, val = dep
            key = ("D", slot)
            if waits.get(key, 0) < val:
                waits[key] = val

    def mark(self, label):
        self.marks = getattr(self, "marks", [])
        self.marks.append((label, dict(self.cnt), sum(self.dma_uses)))

    def barrier(self):
        self._barrier_tokens = (
            [("E", e, c) for e, c in self.cnt.items() if c > 0] +
            [("D", s, 16 * u) for s, u in enumerate(self.dma_uses) if u > 0])
        self._barrier_done = set()
        self.last_writer = {}
        self.readers = {}

    def _desc(self, eng, fn, args, kwargs, dma=False):
        aps = self._aps(args) + self._aps(list(kwargs.values()))
        if dma:
            writes = [kwargs["out"]] if "out" in kwargs else [args[0]]
            reads = [a for a in aps if a is not writes[0]]
        else:
            writes = aps[:1]
            if kwargs.get("accum_out") is not None:
                writes = writes + [kwargs["accum_out"]]
            reads = [a for a in aps if all(a is not w for w in writes)]
        waits = {}
        if getattr(self, "_barrier_tokens", None) and eng not in self._barrier_done:
            for dep in self._barrier_tokens:
                self._dep_wait(waits, dep, None)
            self._barrier_done.add(eng)
        rnames = [a.name for a in reads]
        wnames = [a.name for a in writes]
        for t in rnames:
            w = self.last_writer.get(t)
            if w is not None:
                self._dep_wait(waits, w, eng)
        for t in wnames:
            w = self.last_writer.get(t)
            if w is not None:
                self._dep_wait(waits, w, eng)
            for r in self.readers.get(t, ()):
                self._dep_wait(waits, r, eng)
        if dma:
            slot = self.dma_next
            self.dma_next = (self.dma_next + 1) % self.NDMA
            if self.dma_uses[slot] > 0:
                key = ("D", slot)
                if waits.get(key, 0) < 16 * self.dma_uses[slot]:
                    waits[key] = 16 * self.dma_uses[slot]
            self.dma_uses[slot] += 1
            tok = ("D", slot, 16 * self.dma_uses[slot])
            inc = ("D", slot)
        else:
            self.cnt[eng] += 1
            tok = ("E", eng, self.cnt[eng])
            inc = ("E", eng)
        for t in rnames:
            self.readers.setdefault(t, []).append(tok)
        for t in wnames:
            self.last_writer[t] = tok
            self.readers[t] = []
        self.ops[eng].append((waits, fn, inc))

    def v(self, meth, *a, **k):
        self._desc("vector", lambda e, a=a, k=k, m=meth: getattr(e, m)(*a, **k), a, k)

    def a(self, meth, *a, **k):
        self._desc("scalar", lambda e, a=a, k=k, m=meth: getattr(e, m)(*a, **k), a, k)

    def t(self, meth, *a, **k):
        self._desc("tensor", lambda e, a=a, k=k, m=meth: getattr(e, m)(*a, **k), a, k)

    def g(self, meth, *a, **k):
        self._desc("gpsimd", lambda e, a=a, k=k, m=meth: getattr(e, m)(*a, **k), a, k)

    def dma(self, out, in_, eng=None):
        if eng is None:
            eng = ("sync", "gpsimd")[self.dma_next % 2]
        self._desc(eng, lambda e, o=out, i=in_: e.dma_start(out=o, in_=i),
                   (), {"out": out, "in_": in_}, dma=True)

    def dma_t(self, out, in_, eng=None):
        if eng is None:
            eng = "sync"
        self._desc(eng, lambda e, o=out, i=in_: e.dma_start(out=o, in_=i, transpose=True),
                   (), {"out": out, "in_": in_}, dma=True)

    def mm(self, out, lhsT, rhs, start, stop, tile_position=None):
        self._desc("tensor",
                   lambda e, o=out, l=lhsT, r=rhs, s1=start, s2=stop, tp=tile_position:
                   e.matmul(o, l, r, start=s1, stop=s2, tile_position=tp),
                   (out, lhsT, rhs), {})

    def replay(self, block, sems):
        esem, dsem = sems
        total_e = dict(self.cnt)
        total_d = list(self.dma_uses)

        for name in self.ENGS:
            ops = self.ops[name]

            def section(eng, ops=ops, name=name):
                for (waits, fn, inc) in ops:
                    for (key, val) in sorted(waits.items(), key=str):
                        if key[0] == "E":
                            eng.wait_ge(esem[key[1]], val)
                        else:
                            eng.wait_ge(dsem[key[1]], val)
                    inst = fn(eng)
                    if inc[0] == "E":
                        inst.then_inc(esem[inc[1]], 1)
                    else:
                        inst.then_inc(dsem[inc[1]], 16)
                if name == "sync":
                    for e2, c in total_e.items():
                        if c > 0:
                            eng.wait_ge(esem[e2], c)
                    for s, u in enumerate(total_d):
                        if u > 0:
                            eng.wait_ge(dsem[s], 16 * u)

            getattr(block, name)(section)


def prep_inputs(inp):
    p = {}
    c = np.ascontiguousarray
    p["c_ident128"] = np.eye(128, dtype=bf16)
    p["c_ident32f"] = np.eye(32, dtype=np.float32)
    p["c_ident128f"] = np.eye(128, dtype=np.float32)
    p["c_ones_row_f"] = np.ones((1, 128), dtype=np.float32)
    p["c_ones_row_b"] = np.ones((1, 128), dtype=bf16)
    p["c_triu"] = np.triu(np.ones((128, 128), dtype=np.float32)).astype(bf16)
    mn = np.zeros((128, 128), dtype=np.float32)
    mn[np.tril_indices(128, -1)] = -1e30
    p["c_maskneg"] = mn

    W = np.asarray(inp["L0_attn_a_W_qkv"]).reshape(16, 3, 64, D)
    p["aa_wqT"] = c(W[:, 0].reshape(D, D).T)
    p["aa_wkT"] = c(W[:, 1].reshape(D, D).T)
    p["aa_wvT"] = c(W[:, 2].reshape(D, D).T)

    p["ab_wqkv"] = c(np.asarray(inp["L0_attn_b_w_qkv"]))
    p["ab_wout"] = c(np.asarray(inp["L0_attn_b_w_out"]))
    p["ab_bout"] = c(np.asarray(inp["L0_attn_b_b_out"]).reshape(1, D))
    p["ab_ls"] = c(np.asarray(inp["L0_attn_b_logit_scale"]).reshape(16, 1))

    p["f0_w1T"] = c(np.asarray(inp["L0_ffn_w1"]).T)
    p["f0_w2T"] = c(np.asarray(inp["L0_ffn_w2"]).T)
    p["f0_convw"] = c(np.asarray(inp["L0_ffn_conv_w"]).reshape(4096, 3))
    p["f0_convb"] = c(np.asarray(inp["L0_ffn_conv_b"]).reshape(4096, 1))

    p["sa_winT"] = c(np.asarray(inp["L1_attn_a_W_in"]).T)
    p["sa_convw"] = c(np.asarray(inp["L1_attn_a_conv_w"]))
    p["sa_convb"] = c(np.asarray(inp["L1_attn_a_conv_b"]).reshape(2304, 1))
    p["sa_dtb"] = c(np.asarray(inp["L1_attn_a_dt_bias"]).reshape(32, 1))
    p["sa_alog"] = c(np.asarray(inp["L1_attn_a_A_log"]).reshape(32, 1))
    p["sa_d"] = c(np.asarray(inp["L1_attn_a_D"]).reshape(32, 1))
    p["sa_normw"] = c(np.asarray(inp["L1_attn_a_norm_w"]).reshape(1, 2048))
    p["sa_woutT"] = c(np.asarray(inp["L1_attn_a_W_out"]).T)

    p["db_gatew"] = c(np.asarray(inp["L1_attn_b_gate_w"]).reshape(D, 7))
    p["db_gateb"] = c(np.asarray(inp["L1_attn_b_gate_b"]).reshape(D, 1))
    p["db_candw"] = c(np.asarray(inp["L1_attn_b_cand_w"]).reshape(D, 7))
    p["db_candb"] = c(np.asarray(inp["L1_attn_b_cand_b"]).reshape(D, 1))
    p["db_logdec"] = c(np.asarray(inp["L1_attn_b_log_decay"]).reshape(8, 128).T)
    p["db_woutT"] = c(np.asarray(inp["L1_attn_b_out_w"]).T)
    p["db_bout"] = c(np.asarray(inp["L1_attn_b_out_b"]).reshape(1, D))
    p["db_normw"] = c(np.asarray(inp["L1_attn_b_norm_w"]).reshape(1, D))

    p["f1_w1T"] = c(np.asarray(inp["L1_ffn_W1"]).T)
    p["f1_b1"] = c(np.asarray(inp["L1_ffn_b1"]).reshape(1, 4096))
    p["f1_w2T"] = c(np.asarray(inp["L1_ffn_W2"]).T)
    p["f1_b2"] = c(np.asarray(inp["L1_ffn_b2"]).reshape(1, D))
    return p


PART_INPUTS = {
    "l0aa": ["aa_wqT", "aa_wkT", "aa_wvT"],
    "l0ab": ["ab_wqkv", "ab_wout", "ab_bout", "ab_ls"],
    "l0ffn1": ["f0_w1T", "f0_w2T", "f0_convw", "f0_convb"],
    "l0ffn2": ["f0_w1T", "f0_w2T", "f0_convw", "f0_convb"],
    "l1aa": ["sa_winT", "sa_convw", "sa_convb", "sa_dtb", "sa_alog", "sa_d",
             "sa_normw", "sa_woutT"],
    "l1ab": ["db_gatew", "db_gateb", "db_candw", "db_candb", "db_logdec",
             "db_woutT", "db_bout", "db_normw"],
    "l1ffn1": ["f1_w1T", "f1_b1", "f1_w2T", "f1_b2"],
    "l1ffn2": ["f1_w1T", "f1_b1", "f1_w2T", "f1_b2"],
}
CONST_INPUTS = ["c_ident128", "c_ident32f", "c_ident128f", "c_ones_row_f",
                "c_ones_row_b", "c_triu", "c_maskneg"]


class NCProxy:

    def __init__(self, nc):
        self._nc = nc
        self._used = {}

    def _uniq(self, name):
        n = self._used.get(name, 0)
        self._used[name] = n + 1
        return name if n == 0 else f"{name}__{n}"

    def sbuf_tensor(self, name, *a, **k):
        return self._nc.sbuf_tensor(self._uniq(name), *a, **k)

    def psum_tensor(self, name, *a, **k):
        return self._nc.psum_tensor(self._uniq(name), *a, **k)

    def dram_tensor(self, name, *a, **k):
        return self._nc.dram_tensor(self._uniq(name), *a, **k)

    def __getattr__(self, a):
        return getattr(self._nc, a)


class Prog:

    def __init__(self, nc, em, es, dins):
        self.nc = nc
        self.em = em
        self.es = es
        self.din = dins
        self.sb = {}
        self.uid = 0

    def sbuf(self, name, shape, dtype):
        t = self.es.enter_context(self.nc.sbuf_tensor(name, shape, dtype))
        return t

    def psum(self, name, shape, dtype):
        return self.es.enter_context(self.nc.psum_tensor(name, shape, dtype))

    @contextlib.contextmanager
    def scope(self):
        with contextlib.ExitStack() as es:
            yield es
        self.em.barrier()


def setup_shared(pr):
    em = pr.em
    pr.xs = pr.sbuf("xs", [128, NTT * D], F32)[:]
    pr.ident = pr.sbuf("ident", [128, 128], BF16)[:]
    pr.ident32f = pr.sbuf("ident32f", [32, 32], F32)[:]
    pr.ident128f = pr.sbuf("ident128f", [128, 128], F32)[:]
    pr.ones_row_f = pr.sbuf("ones_row_f", [1, 128], F32)[:]
    pr.ones_row_b = pr.sbuf("ones_row_b", [1, 128], BF16)[:]
    pr.triu = pr.sbuf("triu", [128, 128], BF16)[:]
    pr.maskneg = pr.sbuf("maskneg", [128, 128], F32)[:]
    pr.col1 = pr.sbuf("col1", [128, 1], F32)[:]
    pr.col2 = pr.sbuf("col2", [128, 1], F32)[:]
    pr.col3 = pr.sbuf("col3", [128, 1], F32)[:]
    em.dma(pr.ident[:], pr.din["c_ident128"])
    em.dma(pr.ident32f[:], pr.din["c_ident32f"])
    em.dma(pr.ident128f[:], pr.din["c_ident128f"])
    em.dma(pr.ones_row_f[:], pr.din["c_ones_row_f"])
    em.dma(pr.ones_row_b[:], pr.din["c_ones_row_b"])
    em.dma(pr.triu[:], pr.din["c_triu"])
    em.dma(pr.maskneg[:], pr.din["c_maskneg"])


def load_weight_bf16(pr, name, dram_name, rows, cols, tag=None):
    em = pr.em
    nch = rows // 128
    wb = pr.sbuf(name, [128, nch * cols], BF16)
    src = pr.din[dram_name].rearrange("(c p) f -> c p f", p=128)
    with pr.scope() as es_w:
        stg = es_w.enter_context(pr.nc.sbuf_tensor(name + "_stg", [128, cols], F32))
        for ci in range(nch):
            em.dma(stg[:], src[ci])
            em.v("tensor_copy", wb[:, ci * cols:(ci + 1) * cols], stg[:])
    return wb


def emit_ln_tile(pr, tt, xn_tile, ps_tr, dump, to_xnT=True, xn_f32=None,
                 cols=None):
    em = pr.em
    c1, c2, c3 = cols if cols is not None else (pr.col1, pr.col2, pr.col3)
    xst = pr.xs[:, tt * D:(tt + 1) * D]
    em.v("reduce_sum", c1, xst, AX.X)
    em.v("tensor_scalar", c1, c1, -1.0 / D, None, ALU.mult)
    em.a("activation", dump[:, 0:D], xst, AF.Square, bias=c1, scale=1.0,
         accum_out=c2)
    em.v("tensor_scalar", c2, c2, 1.0 / D, 1e-5, ALU.mult, ALU.add)
    em.a("activation", c3, c2, AF.Ln, bias=0.0, scale=1.0)
    em.a("activation", c3, c3, AF.Exp, bias=0.0, scale=-0.5)
    em.v("tensor_scalar", xn_tile[:], xst, c1, c3, ALU.add, ALU.mult)
    if xn_f32 is not None:
        em.v("tensor_scalar", xn_f32[:], xst, c1, c3, ALU.add, ALU.mult)
    if to_xnT:
        for dc in range(NDC):
            em.t("transpose", ps_tr[:, dc * 128:(dc + 1) * 128],
                 xn_tile[:, dc * 128:(dc + 1) * 128], pr.ident[:])
        dst = pr.xnT[:].rearrange("p (dc t) -> p dc t", t=T)[:, :, tt * 128:(tt + 1) * 128]
        src = ps_tr[:].rearrange("p (dc t) -> p dc t", t=128)
        em.v("tensor_copy", dst, src)


def emit_ln_all(pr, xn_f32_buf=None, stats=None):
    with pr.scope() as es2:
        R = 2
        xn_tile = [es2.enter_context(pr.nc.sbuf_tensor(f"ln_xnt{i}", [128, D], BF16))
                   for i in range(R)]
        dump = [es2.enter_context(pr.nc.sbuf_tensor(f"ln_dump{i}", [128, D], BF16))
                for i in range(R)]
        ps_tr = [es2.enter_context(pr.nc.psum_tensor(f"ln_pstr{i}", [128, NDC * 128], BF16))
                 for i in range(R)]
        colr = [tuple(es2.enter_context(pr.nc.sbuf_tensor(f"ln_c{i}_{j}", [128, 1], F32))[:]
                      for j in range(3)) for i in range(R)]
        for tt in range(NTT):
            i = tt % R
            xf = None if xn_f32_buf is None else xn_f32_buf[:, tt * D:(tt + 1) * D]
            emit_ln_tile(pr, tt, xn_tile[i], ps_tr[i], dump[i], xn_f32=xf,
                         cols=colr[i])
            if stats is not None:
                pr.em.v("tensor_copy", stats[0][:, tt:tt + 1], colr[i][0])
                pr.em.v("tensor_copy", stats[1][:, tt:tt + 1], colr[i][2])


def emit_attention_core(pr, qT, kT, vbuf, evac_fn, scale_imm=None,
                        scale_ap16=None, bias_ap16=None):
    em = pr.em
    with pr.scope() as es2:
        nc = pr.nc
        ps_s = [es2.enter_context(nc.psum_tensor(f"att_s{i}", [128, 1024], F32))
                for i in range(2)]
        ps_av = [es2.enter_context(nc.psum_tensor(f"att_av{i}", [128, 1024], F32))
                 for i in range(2)]
        ebuf = [es2.enter_context(nc.sbuf_tensor(f"att_e{i}", [128, 1024], BF16))
                for i in range(2)]
        rden = es2.enter_context(nc.sbuf_tensor("att_rd", [128, 1], F32))

        def avoff(j):
            return (j // 4) * 512 + (j % 4) * 65

        for h in range(16):
            bp = 64 * (h % 2)
            cb = (h // 2) * T
            for qh in range(2):
                av = ps_av[(h * 2 + qh) % 2]

                def front(kc, h=h, bp=bp, cb=cb, qh=qh):
                    pb = kc % 2
                    for qs in range(2):
                        em.mm(ps_s[pb][:, qs * 512:(qs + 1) * 512],
                              kT[bp:bp + 64, cb + kc * 128:cb + (kc + 1) * 128],
                              qT[bp:bp + 64, cb + qh * 1024 + qs * 512:cb + qh * 1024 + (qs + 1) * 512],
                              start=True, stop=True, tile_position=(bp, 0))
                    if scale_imm is not None:
                        em.a("activation", ebuf[pb][:], ps_s[pb][:],
                             AF.Exp, bias=0.0, scale=scale_imm)
                    else:
                        em.a("activation", ebuf[pb][:], ps_s[pb][:],
                             AF.Exp, bias=bias_ap16[:, h:h + 1], scale=scale_ap16[:, h:h + 1])

                def back(kc, h=h, av=av):
                    pb = kc % 2
                    for j in range(8):
                        em.mm(av[:, avoff(j):avoff(j) + 65],
                              ebuf[pb][:, j * 128:(j + 1) * 128],
                              vbuf[:, kc * 1040 + h * 65:kc * 1040 + (h + 1) * 65],
                              start=(kc == 0 and j % 4 == 0),
                              stop=(kc == NTT - 1 and j % 4 == 3))

                for kc in range(NTT + 1):
                    if kc < NTT:
                        front(kc)
                    if kc > 0:
                        back(kc - 1)
                for j in range(8):
                    em.v("reciprocal", rden[:], av[:, avoff(j) + 64:avoff(j) + 65])
                    evac_fn(h, qh * 8 + j, av[:, avoff(j):avoff(j) + 64], rden)


def emit_l0_attn_a(pr):
    em = pr.em
    nc = pr.nc
    with pr.scope() as es2:
        qT = es2.enter_context(nc.sbuf_tensor("aa_qT", [128, NDC * T], BF16))
        kT = es2.enter_context(nc.sbuf_tensor("aa_kT", [128, NDC * T], BF16))
        vbuf = es2.enter_context(nc.sbuf_tensor("aa_v", [128, NTT * D], BF16))
        with pr.scope() as es3:
            pr.xnT = es3.enter_context(nc.sbuf_tensor("aa_xnT", [128, NDC * T], BF16))[:]
            emit_ln_all(pr)
            ps_qk = es3.enter_context(nc.psum_tensor("aa_psqk", [128, 2048], F32))
            ps_v = es3.enter_context(nc.psum_tensor("aa_psv", [128, 1024], F32))
            with pr.scope() as es4:
                wstg = es4.enter_context(nc.sbuf_tensor("aa_wstg", [128, 128], F32))
                wsl = es4.enter_context(nc.sbuf_tensor("aa_wsl", [128, 128], BF16))
                for side, wname in ((qT, "aa_wqT"), (kT, "aa_wkT")):
                    wsrc = pr.din[wname].rearrange("(dc p) f -> dc p f", p=128)
                    for fc in range(NDC):
                        for dc in range(NDC):
                            em.dma(wstg[:], wsrc[dc, :, fc * 128:(fc + 1) * 128])
                            em.v("tensor_copy", wsl[:], wstg[:])
                            for qs in range(4):
                                em.mm(ps_qk[:, qs * 512:(qs + 1) * 512],
                                      wsl[:],
                                      pr.xnT[:, dc * T + qs * 512:dc * T + (qs + 1) * 512],
                                      start=(dc == 0), stop=(dc == NDC - 1))
                        em.v("tensor_copy", side[:, fc * T:(fc + 1) * T], ps_qk[:])
            with pr.scope() as es4:
                wvh = es4.enter_context(nc.sbuf_tensor("aa_wvh", [128, NDC * 512], BF16))
                wstg2 = es4.enter_context(nc.sbuf_tensor("aa_wstg2", [128, 512], F32))
                wsrc = pr.din["aa_wvT"].rearrange("(dc p) f -> dc p f", p=128)
                for vs in range(2):
                    for dc in range(NDC):
                        em.dma(wstg2[:], wsrc[dc, :, vs * 512:(vs + 1) * 512])
                        em.v("tensor_copy", wvh[:, dc * 512:(dc + 1) * 512], wstg2[:])
                    for tt in range(NTT):
                        for dc in range(NDC):
                            em.mm(ps_v[:, vs * 512:(vs + 1) * 512],
                                  pr.xnT[:, dc * T + tt * 128:dc * T + (tt + 1) * 128],
                                  wvh[:, dc * 512:(dc + 1) * 512],
                                  start=(dc == 0), stop=(dc == NDC - 1))
                        em.v("tensor_copy",
                             vbuf[:, tt * D + vs * 512:tt * D + (vs + 1) * 512],
                             ps_v[:, vs * 512:(vs + 1) * 512])

        with pr.scope() as es3:
            NSL = 2
            ps_s = es3.enter_context(nc.psum_tensor("aa_pss", [128, 2048], F32))
            ps_av = [es3.enter_context(nc.psum_tensor(f"aa_psav{j}", [128, 512], F32))
                     for j in range(NSL)]
            ebuf = [es3.enter_context(nc.sbuf_tensor(f"aa_e{j}", [128, 2048], BF16))
                    for j in range(NSL)]
            eT = [es3.enter_context(nc.sbuf_tensor(f"aa_eT{j}", [128, 2048], BF16))
                  for j in range(NSL)]
            nmx = [es3.enter_context(nc.sbuf_tensor(f"aa_nmx{j}", [128, 1], F32))
                   for j in range(NSL)]
            den = [es3.enter_context(nc.sbuf_tensor(f"aa_den{j}", [128, 1], F32))
                   for j in range(NSL)]
            rden = [es3.enter_context(nc.sbuf_tensor(f"aa_rden{j}", [128, 1], F32))
                    for j in range(NSL)]
            iters = [(h, qt) for h in range(16) for qt in range(NTT)]

            def front(i):
                h, qt = iters[i]
                j = i % NSL
                bp = 64 * (h % 2)
                cb = (h // 2) * T
                for ks in range(4):
                    em.mm(ps_s[:, ks * 512:(ks + 1) * 512],
                          qT[bp:bp + 64, cb + qt * 128:cb + (qt + 1) * 128],
                          kT[bp:bp + 64, cb + ks * 512:cb + (ks + 1) * 512],
                          start=True, stop=True, tile_position=(bp, 0))
                em.v("reduce_max", nmx[j][:], ps_s[:], AX.X, negate=True)
                em.v("tensor_scalar", nmx[j][:], nmx[j][:], 0.125, None, ALU.mult)
                em.a("activation", ebuf[j][:], ps_s[:], AF.Exp, bias=nmx[j][:],
                     scale=0.125, accum_out=den[j][:])
                em.v("reciprocal", rden[j][:], den[j][:])
                em.dma_t(eT[j][:].rearrange("p (c q) -> p c q", q=128),
                         ebuf[j][:].rearrange("p (c k) -> p c k", k=128),
                         eng="sync")

            def back(i):
                h, qt = iters[i]
                j = i % NSL
                for kc in range(NTT):
                    em.mm(ps_av[j][:, 0:64], eT[j][:, kc * 128:(kc + 1) * 128],
                          vbuf[:, kc * D + h * 64:kc * D + (h + 1) * 64],
                          start=(kc == 0), stop=(kc == NTT - 1))
                xsl = pr.xs[:, qt * D + h * 64:qt * D + (h + 1) * 64]
                em.v("scalar_tensor_tensor", xsl, ps_av[j][:, 0:64], rden[j][:], xsl,
                     ALU.mult, ALU.add)

            for i in range(len(iters) + 1):
                if i < len(iters):
                    front(i)
                if i > 0:
                    back(i - 1)


def emit_l0_attn_b(pr):
    em = pr.em
    nc = pr.nc
    with pr.scope() as es2:
        pr.xnT = es2.enter_context(nc.sbuf_tensor("ab_xnT", [128, NDC * T], BF16))[:]
        emit_ln_all(pr)
        scb = es2.enter_context(nc.sbuf_tensor("ab_scb", [128, 16], F32))
        nscb = es2.enter_context(nc.sbuf_tensor("ab_nscb", [128, 16], F32))
        ao = pr.xnT

        with pr.scope() as es3:
            lsc = es3.enter_context(nc.sbuf_tensor("ab_lsc", [16, 1], F32))
            lsr = es3.enter_context(nc.sbuf_tensor("ab_lsr", [1, 16], F32))
            ps16 = es3.enter_context(nc.psum_tensor("ab_ps16", [128, 512], F32))
            em.dma(lsc[:], pr.din["ab_ls"])
            em.v("tensor_scalar", lsc[:], lsc[:], float(np.log(100.0)), None, ALU.min)
            em.a("activation", lsc[:], lsc[:], AF.Exp, bias=0.0, scale=1.0)
            em.t("transpose", ps16[0:1, 0:16], lsc[:], pr.ident32f[0:16, 0:16])
            em.v("tensor_copy", lsr[:], ps16[0:1, 0:16])
            em.mm(ps16[:, 0:16], pr.ones_row_f[:], lsr[:], start=True, stop=True)
            em.v("tensor_copy", scb[:], ps16[:, 0:16])
            em.v("tensor_scalar", nscb[:], scb[:], -1.0, None, ALU.mult)

        with pr.scope() as esA:
            qT = esA.enter_context(nc.sbuf_tensor("ab_qT", [128, NDC * T], BF16))
            kT = esA.enter_context(nc.sbuf_tensor("ab_kT", [128, NDC * T], BF16))
            vbuf = esA.enter_context(nc.sbuf_tensor("ab_v", [128, NTT * 1040], BF16))

            with pr.scope() as es3:
                TB = 3
                ps_q3 = [es3.enter_context(nc.psum_tensor(f"ab_psq{i}", [128, 1024], F32))
                         for i in range(TB)]
                ps_tr = es3.enter_context(nc.psum_tensor("ab_pstr", [128, 1024], BF16))
                qn = es3.enter_context(nc.sbuf_tensor("ab_qn", [128, D], BF16))
                ssq = es3.enter_context(nc.sbuf_tensor("ab_ssq", [128, 16], F32))
                dump = es3.enter_context(nc.sbuf_tensor("ab_dump", [128, D], F32))
                wstg = es3.enter_context(nc.sbuf_tensor("ab_wstg", [128, 512], F32))
                wslb = es3.enter_context(nc.sbuf_tensor("ab_wslb", [128, 512], BF16))
                ones_ap = vbuf[:].rearrange("p (t h w) -> p t h w", h=16, w=65)[:, :, :, 64:65]
                em.v("memset", ones_ap, 1.0)
                wsrc = pr.din["ab_wqkv"].rearrange("(dc p) f -> dc p f", p=128)
                for part, dstT in ((0, qT), (1, kT), (2, None)):
                    for t0 in range(0, NTT, TB):
                        tts = list(range(t0, min(t0 + TB, NTT)))
                        for qs in range(2):
                            for dc in range(NDC):
                                em.dma(wstg[:], wsrc[dc, :, part * 1024 + qs * 512:
                                                       part * 1024 + (qs + 1) * 512])
                                em.v("tensor_copy", wslb[:], wstg[:])
                                for i, tt in enumerate(tts):
                                    em.mm(ps_q3[i][:, qs * 512:(qs + 1) * 512],
                                          pr.xnT[:, dc * T + tt * 128:dc * T + (tt + 1) * 128],
                                          wslb[:],
                                          start=(dc == 0), stop=(dc == NDC - 1))
                        for i, tt in enumerate(tts):
                            ps_q = ps_q3[i]
                            emit_ab_qkv_post(pr, part, tt, ps_q, ps_tr, qn, ssq, dump,
                                             qT, kT, vbuf)

            _ab_attention_and_outproj(pr, qT, kT, vbuf, scb, nscb)
        _ab_outproj(pr)


def emit_ab_qkv_post(pr, part, tt, ps_q, ps_tr, qn, ssq, dump, qT, kT, vbuf):
    em = pr.em
    dstT = (qT, kT, None)[part]
    if part < 2:
        em.a("activation", dump[:, 0:D], ps_q[:], AF.Square, bias=0.0, scale=1.0)
        em.v("tensor_reduce",
             ssq[:].rearrange("p (h one) -> p h one", one=1),
             dump[:, 0:D].rearrange("p (h w) -> p h w", w=64),
             AX.X, ALU.add)
        em.v("tensor_scalar", ssq[:], ssq[:], 1e-24, None, ALU.max)
        em.a("activation", ssq[:], ssq[:], AF.Ln, bias=0.0, scale=1.0)
        em.a("activation", ssq[:], ssq[:], AF.Exp, bias=0.0, scale=-0.5)
        em.v("tensor_tensor", qn[:].rearrange("p (h w) -> p h w", w=64),
             ps_q[:].rearrange("p (h w) -> p h w", w=64),
             ssq[:].unsqueeze(2).broadcast_to([128, 16, 64]), ALU.mult)
        for dc in range(NDC):
            em.t("transpose", ps_tr[:, dc * 128:(dc + 1) * 128],
                 qn[:, dc * 128:(dc + 1) * 128], pr.ident[:])
        dst = dstT[:].rearrange("p (dc t) -> p dc t", t=T)[:, :, tt * 128:(tt + 1) * 128]
        em.v("tensor_copy", dst, ps_tr[:].rearrange("p (dc t) -> p dc t", t=128))
    else:
        dst = vbuf[:, tt * 1040:(tt + 1) * 1040].rearrange(
            "p (h w) -> p h w", w=65)[:, :, 0:64]
        em.v("tensor_copy", dst, ps_q[:].rearrange("p (h w) -> p h w", w=64))


def _ab_attention_and_outproj(pr, qT, kT, vbuf, scb, nscb):
    em = pr.em
    nc = pr.nc
    ao = pr.xnT

    def evac(h, qt, ps_block, rden):
        em.v("tensor_scalar", ao[:, qt * D + h * 64:qt * D + (h + 1) * 64],
             ps_block, rden[:], None, ALU.mult)

    emit_attention_core(pr, qT, kT, vbuf, evac, scale_ap16=scb, bias_ap16=nscb)


def _ab_outproj(pr):
    em = pr.em
    nc = pr.nc
    ao = pr.xnT
    with pr.scope() as es3:
        aT = es3.enter_context(nc.sbuf_tensor("ab_aT", [128, NDC * T], BF16))
        ps_tr = es3.enter_context(nc.psum_tensor("ab_pstr2", [128, 1024], BF16))
        ps_o = es3.enter_context(nc.psum_tensor("ab_pso", [128, 1024], F32))
        bout = es3.enter_context(nc.sbuf_tensor("ab_boutb", [1, D], BF16))
        boutf = es3.enter_context(nc.sbuf_tensor("ab_boutf", [1, D], F32))
        pr.es, saved_es = es3, pr.es
        wo = load_weight_bf16(pr, "ab_wob", "ab_wout", D, D)
        pr.es = saved_es
        em.dma(boutf[:], pr.din["ab_bout"])
        em.v("tensor_copy", bout[:], boutf[:])
        for tt in range(NTT):
            for dc in range(NDC):
                em.t("transpose", ps_tr[:, dc * 128:(dc + 1) * 128],
                     ao[:, tt * D + dc * 128:tt * D + (dc + 1) * 128], pr.ident[:])
            em.v("tensor_copy",
                 aT[:].rearrange("p (dc t) -> p dc t", t=T)[:, :, tt * 128:(tt + 1) * 128],
                 ps_tr[:].rearrange("p (dc t) -> p dc t", t=128))
        for tt in range(NTT):
            for os_ in range(2):
                for dc in range(NDC):
                    em.mm(ps_o[:, os_ * 512:(os_ + 1) * 512],
                          aT[:, dc * T + tt * 128:dc * T + (tt + 1) * 128],
                          wo[:, dc * D + os_ * 512:dc * D + (os_ + 1) * 512],
                          start=(dc == 0), stop=False)
                em.mm(ps_o[:, os_ * 512:(os_ + 1) * 512], pr.ones_row_b[:],
                      bout[:, os_ * 512:(os_ + 1) * 512], start=False, stop=True)
            em.v("tensor_tensor", pr.xs[:, tt * D:(tt + 1) * D], ps_o[:],
                 pr.xs[:, tt * D:(tt + 1) * D], ALU.add)



def bcast_1x1_to_col(pr, src11, dst_col, ps512):
    em = pr.em
    em.mm(ps512[:, 0:1], pr.ones_row_f, src11, start=True, stop=True)
    em.v("tensor_copy", dst_col, ps512[:, 0:1])


def bcast_row_to_tile(pr, src_row, dst, ps512, width):
    em = pr.em
    for off in range(0, width, 512):
        w = min(512, width - off)
        em.mm(ps512[:, 0:w], pr.ones_row_f, src_row[:, off:off + w],
              start=True, stop=True)
        em.v("tensor_copy", dst[:, off:off + w], ps512[:, 0:w])


def emit_neground2(pr, v, c_i32, d, r, out_bf, width):
    em = pr.em
    em.v("tensor_copy", c_i32[:, 0:width], v)
    em.v("tensor_sub", d[:, 0:width], v, c_i32[:, 0:width])
    em.v("scalar_tensor_tensor", r[:, 0:width], d[:, 0:width], 0.5,
         c_i32[:, 0:width], ALU.is_gt, ALU.add)
    em.v("scalar_tensor_tensor", out_bf, d[:, 0:width], -0.5, r[:, 0:width],
         ALU.is_lt, ALU.subtract)


def emit_mean_abs(pr, dram_name, rows, cols, out11, stg, abscols, ps512, onescol):
    em = pr.em
    nch = rows // 128
    src = pr.din[dram_name].rearrange("(c p) f -> c p f", p=128)
    for ci in range(nch):
        em.dma(stg[:, 0:cols], src[ci])
        em.a("activation", stg[:, 0:cols], stg[:, 0:cols], AF.Abs, bias=0.0,
             scale=1.0, accum_out=abscols[:, ci:ci + 1])
    em.v("reduce_sum", pr.col1, abscols[:, 0:nch], AX.X)
    em.mm(ps512[0:1, 0:1], pr.col1, onescol, start=True, stop=True)
    em.v("tensor_scalar", out11, ps512[0:1, 0:1], 1.0 / (rows * cols), None,
         ALU.mult)


def emit_l0_ffn(pr):
    em = pr.em
    nc = pr.nc
    H = 4096
    NHC = H // 128
    gdram = nc.dram_tensor(f"f0_g_{pr.uid}", [H, T], BF16, kind="Internal").ap()
    r2dram = nc.dram_tensor(f"f0_r2_{pr.uid}", [H, D], BF16, kind="Internal").ap()
    pr.uid += 1
    with pr.scope() as es2:
        sb = lambda n, s, d: es2.enter_context(nc.sbuf_tensor(n, s, d))
        ps = lambda n, s, d: es2.enter_context(nc.psum_tensor(n, s, d))
        q1T = sb("f0_q1T", [128, NDC * T], BF16)[:]
        pr.xnT = q1T
        onescol = sb("f0_ones", [128, 1], F32)
        g1 = sb("f0_g1", [1, 1], F32)
        g2 = sb("f0_g2", [1, 1], F32)
        t11 = sb("f0_t11", [1, 1], F32)
        thr1p = sb("f0_thr1p", [128, 1], F32)
        thr1n = sb("f0_thr1n", [128, 1], F32)
        thr2p = sb("f0_thr2p", [128, 1], F32)
        thr2n = sb("f0_thr2n", [128, 1], F32)
        geff1 = sb("f0_geff1", [1, 1], F32)
        geff2 = sb("f0_geff2", [1, 1], F32)
        comb1 = sb("f0_comb1", [128, T], F32)
        s2b = sb("f0_s2b", [128, T], F32)
        fin2 = sb("f0_fin2", [128, T], F32)
        rm2cols = sb("f0_rm2c", [128, NTT * NHC], F32)
        ps512 = ps("f0_ps512", [128, 512], F32)

        em.v("memset", onescol[:], 1.0)
        with pr.scope() as esG:
            stgG = esG.enter_context(nc.sbuf_tensor("f0_stgG", [128, 4096], F32))
            abscols = esG.enter_context(nc.sbuf_tensor("f0_absc", [128, 32], F32))
            emit_mean_abs(pr, "f0_w1T", D, H, g1[:], stgG, abscols, ps512, onescol[:])
            emit_mean_abs(pr, "f0_w2T", H, D, g2[:], stgG, abscols, ps512, onescol[:])
        for g, thrp, thrn, geff in ((g1, thr1p, thr1n, geff1),
                                    (g2, thr2p, thr2n, geff2)):
            em.v("tensor_scalar", g[:], g[:], 1e-5, None, ALU.max)
            em.v("reciprocal", t11[:], g[:])
            em.v("reciprocal", geff[:], t11[:])
            em.v("tensor_scalar", t11[:], g[:], 0.5, None, ALU.mult)
            bcast_1x1_to_col(pr, t11[:], thrp[:], ps512)
            em.v("tensor_scalar", thrn[:], thrp[:], -1.0, None, ALU.mult)

        with pr.scope() as es3:
            R = 2
            xn = [es3.enter_context(nc.sbuf_tensor(f"f0_xn{i}", [128, D], F32)) for i in range(R)]
            vq = [es3.enter_context(nc.sbuf_tensor(f"f0_vq{i}", [128, D], F32)) for i in range(R)]
            dq = [es3.enter_context(nc.sbuf_tensor(f"f0_dq{i}", [128, D], F32)) for i in range(R)]
            rq = [es3.enter_context(nc.sbuf_tensor(f"f0_rq{i}", [128, D], F32)) for i in range(R)]
            ci32 = [es3.enter_context(nc.sbuf_tensor(f"f0_ci{i}", [128, D], I32)) for i in range(R)]
            qb = [es3.enter_context(nc.sbuf_tensor(f"f0_qb{i}", [128, D], BF16)) for i in range(R)]
            cA = [es3.enter_context(nc.sbuf_tensor(f"f0_cA{i}", [128, 1], F32))[:] for i in range(R)]
            cB = [es3.enter_context(nc.sbuf_tensor(f"f0_cB{i}", [128, 1], F32))[:] for i in range(R)]
            cC = [es3.enter_context(nc.sbuf_tensor(f"f0_cC{i}", [128, 1], F32))[:] for i in range(R)]
            row1 = es3.enter_context(nc.sbuf_tensor("f0_row1", [1, T], F32))
            ps_tr = [es3.enter_context(nc.psum_tensor(f"f0_pstr{i}", [128, NDC * 128], BF16)) for i in range(R)]
            ps_th = [es3.enter_context(nc.psum_tensor(f"f0_psth{i}", [128, 512], F32)) for i in range(R)]
            for tt in range(NTT):
                i = tt % R
                emit_ln_tile_f32(pr, tt, xn[i], dq[i], cols=(cA[i], cB[i], cC[i]))
                em.v("reduce_max", cA[i], xn[i][:], AX.X, apply_absolute_value=True)
                em.v("tensor_scalar", cA[i], cA[i], 1e-5, None, ALU.max)
                em.v("reciprocal", cB[i], cA[i])
                em.v("tensor_scalar", cB[i], cB[i], 127.0, None, ALU.mult)
                em.v("tensor_scalar", vq[i][:], xn[i][:], cB[i], None, ALU.mult)
                emit_neground2(pr, vq[i][:], ci32[i], dq[i], rq[i], qb[i][:], D)
                for dc in range(NDC):
                    em.t("transpose", ps_tr[i][:, dc * 128:(dc + 1) * 128],
                         qb[i][:, dc * 128:(dc + 1) * 128], pr.ident[:])
                em.v("tensor_copy",
                     q1T[:].rearrange("p (dc t) -> p dc t", t=T)[:, :, tt * 128:(tt + 1) * 128],
                     ps_tr[i][:].rearrange("p (dc t) -> p dc t", t=128))
                em.v("tensor_scalar", cA[i], cA[i], 1.0 / 127.0, None, ALU.mult)
                em.t("transpose", ps_th[i][0:1, 0:128], cA[i], pr.ident128f)
                em.v("tensor_copy", row1[:, tt * 128:(tt + 1) * 128], ps_th[i][0:1, 0:128])
            em.v("tensor_scalar", row1[:], row1[:], geff1[:], None, ALU.mult)
            bcast_row_to_tile(pr, row1, comb1, ps512, T)

        with pr.scope() as es3:
            R = 2
            stg = [es3.enter_context(nc.sbuf_tensor(f"f0_stgA{i}", [128, 4096], F32)) for i in range(R)]
            r1sl = [es3.enter_context(nc.sbuf_tensor(f"f0_r1sl{i}", [128, NDC * 128], BF16)) for i in range(R)]
            r2sl = [es3.enter_context(nc.sbuf_tensor(f"f0_r2sl{i}", [128, D], BF16)) for i in range(R)]
            hs = [es3.enter_context(nc.sbuf_tensor(f"f0_hs{i}", [128, T], F32)) for i in range(R)]
            acc = [es3.enter_context(nc.sbuf_tensor(f"f0_acc{i}", [128, T], F32)) for i in range(R)]
            gch = [es3.enter_context(nc.sbuf_tensor(f"f0_gch{i}", [128, T], BF16)) for i in range(R)]
            cw = [es3.enter_context(nc.sbuf_tensor(f"f0_cw{i}", [128, 3], F32)) for i in range(R)]
            cb = [es3.enter_context(nc.sbuf_tensor(f"f0_cb{i}", [128, 1], F32)) for i in range(R)]
            ps_h = [es3.enter_context(nc.psum_tensor("f0_psh", [128, 2048], F32))] * R
            ps_t2 = [es3.enter_context(nc.psum_tensor("f0_pst2", [128, 2048], BF16))] * R
            w1src = pr.din["f0_w1T"].rearrange("(dc p) f -> dc p f", p=128)
            w2src = pr.din["f0_w2T"].rearrange("(hc p) f -> hc p f", p=128)
            cwsrc = pr.din["f0_convw"]
            cbsrc = pr.din["f0_convb"]
            for hc in range(NHC):
                i_ = hc % R
                stg_, r1sl_, r2sl_, hs_, acc_, gch_, cw_, cb_ = (
                    stg[i_], r1sl[i_], r2sl[i_], hs[i_], acc[i_], gch[i_], cw[i_], cb[i_])
                ps_h_, ps_t2_ = ps_h[i_], ps_t2[i_]
                for dc in range(NDC):
                    em.dma(stg_[:, dc * 128:(dc + 1) * 128],
                           w1src[dc, :, hc * 128:(hc + 1) * 128])
                em.v("tensor_scalar", stg_[:, 1024:2048], stg_[:, 0:1024], thr1p[:],
                     None, ALU.is_gt)
                em.v("scalar_tensor_tensor", r1sl_[:], stg_[:, 0:1024], thr1n[:],
                     stg_[:, 1024:2048], ALU.is_lt, ALU.subtract)
                em.dma(stg_[:, 2048:3072], w2src[hc])
                em.v("tensor_scalar", stg_[:, 3072:4096], stg_[:, 2048:3072], thr2p[:],
                     None, ALU.is_gt)
                em.v("scalar_tensor_tensor", r2sl_[:], stg_[:, 2048:3072], thr2n[:],
                     stg_[:, 3072:4096], ALU.is_lt, ALU.subtract)
                em.dma(r2dram.rearrange("(hc p) f -> hc p f", p=128)[hc], r2sl_[:])
                for ts4 in range(4):
                    for dc in range(NDC):
                        em.mm(ps_h_[:, ts4 * 512:(ts4 + 1) * 512],
                              r1sl_[:, dc * 128:(dc + 1) * 128],
                              q1T[:, dc * T + ts4 * 512:dc * T + (ts4 + 1) * 512],
                              start=(dc == 0), stop=(dc == NDC - 1))
                em.v("tensor_tensor", hs_[:], ps_h_[:], comb1[:], ALU.mult)
                em.dma(cw_[:], cwsrc[hc * 128:(hc + 1) * 128, :])
                em.dma(cb_[:], cbsrc[hc * 128:(hc + 1) * 128, :])
                em.v("tensor_scalar", acc_[:, 1:T], hs_[:, 0:T - 1], cw_[:, 0:1],
                     cb_[:], ALU.mult, ALU.add)
                em.v("tensor_copy", acc_[:, 0:1], cb_[:])
                em.v("scalar_tensor_tensor", acc_[:], hs_[:], cw_[:, 1:2], acc_[:],
                     ALU.mult, ALU.add)
                em.v("scalar_tensor_tensor", acc_[:, 0:T - 1], hs_[:, 1:T], cw_[:, 2:3],
                     acc_[:, 0:T - 1], ALU.mult, ALU.add)
                em.a("activation", gch_[:], acc_[:], AF.Silu, bias=0.0, scale=1.0)
                em.dma(gdram.rearrange("(hc p) f -> hc p f", p=128)[hc], gch_[:])
                for i in range(NTT):
                    em.t("transpose", ps_t2_[:, (i % 8) * 128 + (i // 8) * 1024:
                                             (i % 8) * 128 + (i // 8) * 1024 + 128],
                         gch_[:, i * 128:(i + 1) * 128], pr.ident[:])
                for i in range(NTT):
                    em.v("reduce_max", rm2cols[:, i * NHC + hc:i * NHC + hc + 1],
                         ps_t2_[:, (i % 8) * 128 + (i // 8) * 1024:
                                  (i % 8) * 128 + (i // 8) * 1024 + 128],
                         AX.X, apply_absolute_value=True)

        with pr.scope() as es3:
            ps_th = es3.enter_context(nc.psum_tensor("f0_psth2", [128, 512], F32))
            row2 = es3.enter_context(nc.sbuf_tensor("f0_row2", [1, T], F32))
            row3 = es3.enter_context(nc.sbuf_tensor("f0_row3", [1, T], F32))
            for tt in range(NTT):
                em.v("reduce_max", pr.col1, rm2cols[:, tt * NHC:(tt + 1) * NHC], AX.X)
                em.v("tensor_scalar", pr.col1, pr.col1, 1e-5, None, ALU.max)
                em.v("reciprocal", pr.col2, pr.col1)
                em.v("tensor_scalar", pr.col2, pr.col2, 127.0, None, ALU.mult)
                em.v("tensor_scalar", pr.col1, pr.col1, 1.0 / 127.0, None, ALU.mult)
                em.t("transpose", ps_th[0:1, 0:128], pr.col1, pr.ident128f)
                em.v("tensor_copy", row2[:, tt * 128:(tt + 1) * 128], ps_th[0:1, 0:128])
                em.t("transpose", ps_th[0:1, 128:256], pr.col2, pr.ident128f)
                em.v("tensor_copy", row3[:, tt * 128:(tt + 1) * 128], ps_th[0:1, 128:256])
            em.v("tensor_scalar", row2[:], row2[:], geff2[:], None, ALU.mult)
            bcast_row_to_tile(pr, row3, s2b, ps512, T)
            bcast_row_to_tile(pr, row2, fin2, ps512, T)

        with pr.scope() as es3:
            R = 3
            gsl = [es3.enter_context(nc.sbuf_tensor(f"f0_gsl{i}", [128, 512], BF16)) for i in range(R)]
            vq = [es3.enter_context(nc.sbuf_tensor(f"f0_vq2{i}", [128, 512], F32)) for i in range(R)]
            dq = [es3.enter_context(nc.sbuf_tensor(f"f0_dq2{i}", [128, 512], F32)) for i in range(R)]
            rq = [es3.enter_context(nc.sbuf_tensor(f"f0_rq2{i}", [128, 512], F32)) for i in range(R)]
            ci32 = [es3.enter_context(nc.sbuf_tensor(f"f0_ci2{i}", [128, 512], I32)) for i in range(R)]
            r2b = [es3.enter_context(nc.sbuf_tensor(f"f0_r2b{i}", [128, D], BF16)) for i in range(2)]
            q2buf = es3.enter_context(nc.sbuf_tensor("f0_q2b", [128, 32 * 512], BF16))
            fout = es3.enter_context(nc.sbuf_tensor("f0_fout", [128, NDC * 512], BF16))
            ps_o = es3.enter_context(nc.psum_tensor("f0_pso", [128, 2048], F32))
            ps_tb = [es3.enter_context(nc.psum_tensor(f"f0_pstb{i}", [128, 1024], BF16)) for i in range(2)]
            gsrc = gdram.rearrange("(hc p) f -> hc p f", p=128)
            r2src = r2dram.rearrange("(hc p) f -> hc p f", p=128)
            for tb in range(4):
                for hc in range(NHC):
                    i_ = hc % R
                    em.dma(gsl[i_][:], gsrc[hc, :, tb * 512:(tb + 1) * 512])
                    em.v("tensor_tensor", vq[i_][:], gsl[i_][:],
                         s2b[:, tb * 512:(tb + 1) * 512], ALU.mult)
                    emit_neground2(pr, vq[i_][:], ci32[i_], dq[i_], rq[i_],
                                   q2buf[:, hc * 512:(hc + 1) * 512], 512)
                for grp in range(2):
                    for hc in range(NHC):
                        em.dma(r2b[hc % 2][:], r2src[hc])
                        for dmi in range(4):
                            dmc = grp * 4 + dmi
                            em.mm(ps_o[:, dmi * 512:(dmi + 1) * 512],
                                  r2b[hc % 2][:, dmc * 128:(dmc + 1) * 128],
                                  q2buf[:, hc * 512:(hc + 1) * 512],
                                  start=(hc == 0), stop=(hc == NHC - 1))
                    for dmi in range(4):
                        dmc = grp * 4 + dmi
                        em.v("tensor_tensor", fout[:, dmc * 512:(dmc + 1) * 512],
                             ps_o[:, dmi * 512:(dmi + 1) * 512],
                             fin2[:, tb * 512:(tb + 1) * 512], ALU.mult)
                for dmc in range(NDC):
                    pstb_ = ps_tb[dmc % 2]
                    for i in range(4):
                        em.t("transpose", pstb_[:, (i % 4) * 256:(i % 4) * 256 + 128],
                             fout[:, dmc * 512 + i * 128:dmc * 512 + (i + 1) * 128],
                             pr.ident[:])
                    for i in range(4):
                        tt = tb * 4 + i
                        em.v("tensor_tensor",
                             pr.xs[:, tt * D + dmc * 128:tt * D + (dmc + 1) * 128],
                             pstb_[:, (i % 4) * 256:(i % 4) * 256 + 128],
                             pr.xs[:, tt * D + dmc * 128:tt * D + (dmc + 1) * 128],
                             ALU.add)


def emit_ln_tile_f32(pr, tt, xn_f32, dump, cols=None):
    em = pr.em
    c1, c2, c3 = cols if cols is not None else (pr.col1, pr.col2, pr.col3)
    xst = pr.xs[:, tt * D:(tt + 1) * D]
    em.v("reduce_sum", c1, xst, AX.X)
    em.v("tensor_scalar", c1, c1, -1.0 / D, None, ALU.mult)
    em.a("activation", dump[:, 0:D], xst, AF.Square, bias=c1, scale=1.0,
         accum_out=c2)
    em.v("tensor_scalar", c2, c2, 1.0 / D, 1e-5, ALU.mult, ALU.add)
    em.a("activation", c3, c2, AF.Ln, bias=0.0, scale=1.0)
    em.a("activation", c3, c3, AF.Exp, bias=0.0, scale=-0.5)
    em.v("tensor_scalar", xn_f32[:], xst, c1, c3, ALU.add, ALU.mult)



def emit_l1_ffn(pr):
    em = pr.em
    nc = pr.nc
    H = 4096
    with pr.scope() as es2:
        sb = lambda n, s, d: es2.enter_context(nc.sbuf_tensor(n, s, d))
        pr.xnT = sb("f1_xnT", [128, NDC * T], BF16)[:]
        emit_ln_all(pr)
        onescol = sb("f1_ones", [128, 1], F32)
        g1 = sb("f1_g1", [1, 1], F32)
        g2 = sb("f1_g2", [1, 1], F32)
        t11 = sb("f1_t11", [1, 1], F32)
        thr1p = sb("f1_thr1p", [128, 1], F32)
        thr1n = sb("f1_thr1n", [128, 1], F32)
        thr2p = sb("f1_thr2p", [128, 1], F32)
        thr2n = sb("f1_thr2n", [128, 1], F32)
        em.v("memset", onescol[:], 1.0)
        with pr.scope() as esG:
            ps512 = esG.enter_context(nc.psum_tensor("f1_ps512", [128, 512], F32))
            stgG = esG.enter_context(nc.sbuf_tensor("f1_stgG", [128, 4096], F32))
            abscols = esG.enter_context(nc.sbuf_tensor("f1_absc", [128, 32], F32))
            emit_mean_abs(pr, "f1_w1T", D, H, g1[:], stgG, abscols, ps512, onescol[:])
            emit_mean_abs(pr, "f1_w2T", H, D, g2[:], stgG, abscols, ps512, onescol[:])
            for g, thrp, thrn in ((g1, thr1p, thr1n), (g2, thr2p, thr2n)):
                em.v("tensor_scalar", t11[:], g[:], 1e-5, 0.5, ALU.add, ALU.mult)
                bcast_1x1_to_col(pr, t11[:], thrp[:], ps512)
                em.v("tensor_scalar", thrn[:], thrp[:], -1.0, None, ALU.mult)

        w1src = pr.din["f1_w1T"].rearrange("(dc p) f -> dc p f", p=128)
        w2src = pr.din["f1_w2T"].rearrange("(hc p) f -> hc p f", p=128)
        with pr.scope() as es3:
            r1h = es3.enter_context(nc.sbuf_tensor("f1_r1h", [128, NDC * 2048], BF16))
            r2h = es3.enter_context(nc.sbuf_tensor("f1_r2h", [128, 16 * D], BF16))
            stgq = [es3.enter_context(nc.sbuf_tensor(f"f1_stgq{i}", [128, 2048], F32))
                    for i in range(2)]
            b1rf = es3.enter_context(nc.sbuf_tensor("f1_b1rf", [1, 512], F32))
            b1rb = es3.enter_context(nc.sbuf_tensor("f1_b1rb", [1, 2048], BF16))
            b2rf = es3.enter_context(nc.sbuf_tensor("f1_b2rf", [1, D], F32))
            b2rb = es3.enter_context(nc.sbuf_tensor("f1_b2rb", [1, D], BF16))
            hneg = [es3.enter_context(nc.sbuf_tensor(f"f1_hneg{i}", [128, 2048], BF16))
                    for i in range(2)]
            hT = [es3.enter_context(nc.sbuf_tensor(f"f1_hT{i}", [128, 2048], BF16))
                  for i in range(2)]
            ps_h = es3.enter_context(nc.psum_tensor("f1_psh", [128, 2048], F32))
            ps_tr = es3.enter_context(nc.psum_tensor("f1_pstr", [128, 1024], BF16))
            ps_o = es3.enter_context(nc.psum_tensor("f1_pso", [128, 1024], F32))
            em.dma(b2rf[:], pr.din["f1_b2"])
            em.v("tensor_copy", b2rb[:], b2rf[:])
            for half in range(2):
                h0 = half * 2048
                for dc in range(NDC):
                    for piece in range(2):
                        sq = stgq[(dc * 2 + piece) % 2]
                        sl = sq[:, 0:1024]
                        em.dma(sl, w1src[dc, :, h0 + piece * 1024:h0 + (piece + 1) * 1024])
                        em.v("tensor_scalar", sq[:, 1024:2048], sl, thr1p[:],
                             None, ALU.is_gt)
                        em.v("scalar_tensor_tensor",
                             r1h[:, dc * 2048 + piece * 1024:dc * 2048 + (piece + 1) * 1024],
                             sl, thr1n[:], sq[:, 1024:2048], ALU.is_lt, ALU.subtract)
                for hc in range(16):
                    sq = stgq[hc % 2]
                    sl = sq[:, 0:1024]
                    em.dma(sl, w2src[half * 16 + hc])
                    em.v("tensor_scalar", sq[:, 1024:2048], sl, thr2p[:],
                         None, ALU.is_gt)
                    em.v("scalar_tensor_tensor", r2h[:, hc * D:(hc + 1) * D],
                         sl, thr2n[:], sq[:, 1024:2048], ALU.is_lt, ALU.subtract)
                for pc in range(4):
                    em.dma(b1rf[:], pr.din["f1_b1"][:, h0 + pc * 512:h0 + (pc + 1) * 512])
                    em.v("tensor_scalar", b1rb[:, pc * 512:(pc + 1) * 512], b1rf[:],
                         -1.0, None, ALU.mult)
                for tt in range(NTT):
                    for hs4 in range(4):
                        for dc in range(NDC):
                            em.mm(ps_h[:, hs4 * 512:(hs4 + 1) * 512],
                                  pr.xnT[:, dc * T + tt * 128:dc * T + (tt + 1) * 128],
                                  r1h[:, dc * 2048 + hs4 * 512:dc * 2048 + (hs4 + 1) * 512],
                                  start=(dc == 0), stop=False)
                        em.mm(ps_h[:, hs4 * 512:(hs4 + 1) * 512], pr.ones_row_b,
                              b1rb[:, hs4 * 512:(hs4 + 1) * 512], start=False, stop=True)
                    hn, hT_ = hneg[tt % 2], hT[tt % 2]
                    em.v("tensor_scalar", hn[:], ps_h[:], 0.0, None, ALU.min)
                    for grp8 in range(2):
                        for i8 in range(8):
                            i = grp8 * 8 + i8
                            em.t("transpose", ps_tr[:, i8 * 128:(i8 + 1) * 128],
                                 hn[:, i * 128:(i + 1) * 128], pr.ident[:])
                        em.v("tensor_copy", hT_[:, grp8 * 1024:(grp8 + 1) * 1024], ps_tr[:])
                    for os_ in range(2):
                        for hc in range(16):
                            em.mm(ps_o[:, os_ * 512:(os_ + 1) * 512],
                                  hT_[:, hc * 128:(hc + 1) * 128],
                                  r2h[:, hc * D + os_ * 512:hc * D + (os_ + 1) * 512],
                                  start=(hc == 0), stop=(half == 1 and hc == 15))
                        if half == 0:
                            em.mm(ps_o[:, os_ * 512:(os_ + 1) * 512], pr.ones_row_b,
                                  b2rb[:, os_ * 512:(os_ + 1) * 512], start=False,
                                  stop=True)
                    em.v("tensor_tensor", pr.xs[:, tt * D:(tt + 1) * D], ps_o[:],
                         pr.xs[:, tt * D:(tt + 1) * D], ALU.add)



import os
_KPHASE = int(os.environ.get("KPHASE", "9"))


def emit_l1_attn_b(pr):
    em = pr.em
    nc = pr.nc
    with pr.scope() as es2:
        sb = lambda n, s, d: es2.enter_context(nc.sbuf_tensor(n, s, d))
        pr.xnT = sb("db_xnT", [128, NDC * T], BF16)[:]
        negm16 = sb("db_negm", [128, 16], F32)
        rstd16 = sb("db_rstd", [128, 16], F32)
        emit_ln_all(pr, stats=(negm16, rstd16))
        gated = sb("db_gated", [128, NDC * T], BF16)
        dec8 = sb("db_dec8", [128, 8], F32)
        nwb = sb("db_nwb", [128, D], F32)
        em.dma(dec8[:], pr.din["db_logdec"])
        em.a("activation", dec8[:], dec8[:], AF.Exp, bias=0.0, scale=1.0)
        with pr.scope() as es3:
            nwrow = es3.enter_context(nc.sbuf_tensor("db_nwrow", [1, D], F32))
            ps512 = es3.enter_context(nc.psum_tensor("db_ps512", [128, 512], F32))
            em.dma(nwrow[:], pr.din["db_normw"])
            bcast_row_to_tile(pr, nwrow, nwb, ps512, D)
        if _KPHASE < 1:
            return
        with pr.scope() as es3:
            acc1 = es3.enter_context(nc.sbuf_tensor("db_acc1", [128, T], F32))
            acc2 = es3.enter_context(nc.sbuf_tensor("db_acc2", [128, T], F32))
            gs = es3.enter_context(nc.sbuf_tensor("db_gs", [128, T], BF16))
            cs = es3.enter_context(nc.sbuf_tensor("db_cs", [128, T], BF16))
            cw7 = es3.enter_context(nc.sbuf_tensor("db_cw7", [128, 7], F32))
            cb1 = es3.enter_context(nc.sbuf_tensor("db_cb1", [128, 1], F32))
            decb = es3.enter_context(nc.sbuf_tensor("db_decb", [128, T], F32))
            for c in range(NDC):
                xc = pr.xnT[:, c * T:(c + 1) * T]
                for (wname, bname, acc, outb, fn) in (
                        ("db_gatew", "db_gateb", acc1, gs, AF.Sigmoid),
                        ("db_candw", "db_candb", acc2, cs, AF.Tanh)):
                    em.dma(cw7[:], pr.din[wname][c * 128:(c + 1) * 128, :])
                    em.dma(cb1[:], pr.din[bname][c * 128:(c + 1) * 128, :])
                    em.v("tensor_scalar", acc[:], xc, cw7[:, 3:4], cb1[:],
                         ALU.mult, ALU.add)
                    for k in (0, 1, 2, 4, 5, 6):
                        sh = k - 3
                        if sh < 0:
                            em.v("scalar_tensor_tensor", acc[:, -sh:T],
                                 xc[:, 0:T + sh], cw7[:, k:k + 1],
                                 acc[:, -sh:T], ALU.mult, ALU.add)
                        else:
                            em.v("scalar_tensor_tensor", acc[:, 0:T - sh],
                                 xc[:, sh:T], cw7[:, k:k + 1],
                                 acc[:, 0:T - sh], ALU.mult, ALU.add)
                    em.a("activation", outb[:], acc[:], fn, bias=0.0, scale=1.0)
                gc = gated[:, c * T:(c + 1) * T]
                em.v("tensor_tensor", gc, gs[:], cs[:], ALU.mult)
                if _KPHASE >= 2:
                    em.v("tensor_scalar", decb[:], acc1[:], 0.0, dec8[:, c:c + 1],
                         ALU.mult, ALU.add)
                    em.v("tensor_tensor_scan", gc, decb[:], gc, 0.0, ALU.mult, ALU.add)
        if _KPHASE < 3:
            return
        with pr.scope() as es3:
            pr.es, saved_es = es3, pr.es
            wo = load_weight_bf16(pr, "db_wob", "db_woutT", D, D)
            pr.es = saved_es
            boutf = es3.enter_context(nc.sbuf_tensor("db_boutf", [1, D], F32))
            boutb = es3.enter_context(nc.sbuf_tensor("db_boutb", [1, D], BF16))
            yb = es3.enter_context(nc.sbuf_tensor("db_yb", [128, D], F32))
            u = es3.enter_context(nc.sbuf_tensor("db_u", [128, D], F32))
            dump = es3.enter_context(nc.sbuf_tensor("db_dump", [128, D], F32))
            ps_o = es3.enter_context(nc.psum_tensor("db_pso", [128, 1024], F32))
            em.dma(boutf[:], pr.din["db_bout"])
            em.v("tensor_copy", boutb[:], boutf[:])
            for tt in range(NTT):
                for os_ in range(2):
                    for c in range(NDC):
                        em.mm(ps_o[:, os_ * 512:(os_ + 1) * 512],
                              gated[:, c * T + tt * 128:c * T + (tt + 1) * 128],
                              wo[:, c * D + os_ * 512:c * D + (os_ + 1) * 512],
                              start=(c == 0), stop=False)
                    em.mm(ps_o[:, os_ * 512:(os_ + 1) * 512], pr.ones_row_b,
                          boutb[:, os_ * 512:(os_ + 1) * 512], start=False, stop=True)
                em.v("tensor_scalar", u[:], pr.xs[:, tt * D:(tt + 1) * D],
                     negm16[:, tt:tt + 1], rstd16[:, tt:tt + 1], ALU.add, ALU.mult)
                em.v("tensor_tensor", yb[:], ps_o[:], u[:], ALU.add)
                em.a("activation", dump[:], yb[:], AF.Square, bias=0.0, scale=1.0,
                     accum_out=pr.col2)
                em.v("tensor_scalar", pr.col2, pr.col2, 1.0 / D, 1.1920929e-07,
                     ALU.mult, ALU.add)
                em.a("activation", pr.col3, pr.col2, AF.Ln, bias=0.0, scale=1.0)
                em.a("activation", pr.col3, pr.col3, AF.Exp, bias=0.0, scale=-0.5)
                em.v("tensor_tensor", u[:], yb[:], nwb[:], ALU.mult)
                em.v("scalar_tensor_tensor", pr.xs[:, tt * D:(tt + 1) * D],
                     u[:], pr.col3, pr.xs[:, tt * D:(tt + 1) * D],
                     ALU.mult, ALU.add)



def emit_l1_attn_a(pr):
    em = pr.em
    nc = pr.nc
    DI = 2048
    NH = 32
    PP = 64
    zdram = nc.dram_tensor(f"sa_z_{pr.uid}", [T, DI], BF16, kind="Internal").ap()
    ydram = nc.dram_tensor(f"sa_y_{pr.uid}", [T, DI], BF16, kind="Internal").ap()
    xdram = nc.dram_tensor(f"sa_x_{pr.uid}", [DI, T], BF16, kind="Internal").ap()
    lscr = nc.dram_tensor(f"sa_l_{pr.uid}", [NH, 128], F32, kind="Internal").ap()
    pr.uid += 1
    with pr.scope() as es2:
        sb = lambda n, s, d: es2.enter_context(nc.sbuf_tensor(n, s, d))
        bc_sb = sb("sa_bc", [128, 2 * T], BF16)
        lam = sb("sa_lam", [32, T], F32)
        U = sb("sa_U", [32, T], F32)
        acol = sb("sa_acol", [32, 1], F32)
        dtb = sb("sa_dtbc", [32, 1], F32)
        ones32 = sb("sa_ones32", [32, 128], F32)
        em.v("memset", ones32[:], 1.0)
        em.dma(dtb[:], pr.din["sa_dtb"])
        em.dma(acol[:], pr.din["sa_alog"])
        em.a("activation", acol[:], acol[:], AF.Exp, bias=0.0, scale=1.0)
        em.v("tensor_scalar", acol[:], acol[:], -1.0, None, ALU.mult)

        with pr.scope() as es3:
            sb3 = lambda n, s, d: es3.enter_context(nc.sbuf_tensor(n, s, d))
            xbc = sb3("sa_xbc", [128, 16 * T], BF16)
            dtR = sb3("sa_dt", [32, T], F32)
            with pr.scope() as es4:
                xnT = es4.enter_context(nc.sbuf_tensor("sa_xnT", [128, NDC * T], BF16))
                pr.xnT = xnT[:]
                emit_ln_all(pr)
                psA = es4.enter_context(nc.psum_tensor("sa_psA", [128, 2048], F32))
                wstg = es4.enter_context(nc.sbuf_tensor("sa_wstg", [128, 512], F32))
                wslb = es4.enter_context(nc.sbuf_tensor("sa_wslb", [128, 512], BF16))
                zb = es4.enter_context(nc.sbuf_tensor("sa_zb", [128, 1024], BF16))
                wsrc = pr.din["sa_winT"].rearrange("(dc p) f -> dc p f", p=128)
                for t0 in range(0, NTT, 2):
                    for zh in range(2):
                        for zs in range(2):
                            for dc in range(NDC):
                                em.dma(wstg[:], wsrc[dc, :, zh * 1024 + zs * 512:
                                                       zh * 1024 + (zs + 1) * 512])
                                em.v("tensor_copy", wslb[:], wstg[:])
                                for i in range(2):
                                    em.mm(psA[:, i * 1024 + zs * 512:
                                              i * 1024 + (zs + 1) * 512],
                                          pr.xnT[:, dc * T + (t0 + i) * 128:
                                                 dc * T + (t0 + i + 1) * 128],
                                          wslb[:], start=(dc == 0), stop=(dc == NDC - 1))
                        for i in range(2):
                            em.v("tensor_copy", zb[:], psA[:, i * 1024:(i + 1) * 1024])
                            em.dma(zdram.rearrange("(tt p) f -> tt p f", p=128)
                                   [t0 + i, :, zh * 1024:(zh + 1) * 1024], zb[:])
                for fc in range(18):
                    for dc in range(NDC):
                        em.dma(wstg[:, 0:128], wsrc[dc, :, DI + fc * 128:DI + (fc + 1) * 128])
                        em.v("tensor_copy", wslb[:, 0:128], wstg[:, 0:128])
                        for ts4 in range(4):
                            em.mm(psA[:, ts4 * 512:(ts4 + 1) * 512],
                                  wslb[:, 0:128],
                                  pr.xnT[:, dc * T + ts4 * 512:dc * T + (ts4 + 1) * 512],
                                  start=(dc == 0), stop=(dc == NDC - 1))
                    dst = (xbc[:, fc * T:(fc + 1) * T] if fc < 16
                           else bc_sb[:, (fc - 16) * T:(fc - 15) * T])
                    em.v("tensor_copy", dst, psA[:])
                for dc in range(NDC):
                    em.dma(wstg[:, 0:32], wsrc[dc, :, DI + 2304:DI + 2304 + 32])
                    em.v("tensor_copy", wslb[:, 0:32], wstg[:, 0:32])
                    for ts4 in range(4):
                        em.mm(psA[0:32, ts4 * 512:(ts4 + 1) * 512],
                              wslb[:, 0:32],
                              pr.xnT[:, dc * T + ts4 * 512:dc * T + (ts4 + 1) * 512],
                              start=(dc == 0), stop=(dc == NDC - 1))
                em.v("tensor_scalar", dtR[:], psA[0:32, :], dtb[:], None, ALU.add)

            with pr.scope() as es4:
                accC = es4.enter_context(nc.sbuf_tensor("sa_accC", [128, T], F32))
                cw4 = es4.enter_context(nc.sbuf_tensor("sa_cw4", [128, 4], F32))
                cb1 = es4.enter_context(nc.sbuf_tensor("sa_cb1", [128, 1], F32))
                for fc in range(18):
                    xc = (xbc[:, fc * T:(fc + 1) * T] if fc < 16
                          else bc_sb[:, (fc - 16) * T:(fc - 15) * T])
                    em.dma(cw4[:], pr.din["sa_convw"][fc * 128:(fc + 1) * 128, :])
                    em.dma(cb1[:], pr.din["sa_convb"][fc * 128:(fc + 1) * 128, :])
                    em.v("tensor_scalar", accC[:], xc, cw4[:, 3:4], cb1[:],
                         ALU.mult, ALU.add)
                    for k in (0, 1, 2):
                        sh = 3 - k
                        em.v("scalar_tensor_tensor", accC[:, sh:T], xc[:, 0:T - sh],
                             cw4[:, k:k + 1], accC[:, sh:T], ALU.mult, ALU.add)
                    if fc < 16:
                        em.a("activation", xc, accC[:], AF.Silu, bias=0.0, scale=1.0)
                        em.dma(xdram.rearrange("(fc p) t -> fc p t", p=128)[fc], xc)
                    else:
                        dst = bc_sb[:, (fc - 16) * T:(fc - 15) * T]
                        em.a("activation", dst, accC[:], AF.Silu, bias=0.0, scale=1.0)

            with pr.scope() as es4:
                t32 = es4.enter_context(nc.sbuf_tensor("sa_t32", [32, T], F32))
                em.a("activation", t32[:], dtR[:], AF.Abs, bias=0.0, scale=1.0)
                em.a("activation", t32[:], t32[:], AF.Exp, bias=0.0, scale=-1.0)
                em.a("activation", t32[:], t32[:], AF.Ln, bias=1.0, scale=1.0)
                em.v("scalar_tensor_tensor", dtR[:], dtR[:], 0.0, t32[:],
                     ALU.max, ALU.add)
                em.v("tensor_scalar", t32[:], dtR[:], 1e-38, None, ALU.max)
                em.a("activation", U[:], t32[:], AF.Ln, bias=0.0, scale=1.0)
                em.v("tensor_scalar", t32[:], dtR[:], acol[:], None, ALU.mult)
                for c in range(NTT):
                    em.v("tensor_tensor_scan", lam[:, c * 128:(c + 1) * 128],
                         ones32[:], t32[:, c * 128:(c + 1) * 128], 0.0,
                         ALU.mult, ALU.add)
                em.v("tensor_sub", U[:], U[:], lam[:])

        with pr.scope() as es3:
            sb3 = lambda n, s, d: es3.enter_context(nc.sbuf_tensor(n, s, d))
            H = sb3("sa_H", [128, DI], F32)
            Hb = sb3("sa_Hb", [128, DI], BF16)
            Mw = sb3("sa_Mw", [128, NH * 128], BF16)
            lamM = sb3("sa_lamM", [128, NH * 128], F32)
            Xstage = sb3("sa_Xstage", [128, DI], BF16)
            Xtok = sb3("sa_Xtok", [128, DI], BF16)
            Xw = sb3("sa_Xw", [128, DI], BF16)
            Btok = sb3("sa_Btok", [128, 128], BF16)
            Gb = sb3("sa_Gb", [128, 128], BF16)
            Utok = sb3("sa_Utok", [128, 32], F32)
            Ltok = sb3("sa_Ltok", [128, 32], F32)
            lend = sb3("sa_lend", [128, 32], F32)
            wS = sb3("sa_wS", [128, 32], F32)
            gam = sb3("sa_gam", [128, 32], F32)
            expl = sb3("sa_expl", [128, 32], F32)
            Db32 = sb3("sa_Db32", [128, 32], F32)
            dcol = sb3("sa_dcol", [32, 1], F32)
            ycbf = sb3("sa_ycbf", [128, DI], BF16)
            lrow = sb3("sa_lrow", [1, 2048], F32)
            r11 = sb3("sa_r11", [1, 32], F32)
            psS = es3.enter_context(nc.psum_tensor("sa_psYS", [128, 2048], F32))
            psY = psS
            psT = es3.enter_context(nc.psum_tensor("sa_psT", [128, 2048], BF16))
            psF = es3.enter_context(nc.psum_tensor("sa_psF", [128, 512], F32))
            em.v("memset", H[:], 0.0)
            em.v("memset", Hb[:], 0.0)
            em.dma(dcol[:], pr.din["sa_d"])
            em.t("transpose", psF[0:1, 0:32], dcol[:], pr.ident32f)
            em.v("tensor_copy", r11[:], psF[0:1, 0:32])
            em.mm(psF[:, 32:64], pr.ones_row_f, r11[:], start=True, stop=True)
            em.v("tensor_copy", Db32[:], psF[:, 32:64])
            lam3 = lamM[:].rearrange("p (h t) -> p h t", t=128)
            Mw3 = Mw[:].rearrange("p (h t) -> p h t", t=128)
            for c in range(NTT):
                cs_ = slice(c * 128, (c + 1) * 128)
                em.t("transpose", psF[:, 32:64], U[:, cs_], pr.ident32f)
                em.v("tensor_copy", Utok[:], psF[:, 32:64])
                em.t("transpose", psF[:, 64:96], lam[:, cs_], pr.ident32f)
                em.v("tensor_copy", Ltok[:], psF[:, 64:96])
                em.t("transpose", psF[0:1, 96:128],
                     lam[:, c * 128 + 127:c * 128 + 128], pr.ident32f)
                em.v("tensor_copy", r11[:], psF[0:1, 96:128])
                em.mm(psF[:, 128:160], pr.ones_row_f, r11[:], start=True, stop=True)
                em.v("tensor_copy", lend[:], psF[:, 128:160])
                em.v("tensor_tensor", wS[:], Utok[:], lend[:], ALU.add)
                em.a("activation", wS[:], wS[:], AF.Exp, bias=0.0, scale=1.0)
                em.a("activation", gam[:], lend[:], AF.Exp, bias=0.0, scale=1.0)
                em.a("activation", expl[:], Ltok[:], AF.Exp, bias=0.0, scale=1.0)
                em.dma(lscr, lam[:, cs_])
                for half in range(2):
                    em.dma(lrow[:], lscr.rearrange("p t -> (p t)").rearrange(
                        "(one f) -> one f", one=1)[:, half * 2048:(half + 1) * 2048])
                    for q4 in range(4):
                        em.mm(psS[:, q4 * 512:(q4 + 1) * 512], pr.ones_row_f,
                              lrow[:, q4 * 512:(q4 + 1) * 512], start=True, stop=True)
                    em.v("tensor_copy", lamM[:, half * 2048:(half + 1) * 2048], psS[:])
                em.v("tensor_tensor", lam3,
                     pr.maskneg.unsqueeze(1).broadcast_to([128, NH, 128]),
                     lam3, ALU.add)
                em.v("tensor_tensor", lam3,
                     Utok[:].unsqueeze(2).broadcast_to([128, NH, 128]),
                     lam3, ALU.add)
                em.a("activation", Mw[:], lamM[:], AF.Exp, bias=0.0, scale=1.0)
                em.mm(psF[:, 0:128], bc_sb[:, cs_],
                      bc_sb[:, T + c * 128:T + (c + 1) * 128], start=True, stop=True)
                em.v("tensor_copy", Gb[:], psF[:, 0:128])
                em.v("tensor_tensor", Mw3, Gb[:].unsqueeze(1).broadcast_to([128, NH, 128]),
                     Mw3, ALU.mult)
                em.dma(Xstage[:].rearrange("p (i t) -> p i t", t=128),
                       xdram.rearrange("(i p) t -> p i t", p=128)[:, :, cs_])
                for i in range(16):
                    em.t("transpose", psT[:, i * 128:(i + 1) * 128],
                         Xstage[:, i * 128:(i + 1) * 128], pr.ident[:])
                em.v("tensor_copy", Xtok[:], psT[:])
                em.t("transpose", psT[:, 0:128], bc_sb[:, cs_], pr.ident[:])
                em.v("tensor_copy", Btok[:], psT[:, 0:128])
                em.v("tensor_tensor", Xw[:].rearrange("p (h w) -> p h w", w=PP),
                     Xtok[:].rearrange("p (h w) -> p h w", w=PP),
                     wS[:].unsqueeze(2).broadcast_to([128, NH, PP]), ALU.mult)
                XDv = lamM[:, 2048:4096].rearrange("p (h w) -> p h w", w=PP)
                em.v("tensor_tensor", XDv,
                     Xtok[:].rearrange("p (h w) -> p h w", w=PP),
                     Db32[:].unsqueeze(2).broadcast_to([128, NH, PP]), ALU.mult)
                for hh in range(2):
                    for hi in range(16):
                        h = hh * 16 + hi
                        em.mm(psY[:, hi * PP:(hi + 1) * PP],
                              Mw[:, h * 128:(h + 1) * 128],
                              Xtok[:, h * PP:(h + 1) * PP],
                              start=(hi % 8 == 0), stop=(hi % 8 == 7))
                        em.mm(psY[:, 1024 + hi * PP:1024 + (hi + 1) * PP],
                              bc_sb[:, T + c * 128:T + (c + 1) * 128],
                              Hb[:, h * PP:(h + 1) * PP],
                              start=(hi % 8 == 0), stop=(hi % 8 == 7))
                    ytmp = lamM[:, 0:1024]
                    em.v("tensor_tensor", ytmp.rearrange("p (h w) -> p h w", w=PP),
                         psY[:, 1024:2048].rearrange("p (h w) -> p h w", w=PP),
                         expl[:, hh * 16:(hh + 1) * 16].unsqueeze(2)
                         .broadcast_to([128, 16, PP]), ALU.mult)
                    em.v("tensor_tensor", ytmp, psY[:, 0:1024], ytmp, ALU.add)
                    em.v("tensor_tensor", ycbf[:, hh * 1024:(hh + 1) * 1024], ytmp,
                         lamM[:, 2048 + hh * 1024:2048 + (hh + 1) * 1024], ALU.add)
                em.dma(ydram.rearrange("(c p) f -> c p f", p=128)[c], ycbf[:])
                for h in range(NH):
                    em.mm(psS[:, h * PP:(h + 1) * PP], Btok[:],
                          Xw[:, h * PP:(h + 1) * PP],
                          start=(h % 8 == 0), stop=(h % 8 == 7))
                em.v("tensor_tensor", H[:].rearrange("p (h w) -> p h w", w=PP),
                     H[:].rearrange("p (h w) -> p h w", w=PP),
                     gam[:].unsqueeze(2).broadcast_to([128, NH, PP]), ALU.mult)
                em.v("tensor_tensor", H[:], psS[:], H[:], ALU.add)
                em.v("tensor_copy", Hb[:], H[:])

        with pr.scope() as es3:
            sb3 = lambda n, s, d: es3.enter_context(nc.sbuf_tensor(n, s, d))
            nw2b = sb3("sa_nw2b", [128, DI], F32)
            ytile = sb3("sa_ytile", [128, DI], BF16)
            ztile = sb3("sa_ztile", [128, DI], BF16)
            zs_ = sb3("sa_zs", [128, DI], F32)
            yz = sb3("sa_yz", [128, DI], F32)
            uu = sb3("sa_uu", [128, DI], F32)
            ynb = sb3("sa_ynb", [128, DI], BF16)
            ynT = sb3("sa_ynT", [128, DI], BF16)
            dump = sb3("sa_dump", [128, DI], F32)
            nwrow = sb3("sa_nwrow", [1, DI], F32)
            psO = es3.enter_context(nc.psum_tensor("sa_psO", [128, 1024], F32))
            psT2 = es3.enter_context(nc.psum_tensor("sa_psT2", [128, 2048], BF16))
            ps512b = es3.enter_context(nc.psum_tensor("sa_ps512b", [128, 512], F32))
            pr.es, saved_es = es3, pr.es
            wo = load_weight_bf16(pr, "sa_wob", "sa_woutT", DI, D)
            pr.es = saved_es
            em.dma(nwrow[:], pr.din["sa_normw"])
            bcast_row_to_tile(pr, nwrow, nw2b, ps512b, DI)
            for tt in range(NTT):
                em.dma(ytile[:], ydram.rearrange("(c p) f -> c p f", p=128)[tt])
                em.dma(ztile[:], zdram.rearrange("(c p) f -> c p f", p=128)[tt])
                em.a("activation", zs_[:], ztile[:], AF.Silu, bias=0.0, scale=1.0)
                em.v("tensor_tensor", yz[:], ytile[:], zs_[:], ALU.mult)
                em.a("activation", dump[:], yz[:], AF.Square, bias=0.0, scale=1.0,
                     accum_out=pr.col2)
                em.v("tensor_scalar", pr.col2, pr.col2, 1.0 / DI, 1e-5,
                     ALU.mult, ALU.add)
                em.a("activation", pr.col3, pr.col2, AF.Ln, bias=0.0, scale=1.0)
                em.a("activation", pr.col3, pr.col3, AF.Exp, bias=0.0, scale=-0.5)
                em.v("tensor_tensor", uu[:], yz[:], nw2b[:], ALU.mult)
                em.v("tensor_scalar", ynb[:], uu[:], pr.col3, None, ALU.mult)
                for i in range(16):
                    em.t("transpose", psT2[:, i * 128:(i + 1) * 128],
                         ynb[:, i * 128:(i + 1) * 128], pr.ident[:])
                em.v("tensor_copy", ynT[:], psT2[:])
                for os_ in range(2):
                    for ic in range(16):
                        em.mm(psO[:, os_ * 512:(os_ + 1) * 512],
                              ynT[:, ic * 128:(ic + 1) * 128],
                              wo[:, ic * D + os_ * 512:ic * D + (os_ + 1) * 512],
                              start=(ic == 0), stop=(ic == 15))
                em.v("tensor_tensor", pr.xs[:, tt * D:(tt + 1) * D], psO[:],
                     pr.xs[:, tt * D:(tt + 1) * D], ALU.add)


EMITTERS = {
    "l0aa": emit_l0_attn_a,
    "l0ab": emit_l0_attn_b,
    "l0ffn1": emit_l0_ffn,
    "l0ffn2": emit_l0_ffn,
    "l1ffn1": emit_l1_ffn,
    "l1ffn2": emit_l1_ffn,
    "l1ab": emit_l1_attn_b,
    "l1aa": emit_l1_attn_a,
}


def build_program(parts):
    nc = bass.Bass("TRN2", target_bir_lowering=False)
    x_in = nc.dram_tensor("x_in", [T, D], F32, kind="ExternalInput").ap()
    x_out = nc.dram_tensor("x_out", [T, D], F32, kind="ExternalOutput").ap()
    names = set(CONST_INPUTS)
    for p_ in parts:
        names.update(PART_INPUTS[p_])
    dins = {}
    for n in sorted(names):
        shp = PREP_SHAPES[n]
        dt = BF16 if n in ("c_ident128", "c_ones_row_b", "c_triu") else F32
        dins[n] = nc.dram_tensor(n, list(shp), dt, kind="ExternalInput").ap()

    em = E(nc)
    global _LAST_EM
    _LAST_EM = em
    with contextlib.ExitStack() as es:
        pr = Prog(NCProxy(nc), em, es, dins)
        setup_shared(pr)
        em.dma(pr.xs[:].rearrange("p (tt d) -> p tt d", d=D),
               x_in.rearrange("(tt p) d -> p tt d", p=128))
        for p_ in parts:
            em.mark(p_)
            EMITTERS[p_](pr)
        em.mark("end")
        em.dma(x_out.rearrange("(tt p) d -> p tt d", p=128),
               pr.xs[:].rearrange("p (tt d) -> p tt d", d=D))
        esem = {e: es.enter_context(nc.semaphore(f"SE_{e}")) for e in E.ENGS}
        dsem = [es.enter_context(nc.semaphore(f"SD_{i}")) for i in range(E.NDMA)]
        block = es.enter_context(nc.Block())
        em.replay(block, (esem, dsem))
    return nc, sorted(names)


PREP_SHAPES = {
    "c_ident128": (128, 128), "c_ident32f": (32, 32), "c_ident128f": (128, 128),
    "c_ones_row_f": (1, 128), "c_maskneg": (128, 128),
    "c_ones_row_b": (1, 128), "c_triu": (128, 128),
    "aa_wqT": (D, D), "aa_wkT": (D, D), "aa_wvT": (D, D),
    "ab_wqkv": (D, 3072), "ab_wout": (D, D), "ab_bout": (1, D), "ab_ls": (16, 1),
    "f0_w1T": (D, 4096), "f0_w2T": (4096, D), "f0_convw": (4096, 3), "f0_convb": (4096, 1),
    "sa_winT": (D, 4384), "sa_convw": (2304, 4), "sa_convb": (2304, 1),
    "sa_dtb": (32, 1), "sa_alog": (32, 1), "sa_d": (32, 1), "sa_normw": (1, 2048),
    "sa_woutT": (2048, D),
    "db_gatew": (D, 7), "db_gateb": (D, 1), "db_candw": (D, 7), "db_candb": (D, 1),
    "db_logdec": (128, 8), "db_woutT": (D, D), "db_bout": (1, D), "db_normw": (1, D),
    "f1_w1T": (D, 4096), "f1_b1": (1, 4096), "f1_w2T": (4096, D), "f1_b2": (1, D),
}


def run_parts(x_full, prepped, parts, sim=False):
    nc, names = build_program(parts)
    in_maps = []
    ncores = 1 if sim else 8
    for b in range(ncores):
        m = {"x_in": np.ascontiguousarray(x_full[b])}
        for n in names:
            arr = prepped[n]
            want_bf = n in ("c_ident128", "c_ones_row_b", "c_triu")
            m[n] = np.ascontiguousarray(arr.astype(bf16 if want_bf else np.float32))
        in_maps.append(m)
    if sim:
        from concourse.bass_interp import MultiCoreSim
        nc.finalize()
        s = MultiCoreSim(nc, num_cores=1, num_workers=1, trace=False)
        for k, v in in_maps[0].items():
            s.cores[0].tensor(k)[:] = v
        s.simulate()
        return np.asarray(s.cores[0].tensor("x_out"))[None]
    res = run_bass_kernel_spmd(nc, in_maps, core_ids=list(range(8)))
    return np.stack([r["x_out"] for r in res.results])


def kernel(**inputs):
    x = np.ascontiguousarray(np.asarray(inputs["L0_attn_a_z"], dtype=np.float32))
    prepped = prep_inputs(inputs)
    return run_parts(x, prepped, ALL_PARTS)


# version 16
# speedup vs baseline: 1.3785x; 1.2759x over previous
import contextlib
import numpy as np
import ml_dtypes
import concourse.bass as bass
import concourse.mybir as mybir
from concourse.bass_utils import run_bass_kernel_spmd

F32 = mybir.dt.float32
BF16 = mybir.dt.bfloat16
I32 = mybir.dt.int32
AF = mybir.ActivationFunctionType
ALU = mybir.AluOpType
AX = mybir.AxisListType
bf16 = ml_dtypes.bfloat16

T = 2048
D = 1024
NTT = T // 128
NDC = D // 128

ALL_PARTS = ["l0aa", "l0ffn1", "l0ab", "l0ffn2", "l1aa", "l1ffn1", "l1ab", "l1ffn2"]
_LAST_EM = None


class E:

    ENGS = ("vector", "scalar", "tensor", "gpsimd", "sync")
    NDMA = 16

    def __init__(self, nc):
        self.nc = nc
        self.ops = {e: [] for e in self.ENGS}
        self.cnt = {e: 0 for e in self.ENGS}
        self.dma_uses = [0] * self.NDMA
        self.dma_next = 0
        self.highwater = {}
        self.last_writer = {}
        self.readers = {}

    @staticmethod
    def _aps(x):
        return [a for a in x if type(a).__name__ == "AP"]

    def _dep_wait(self, waits, dep, eng=None):
        kind = dep[0]
        if kind == "E":
            _, deng, dcnt = dep
            if deng == eng and eng == "tensor":
                return
            key = ("E", deng)
            if waits.get(key, 0) < dcnt:
                waits[key] = dcnt
        else:
            _, slot, val = dep
            key = ("D", slot)
            if waits.get(key, 0) < val:
                waits[key] = val

    def mark(self, label):
        self.marks = getattr(self, "marks", [])
        self.marks.append((label, dict(self.cnt), sum(self.dma_uses)))

    def barrier(self):
        self._barrier_tokens = (
            [("E", e, c) for e, c in self.cnt.items() if c > 0] +
            [("D", s, 16 * u) for s, u in enumerate(self.dma_uses) if u > 0])
        self._barrier_done = set()
        self.last_writer = {}
        self.readers = {}

    def _desc(self, eng, fn, args, kwargs, dma=False):
        aps = self._aps(args) + self._aps(list(kwargs.values()))
        if dma:
            writes = [kwargs["out"]] if "out" in kwargs else [args[0]]
            reads = [a for a in aps if a is not writes[0]]
        else:
            writes = aps[:1]
            if kwargs.get("accum_out") is not None:
                writes = writes + [kwargs["accum_out"]]
            reads = [a for a in aps if all(a is not w for w in writes)]
        waits = {}
        if getattr(self, "_barrier_tokens", None) and eng not in self._barrier_done:
            for dep in self._barrier_tokens:
                self._dep_wait(waits, dep, None)
            self._barrier_done.add(eng)
        rnames = [a.name for a in reads]
        wnames = [a.name for a in writes]
        for t in rnames:
            w = self.last_writer.get(t)
            if w is not None:
                self._dep_wait(waits, w, eng)
        for t in wnames:
            w = self.last_writer.get(t)
            if w is not None:
                self._dep_wait(waits, w, eng)
            for r in self.readers.get(t, ()):
                self._dep_wait(waits, r, eng)
        if dma:
            slot = self.dma_next
            self.dma_next = (self.dma_next + 1) % self.NDMA
            if self.dma_uses[slot] > 0:
                key = ("D", slot)
                if waits.get(key, 0) < 16 * self.dma_uses[slot]:
                    waits[key] = 16 * self.dma_uses[slot]
            self.dma_uses[slot] += 1
            tok = ("D", slot, 16 * self.dma_uses[slot])
            inc = ("D", slot)
        else:
            self.cnt[eng] += 1
            tok = ("E", eng, self.cnt[eng])
            inc = ("E", eng)
        for t in rnames:
            self.readers.setdefault(t, []).append(tok)
        for t in wnames:
            self.last_writer[t] = tok
            self.readers[t] = []
        hw = self.highwater.setdefault(eng, {})
        pruned = {}
        for key, val in waits.items():
            if hw.get(key, -1) < val:
                pruned[key] = val
                hw[key] = val
        self.ops[eng].append((pruned, fn, inc))

    def v(self, meth, *a, **k):
        self._desc("vector", lambda e, a=a, k=k, m=meth: getattr(e, m)(*a, **k), a, k)

    def a(self, meth, *a, **k):
        self._desc("scalar", lambda e, a=a, k=k, m=meth: getattr(e, m)(*a, **k), a, k)

    def t(self, meth, *a, **k):
        self._desc("tensor", lambda e, a=a, k=k, m=meth: getattr(e, m)(*a, **k), a, k)

    def g(self, meth, *a, **k):
        self._desc("gpsimd", lambda e, a=a, k=k, m=meth: getattr(e, m)(*a, **k), a, k)

    def dma(self, out, in_, eng=None):
        if eng is None:
            eng = ("sync", "gpsimd")[self.dma_next % 2]
        self._desc(eng, lambda e, o=out, i=in_: e.dma_start(out=o, in_=i),
                   (), {"out": out, "in_": in_}, dma=True)

    def dma_t(self, out, in_, eng=None):
        if eng is None:
            eng = "sync"
        self._desc(eng, lambda e, o=out, i=in_: e.dma_start(out=o, in_=i, transpose=True),
                   (), {"out": out, "in_": in_}, dma=True)

    def mm(self, out, lhsT, rhs, start, stop, tile_position=None):
        self._desc("tensor",
                   lambda e, o=out, l=lhsT, r=rhs, s1=start, s2=stop, tp=tile_position:
                   e.matmul(o, l, r, start=s1, stop=s2, tile_position=tp),
                   (out, lhsT, rhs), {})

    def replay(self, block, sems):
        esem, dsem = sems
        total_e = dict(self.cnt)
        total_d = list(self.dma_uses)

        for name in self.ENGS:
            ops = self.ops[name]

            def section(eng, ops=ops, name=name):
                for (waits, fn, inc) in ops:
                    for (key, val) in sorted(waits.items(), key=str):
                        if key[0] == "E":
                            eng.wait_ge(esem[key[1]], val)
                        else:
                            eng.wait_ge(dsem[key[1]], val)
                    inst = fn(eng)
                    if inc[0] == "E":
                        inst.then_inc(esem[inc[1]], 1)
                    else:
                        inst.then_inc(dsem[inc[1]], 16)
                if name == "sync":
                    for e2, c in total_e.items():
                        if c > 0:
                            eng.wait_ge(esem[e2], c)
                    for s, u in enumerate(total_d):
                        if u > 0:
                            eng.wait_ge(dsem[s], 16 * u)

            getattr(block, name)(section)


def prep_inputs(inp):
    p = {}
    c = np.ascontiguousarray
    p["c_ident128"] = np.eye(128, dtype=bf16)
    p["c_ident32f"] = np.eye(32, dtype=np.float32)
    p["c_ident128f"] = np.eye(128, dtype=np.float32)
    p["c_ones_row_f"] = np.ones((1, 128), dtype=np.float32)
    p["c_ones_row_b"] = np.ones((1, 128), dtype=bf16)
    p["c_triu"] = np.triu(np.ones((128, 128), dtype=np.float32)).astype(bf16)
    mn = np.zeros((128, 128), dtype=np.float32)
    mn[np.tril_indices(128, -1)] = -1e30
    p["c_maskneg"] = mn

    W = np.asarray(inp["L0_attn_a_W_qkv"]).reshape(16, 3, 64, D)
    p["aa_wqT"] = c(W[:, 0].reshape(D, D).T)
    p["aa_wkT"] = c(W[:, 1].reshape(D, D).T)
    p["aa_wvT"] = c(W[:, 2].reshape(D, D).T)

    p["ab_wqkv"] = c(np.asarray(inp["L0_attn_b_w_qkv"]))
    p["ab_wout"] = c(np.asarray(inp["L0_attn_b_w_out"]))
    p["ab_bout"] = c(np.asarray(inp["L0_attn_b_b_out"]).reshape(1, D))
    p["ab_ls"] = c(np.asarray(inp["L0_attn_b_logit_scale"]).reshape(16, 1))

    p["f0_w1T"] = c(np.asarray(inp["L0_ffn_w1"]).T)
    p["f0_w2T"] = c(np.asarray(inp["L0_ffn_w2"]).T)
    p["f0_convw"] = c(np.asarray(inp["L0_ffn_conv_w"]).reshape(4096, 3))
    p["f0_convb"] = c(np.asarray(inp["L0_ffn_conv_b"]).reshape(4096, 1))

    p["sa_winT"] = c(np.asarray(inp["L1_attn_a_W_in"]).T)
    p["sa_convw"] = c(np.asarray(inp["L1_attn_a_conv_w"]))
    p["sa_convb"] = c(np.asarray(inp["L1_attn_a_conv_b"]).reshape(2304, 1))
    p["sa_dtb"] = c(np.asarray(inp["L1_attn_a_dt_bias"]).reshape(32, 1))
    p["sa_alog"] = c(np.asarray(inp["L1_attn_a_A_log"]).reshape(32, 1))
    p["sa_d"] = c(np.asarray(inp["L1_attn_a_D"]).reshape(32, 1))
    p["sa_normw"] = c(np.asarray(inp["L1_attn_a_norm_w"]).reshape(1, 2048))
    p["sa_woutT"] = c(np.asarray(inp["L1_attn_a_W_out"]).T)

    p["db_gatew"] = c(np.asarray(inp["L1_attn_b_gate_w"]).reshape(D, 7))
    p["db_gateb"] = c(np.asarray(inp["L1_attn_b_gate_b"]).reshape(D, 1))
    p["db_candw"] = c(np.asarray(inp["L1_attn_b_cand_w"]).reshape(D, 7))
    p["db_candb"] = c(np.asarray(inp["L1_attn_b_cand_b"]).reshape(D, 1))
    p["db_logdec"] = c(np.asarray(inp["L1_attn_b_log_decay"]).reshape(8, 128).T)
    p["db_woutT"] = c(np.asarray(inp["L1_attn_b_out_w"]).T)
    p["db_bout"] = c(np.asarray(inp["L1_attn_b_out_b"]).reshape(1, D))
    p["db_normw"] = c(np.asarray(inp["L1_attn_b_norm_w"]).reshape(1, D))

    p["f1_w1T"] = c(np.asarray(inp["L1_ffn_W1"]).T)
    p["f1_b1"] = c(np.asarray(inp["L1_ffn_b1"]).reshape(1, 4096))
    p["f1_w2T"] = c(np.asarray(inp["L1_ffn_W2"]).T)
    p["f1_b2"] = c(np.asarray(inp["L1_ffn_b2"]).reshape(1, D))
    return p


PART_INPUTS = {
    "l0aa": ["aa_wqT", "aa_wkT", "aa_wvT"],
    "l0ab": ["ab_wqkv", "ab_wout", "ab_bout", "ab_ls"],
    "l0ffn1": ["f0_w1T", "f0_w2T", "f0_convw", "f0_convb"],
    "l0ffn2": ["f0_w1T", "f0_w2T", "f0_convw", "f0_convb"],
    "l1aa": ["sa_winT", "sa_convw", "sa_convb", "sa_dtb", "sa_alog", "sa_d",
             "sa_normw", "sa_woutT"],
    "l1ab": ["db_gatew", "db_gateb", "db_candw", "db_candb", "db_logdec",
             "db_woutT", "db_bout", "db_normw"],
    "l1ffn1": ["f1_w1T", "f1_b1", "f1_w2T", "f1_b2"],
    "l1ffn2": ["f1_w1T", "f1_b1", "f1_w2T", "f1_b2"],
}
CONST_INPUTS = ["c_ident128", "c_ident32f", "c_ident128f", "c_ones_row_f",
                "c_ones_row_b", "c_triu", "c_maskneg"]


class NCProxy:

    def __init__(self, nc):
        self._nc = nc
        self._used = {}

    def _uniq(self, name):
        n = self._used.get(name, 0)
        self._used[name] = n + 1
        return name if n == 0 else f"{name}__{n}"

    def sbuf_tensor(self, name, *a, **k):
        return self._nc.sbuf_tensor(self._uniq(name), *a, **k)

    def psum_tensor(self, name, *a, **k):
        return self._nc.psum_tensor(self._uniq(name), *a, **k)

    def dram_tensor(self, name, *a, **k):
        return self._nc.dram_tensor(self._uniq(name), *a, **k)

    def __getattr__(self, a):
        return getattr(self._nc, a)


class Prog:

    def __init__(self, nc, em, es, dins):
        self.nc = nc
        self.em = em
        self.es = es
        self.din = dins
        self.sb = {}
        self.uid = 0

    def sbuf(self, name, shape, dtype):
        t = self.es.enter_context(self.nc.sbuf_tensor(name, shape, dtype))
        return t

    def psum(self, name, shape, dtype):
        return self.es.enter_context(self.nc.psum_tensor(name, shape, dtype))

    @contextlib.contextmanager
    def scope(self):
        with contextlib.ExitStack() as es:
            yield es
        self.em.barrier()


def setup_shared(pr):
    em = pr.em
    pr.xs = pr.sbuf("xs", [128, NTT * D], F32)[:]
    pr.ident = pr.sbuf("ident", [128, 128], BF16)[:]
    pr.ident32f = pr.sbuf("ident32f", [32, 32], F32)[:]
    pr.ident128f = pr.sbuf("ident128f", [128, 128], F32)[:]
    pr.ones_row_f = pr.sbuf("ones_row_f", [1, 128], F32)[:]
    pr.ones_row_b = pr.sbuf("ones_row_b", [1, 128], BF16)[:]
    pr.triu = pr.sbuf("triu", [128, 128], BF16)[:]
    pr.maskneg = pr.sbuf("maskneg", [128, 128], F32)[:]
    pr.col1 = pr.sbuf("col1", [128, 1], F32)[:]
    pr.col2 = pr.sbuf("col2", [128, 1], F32)[:]
    pr.col3 = pr.sbuf("col3", [128, 1], F32)[:]
    em.dma(pr.ident[:], pr.din["c_ident128"])
    em.dma(pr.ident32f[:], pr.din["c_ident32f"])
    em.dma(pr.ident128f[:], pr.din["c_ident128f"])
    em.dma(pr.ones_row_f[:], pr.din["c_ones_row_f"])
    em.dma(pr.ones_row_b[:], pr.din["c_ones_row_b"])
    em.dma(pr.triu[:], pr.din["c_triu"])
    em.dma(pr.maskneg[:], pr.din["c_maskneg"])


def load_weight_bf16(pr, name, dram_name, rows, cols, tag=None):
    em = pr.em
    nch = rows // 128
    wb = pr.sbuf(name, [128, nch * cols], BF16)
    src = pr.din[dram_name].rearrange("(c p) f -> c p f", p=128)
    with pr.scope() as es_w:
        stg = es_w.enter_context(pr.nc.sbuf_tensor(name + "_stg", [128, cols], F32))
        for ci in range(nch):
            em.dma(stg[:], src[ci])
            em.v("tensor_copy", wb[:, ci * cols:(ci + 1) * cols], stg[:])
    return wb


def emit_ln_tile(pr, tt, xn_tile, ps_tr, dump, to_xnT=True, xn_f32=None,
                 cols=None):
    em = pr.em
    c1, c2, c3 = cols if cols is not None else (pr.col1, pr.col2, pr.col3)
    xst = pr.xs[:, tt * D:(tt + 1) * D]
    em.v("reduce_sum", c1, xst, AX.X)
    em.v("tensor_scalar", c1, c1, -1.0 / D, None, ALU.mult)
    em.a("activation", dump[:, 0:D], xst, AF.Square, bias=c1, scale=1.0,
         accum_out=c2)
    em.v("tensor_scalar", c2, c2, 1.0 / D, 1e-5, ALU.mult, ALU.add)
    em.a("activation", c3, c2, AF.Ln, bias=0.0, scale=1.0)
    em.a("activation", c3, c3, AF.Exp, bias=0.0, scale=-0.5)
    em.v("tensor_scalar", xn_tile[:], xst, c1, c3, ALU.add, ALU.mult)
    if xn_f32 is not None:
        em.v("tensor_scalar", xn_f32[:], xst, c1, c3, ALU.add, ALU.mult)
    if to_xnT:
        for dc in range(NDC):
            em.t("transpose", ps_tr[:, dc * 128:(dc + 1) * 128],
                 xn_tile[:, dc * 128:(dc + 1) * 128], pr.ident[:])
        dst = pr.xnT[:].rearrange("p (dc t) -> p dc t", t=T)[:, :, tt * 128:(tt + 1) * 128]
        src = ps_tr[:].rearrange("p (dc t) -> p dc t", t=128)
        em.v("tensor_copy", dst, src)


def emit_ln_all(pr, xn_f32_buf=None, stats=None):
    with pr.scope() as es2:
        R = 2
        xn_tile = [es2.enter_context(pr.nc.sbuf_tensor(f"ln_xnt{i}", [128, D], BF16))
                   for i in range(R)]
        dump = [es2.enter_context(pr.nc.sbuf_tensor(f"ln_dump{i}", [128, D], BF16))
                for i in range(R)]
        ps_tr = [es2.enter_context(pr.nc.psum_tensor(f"ln_pstr{i}", [128, NDC * 128], BF16))
                 for i in range(R)]
        colr = [tuple(es2.enter_context(pr.nc.sbuf_tensor(f"ln_c{i}_{j}", [128, 1], F32))[:]
                      for j in range(3)) for i in range(R)]
        for tt in range(NTT):
            i = tt % R
            xf = None if xn_f32_buf is None else xn_f32_buf[:, tt * D:(tt + 1) * D]
            emit_ln_tile(pr, tt, xn_tile[i], ps_tr[i], dump[i], xn_f32=xf,
                         cols=colr[i])
            if stats is not None:
                pr.em.v("tensor_copy", stats[0][:, tt:tt + 1], colr[i][0])
                pr.em.v("tensor_copy", stats[1][:, tt:tt + 1], colr[i][2])


def emit_attention_core(pr, qT, kT, vbuf, evac_fn, scale_imm=None,
                        scale_ap16=None, bias_ap16=None):
    em = pr.em
    with pr.scope() as es2:
        nc = pr.nc
        ps_s = [es2.enter_context(nc.psum_tensor(f"att_s{i}", [128, 1024], F32))
                for i in range(2)]
        ps_av = [es2.enter_context(nc.psum_tensor(f"att_av{i}", [128, 1024], F32))
                 for i in range(2)]
        ebuf = [es2.enter_context(nc.sbuf_tensor(f"att_e{i}", [128, 1024], BF16))
                for i in range(2)]
        rden = es2.enter_context(nc.sbuf_tensor("att_rd", [128, 1], F32))

        def avoff(j):
            return (j // 4) * 512 + (j % 4) * 65

        for h in range(16):
            bp = 64 * (h % 2)
            cb = (h // 2) * T
            for qh in range(2):
                av = ps_av[(h * 2 + qh) % 2]

                def front(kc, h=h, bp=bp, cb=cb, qh=qh):
                    pb = kc % 2
                    for qs in range(2):
                        em.mm(ps_s[pb][:, qs * 512:(qs + 1) * 512],
                              kT[bp:bp + 64, cb + kc * 128:cb + (kc + 1) * 128],
                              qT[bp:bp + 64, cb + qh * 1024 + qs * 512:cb + qh * 1024 + (qs + 1) * 512],
                              start=True, stop=True, tile_position=(bp, 0))
                    if scale_imm is not None:
                        em.a("activation", ebuf[pb][:], ps_s[pb][:],
                             AF.Exp, bias=0.0, scale=scale_imm)
                    else:
                        em.a("activation", ebuf[pb][:], ps_s[pb][:],
                             AF.Exp, bias=bias_ap16[:, h:h + 1], scale=scale_ap16[:, h:h + 1])

                def back(kc, h=h, av=av):
                    pb = kc % 2
                    for j in range(8):
                        em.mm(av[:, avoff(j):avoff(j) + 65],
                              ebuf[pb][:, j * 128:(j + 1) * 128],
                              vbuf[:, kc * 1040 + h * 65:kc * 1040 + (h + 1) * 65],
                              start=(kc == 0 and j % 4 == 0),
                              stop=(kc == NTT - 1 and j % 4 == 3))

                for kc in range(NTT + 1):
                    if kc < NTT:
                        front(kc)
                    if kc > 0:
                        back(kc - 1)
                for j in range(8):
                    em.v("reciprocal", rden[:], av[:, avoff(j) + 64:avoff(j) + 65])
                    evac_fn(h, qh * 8 + j, av[:, avoff(j):avoff(j) + 64], rden)


def emit_l0_attn_a(pr):
    em = pr.em
    nc = pr.nc
    with pr.scope() as es2:
        qT = es2.enter_context(nc.sbuf_tensor("aa_qT", [128, NDC * T], BF16))
        kT = es2.enter_context(nc.sbuf_tensor("aa_kT", [128, NDC * T], BF16))
        vbuf = es2.enter_context(nc.sbuf_tensor("aa_v", [128, NTT * D], BF16))
        with pr.scope() as es3:
            pr.xnT = es3.enter_context(nc.sbuf_tensor("aa_xnT", [128, NDC * T], BF16))[:]
            emit_ln_all(pr)
            ps_qk = es3.enter_context(nc.psum_tensor("aa_psqk", [128, 2048], F32))
            ps_v = es3.enter_context(nc.psum_tensor("aa_psv", [128, 1024], F32))
            with pr.scope() as es4:
                wstg = es4.enter_context(nc.sbuf_tensor("aa_wstg", [128, 128], F32))
                wsl = es4.enter_context(nc.sbuf_tensor("aa_wsl", [128, 128], BF16))
                for side, wname in ((qT, "aa_wqT"), (kT, "aa_wkT")):
                    wsrc = pr.din[wname].rearrange("(dc p) f -> dc p f", p=128)
                    for fc in range(NDC):
                        for dc in range(NDC):
                            em.dma(wstg[:], wsrc[dc, :, fc * 128:(fc + 1) * 128])
                            em.v("tensor_copy", wsl[:], wstg[:])
                            for qs in range(4):
                                em.mm(ps_qk[:, qs * 512:(qs + 1) * 512],
                                      wsl[:],
                                      pr.xnT[:, dc * T + qs * 512:dc * T + (qs + 1) * 512],
                                      start=(dc == 0), stop=(dc == NDC - 1))
                        em.v("tensor_copy", side[:, fc * T:(fc + 1) * T], ps_qk[:])
            with pr.scope() as es4:
                wvh = es4.enter_context(nc.sbuf_tensor("aa_wvh", [128, NDC * 512], BF16))
                wstg2 = es4.enter_context(nc.sbuf_tensor("aa_wstg2", [128, 512], F32))
                wsrc = pr.din["aa_wvT"].rearrange("(dc p) f -> dc p f", p=128)
                for vs in range(2):
                    for dc in range(NDC):
                        em.dma(wstg2[:], wsrc[dc, :, vs * 512:(vs + 1) * 512])
                        em.v("tensor_copy", wvh[:, dc * 512:(dc + 1) * 512], wstg2[:])
                    for tt in range(NTT):
                        for dc in range(NDC):
                            em.mm(ps_v[:, vs * 512:(vs + 1) * 512],
                                  pr.xnT[:, dc * T + tt * 128:dc * T + (tt + 1) * 128],
                                  wvh[:, dc * 512:(dc + 1) * 512],
                                  start=(dc == 0), stop=(dc == NDC - 1))
                        em.v("tensor_copy",
                             vbuf[:, tt * D + vs * 512:tt * D + (vs + 1) * 512],
                             ps_v[:, vs * 512:(vs + 1) * 512])

        with pr.scope() as es3:
            NSL = 2
            ps_s = es3.enter_context(nc.psum_tensor("aa_pss", [128, 2048], F32))
            ps_av = [es3.enter_context(nc.psum_tensor(f"aa_psav{j}", [128, 512], F32))
                     for j in range(NSL)]
            ebuf = [es3.enter_context(nc.sbuf_tensor(f"aa_e{j}", [128, 2048], BF16))
                    for j in range(NSL)]
            eT = [es3.enter_context(nc.sbuf_tensor(f"aa_eT{j}", [128, 2048], BF16))
                  for j in range(NSL)]
            nmx = [es3.enter_context(nc.sbuf_tensor(f"aa_nmx{j}", [128, 1], F32))
                   for j in range(NSL)]
            den = [es3.enter_context(nc.sbuf_tensor(f"aa_den{j}", [128, 1], F32))
                   for j in range(NSL)]
            rden = [es3.enter_context(nc.sbuf_tensor(f"aa_rden{j}", [128, 1], F32))
                    for j in range(NSL)]
            iters = [(h, qt) for h in range(16) for qt in range(NTT)]

            def front(i):
                h, qt = iters[i]
                j = i % NSL
                bp = 64 * (h % 2)
                cb = (h // 2) * T
                for ks in range(4):
                    em.mm(ps_s[:, ks * 512:(ks + 1) * 512],
                          qT[bp:bp + 64, cb + qt * 128:cb + (qt + 1) * 128],
                          kT[bp:bp + 64, cb + ks * 512:cb + (ks + 1) * 512],
                          start=True, stop=True, tile_position=(bp, 0))
                em.v("reduce_max", nmx[j][:], ps_s[:], AX.X, negate=True)
                em.v("tensor_scalar", nmx[j][:], nmx[j][:], 0.125, None, ALU.mult)
                em.a("activation", ebuf[j][:], ps_s[:], AF.Exp, bias=nmx[j][:],
                     scale=0.125, accum_out=den[j][:])
                em.v("reciprocal", rden[j][:], den[j][:])
                em.dma_t(eT[j][:].rearrange("p (c q) -> p c q", q=128),
                         ebuf[j][:].rearrange("p (c k) -> p c k", k=128),
                         eng="sync")

            def back(i):
                h, qt = iters[i]
                j = i % NSL
                for kc in range(NTT):
                    em.mm(ps_av[j][:, 0:64], eT[j][:, kc * 128:(kc + 1) * 128],
                          vbuf[:, kc * D + h * 64:kc * D + (h + 1) * 64],
                          start=(kc == 0), stop=(kc == NTT - 1))
                xsl = pr.xs[:, qt * D + h * 64:qt * D + (h + 1) * 64]
                em.v("scalar_tensor_tensor", xsl, ps_av[j][:, 0:64], rden[j][:], xsl,
                     ALU.mult, ALU.add)

            for i in range(len(iters) + 1):
                if i < len(iters):
                    front(i)
                if i > 0:
                    back(i - 1)


def emit_l0_attn_b(pr):
    em = pr.em
    nc = pr.nc
    with pr.scope() as es2:
        pr.xnT = es2.enter_context(nc.sbuf_tensor("ab_xnT", [128, NDC * T], BF16))[:]
        emit_ln_all(pr)
        scb = es2.enter_context(nc.sbuf_tensor("ab_scb", [128, 16], F32))
        nscb = es2.enter_context(nc.sbuf_tensor("ab_nscb", [128, 16], F32))
        ao = pr.xnT

        with pr.scope() as es3:
            lsc = es3.enter_context(nc.sbuf_tensor("ab_lsc", [16, 1], F32))
            lsr = es3.enter_context(nc.sbuf_tensor("ab_lsr", [1, 16], F32))
            ps16 = es3.enter_context(nc.psum_tensor("ab_ps16", [128, 512], F32))
            em.dma(lsc[:], pr.din["ab_ls"])
            em.v("tensor_scalar", lsc[:], lsc[:], float(np.log(100.0)), None, ALU.min)
            em.a("activation", lsc[:], lsc[:], AF.Exp, bias=0.0, scale=1.0)
            em.t("transpose", ps16[0:1, 0:16], lsc[:], pr.ident32f[0:16, 0:16])
            em.v("tensor_copy", lsr[:], ps16[0:1, 0:16])
            em.mm(ps16[:, 0:16], pr.ones_row_f[:], lsr[:], start=True, stop=True)
            em.v("tensor_copy", scb[:], ps16[:, 0:16])
            em.v("tensor_scalar", nscb[:], scb[:], -1.0, None, ALU.mult)

        with pr.scope() as esA:
            qT = esA.enter_context(nc.sbuf_tensor("ab_qT", [128, NDC * T], BF16))
            kT = esA.enter_context(nc.sbuf_tensor("ab_kT", [128, NDC * T], BF16))
            vbuf = esA.enter_context(nc.sbuf_tensor("ab_v", [128, NTT * 1040], BF16))

            with pr.scope() as es3:
                TB = 3
                ps_q3 = [es3.enter_context(nc.psum_tensor(f"ab_psq{i}", [128, 1024], F32))
                         for i in range(TB)]
                ps_tr = es3.enter_context(nc.psum_tensor("ab_pstr", [128, 1024], BF16))
                qn = es3.enter_context(nc.sbuf_tensor("ab_qn", [128, D], BF16))
                ssq = es3.enter_context(nc.sbuf_tensor("ab_ssq", [128, 16], F32))
                dump = es3.enter_context(nc.sbuf_tensor("ab_dump", [128, D], F32))
                wstg = es3.enter_context(nc.sbuf_tensor("ab_wstg", [128, 512], F32))
                wslb = es3.enter_context(nc.sbuf_tensor("ab_wslb", [128, 512], BF16))
                ones_ap = vbuf[:].rearrange("p (t h w) -> p t h w", h=16, w=65)[:, :, :, 64:65]
                em.v("memset", ones_ap, 1.0)
                wsrc = pr.din["ab_wqkv"].rearrange("(dc p) f -> dc p f", p=128)
                for part, dstT in ((0, qT), (1, kT), (2, None)):
                    for t0 in range(0, NTT, TB):
                        tts = list(range(t0, min(t0 + TB, NTT)))
                        for qs in range(2):
                            for dc in range(NDC):
                                em.dma(wstg[:], wsrc[dc, :, part * 1024 + qs * 512:
                                                       part * 1024 + (qs + 1) * 512])
                                em.v("tensor_copy", wslb[:], wstg[:])
                                for i, tt in enumerate(tts):
                                    em.mm(ps_q3[i][:, qs * 512:(qs + 1) * 512],
                                          pr.xnT[:, dc * T + tt * 128:dc * T + (tt + 1) * 128],
                                          wslb[:],
                                          start=(dc == 0), stop=(dc == NDC - 1))
                        for i, tt in enumerate(tts):
                            ps_q = ps_q3[i]
                            emit_ab_qkv_post(pr, part, tt, ps_q, ps_tr, qn, ssq, dump,
                                             qT, kT, vbuf)

            _ab_attention_and_outproj(pr, qT, kT, vbuf, scb, nscb)
        _ab_outproj(pr)


def emit_ab_qkv_post(pr, part, tt, ps_q, ps_tr, qn, ssq, dump, qT, kT, vbuf):
    em = pr.em
    dstT = (qT, kT, None)[part]
    if part < 2:
        em.a("activation", dump[:, 0:D], ps_q[:], AF.Square, bias=0.0, scale=1.0)
        em.v("tensor_reduce",
             ssq[:].rearrange("p (h one) -> p h one", one=1),
             dump[:, 0:D].rearrange("p (h w) -> p h w", w=64),
             AX.X, ALU.add)
        em.v("tensor_scalar", ssq[:], ssq[:], 1e-24, None, ALU.max)
        em.a("activation", ssq[:], ssq[:], AF.Ln, bias=0.0, scale=1.0)
        em.a("activation", ssq[:], ssq[:], AF.Exp, bias=0.0, scale=-0.5)
        em.v("tensor_tensor", qn[:].rearrange("p (h w) -> p h w", w=64),
             ps_q[:].rearrange("p (h w) -> p h w", w=64),
             ssq[:].unsqueeze(2).broadcast_to([128, 16, 64]), ALU.mult)
        for dc in range(NDC):
            em.t("transpose", ps_tr[:, dc * 128:(dc + 1) * 128],
                 qn[:, dc * 128:(dc + 1) * 128], pr.ident[:])
        dst = dstT[:].rearrange("p (dc t) -> p dc t", t=T)[:, :, tt * 128:(tt + 1) * 128]
        em.v("tensor_copy", dst, ps_tr[:].rearrange("p (dc t) -> p dc t", t=128))
    else:
        dst = vbuf[:, tt * 1040:(tt + 1) * 1040].rearrange(
            "p (h w) -> p h w", w=65)[:, :, 0:64]
        em.v("tensor_copy", dst, ps_q[:].rearrange("p (h w) -> p h w", w=64))


def _ab_attention_and_outproj(pr, qT, kT, vbuf, scb, nscb):
    em = pr.em
    nc = pr.nc
    ao = pr.xnT

    def evac(h, qt, ps_block, rden):
        em.v("tensor_scalar", ao[:, qt * D + h * 64:qt * D + (h + 1) * 64],
             ps_block, rden[:], None, ALU.mult)

    emit_attention_core(pr, qT, kT, vbuf, evac, scale_ap16=scb, bias_ap16=nscb)


def _ab_outproj(pr):
    em = pr.em
    nc = pr.nc
    ao = pr.xnT
    with pr.scope() as es3:
        aT = es3.enter_context(nc.sbuf_tensor("ab_aT", [128, NDC * T], BF16))
        ps_tr = es3.enter_context(nc.psum_tensor("ab_pstr2", [128, 1024], BF16))
        ps_o = es3.enter_context(nc.psum_tensor("ab_pso", [128, 1024], F32))
        bout = es3.enter_context(nc.sbuf_tensor("ab_boutb", [1, D], BF16))
        boutf = es3.enter_context(nc.sbuf_tensor("ab_boutf", [1, D], F32))
        pr.es, saved_es = es3, pr.es
        wo = load_weight_bf16(pr, "ab_wob", "ab_wout", D, D)
        pr.es = saved_es
        em.dma(boutf[:], pr.din["ab_bout"])
        em.v("tensor_copy", bout[:], boutf[:])
        for tt in range(NTT):
            for dc in range(NDC):
                em.t("transpose", ps_tr[:, dc * 128:(dc + 1) * 128],
                     ao[:, tt * D + dc * 128:tt * D + (dc + 1) * 128], pr.ident[:])
            em.v("tensor_copy",
                 aT[:].rearrange("p (dc t) -> p dc t", t=T)[:, :, tt * 128:(tt + 1) * 128],
                 ps_tr[:].rearrange("p (dc t) -> p dc t", t=128))
        for tt in range(NTT):
            for os_ in range(2):
                for dc in range(NDC):
                    em.mm(ps_o[:, os_ * 512:(os_ + 1) * 512],
                          aT[:, dc * T + tt * 128:dc * T + (tt + 1) * 128],
                          wo[:, dc * D + os_ * 512:dc * D + (os_ + 1) * 512],
                          start=(dc == 0), stop=False)
                em.mm(ps_o[:, os_ * 512:(os_ + 1) * 512], pr.ones_row_b[:],
                      bout[:, os_ * 512:(os_ + 1) * 512], start=False, stop=True)
            em.v("tensor_tensor", pr.xs[:, tt * D:(tt + 1) * D], ps_o[:],
                 pr.xs[:, tt * D:(tt + 1) * D], ALU.add)



def bcast_1x1_to_col(pr, src11, dst_col, ps512):
    em = pr.em
    em.mm(ps512[:, 0:1], pr.ones_row_f, src11, start=True, stop=True)
    em.v("tensor_copy", dst_col, ps512[:, 0:1])


def bcast_row_to_tile(pr, src_row, dst, ps512, width):
    em = pr.em
    for off in range(0, width, 512):
        w = min(512, width - off)
        em.mm(ps512[:, 0:w], pr.ones_row_f, src_row[:, off:off + w],
              start=True, stop=True)
        em.v("tensor_copy", dst[:, off:off + w], ps512[:, 0:w])


def emit_neground2(pr, v, c_i32, d, r, out_bf, width):
    em = pr.em
    em.v("tensor_copy", c_i32[:, 0:width], v)
    em.v("tensor_sub", d[:, 0:width], v, c_i32[:, 0:width])
    em.v("scalar_tensor_tensor", r[:, 0:width], d[:, 0:width], 0.5,
         c_i32[:, 0:width], ALU.is_gt, ALU.add)
    em.v("scalar_tensor_tensor", out_bf, d[:, 0:width], -0.5, r[:, 0:width],
         ALU.is_lt, ALU.subtract)


def emit_mean_abs(pr, dram_name, rows, cols, out11, stg, abscols, ps512, onescol):
    em = pr.em
    nch = rows // 128
    src = pr.din[dram_name].rearrange("(c p) f -> c p f", p=128)
    for ci in range(nch):
        em.dma(stg[:, 0:cols], src[ci])
        em.a("activation", stg[:, 0:cols], stg[:, 0:cols], AF.Abs, bias=0.0,
             scale=1.0, accum_out=abscols[:, ci:ci + 1])
    em.v("reduce_sum", pr.col1, abscols[:, 0:nch], AX.X)
    em.mm(ps512[0:1, 0:1], pr.col1, onescol, start=True, stop=True)
    em.v("tensor_scalar", out11, ps512[0:1, 0:1], 1.0 / (rows * cols), None,
         ALU.mult)


def emit_l0_ffn(pr):
    em = pr.em
    nc = pr.nc
    H = 4096
    NHC = H // 128
    gdram = nc.dram_tensor(f"f0_g_{pr.uid}", [H, T], BF16, kind="Internal").ap()
    r2dram = nc.dram_tensor(f"f0_r2_{pr.uid}", [H, D], BF16, kind="Internal").ap()
    pr.uid += 1
    with pr.scope() as es2:
        sb = lambda n, s, d: es2.enter_context(nc.sbuf_tensor(n, s, d))
        ps = lambda n, s, d: es2.enter_context(nc.psum_tensor(n, s, d))
        q1T = sb("f0_q1T", [128, NDC * T], BF16)[:]
        pr.xnT = q1T
        onescol = sb("f0_ones", [128, 1], F32)
        g1 = sb("f0_g1", [1, 1], F32)
        g2 = sb("f0_g2", [1, 1], F32)
        t11 = sb("f0_t11", [1, 1], F32)
        thr1p = sb("f0_thr1p", [128, 1], F32)
        thr1n = sb("f0_thr1n", [128, 1], F32)
        thr2p = sb("f0_thr2p", [128, 1], F32)
        thr2n = sb("f0_thr2n", [128, 1], F32)
        geff1 = sb("f0_geff1", [1, 1], F32)
        geff2 = sb("f0_geff2", [1, 1], F32)
        comb1 = sb("f0_comb1", [128, T], F32)
        s2b = sb("f0_s2b", [128, T], F32)
        fin2 = sb("f0_fin2", [128, T], F32)
        rm2cols = sb("f0_rm2c", [128, NTT * NHC], F32)
        ps512 = ps("f0_ps512", [128, 512], F32)

        em.v("memset", onescol[:], 1.0)
        with pr.scope() as esG:
            stgG = esG.enter_context(nc.sbuf_tensor("f0_stgG", [128, 4096], F32))
            abscols = esG.enter_context(nc.sbuf_tensor("f0_absc", [128, 32], F32))
            emit_mean_abs(pr, "f0_w1T", D, H, g1[:], stgG, abscols, ps512, onescol[:])
            emit_mean_abs(pr, "f0_w2T", H, D, g2[:], stgG, abscols, ps512, onescol[:])
        for g, thrp, thrn, geff in ((g1, thr1p, thr1n, geff1),
                                    (g2, thr2p, thr2n, geff2)):
            em.v("tensor_scalar", g[:], g[:], 1e-5, None, ALU.max)
            em.v("reciprocal", t11[:], g[:])
            em.v("reciprocal", geff[:], t11[:])
            em.v("tensor_scalar", t11[:], g[:], 0.5, None, ALU.mult)
            bcast_1x1_to_col(pr, t11[:], thrp[:], ps512)
            em.v("tensor_scalar", thrn[:], thrp[:], -1.0, None, ALU.mult)

        with pr.scope() as es3:
            R = 2
            xn = [es3.enter_context(nc.sbuf_tensor(f"f0_xn{i}", [128, D], F32)) for i in range(R)]
            vq = [es3.enter_context(nc.sbuf_tensor(f"f0_vq{i}", [128, D], F32)) for i in range(R)]
            dq = [es3.enter_context(nc.sbuf_tensor(f"f0_dq{i}", [128, D], F32)) for i in range(R)]
            rq = [es3.enter_context(nc.sbuf_tensor(f"f0_rq{i}", [128, D], F32)) for i in range(R)]
            ci32 = [es3.enter_context(nc.sbuf_tensor(f"f0_ci{i}", [128, D], I32)) for i in range(R)]
            qb = [es3.enter_context(nc.sbuf_tensor(f"f0_qb{i}", [128, D], BF16)) for i in range(R)]
            cA = [es3.enter_context(nc.sbuf_tensor(f"f0_cA{i}", [128, 1], F32))[:] for i in range(R)]
            cB = [es3.enter_context(nc.sbuf_tensor(f"f0_cB{i}", [128, 1], F32))[:] for i in range(R)]
            cC = [es3.enter_context(nc.sbuf_tensor(f"f0_cC{i}", [128, 1], F32))[:] for i in range(R)]
            row1 = es3.enter_context(nc.sbuf_tensor("f0_row1", [1, T], F32))
            ps_tr = [es3.enter_context(nc.psum_tensor(f"f0_pstr{i}", [128, NDC * 128], BF16)) for i in range(R)]
            ps_th = [es3.enter_context(nc.psum_tensor(f"f0_psth{i}", [128, 512], F32)) for i in range(R)]
            for tt in range(NTT):
                i = tt % R
                emit_ln_tile_f32(pr, tt, xn[i], dq[i], cols=(cA[i], cB[i], cC[i]))
                em.v("reduce_max", cA[i], xn[i][:], AX.X, apply_absolute_value=True)
                em.v("tensor_scalar", cA[i], cA[i], 1e-5, None, ALU.max)
                em.v("reciprocal", cB[i], cA[i])
                em.v("tensor_scalar", cB[i], cB[i], 127.0, None, ALU.mult)
                em.v("tensor_scalar", vq[i][:], xn[i][:], cB[i], None, ALU.mult)
                emit_neground2(pr, vq[i][:], ci32[i], dq[i], rq[i], qb[i][:], D)
                for dc in range(NDC):
                    em.t("transpose", ps_tr[i][:, dc * 128:(dc + 1) * 128],
                         qb[i][:, dc * 128:(dc + 1) * 128], pr.ident[:])
                em.v("tensor_copy",
                     q1T[:].rearrange("p (dc t) -> p dc t", t=T)[:, :, tt * 128:(tt + 1) * 128],
                     ps_tr[i][:].rearrange("p (dc t) -> p dc t", t=128))
                em.v("tensor_scalar", cA[i], cA[i], 1.0 / 127.0, None, ALU.mult)
                em.t("transpose", ps_th[i][0:1, 0:128], cA[i], pr.ident128f)
                em.v("tensor_copy", row1[:, tt * 128:(tt + 1) * 128], ps_th[i][0:1, 0:128])
            em.v("tensor_scalar", row1[:], row1[:], geff1[:], None, ALU.mult)
            bcast_row_to_tile(pr, row1, comb1, ps512, T)

        with pr.scope() as es3:
            R = 2
            stg = [es3.enter_context(nc.sbuf_tensor(f"f0_stgA{i}", [128, 4096], F32)) for i in range(R)]
            r1sl = [es3.enter_context(nc.sbuf_tensor(f"f0_r1sl{i}", [128, NDC * 128], BF16)) for i in range(R)]
            r2sl = [es3.enter_context(nc.sbuf_tensor(f"f0_r2sl{i}", [128, D], BF16)) for i in range(R)]
            hs = [es3.enter_context(nc.sbuf_tensor(f"f0_hs{i}", [128, T], F32)) for i in range(R)]
            acc = [es3.enter_context(nc.sbuf_tensor(f"f0_acc{i}", [128, T], F32)) for i in range(R)]
            gch = [es3.enter_context(nc.sbuf_tensor(f"f0_gch{i}", [128, T], BF16)) for i in range(R)]
            cw = [es3.enter_context(nc.sbuf_tensor(f"f0_cw{i}", [128, 3], F32)) for i in range(R)]
            cb = [es3.enter_context(nc.sbuf_tensor(f"f0_cb{i}", [128, 1], F32)) for i in range(R)]
            ps_h = [es3.enter_context(nc.psum_tensor("f0_psh", [128, 2048], F32))] * R
            ps_t2 = [es3.enter_context(nc.psum_tensor("f0_pst2", [128, 2048], BF16))] * R
            w1src = pr.din["f0_w1T"].rearrange("(dc p) f -> dc p f", p=128)
            w2src = pr.din["f0_w2T"].rearrange("(hc p) f -> hc p f", p=128)
            cwsrc = pr.din["f0_convw"]
            cbsrc = pr.din["f0_convb"]
            for hc in range(NHC):
                i_ = hc % R
                stg_, r1sl_, r2sl_, hs_, acc_, gch_, cw_, cb_ = (
                    stg[i_], r1sl[i_], r2sl[i_], hs[i_], acc[i_], gch[i_], cw[i_], cb[i_])
                ps_h_, ps_t2_ = ps_h[i_], ps_t2[i_]
                for dc in range(NDC):
                    em.dma(stg_[:, dc * 128:(dc + 1) * 128],
                           w1src[dc, :, hc * 128:(hc + 1) * 128])
                em.v("tensor_scalar", stg_[:, 1024:2048], stg_[:, 0:1024], thr1p[:],
                     None, ALU.is_gt)
                em.v("scalar_tensor_tensor", r1sl_[:], stg_[:, 0:1024], thr1n[:],
                     stg_[:, 1024:2048], ALU.is_lt, ALU.subtract)
                em.dma(stg_[:, 2048:3072], w2src[hc])
                em.v("tensor_scalar", stg_[:, 3072:4096], stg_[:, 2048:3072], thr2p[:],
                     None, ALU.is_gt)
                em.v("scalar_tensor_tensor", r2sl_[:], stg_[:, 2048:3072], thr2n[:],
                     stg_[:, 3072:4096], ALU.is_lt, ALU.subtract)
                em.dma(r2dram.rearrange("(hc p) f -> hc p f", p=128)[hc], r2sl_[:])
                for ts4 in range(4):
                    for dc in range(NDC):
                        em.mm(ps_h_[:, ts4 * 512:(ts4 + 1) * 512],
                              r1sl_[:, dc * 128:(dc + 1) * 128],
                              q1T[:, dc * T + ts4 * 512:dc * T + (ts4 + 1) * 512],
                              start=(dc == 0), stop=(dc == NDC - 1))
                em.v("tensor_tensor", hs_[:], ps_h_[:], comb1[:], ALU.mult)
                em.dma(cw_[:], cwsrc[hc * 128:(hc + 1) * 128, :])
                em.dma(cb_[:], cbsrc[hc * 128:(hc + 1) * 128, :])
                em.v("tensor_scalar", acc_[:, 1:T], hs_[:, 0:T - 1], cw_[:, 0:1],
                     cb_[:], ALU.mult, ALU.add)
                em.v("tensor_copy", acc_[:, 0:1], cb_[:])
                em.v("scalar_tensor_tensor", acc_[:], hs_[:], cw_[:, 1:2], acc_[:],
                     ALU.mult, ALU.add)
                em.v("scalar_tensor_tensor", acc_[:, 0:T - 1], hs_[:, 1:T], cw_[:, 2:3],
                     acc_[:, 0:T - 1], ALU.mult, ALU.add)
                em.a("activation", gch_[:], acc_[:], AF.Silu, bias=0.0, scale=1.0)
                em.dma(gdram.rearrange("(hc p) f -> hc p f", p=128)[hc], gch_[:])
                for i in range(NTT):
                    em.t("transpose", ps_t2_[:, (i % 8) * 128 + (i // 8) * 1024:
                                             (i % 8) * 128 + (i // 8) * 1024 + 128],
                         gch_[:, i * 128:(i + 1) * 128], pr.ident[:])
                for i in range(NTT):
                    em.v("reduce_max", rm2cols[:, i * NHC + hc:i * NHC + hc + 1],
                         ps_t2_[:, (i % 8) * 128 + (i // 8) * 1024:
                                  (i % 8) * 128 + (i // 8) * 1024 + 128],
                         AX.X, apply_absolute_value=True)

        with pr.scope() as es3:
            ps_th = es3.enter_context(nc.psum_tensor("f0_psth2", [128, 512], F32))
            row2 = es3.enter_context(nc.sbuf_tensor("f0_row2", [1, T], F32))
            row3 = es3.enter_context(nc.sbuf_tensor("f0_row3", [1, T], F32))
            for tt in range(NTT):
                em.v("reduce_max", pr.col1, rm2cols[:, tt * NHC:(tt + 1) * NHC], AX.X)
                em.v("tensor_scalar", pr.col1, pr.col1, 1e-5, None, ALU.max)
                em.v("reciprocal", pr.col2, pr.col1)
                em.v("tensor_scalar", pr.col2, pr.col2, 127.0, None, ALU.mult)
                em.v("tensor_scalar", pr.col1, pr.col1, 1.0 / 127.0, None, ALU.mult)
                em.t("transpose", ps_th[0:1, 0:128], pr.col1, pr.ident128f)
                em.v("tensor_copy", row2[:, tt * 128:(tt + 1) * 128], ps_th[0:1, 0:128])
                em.t("transpose", ps_th[0:1, 128:256], pr.col2, pr.ident128f)
                em.v("tensor_copy", row3[:, tt * 128:(tt + 1) * 128], ps_th[0:1, 128:256])
            em.v("tensor_scalar", row2[:], row2[:], geff2[:], None, ALU.mult)
            bcast_row_to_tile(pr, row3, s2b, ps512, T)
            bcast_row_to_tile(pr, row2, fin2, ps512, T)

        with pr.scope() as es3:
            R = 3
            gsl = [es3.enter_context(nc.sbuf_tensor(f"f0_gsl{i}", [128, 512], BF16)) for i in range(R)]
            vq = [es3.enter_context(nc.sbuf_tensor(f"f0_vq2{i}", [128, 512], F32)) for i in range(R)]
            dq = [es3.enter_context(nc.sbuf_tensor(f"f0_dq2{i}", [128, 512], F32)) for i in range(R)]
            rq = [es3.enter_context(nc.sbuf_tensor(f"f0_rq2{i}", [128, 512], F32)) for i in range(R)]
            ci32 = [es3.enter_context(nc.sbuf_tensor(f"f0_ci2{i}", [128, 512], I32)) for i in range(R)]
            r2b = [es3.enter_context(nc.sbuf_tensor(f"f0_r2b{i}", [128, D], BF16)) for i in range(2)]
            q2buf = es3.enter_context(nc.sbuf_tensor("f0_q2b", [128, 32 * 512], BF16))
            fout = es3.enter_context(nc.sbuf_tensor("f0_fout", [128, NDC * 512], BF16))
            ps_o = es3.enter_context(nc.psum_tensor("f0_pso", [128, 2048], F32))
            ps_tb = [es3.enter_context(nc.psum_tensor(f"f0_pstb{i}", [128, 1024], BF16)) for i in range(2)]
            gsrc = gdram.rearrange("(hc p) f -> hc p f", p=128)
            r2src = r2dram.rearrange("(hc p) f -> hc p f", p=128)
            for tb in range(4):
                for hc in range(NHC):
                    i_ = hc % R
                    em.dma(gsl[i_][:], gsrc[hc, :, tb * 512:(tb + 1) * 512])
                    em.v("tensor_tensor", vq[i_][:], gsl[i_][:],
                         s2b[:, tb * 512:(tb + 1) * 512], ALU.mult)
                    emit_neground2(pr, vq[i_][:], ci32[i_], dq[i_], rq[i_],
                                   q2buf[:, hc * 512:(hc + 1) * 512], 512)
                for grp in range(2):
                    for hc in range(NHC):
                        em.dma(r2b[hc % 2][:], r2src[hc])
                        for dmi in range(4):
                            dmc = grp * 4 + dmi
                            em.mm(ps_o[:, dmi * 512:(dmi + 1) * 512],
                                  r2b[hc % 2][:, dmc * 128:(dmc + 1) * 128],
                                  q2buf[:, hc * 512:(hc + 1) * 512],
                                  start=(hc == 0), stop=(hc == NHC - 1))
                    for dmi in range(4):
                        dmc = grp * 4 + dmi
                        em.v("tensor_tensor", fout[:, dmc * 512:(dmc + 1) * 512],
                             ps_o[:, dmi * 512:(dmi + 1) * 512],
                             fin2[:, tb * 512:(tb + 1) * 512], ALU.mult)
                for dmc in range(NDC):
                    pstb_ = ps_tb[dmc % 2]
                    for i in range(4):
                        em.t("transpose", pstb_[:, (i % 4) * 256:(i % 4) * 256 + 128],
                             fout[:, dmc * 512 + i * 128:dmc * 512 + (i + 1) * 128],
                             pr.ident[:])
                    for i in range(4):
                        tt = tb * 4 + i
                        em.v("tensor_tensor",
                             pr.xs[:, tt * D + dmc * 128:tt * D + (dmc + 1) * 128],
                             pstb_[:, (i % 4) * 256:(i % 4) * 256 + 128],
                             pr.xs[:, tt * D + dmc * 128:tt * D + (dmc + 1) * 128],
                             ALU.add)


def emit_ln_tile_f32(pr, tt, xn_f32, dump, cols=None):
    em = pr.em
    c1, c2, c3 = cols if cols is not None else (pr.col1, pr.col2, pr.col3)
    xst = pr.xs[:, tt * D:(tt + 1) * D]
    em.v("reduce_sum", c1, xst, AX.X)
    em.v("tensor_scalar", c1, c1, -1.0 / D, None, ALU.mult)
    em.a("activation", dump[:, 0:D], xst, AF.Square, bias=c1, scale=1.0,
         accum_out=c2)
    em.v("tensor_scalar", c2, c2, 1.0 / D, 1e-5, ALU.mult, ALU.add)
    em.a("activation", c3, c2, AF.Ln, bias=0.0, scale=1.0)
    em.a("activation", c3, c3, AF.Exp, bias=0.0, scale=-0.5)
    em.v("tensor_scalar", xn_f32[:], xst, c1, c3, ALU.add, ALU.mult)



def emit_l1_ffn(pr):
    em = pr.em
    nc = pr.nc
    H = 4096
    with pr.scope() as es2:
        sb = lambda n, s, d: es2.enter_context(nc.sbuf_tensor(n, s, d))
        pr.xnT = sb("f1_xnT", [128, NDC * T], BF16)[:]
        emit_ln_all(pr)
        onescol = sb("f1_ones", [128, 1], F32)
        g1 = sb("f1_g1", [1, 1], F32)
        g2 = sb("f1_g2", [1, 1], F32)
        t11 = sb("f1_t11", [1, 1], F32)
        thr1p = sb("f1_thr1p", [128, 1], F32)
        thr1n = sb("f1_thr1n", [128, 1], F32)
        thr2p = sb("f1_thr2p", [128, 1], F32)
        thr2n = sb("f1_thr2n", [128, 1], F32)
        em.v("memset", onescol[:], 1.0)
        with pr.scope() as esG:
            ps512 = esG.enter_context(nc.psum_tensor("f1_ps512", [128, 512], F32))
            stgG = esG.enter_context(nc.sbuf_tensor("f1_stgG", [128, 4096], F32))
            abscols = esG.enter_context(nc.sbuf_tensor("f1_absc", [128, 32], F32))
            emit_mean_abs(pr, "f1_w1T", D, H, g1[:], stgG, abscols, ps512, onescol[:])
            emit_mean_abs(pr, "f1_w2T", H, D, g2[:], stgG, abscols, ps512, onescol[:])
            for g, thrp, thrn in ((g1, thr1p, thr1n), (g2, thr2p, thr2n)):
                em.v("tensor_scalar", t11[:], g[:], 1e-5, 0.5, ALU.add, ALU.mult)
                bcast_1x1_to_col(pr, t11[:], thrp[:], ps512)
                em.v("tensor_scalar", thrn[:], thrp[:], -1.0, None, ALU.mult)

        w1src = pr.din["f1_w1T"].rearrange("(dc p) f -> dc p f", p=128)
        w2src = pr.din["f1_w2T"].rearrange("(hc p) f -> hc p f", p=128)
        with pr.scope() as es3:
            r1h = es3.enter_context(nc.sbuf_tensor("f1_r1h", [128, NDC * 2048], BF16))
            r2h = es3.enter_context(nc.sbuf_tensor("f1_r2h", [128, 16 * D], BF16))
            stgq = [es3.enter_context(nc.sbuf_tensor(f"f1_stgq{i}", [128, 2048], F32))
                    for i in range(2)]
            b1rf = es3.enter_context(nc.sbuf_tensor("f1_b1rf", [1, 512], F32))
            b1rb = es3.enter_context(nc.sbuf_tensor("f1_b1rb", [1, 2048], BF16))
            b2rf = es3.enter_context(nc.sbuf_tensor("f1_b2rf", [1, D], F32))
            b2rb = es3.enter_context(nc.sbuf_tensor("f1_b2rb", [1, D], BF16))
            hneg = [es3.enter_context(nc.sbuf_tensor(f"f1_hneg{i}", [128, 2048], BF16))
                    for i in range(2)]
            hT = [es3.enter_context(nc.sbuf_tensor(f"f1_hT{i}", [128, 2048], BF16))
                  for i in range(2)]
            ps_h = es3.enter_context(nc.psum_tensor("f1_psh", [128, 2048], F32))
            ps_tr = es3.enter_context(nc.psum_tensor("f1_pstr", [128, 1024], BF16))
            ps_o = es3.enter_context(nc.psum_tensor("f1_pso", [128, 1024], F32))
            em.dma(b2rf[:], pr.din["f1_b2"])
            em.v("tensor_copy", b2rb[:], b2rf[:])
            for half in range(2):
                h0 = half * 2048
                for dc in range(NDC):
                    for piece in range(2):
                        sq = stgq[(dc * 2 + piece) % 2]
                        sl = sq[:, 0:1024]
                        em.dma(sl, w1src[dc, :, h0 + piece * 1024:h0 + (piece + 1) * 1024])
                        em.v("tensor_scalar", sq[:, 1024:2048], sl, thr1p[:],
                             None, ALU.is_gt)
                        em.v("scalar_tensor_tensor",
                             r1h[:, dc * 2048 + piece * 1024:dc * 2048 + (piece + 1) * 1024],
                             sl, thr1n[:], sq[:, 1024:2048], ALU.is_lt, ALU.subtract)
                for hc in range(16):
                    sq = stgq[hc % 2]
                    sl = sq[:, 0:1024]
                    em.dma(sl, w2src[half * 16 + hc])
                    em.v("tensor_scalar", sq[:, 1024:2048], sl, thr2p[:],
                         None, ALU.is_gt)
                    em.v("scalar_tensor_tensor", r2h[:, hc * D:(hc + 1) * D],
                         sl, thr2n[:], sq[:, 1024:2048], ALU.is_lt, ALU.subtract)
                for pc in range(4):
                    em.dma(b1rf[:], pr.din["f1_b1"][:, h0 + pc * 512:h0 + (pc + 1) * 512])
                    em.v("tensor_scalar", b1rb[:, pc * 512:(pc + 1) * 512], b1rf[:],
                         -1.0, None, ALU.mult)
                for tt in range(NTT):
                    for hs4 in range(4):
                        for dc in range(NDC):
                            em.mm(ps_h[:, hs4 * 512:(hs4 + 1) * 512],
                                  pr.xnT[:, dc * T + tt * 128:dc * T + (tt + 1) * 128],
                                  r1h[:, dc * 2048 + hs4 * 512:dc * 2048 + (hs4 + 1) * 512],
                                  start=(dc == 0), stop=False)
                        em.mm(ps_h[:, hs4 * 512:(hs4 + 1) * 512], pr.ones_row_b,
                              b1rb[:, hs4 * 512:(hs4 + 1) * 512], start=False, stop=True)
                    hn, hT_ = hneg[tt % 2], hT[tt % 2]
                    em.v("tensor_scalar", hn[:], ps_h[:], 0.0, None, ALU.min)
                    for grp8 in range(2):
                        for i8 in range(8):
                            i = grp8 * 8 + i8
                            em.t("transpose", ps_tr[:, i8 * 128:(i8 + 1) * 128],
                                 hn[:, i * 128:(i + 1) * 128], pr.ident[:])
                        em.v("tensor_copy", hT_[:, grp8 * 1024:(grp8 + 1) * 1024], ps_tr[:])
                    for os_ in range(2):
                        for hc in range(16):
                            em.mm(ps_o[:, os_ * 512:(os_ + 1) * 512],
                                  hT_[:, hc * 128:(hc + 1) * 128],
                                  r2h[:, hc * D + os_ * 512:hc * D + (os_ + 1) * 512],
                                  start=(hc == 0), stop=(half == 1 and hc == 15))
                        if half == 0:
                            em.mm(ps_o[:, os_ * 512:(os_ + 1) * 512], pr.ones_row_b,
                                  b2rb[:, os_ * 512:(os_ + 1) * 512], start=False,
                                  stop=True)
                    em.v("tensor_tensor", pr.xs[:, tt * D:(tt + 1) * D], ps_o[:],
                         pr.xs[:, tt * D:(tt + 1) * D], ALU.add)



import os
_KPHASE = int(os.environ.get("KPHASE", "9"))


def emit_l1_attn_b(pr):
    em = pr.em
    nc = pr.nc
    with pr.scope() as es2:
        sb = lambda n, s, d: es2.enter_context(nc.sbuf_tensor(n, s, d))
        pr.xnT = sb("db_xnT", [128, NDC * T], BF16)[:]
        negm16 = sb("db_negm", [128, 16], F32)
        rstd16 = sb("db_rstd", [128, 16], F32)
        emit_ln_all(pr, stats=(negm16, rstd16))
        gated = sb("db_gated", [128, NDC * T], BF16)
        dec8 = sb("db_dec8", [128, 8], F32)
        nwb = sb("db_nwb", [128, D], F32)
        em.dma(dec8[:], pr.din["db_logdec"])
        em.a("activation", dec8[:], dec8[:], AF.Exp, bias=0.0, scale=1.0)
        with pr.scope() as es3:
            nwrow = es3.enter_context(nc.sbuf_tensor("db_nwrow", [1, D], F32))
            ps512 = es3.enter_context(nc.psum_tensor("db_ps512", [128, 512], F32))
            em.dma(nwrow[:], pr.din["db_normw"])
            bcast_row_to_tile(pr, nwrow, nwb, ps512, D)
        if _KPHASE < 1:
            return
        with pr.scope() as es3:
            acc1 = es3.enter_context(nc.sbuf_tensor("db_acc1", [128, T], F32))
            acc2 = es3.enter_context(nc.sbuf_tensor("db_acc2", [128, T], F32))
            gs = es3.enter_context(nc.sbuf_tensor("db_gs", [128, T], BF16))
            cs = es3.enter_context(nc.sbuf_tensor("db_cs", [128, T], BF16))
            cw7 = es3.enter_context(nc.sbuf_tensor("db_cw7", [128, 7], F32))
            cb1 = es3.enter_context(nc.sbuf_tensor("db_cb1", [128, 1], F32))
            decb = es3.enter_context(nc.sbuf_tensor("db_decb", [128, T], F32))
            for c in range(NDC):
                xc = pr.xnT[:, c * T:(c + 1) * T]
                for (wname, bname, acc, outb, fn) in (
                        ("db_gatew", "db_gateb", acc1, gs, AF.Sigmoid),
                        ("db_candw", "db_candb", acc2, cs, AF.Tanh)):
                    em.dma(cw7[:], pr.din[wname][c * 128:(c + 1) * 128, :])
                    em.dma(cb1[:], pr.din[bname][c * 128:(c + 1) * 128, :])
                    em.v("tensor_scalar", acc[:], xc, cw7[:, 3:4], cb1[:],
                         ALU.mult, ALU.add)
                    for k in (0, 1, 2, 4, 5, 6):
                        sh = k - 3
                        if sh < 0:
                            em.v("scalar_tensor_tensor", acc[:, -sh:T],
                                 xc[:, 0:T + sh], cw7[:, k:k + 1],
                                 acc[:, -sh:T], ALU.mult, ALU.add)
                        else:
                            em.v("scalar_tensor_tensor", acc[:, 0:T - sh],
                                 xc[:, sh:T], cw7[:, k:k + 1],
                                 acc[:, 0:T - sh], ALU.mult, ALU.add)
                    em.a("activation", outb[:], acc[:], fn, bias=0.0, scale=1.0)
                gc = gated[:, c * T:(c + 1) * T]
                em.v("tensor_tensor", gc, gs[:], cs[:], ALU.mult)
                if _KPHASE >= 2:
                    em.v("tensor_scalar", decb[:], acc1[:], 0.0, dec8[:, c:c + 1],
                         ALU.mult, ALU.add)
                    em.v("tensor_tensor_scan", gc, decb[:], gc, 0.0, ALU.mult, ALU.add)
        if _KPHASE < 3:
            return
        with pr.scope() as es3:
            pr.es, saved_es = es3, pr.es
            wo = load_weight_bf16(pr, "db_wob", "db_woutT", D, D)
            pr.es = saved_es
            boutf = es3.enter_context(nc.sbuf_tensor("db_boutf", [1, D], F32))
            boutb = es3.enter_context(nc.sbuf_tensor("db_boutb", [1, D], BF16))
            yb = es3.enter_context(nc.sbuf_tensor("db_yb", [128, D], F32))
            u = es3.enter_context(nc.sbuf_tensor("db_u", [128, D], F32))
            dump = es3.enter_context(nc.sbuf_tensor("db_dump", [128, D], F32))
            ps_o = es3.enter_context(nc.psum_tensor("db_pso", [128, 1024], F32))
            em.dma(boutf[:], pr.din["db_bout"])
            em.v("tensor_copy", boutb[:], boutf[:])
            for tt in range(NTT):
                for os_ in range(2):
                    for c in range(NDC):
                        em.mm(ps_o[:, os_ * 512:(os_ + 1) * 512],
                              gated[:, c * T + tt * 128:c * T + (tt + 1) * 128],
                              wo[:, c * D + os_ * 512:c * D + (os_ + 1) * 512],
                              start=(c == 0), stop=False)
                    em.mm(ps_o[:, os_ * 512:(os_ + 1) * 512], pr.ones_row_b,
                          boutb[:, os_ * 512:(os_ + 1) * 512], start=False, stop=True)
                em.v("tensor_scalar", u[:], pr.xs[:, tt * D:(tt + 1) * D],
                     negm16[:, tt:tt + 1], rstd16[:, tt:tt + 1], ALU.add, ALU.mult)
                em.v("tensor_tensor", yb[:], ps_o[:], u[:], ALU.add)
                em.a("activation", dump[:], yb[:], AF.Square, bias=0.0, scale=1.0,
                     accum_out=pr.col2)
                em.v("tensor_scalar", pr.col2, pr.col2, 1.0 / D, 1.1920929e-07,
                     ALU.mult, ALU.add)
                em.a("activation", pr.col3, pr.col2, AF.Ln, bias=0.0, scale=1.0)
                em.a("activation", pr.col3, pr.col3, AF.Exp, bias=0.0, scale=-0.5)
                em.v("tensor_tensor", u[:], yb[:], nwb[:], ALU.mult)
                em.v("scalar_tensor_tensor", pr.xs[:, tt * D:(tt + 1) * D],
                     u[:], pr.col3, pr.xs[:, tt * D:(tt + 1) * D],
                     ALU.mult, ALU.add)



def emit_l1_attn_a(pr):
    em = pr.em
    nc = pr.nc
    DI = 2048
    NH = 32
    PP = 64
    zdram = nc.dram_tensor(f"sa_z_{pr.uid}", [T, DI], BF16, kind="Internal").ap()
    ydram = nc.dram_tensor(f"sa_y_{pr.uid}", [T, DI], BF16, kind="Internal").ap()
    xdram = nc.dram_tensor(f"sa_x_{pr.uid}", [DI, T], BF16, kind="Internal").ap()
    lscr = nc.dram_tensor(f"sa_l_{pr.uid}", [NH, 128], F32, kind="Internal").ap()
    pr.uid += 1
    with pr.scope() as es2:
        sb = lambda n, s, d: es2.enter_context(nc.sbuf_tensor(n, s, d))
        bc_sb = sb("sa_bc", [128, 2 * T], BF16)
        lam = sb("sa_lam", [32, T], F32)
        U = sb("sa_U", [32, T], F32)
        acol = sb("sa_acol", [32, 1], F32)
        dtb = sb("sa_dtbc", [32, 1], F32)
        ones32 = sb("sa_ones32", [32, 128], F32)
        em.v("memset", ones32[:], 1.0)
        em.dma(dtb[:], pr.din["sa_dtb"])
        em.dma(acol[:], pr.din["sa_alog"])
        em.a("activation", acol[:], acol[:], AF.Exp, bias=0.0, scale=1.0)
        em.v("tensor_scalar", acol[:], acol[:], -1.0, None, ALU.mult)

        with pr.scope() as es3:
            sb3 = lambda n, s, d: es3.enter_context(nc.sbuf_tensor(n, s, d))
            xbc = sb3("sa_xbc", [128, 16 * T], BF16)
            dtR = sb3("sa_dt", [32, T], F32)
            with pr.scope() as es4:
                xnT = es4.enter_context(nc.sbuf_tensor("sa_xnT", [128, NDC * T], BF16))
                pr.xnT = xnT[:]
                emit_ln_all(pr)
                psA = es4.enter_context(nc.psum_tensor("sa_psA", [128, 2048], F32))
                wstg = es4.enter_context(nc.sbuf_tensor("sa_wstg", [128, 512], F32))
                wslb = es4.enter_context(nc.sbuf_tensor("sa_wslb", [128, 512], BF16))
                zb = es4.enter_context(nc.sbuf_tensor("sa_zb", [128, 1024], BF16))
                wsrc = pr.din["sa_winT"].rearrange("(dc p) f -> dc p f", p=128)
                for t0 in range(0, NTT, 2):
                    for zh in range(2):
                        for zs in range(2):
                            for dc in range(NDC):
                                em.dma(wstg[:], wsrc[dc, :, zh * 1024 + zs * 512:
                                                       zh * 1024 + (zs + 1) * 512])
                                em.v("tensor_copy", wslb[:], wstg[:])
                                for i in range(2):
                                    em.mm(psA[:, i * 1024 + zs * 512:
                                              i * 1024 + (zs + 1) * 512],
                                          pr.xnT[:, dc * T + (t0 + i) * 128:
                                                 dc * T + (t0 + i + 1) * 128],
                                          wslb[:], start=(dc == 0), stop=(dc == NDC - 1))
                        for i in range(2):
                            em.v("tensor_copy", zb[:], psA[:, i * 1024:(i + 1) * 1024])
                            em.dma(zdram.rearrange("(tt p) f -> tt p f", p=128)
                                   [t0 + i, :, zh * 1024:(zh + 1) * 1024], zb[:])
                for fc in range(18):
                    for dc in range(NDC):
                        em.dma(wstg[:, 0:128], wsrc[dc, :, DI + fc * 128:DI + (fc + 1) * 128])
                        em.v("tensor_copy", wslb[:, 0:128], wstg[:, 0:128])
                        for ts4 in range(4):
                            em.mm(psA[:, ts4 * 512:(ts4 + 1) * 512],
                                  wslb[:, 0:128],
                                  pr.xnT[:, dc * T + ts4 * 512:dc * T + (ts4 + 1) * 512],
                                  start=(dc == 0), stop=(dc == NDC - 1))
                    dst = (xbc[:, fc * T:(fc + 1) * T] if fc < 16
                           else bc_sb[:, (fc - 16) * T:(fc - 15) * T])
                    em.v("tensor_copy", dst, psA[:])
                for dc in range(NDC):
                    em.dma(wstg[:, 0:32], wsrc[dc, :, DI + 2304:DI + 2304 + 32])
                    em.v("tensor_copy", wslb[:, 0:32], wstg[:, 0:32])
                    for ts4 in range(4):
                        em.mm(psA[0:32, ts4 * 512:(ts4 + 1) * 512],
                              wslb[:, 0:32],
                              pr.xnT[:, dc * T + ts4 * 512:dc * T + (ts4 + 1) * 512],
                              start=(dc == 0), stop=(dc == NDC - 1))
                em.v("tensor_scalar", dtR[:], psA[0:32, :], dtb[:], None, ALU.add)

            with pr.scope() as es4:
                accC = es4.enter_context(nc.sbuf_tensor("sa_accC", [128, T], F32))
                cw4 = es4.enter_context(nc.sbuf_tensor("sa_cw4", [128, 4], F32))
                cb1 = es4.enter_context(nc.sbuf_tensor("sa_cb1", [128, 1], F32))
                for fc in range(18):
                    xc = (xbc[:, fc * T:(fc + 1) * T] if fc < 16
                          else bc_sb[:, (fc - 16) * T:(fc - 15) * T])
                    em.dma(cw4[:], pr.din["sa_convw"][fc * 128:(fc + 1) * 128, :])
                    em.dma(cb1[:], pr.din["sa_convb"][fc * 128:(fc + 1) * 128, :])
                    em.v("tensor_scalar", accC[:], xc, cw4[:, 3:4], cb1[:],
                         ALU.mult, ALU.add)
                    for k in (0, 1, 2):
                        sh = 3 - k
                        em.v("scalar_tensor_tensor", accC[:, sh:T], xc[:, 0:T - sh],
                             cw4[:, k:k + 1], accC[:, sh:T], ALU.mult, ALU.add)
                    if fc < 16:
                        em.a("activation", xc, accC[:], AF.Silu, bias=0.0, scale=1.0)
                        em.dma(xdram.rearrange("(fc p) t -> fc p t", p=128)[fc], xc)
                    else:
                        dst = bc_sb[:, (fc - 16) * T:(fc - 15) * T]
                        em.a("activation", dst, accC[:], AF.Silu, bias=0.0, scale=1.0)

            with pr.scope() as es4:
                t32 = es4.enter_context(nc.sbuf_tensor("sa_t32", [32, T], F32))
                em.a("activation", t32[:], dtR[:], AF.Abs, bias=0.0, scale=1.0)
                em.a("activation", t32[:], t32[:], AF.Exp, bias=0.0, scale=-1.0)
                em.a("activation", t32[:], t32[:], AF.Ln, bias=1.0, scale=1.0)
                em.v("scalar_tensor_tensor", dtR[:], dtR[:], 0.0, t32[:],
                     ALU.max, ALU.add)
                em.v("tensor_scalar", t32[:], dtR[:], 1e-38, None, ALU.max)
                em.a("activation", U[:], t32[:], AF.Ln, bias=0.0, scale=1.0)
                em.v("tensor_scalar", t32[:], dtR[:], acol[:], None, ALU.mult)
                for c in range(NTT):
                    em.v("tensor_tensor_scan", lam[:, c * 128:(c + 1) * 128],
                         ones32[:], t32[:, c * 128:(c + 1) * 128], 0.0,
                         ALU.mult, ALU.add)
                em.v("tensor_sub", U[:], U[:], lam[:])

        with pr.scope() as es3:
            sb3 = lambda n, s, d: es3.enter_context(nc.sbuf_tensor(n, s, d))
            H = sb3("sa_H", [128, DI], F32)
            Hb = sb3("sa_Hb", [128, DI], BF16)
            Mw = sb3("sa_Mw", [128, NH * 128], BF16)
            lamM = sb3("sa_lamM", [128, NH * 128], F32)
            Xstage = sb3("sa_Xstage", [128, DI], BF16)
            Xtok = sb3("sa_Xtok", [128, DI], BF16)
            Xw = sb3("sa_Xw", [128, DI], BF16)
            Btok = sb3("sa_Btok", [128, 128], BF16)
            Gb = sb3("sa_Gb", [128, 128], BF16)
            Utok = sb3("sa_Utok", [128, 32], F32)
            Ltok = sb3("sa_Ltok", [128, 32], F32)
            lend = sb3("sa_lend", [128, 32], F32)
            wS = sb3("sa_wS", [128, 32], F32)
            gam = sb3("sa_gam", [128, 32], F32)
            expl = sb3("sa_expl", [128, 32], F32)
            Db32 = sb3("sa_Db32", [128, 32], F32)
            dcol = sb3("sa_dcol", [32, 1], F32)
            ycbf = sb3("sa_ycbf", [128, DI], BF16)
            lrow = sb3("sa_lrow", [1, 2048], F32)
            r11 = sb3("sa_r11", [1, 32], F32)
            psS = es3.enter_context(nc.psum_tensor("sa_psYS", [128, 2048], F32))
            psY = psS
            psT = es3.enter_context(nc.psum_tensor("sa_psT", [128, 2048], BF16))
            psF = es3.enter_context(nc.psum_tensor("sa_psF", [128, 512], F32))
            em.v("memset", H[:], 0.0)
            em.v("memset", Hb[:], 0.0)
            em.dma(dcol[:], pr.din["sa_d"])
            em.t("transpose", psF[0:1, 0:32], dcol[:], pr.ident32f)
            em.v("tensor_copy", r11[:], psF[0:1, 0:32])
            em.mm(psF[:, 32:64], pr.ones_row_f, r11[:], start=True, stop=True)
            em.v("tensor_copy", Db32[:], psF[:, 32:64])
            lam3 = lamM[:].rearrange("p (h t) -> p h t", t=128)
            Mw3 = Mw[:].rearrange("p (h t) -> p h t", t=128)
            for c in range(NTT):
                cs_ = slice(c * 128, (c + 1) * 128)
                em.t("transpose", psF[:, 32:64], U[:, cs_], pr.ident32f)
                em.v("tensor_copy", Utok[:], psF[:, 32:64])
                em.t("transpose", psF[:, 64:96], lam[:, cs_], pr.ident32f)
                em.v("tensor_copy", Ltok[:], psF[:, 64:96])
                em.t("transpose", psF[0:1, 96:128],
                     lam[:, c * 128 + 127:c * 128 + 128], pr.ident32f)
                em.v("tensor_copy", r11[:], psF[0:1, 96:128])
                em.mm(psF[:, 128:160], pr.ones_row_f, r11[:], start=True, stop=True)
                em.v("tensor_copy", lend[:], psF[:, 128:160])
                em.v("tensor_tensor", wS[:], Utok[:], lend[:], ALU.add)
                em.a("activation", wS[:], wS[:], AF.Exp, bias=0.0, scale=1.0)
                em.a("activation", gam[:], lend[:], AF.Exp, bias=0.0, scale=1.0)
                em.a("activation", expl[:], Ltok[:], AF.Exp, bias=0.0, scale=1.0)
                em.dma(lscr, lam[:, cs_])
                for half in range(2):
                    em.dma(lrow[:], lscr.rearrange("p t -> (p t)").rearrange(
                        "(one f) -> one f", one=1)[:, half * 2048:(half + 1) * 2048])
                    for q4 in range(4):
                        em.mm(psS[:, q4 * 512:(q4 + 1) * 512], pr.ones_row_f,
                              lrow[:, q4 * 512:(q4 + 1) * 512], start=True, stop=True)
                    em.v("tensor_copy", lamM[:, half * 2048:(half + 1) * 2048], psS[:])
                em.v("tensor_tensor", lam3,
                     pr.maskneg.unsqueeze(1).broadcast_to([128, NH, 128]),
                     lam3, ALU.add)
                em.v("tensor_tensor", lam3,
                     Utok[:].unsqueeze(2).broadcast_to([128, NH, 128]),
                     lam3, ALU.add)
                em.a("activation", Mw[:], lamM[:], AF.Exp, bias=0.0, scale=1.0)
                em.mm(psF[:, 0:128], bc_sb[:, cs_],
                      bc_sb[:, T + c * 128:T + (c + 1) * 128], start=True, stop=True)
                em.v("tensor_copy", Gb[:], psF[:, 0:128])
                em.v("tensor_tensor", Mw3, Gb[:].unsqueeze(1).broadcast_to([128, NH, 128]),
                     Mw3, ALU.mult)
                em.dma(Xstage[:].rearrange("p (i t) -> p i t", t=128),
                       xdram.rearrange("(i p) t -> p i t", p=128)[:, :, cs_])
                for i in range(16):
                    em.t("transpose", psT[:, i * 128:(i + 1) * 128],
                         Xstage[:, i * 128:(i + 1) * 128], pr.ident[:])
                em.v("tensor_copy", Xtok[:], psT[:])
                em.t("transpose", psT[:, 0:128], bc_sb[:, cs_], pr.ident[:])
                em.v("tensor_copy", Btok[:], psT[:, 0:128])
                em.v("tensor_tensor", Xw[:].rearrange("p (h w) -> p h w", w=PP),
                     Xtok[:].rearrange("p (h w) -> p h w", w=PP),
                     wS[:].unsqueeze(2).broadcast_to([128, NH, PP]), ALU.mult)
                XDv = lamM[:, 2048:4096].rearrange("p (h w) -> p h w", w=PP)
                em.v("tensor_tensor", XDv,
                     Xtok[:].rearrange("p (h w) -> p h w", w=PP),
                     Db32[:].unsqueeze(2).broadcast_to([128, NH, PP]), ALU.mult)
                for hh in range(2):
                    for hi in range(16):
                        h = hh * 16 + hi
                        em.mm(psY[:, hi * PP:(hi + 1) * PP],
                              Mw[:, h * 128:(h + 1) * 128],
                              Xtok[:, h * PP:(h + 1) * PP],
                              start=(hi % 8 == 0), stop=(hi % 8 == 7))
                        em.mm(psY[:, 1024 + hi * PP:1024 + (hi + 1) * PP],
                              bc_sb[:, T + c * 128:T + (c + 1) * 128],
                              Hb[:, h * PP:(h + 1) * PP],
                              start=(hi % 8 == 0), stop=(hi % 8 == 7))
                    ytmp = lamM[:, 0:1024]
                    em.v("tensor_tensor", ytmp.rearrange("p (h w) -> p h w", w=PP),
                         psY[:, 1024:2048].rearrange("p (h w) -> p h w", w=PP),
                         expl[:, hh * 16:(hh + 1) * 16].unsqueeze(2)
                         .broadcast_to([128, 16, PP]), ALU.mult)
                    em.v("tensor_tensor", ytmp, psY[:, 0:1024], ytmp, ALU.add)
                    em.v("tensor_tensor", ycbf[:, hh * 1024:(hh + 1) * 1024], ytmp,
                         lamM[:, 2048 + hh * 1024:2048 + (hh + 1) * 1024], ALU.add)
                em.dma(ydram.rearrange("(c p) f -> c p f", p=128)[c], ycbf[:])
                for h in range(NH):
                    em.mm(psS[:, h * PP:(h + 1) * PP], Btok[:],
                          Xw[:, h * PP:(h + 1) * PP],
                          start=(h % 8 == 0), stop=(h % 8 == 7))
                em.v("tensor_tensor", H[:].rearrange("p (h w) -> p h w", w=PP),
                     H[:].rearrange("p (h w) -> p h w", w=PP),
                     gam[:].unsqueeze(2).broadcast_to([128, NH, PP]), ALU.mult)
                em.v("tensor_tensor", H[:], psS[:], H[:], ALU.add)
                em.v("tensor_copy", Hb[:], H[:])

        with pr.scope() as es3:
            sb3 = lambda n, s, d: es3.enter_context(nc.sbuf_tensor(n, s, d))
            nw2b = sb3("sa_nw2b", [128, DI], F32)
            ytile = sb3("sa_ytile", [128, DI], BF16)
            ztile = sb3("sa_ztile", [128, DI], BF16)
            zs_ = sb3("sa_zs", [128, DI], F32)
            yz = sb3("sa_yz", [128, DI], F32)
            uu = sb3("sa_uu", [128, DI], F32)
            ynb = sb3("sa_ynb", [128, DI], BF16)
            ynT = sb3("sa_ynT", [128, DI], BF16)
            dump = sb3("sa_dump", [128, DI], F32)
            nwrow = sb3("sa_nwrow", [1, DI], F32)
            psO = es3.enter_context(nc.psum_tensor("sa_psO", [128, 1024], F32))
            psT2 = es3.enter_context(nc.psum_tensor("sa_psT2", [128, 2048], BF16))
            ps512b = es3.enter_context(nc.psum_tensor("sa_ps512b", [128, 512], F32))
            pr.es, saved_es = es3, pr.es
            wo = load_weight_bf16(pr, "sa_wob", "sa_woutT", DI, D)
            pr.es = saved_es
            em.dma(nwrow[:], pr.din["sa_normw"])
            bcast_row_to_tile(pr, nwrow, nw2b, ps512b, DI)
            for tt in range(NTT):
                em.dma(ytile[:], ydram.rearrange("(c p) f -> c p f", p=128)[tt])
                em.dma(ztile[:], zdram.rearrange("(c p) f -> c p f", p=128)[tt])
                em.a("activation", zs_[:], ztile[:], AF.Silu, bias=0.0, scale=1.0)
                em.v("tensor_tensor", yz[:], ytile[:], zs_[:], ALU.mult)
                em.a("activation", dump[:], yz[:], AF.Square, bias=0.0, scale=1.0,
                     accum_out=pr.col2)
                em.v("tensor_scalar", pr.col2, pr.col2, 1.0 / DI, 1e-5,
                     ALU.mult, ALU.add)
                em.a("activation", pr.col3, pr.col2, AF.Ln, bias=0.0, scale=1.0)
                em.a("activation", pr.col3, pr.col3, AF.Exp, bias=0.0, scale=-0.5)
                em.v("tensor_tensor", uu[:], yz[:], nw2b[:], ALU.mult)
                em.v("tensor_scalar", ynb[:], uu[:], pr.col3, None, ALU.mult)
                for i in range(16):
                    em.t("transpose", psT2[:, i * 128:(i + 1) * 128],
                         ynb[:, i * 128:(i + 1) * 128], pr.ident[:])
                em.v("tensor_copy", ynT[:], psT2[:])
                for os_ in range(2):
                    for ic in range(16):
                        em.mm(psO[:, os_ * 512:(os_ + 1) * 512],
                              ynT[:, ic * 128:(ic + 1) * 128],
                              wo[:, ic * D + os_ * 512:ic * D + (os_ + 1) * 512],
                              start=(ic == 0), stop=(ic == 15))
                em.v("tensor_tensor", pr.xs[:, tt * D:(tt + 1) * D], psO[:],
                     pr.xs[:, tt * D:(tt + 1) * D], ALU.add)


EMITTERS = {
    "l0aa": emit_l0_attn_a,
    "l0ab": emit_l0_attn_b,
    "l0ffn1": emit_l0_ffn,
    "l0ffn2": emit_l0_ffn,
    "l1ffn1": emit_l1_ffn,
    "l1ffn2": emit_l1_ffn,
    "l1ab": emit_l1_attn_b,
    "l1aa": emit_l1_attn_a,
}


def build_program(parts):
    nc = bass.Bass("TRN2", target_bir_lowering=False)
    x_in = nc.dram_tensor("x_in", [T, D], F32, kind="ExternalInput").ap()
    x_out = nc.dram_tensor("x_out", [T, D], F32, kind="ExternalOutput").ap()
    names = set(CONST_INPUTS)
    for p_ in parts:
        names.update(PART_INPUTS[p_])
    dins = {}
    for n in sorted(names):
        shp = PREP_SHAPES[n]
        dt = BF16 if n in ("c_ident128", "c_ones_row_b", "c_triu") else F32
        dins[n] = nc.dram_tensor(n, list(shp), dt, kind="ExternalInput").ap()

    em = E(nc)
    global _LAST_EM
    _LAST_EM = em
    with contextlib.ExitStack() as es:
        pr = Prog(NCProxy(nc), em, es, dins)
        setup_shared(pr)
        em.dma(pr.xs[:].rearrange("p (tt d) -> p tt d", d=D),
               x_in.rearrange("(tt p) d -> p tt d", p=128))
        for p_ in parts:
            em.mark(p_)
            EMITTERS[p_](pr)
        em.mark("end")
        em.dma(x_out.rearrange("(tt p) d -> p tt d", p=128),
               pr.xs[:].rearrange("p (tt d) -> p tt d", d=D))
        esem = {e: es.enter_context(nc.semaphore(f"SE_{e}")) for e in E.ENGS}
        dsem = [es.enter_context(nc.semaphore(f"SD_{i}")) for i in range(E.NDMA)]
        block = es.enter_context(nc.Block())
        em.replay(block, (esem, dsem))
    return nc, sorted(names)


PREP_SHAPES = {
    "c_ident128": (128, 128), "c_ident32f": (32, 32), "c_ident128f": (128, 128),
    "c_ones_row_f": (1, 128), "c_maskneg": (128, 128),
    "c_ones_row_b": (1, 128), "c_triu": (128, 128),
    "aa_wqT": (D, D), "aa_wkT": (D, D), "aa_wvT": (D, D),
    "ab_wqkv": (D, 3072), "ab_wout": (D, D), "ab_bout": (1, D), "ab_ls": (16, 1),
    "f0_w1T": (D, 4096), "f0_w2T": (4096, D), "f0_convw": (4096, 3), "f0_convb": (4096, 1),
    "sa_winT": (D, 4384), "sa_convw": (2304, 4), "sa_convb": (2304, 1),
    "sa_dtb": (32, 1), "sa_alog": (32, 1), "sa_d": (32, 1), "sa_normw": (1, 2048),
    "sa_woutT": (2048, D),
    "db_gatew": (D, 7), "db_gateb": (D, 1), "db_candw": (D, 7), "db_candb": (D, 1),
    "db_logdec": (128, 8), "db_woutT": (D, D), "db_bout": (1, D), "db_normw": (1, D),
    "f1_w1T": (D, 4096), "f1_b1": (1, 4096), "f1_w2T": (4096, D), "f1_b2": (1, D),
}


def run_parts(x_full, prepped, parts, sim=False):
    nc, names = build_program(parts)
    in_maps = []
    ncores = 1 if sim else 8
    for b in range(ncores):
        m = {"x_in": np.ascontiguousarray(x_full[b])}
        for n in names:
            arr = prepped[n]
            want_bf = n in ("c_ident128", "c_ones_row_b", "c_triu")
            m[n] = np.ascontiguousarray(arr.astype(bf16 if want_bf else np.float32))
        in_maps.append(m)
    if sim:
        from concourse.bass_interp import MultiCoreSim
        nc.finalize()
        s = MultiCoreSim(nc, num_cores=1, num_workers=1, trace=False)
        for k, v in in_maps[0].items():
            s.cores[0].tensor(k)[:] = v
        s.simulate()
        return np.asarray(s.cores[0].tensor("x_out"))[None]
    res = run_bass_kernel_spmd(nc, in_maps, core_ids=list(range(8)))
    return np.stack([r["x_out"] for r in res.results])


def kernel(**inputs):
    x = np.ascontiguousarray(np.asarray(inputs["L0_attn_a_z"], dtype=np.float32))
    prepped = prep_inputs(inputs)
    return run_parts(x, prepped, ALL_PARTS)


# version 18
# speedup vs baseline: 1.5831x; 1.1484x over previous
import contextlib
import numpy as np
import ml_dtypes
import concourse.bass as bass
import concourse.mybir as mybir
from concourse.bass_utils import run_bass_kernel_spmd

F32 = mybir.dt.float32
BF16 = mybir.dt.bfloat16
I32 = mybir.dt.int32
AF = mybir.ActivationFunctionType
ALU = mybir.AluOpType
AX = mybir.AxisListType
bf16 = ml_dtypes.bfloat16

T = 2048
D = 1024
NTT = T // 128
NDC = D // 128

ALL_PARTS = ["l0aa", "l0ffn1", "l0ab", "l0ffn2", "l1aa", "l1ffn1", "l1ab", "l1ffn2"]
_LAST_EM = None


class E:

    ENGS = ("vector", "scalar", "tensor", "gpsimd", "sync")
    NDMA = 16

    def __init__(self, nc):
        self.nc = nc
        self.ops = {e: [] for e in self.ENGS}
        self.cnt = {e: 0 for e in self.ENGS}
        self.dma_uses = [0] * self.NDMA
        self.dma_next = 0
        self.highwater = {}
        self.last_writer = {}
        self.readers = {}

    @staticmethod
    def _aps(x):
        return [a for a in x if type(a).__name__ == "AP"]

    def _dep_wait(self, waits, dep, eng=None):
        kind = dep[0]
        if kind == "E":
            _, deng, dcnt = dep
            if deng == eng and eng == "tensor":
                return
            key = ("E", deng)
            if waits.get(key, 0) < dcnt:
                waits[key] = dcnt
        else:
            _, slot, val = dep
            key = ("D", slot)
            if waits.get(key, 0) < val:
                waits[key] = val

    def mark(self, label):
        self.marks = getattr(self, "marks", [])
        self.marks.append((label, dict(self.cnt), sum(self.dma_uses)))

    def barrier(self):
        self._barrier_tokens = (
            [("E", e, c) for e, c in self.cnt.items() if c > 0] +
            [("D", s, 16 * u) for s, u in enumerate(self.dma_uses) if u > 0])
        self._barrier_done = set()
        self.last_writer = {}
        self.readers = {}

    def _desc(self, eng, fn, args, kwargs, dma=False):
        aps = self._aps(args) + self._aps(list(kwargs.values()))
        if dma:
            writes = [kwargs["out"]] if "out" in kwargs else [args[0]]
            reads = [a for a in aps if a is not writes[0]]
        else:
            writes = aps[:1]
            if kwargs.get("accum_out") is not None:
                writes = writes + [kwargs["accum_out"]]
            reads = [a for a in aps if all(a is not w for w in writes)]
        waits = {}
        if getattr(self, "_barrier_tokens", None) and eng not in self._barrier_done:
            for dep in self._barrier_tokens:
                self._dep_wait(waits, dep, None)
            self._barrier_done.add(eng)
        rnames = [a.name for a in reads]
        wnames = [a.name for a in writes]
        for t in rnames:
            w = self.last_writer.get(t)
            if w is not None:
                self._dep_wait(waits, w, eng)
        for t in wnames:
            w = self.last_writer.get(t)
            if w is not None:
                self._dep_wait(waits, w, eng)
            for r in self.readers.get(t, ()):
                self._dep_wait(waits, r, eng)
        if dma:
            slot = self.dma_next
            self.dma_next = (self.dma_next + 1) % self.NDMA
            if self.dma_uses[slot] > 0:
                key = ("D", slot)
                if waits.get(key, 0) < 16 * self.dma_uses[slot]:
                    waits[key] = 16 * self.dma_uses[slot]
            self.dma_uses[slot] += 1
            tok = ("D", slot, 16 * self.dma_uses[slot])
            inc = ("D", slot)
        else:
            self.cnt[eng] += 1
            tok = ("E", eng, self.cnt[eng])
            inc = ("E", eng)
        for t in rnames:
            self.readers.setdefault(t, []).append(tok)
        for t in wnames:
            self.last_writer[t] = tok
            self.readers[t] = []
        hw = self.highwater.setdefault(eng, {})
        pruned = {}
        for key, val in waits.items():
            if hw.get(key, -1) < val:
                pruned[key] = val
                hw[key] = val
        self.ops[eng].append((pruned, fn, inc))

    def v(self, meth, *a, **k):
        self._desc("vector", lambda e, a=a, k=k, m=meth: getattr(e, m)(*a, **k), a, k)

    def a(self, meth, *a, **k):
        self._desc("scalar", lambda e, a=a, k=k, m=meth: getattr(e, m)(*a, **k), a, k)

    def t(self, meth, *a, **k):
        self._desc("tensor", lambda e, a=a, k=k, m=meth: getattr(e, m)(*a, **k), a, k)

    def g(self, meth, *a, **k):
        self._desc("gpsimd", lambda e, a=a, k=k, m=meth: getattr(e, m)(*a, **k), a, k)

    def dma(self, out, in_, eng=None):
        if eng is None:
            eng = ("sync", "gpsimd")[self.dma_next % 2]
        self._desc(eng, lambda e, o=out, i=in_: e.dma_start(out=o, in_=i),
                   (), {"out": out, "in_": in_}, dma=True)

    def dma_t(self, out, in_, eng=None):
        if eng is None:
            eng = "sync"
        self._desc(eng, lambda e, o=out, i=in_: e.dma_start(out=o, in_=i, transpose=True),
                   (), {"out": out, "in_": in_}, dma=True)

    def mm(self, out, lhsT, rhs, start, stop, tile_position=None):
        self._desc("tensor",
                   lambda e, o=out, l=lhsT, r=rhs, s1=start, s2=stop, tp=tile_position:
                   e.matmul(o, l, r, start=s1, stop=s2, tile_position=tp),
                   (out, lhsT, rhs), {})

    def replay(self, block, sems):
        esem, dsem = sems
        total_e = dict(self.cnt)
        total_d = list(self.dma_uses)

        for name in self.ENGS:
            ops = self.ops[name]

            def section(eng, ops=ops, name=name):
                for (waits, fn, inc) in ops:
                    for (key, val) in sorted(waits.items(), key=str):
                        if key[0] == "E":
                            eng.wait_ge(esem[key[1]], val)
                        else:
                            eng.wait_ge(dsem[key[1]], val)
                    inst = fn(eng)
                    if inc[0] == "E":
                        inst.then_inc(esem[inc[1]], 1)
                    else:
                        inst.then_inc(dsem[inc[1]], 16)
                if name == "sync":
                    for e2, c in total_e.items():
                        if c > 0:
                            eng.wait_ge(esem[e2], c)
                    for s, u in enumerate(total_d):
                        if u > 0:
                            eng.wait_ge(dsem[s], 16 * u)

            getattr(block, name)(section)


def prep_inputs(inp):
    p = {}
    c = np.ascontiguousarray
    p["c_ident128"] = np.eye(128, dtype=bf16)
    p["c_ident32f"] = np.eye(32, dtype=np.float32)
    p["c_ident128f"] = np.eye(128, dtype=np.float32)
    p["c_ones_row_f"] = np.ones((1, 128), dtype=np.float32)
    p["c_ones_row_b"] = np.ones((1, 128), dtype=bf16)
    p["c_triu"] = np.triu(np.ones((128, 128), dtype=np.float32)).astype(bf16)
    mn = np.zeros((128, 128), dtype=np.float32)
    mn[np.tril_indices(128, -1)] = -1e30
    p["c_maskneg"] = mn

    W = np.asarray(inp["L0_attn_a_W_qkv"]).reshape(16, 3, 64, D)
    p["aa_wqT"] = c(W[:, 0].reshape(D, D).T)
    p["aa_wkT"] = c(W[:, 1].reshape(D, D).T)
    p["aa_wvT"] = c(W[:, 2].reshape(D, D).T)

    p["ab_wqkv"] = c(np.asarray(inp["L0_attn_b_w_qkv"]))
    p["ab_wout"] = c(np.asarray(inp["L0_attn_b_w_out"]))
    p["ab_bout"] = c(np.asarray(inp["L0_attn_b_b_out"]).reshape(1, D))
    p["ab_ls"] = c(np.asarray(inp["L0_attn_b_logit_scale"]).reshape(16, 1))

    p["f0_w1T"] = c(np.asarray(inp["L0_ffn_w1"]).T)
    p["f0_w2T"] = c(np.asarray(inp["L0_ffn_w2"]).T)
    p["f0_convw"] = c(np.asarray(inp["L0_ffn_conv_w"]).reshape(4096, 3))
    p["f0_convb"] = c(np.asarray(inp["L0_ffn_conv_b"]).reshape(4096, 1))

    p["sa_winT"] = c(np.asarray(inp["L1_attn_a_W_in"]).T)
    p["sa_convw"] = c(np.asarray(inp["L1_attn_a_conv_w"]))
    p["sa_convb"] = c(np.asarray(inp["L1_attn_a_conv_b"]).reshape(2304, 1))
    p["sa_dtb"] = c(np.asarray(inp["L1_attn_a_dt_bias"]).reshape(32, 1))
    p["sa_alog"] = c(np.asarray(inp["L1_attn_a_A_log"]).reshape(32, 1))
    p["sa_d"] = c(np.asarray(inp["L1_attn_a_D"]).reshape(32, 1))
    p["sa_normw"] = c(np.asarray(inp["L1_attn_a_norm_w"]).reshape(1, 2048))
    p["sa_woutT"] = c(np.asarray(inp["L1_attn_a_W_out"]).T)

    p["db_gatew"] = c(np.asarray(inp["L1_attn_b_gate_w"]).reshape(D, 7))
    p["db_gateb"] = c(np.asarray(inp["L1_attn_b_gate_b"]).reshape(D, 1))
    p["db_candw"] = c(np.asarray(inp["L1_attn_b_cand_w"]).reshape(D, 7))
    p["db_candb"] = c(np.asarray(inp["L1_attn_b_cand_b"]).reshape(D, 1))
    p["db_logdec"] = c(np.asarray(inp["L1_attn_b_log_decay"]).reshape(8, 128).T)
    p["db_woutT"] = c(np.asarray(inp["L1_attn_b_out_w"]).T)
    p["db_bout"] = c(np.asarray(inp["L1_attn_b_out_b"]).reshape(1, D))
    p["db_normw"] = c(np.asarray(inp["L1_attn_b_norm_w"]).reshape(1, D))

    p["f1_w1T"] = c(np.asarray(inp["L1_ffn_W1"]).T)
    p["f1_b1"] = c(np.asarray(inp["L1_ffn_b1"]).reshape(1, 4096))
    p["f1_w2T"] = c(np.asarray(inp["L1_ffn_W2"]).T)
    p["f1_b2"] = c(np.asarray(inp["L1_ffn_b2"]).reshape(1, D))
    return p


PART_INPUTS = {
    "l0aa": ["aa_wqT", "aa_wkT", "aa_wvT"],
    "l0ab": ["ab_wqkv", "ab_wout", "ab_bout", "ab_ls"],
    "l0ffn1": ["f0_w1T", "f0_w2T", "f0_convw", "f0_convb"],
    "l0ffn2": ["f0_w1T", "f0_w2T", "f0_convw", "f0_convb"],
    "l1aa": ["sa_winT", "sa_convw", "sa_convb", "sa_dtb", "sa_alog", "sa_d",
             "sa_normw", "sa_woutT"],
    "l1ab": ["db_gatew", "db_gateb", "db_candw", "db_candb", "db_logdec",
             "db_woutT", "db_bout", "db_normw"],
    "l1ffn1": ["f1_w1T", "f1_b1", "f1_w2T", "f1_b2"],
    "l1ffn2": ["f1_w1T", "f1_b1", "f1_w2T", "f1_b2"],
}
CONST_INPUTS = ["c_ident128", "c_ident32f", "c_ident128f", "c_ones_row_f",
                "c_ones_row_b", "c_triu", "c_maskneg"]


class NCProxy:

    def __init__(self, nc):
        self._nc = nc
        self._used = {}

    def _uniq(self, name):
        n = self._used.get(name, 0)
        self._used[name] = n + 1
        return name if n == 0 else f"{name}__{n}"

    def sbuf_tensor(self, name, *a, **k):
        return self._nc.sbuf_tensor(self._uniq(name), *a, **k)

    def psum_tensor(self, name, *a, **k):
        return self._nc.psum_tensor(self._uniq(name), *a, **k)

    def dram_tensor(self, name, *a, **k):
        return self._nc.dram_tensor(self._uniq(name), *a, **k)

    def __getattr__(self, a):
        return getattr(self._nc, a)


class Prog:

    def __init__(self, nc, em, es, dins):
        self.nc = nc
        self.em = em
        self.es = es
        self.din = dins
        self.sb = {}
        self.uid = 0

    def sbuf(self, name, shape, dtype):
        t = self.es.enter_context(self.nc.sbuf_tensor(name, shape, dtype))
        return t

    def psum(self, name, shape, dtype):
        return self.es.enter_context(self.nc.psum_tensor(name, shape, dtype))

    @contextlib.contextmanager
    def scope(self):
        with contextlib.ExitStack() as es:
            yield es
        self.em.barrier()


def setup_shared(pr):
    em = pr.em
    pr.xs = pr.sbuf("xs", [128, NTT * D], F32)[:]
    pr.ident = pr.sbuf("ident", [128, 128], BF16)[:]
    pr.ident32f = pr.sbuf("ident32f", [32, 32], F32)[:]
    pr.ident128f = pr.sbuf("ident128f", [128, 128], F32)[:]
    pr.ones_row_f = pr.sbuf("ones_row_f", [1, 128], F32)[:]
    pr.ones_row_b = pr.sbuf("ones_row_b", [1, 128], BF16)[:]
    pr.triu = pr.sbuf("triu", [128, 128], BF16)[:]
    pr.maskneg = pr.sbuf("maskneg", [128, 128], F32)[:]
    pr.col1 = pr.sbuf("col1", [128, 1], F32)[:]
    pr.col2 = pr.sbuf("col2", [128, 1], F32)[:]
    pr.col3 = pr.sbuf("col3", [128, 1], F32)[:]
    em.dma(pr.ident[:], pr.din["c_ident128"])
    em.dma(pr.ident32f[:], pr.din["c_ident32f"])
    em.dma(pr.ident128f[:], pr.din["c_ident128f"])
    em.dma(pr.ones_row_f[:], pr.din["c_ones_row_f"])
    em.dma(pr.ones_row_b[:], pr.din["c_ones_row_b"])
    em.dma(pr.triu[:], pr.din["c_triu"])
    em.dma(pr.maskneg[:], pr.din["c_maskneg"])


def load_weight_bf16(pr, name, dram_name, rows, cols, tag=None):
    em = pr.em
    nch = rows // 128
    wb = pr.sbuf(name, [128, nch * cols], BF16)
    src = pr.din[dram_name].rearrange("(c p) f -> c p f", p=128)
    with pr.scope() as es_w:
        stg = es_w.enter_context(pr.nc.sbuf_tensor(name + "_stg", [128, cols], F32))
        for ci in range(nch):
            em.dma(stg[:], src[ci])
            em.v("tensor_copy", wb[:, ci * cols:(ci + 1) * cols], stg[:])
    return wb


def emit_ln_tile(pr, tt, xn_tile, ps_tr, dump, to_xnT=True, xn_f32=None,
                 cols=None):
    em = pr.em
    c1, c2, c3 = cols if cols is not None else (pr.col1, pr.col2, pr.col3)
    xst = pr.xs[:, tt * D:(tt + 1) * D]
    em.v("reduce_sum", c1, xst, AX.X)
    em.v("tensor_scalar", c1, c1, -1.0 / D, None, ALU.mult)
    em.a("activation", dump[:, 0:D], xst, AF.Square, bias=c1, scale=1.0,
         accum_out=c2)
    em.v("tensor_scalar", c2, c2, 1.0 / D, 1e-5, ALU.mult, ALU.add)
    em.a("activation", c3, c2, AF.Ln, bias=0.0, scale=1.0)
    em.a("activation", c3, c3, AF.Exp, bias=0.0, scale=-0.5)
    em.v("tensor_scalar", xn_tile[:], xst, c1, c3, ALU.add, ALU.mult)
    if xn_f32 is not None:
        em.v("tensor_scalar", xn_f32[:], xst, c1, c3, ALU.add, ALU.mult)
    if to_xnT:
        for dc in range(NDC):
            em.t("transpose", ps_tr[:, dc * 128:(dc + 1) * 128],
                 xn_tile[:, dc * 128:(dc + 1) * 128], pr.ident[:])
        dst = pr.xnT[:].rearrange("p (dc t) -> p dc t", t=T)[:, :, tt * 128:(tt + 1) * 128]
        src = ps_tr[:].rearrange("p (dc t) -> p dc t", t=128)
        em.v("tensor_copy", dst, src)


def emit_ln_all(pr, xn_f32_buf=None, stats=None):
    with pr.scope() as es2:
        R = 2
        xn_tile = [es2.enter_context(pr.nc.sbuf_tensor(f"ln_xnt{i}", [128, D], BF16))
                   for i in range(R)]
        dump = [es2.enter_context(pr.nc.sbuf_tensor(f"ln_dump{i}", [128, D], BF16))
                for i in range(R)]
        ps_tr = [es2.enter_context(pr.nc.psum_tensor(f"ln_pstr{i}", [128, NDC * 128], BF16))
                 for i in range(R)]
        colr = [tuple(es2.enter_context(pr.nc.sbuf_tensor(f"ln_c{i}_{j}", [128, 1], F32))[:]
                      for j in range(3)) for i in range(R)]
        for tt in range(NTT):
            i = tt % R
            xf = None if xn_f32_buf is None else xn_f32_buf[:, tt * D:(tt + 1) * D]
            emit_ln_tile(pr, tt, xn_tile[i], ps_tr[i], dump[i], xn_f32=xf,
                         cols=colr[i])
            if stats is not None:
                pr.em.v("tensor_copy", stats[0][:, tt:tt + 1], colr[i][0])
                pr.em.v("tensor_copy", stats[1][:, tt:tt + 1], colr[i][2])


def emit_attention_core(pr, qT, kT, vbuf, evac_fn, scale_imm=None,
                        scale_ap16=None, bias_ap16=None):
    em = pr.em
    with pr.scope() as es2:
        nc = pr.nc
        NPB = 3
        ps_s = [es2.enter_context(nc.psum_tensor(f"att_s{i}", [128, 1024], F32))
                for i in range(NPB)]
        ps_av = [es2.enter_context(nc.psum_tensor("att_av", [128, 1024], F32))] * 2
        ebuf = [es2.enter_context(nc.sbuf_tensor(f"att_e{i}", [128, 1024], BF16))
                for i in range(NPB)]
        rden = es2.enter_context(nc.sbuf_tensor("att_rd", [128, 1], F32))

        def avoff(j):
            return (j // 4) * 512 + (j % 4) * 65

        for h in range(16):
            bp = 64 * (h % 2)
            cb = (h // 2) * T
            for qh in range(2):
                av = ps_av[(h * 2 + qh) % 2]

                def front(kc, h=h, bp=bp, cb=cb, qh=qh):
                    pb = kc % 3
                    for qs in range(2):
                        em.mm(ps_s[pb][:, qs * 512:(qs + 1) * 512],
                              kT[bp:bp + 64, cb + kc * 128:cb + (kc + 1) * 128],
                              qT[bp:bp + 64, cb + qh * 1024 + qs * 512:cb + qh * 1024 + (qs + 1) * 512],
                              start=True, stop=True, tile_position=(bp, 0))
                    if scale_imm is not None:
                        em.a("activation", ebuf[pb][:], ps_s[pb][:],
                             AF.Exp, bias=0.0, scale=scale_imm)
                    else:
                        em.a("activation", ebuf[pb][:], ps_s[pb][:],
                             AF.Exp, bias=bias_ap16[:, h:h + 1], scale=scale_ap16[:, h:h + 1])

                def back(kc, h=h, av=av):
                    pb = kc % 3
                    for j in range(8):
                        em.mm(av[:, avoff(j):avoff(j) + 65],
                              ebuf[pb][:, j * 128:(j + 1) * 128],
                              vbuf[:, kc * 1040 + h * 65:kc * 1040 + (h + 1) * 65],
                              start=(kc == 0 and j % 4 == 0),
                              stop=(kc == NTT - 1 and j % 4 == 3))

                for kc in range(NTT + 1):
                    if kc < NTT:
                        front(kc)
                    if kc > 0:
                        back(kc - 1)
                for j in range(8):
                    em.v("reciprocal", rden[:], av[:, avoff(j) + 64:avoff(j) + 65])
                    evac_fn(h, qh * 8 + j, av[:, avoff(j):avoff(j) + 64], rden)


def emit_l0_attn_a(pr):
    em = pr.em
    nc = pr.nc
    with pr.scope() as es2:
        qT = es2.enter_context(nc.sbuf_tensor("aa_qT", [128, NDC * T], BF16))
        kT = es2.enter_context(nc.sbuf_tensor("aa_kT", [128, NDC * T], BF16))
        vbuf = es2.enter_context(nc.sbuf_tensor("aa_v", [128, NTT * D], BF16))
        with pr.scope() as es3:
            pr.xnT = es3.enter_context(nc.sbuf_tensor("aa_xnT", [128, NDC * T], BF16))[:]
            emit_ln_all(pr)
            ps_qk = es3.enter_context(nc.psum_tensor("aa_psqk", [128, 2048], F32))
            ps_v = es3.enter_context(nc.psum_tensor("aa_psv", [128, 1024], F32))
            with pr.scope() as es4:
                wstg = es4.enter_context(nc.sbuf_tensor("aa_wstg", [128, 128], F32))
                wsl = es4.enter_context(nc.sbuf_tensor("aa_wsl", [128, 128], BF16))
                for side, wname in ((qT, "aa_wqT"), (kT, "aa_wkT")):
                    wsrc = pr.din[wname].rearrange("(dc p) f -> dc p f", p=128)
                    for fc in range(NDC):
                        for dc in range(NDC):
                            em.dma(wstg[:], wsrc[dc, :, fc * 128:(fc + 1) * 128])
                            em.v("tensor_copy", wsl[:], wstg[:])
                            for qs in range(4):
                                em.mm(ps_qk[:, qs * 512:(qs + 1) * 512],
                                      wsl[:],
                                      pr.xnT[:, dc * T + qs * 512:dc * T + (qs + 1) * 512],
                                      start=(dc == 0), stop=(dc == NDC - 1))
                        em.v("tensor_copy", side[:, fc * T:(fc + 1) * T], ps_qk[:])
            with pr.scope() as es4:
                wvh = es4.enter_context(nc.sbuf_tensor("aa_wvh", [128, NDC * 512], BF16))
                wstg2 = es4.enter_context(nc.sbuf_tensor("aa_wstg2", [128, 512], F32))
                wsrc = pr.din["aa_wvT"].rearrange("(dc p) f -> dc p f", p=128)
                for vs in range(2):
                    for dc in range(NDC):
                        em.dma(wstg2[:], wsrc[dc, :, vs * 512:(vs + 1) * 512])
                        em.v("tensor_copy", wvh[:, dc * 512:(dc + 1) * 512], wstg2[:])
                    for tt in range(NTT):
                        for dc in range(NDC):
                            em.mm(ps_v[:, vs * 512:(vs + 1) * 512],
                                  pr.xnT[:, dc * T + tt * 128:dc * T + (tt + 1) * 128],
                                  wvh[:, dc * 512:(dc + 1) * 512],
                                  start=(dc == 0), stop=(dc == NDC - 1))
                        em.v("tensor_copy",
                             vbuf[:, tt * D + vs * 512:tt * D + (vs + 1) * 512],
                             ps_v[:, vs * 512:(vs + 1) * 512])

        with pr.scope() as es3:
            NSL = 4
            ps_s = es3.enter_context(nc.psum_tensor("aa_pss", [128, 2048], F32))
            ps_av = [es3.enter_context(nc.psum_tensor(f"aa_psav{j}", [128, 512], F32))
                     for j in range(NSL)]
            ebuf = [es3.enter_context(nc.sbuf_tensor(f"aa_e{j}", [128, 2048], BF16))
                    for j in range(NSL)]
            eT = [es3.enter_context(nc.sbuf_tensor(f"aa_eT{j}", [128, 2048], BF16))
                  for j in range(NSL)]
            nmx = [es3.enter_context(nc.sbuf_tensor(f"aa_nmx{j}", [128, 1], F32))
                   for j in range(NSL)]
            den = [es3.enter_context(nc.sbuf_tensor(f"aa_den{j}", [128, 1], F32))
                   for j in range(NSL)]
            rden = [es3.enter_context(nc.sbuf_tensor(f"aa_rden{j}", [128, 1], F32))
                    for j in range(NSL)]
            iters = [(h, qt) for h in range(16) for qt in range(NTT)]

            def front(i):
                h, qt = iters[i]
                j = i % NSL
                bp = 64 * (h % 2)
                cb = (h // 2) * T
                for ks in range(4):
                    em.mm(ps_s[:, ks * 512:(ks + 1) * 512],
                          qT[bp:bp + 64, cb + qt * 128:cb + (qt + 1) * 128],
                          kT[bp:bp + 64, cb + ks * 512:cb + (ks + 1) * 512],
                          start=True, stop=True, tile_position=(bp, 0))
                em.v("reduce_max", nmx[j][:], ps_s[:], AX.X, negate=True)
                em.v("tensor_scalar", nmx[j][:], nmx[j][:], 0.125, None, ALU.mult)
                em.a("activation", ebuf[j][:], ps_s[:], AF.Exp, bias=nmx[j][:],
                     scale=0.125, accum_out=den[j][:])
                em.v("reciprocal", rden[j][:], den[j][:])
                em.dma_t(eT[j][:].rearrange("p (c q) -> p c q", q=128),
                         ebuf[j][:].rearrange("p (c k) -> p c k", k=128),
                         eng="sync")

            def back(i):
                h, qt = iters[i]
                j = i % NSL
                for kc in range(NTT):
                    em.mm(ps_av[j][:, 0:64], eT[j][:, kc * 128:(kc + 1) * 128],
                          vbuf[:, kc * D + h * 64:kc * D + (h + 1) * 64],
                          start=(kc == 0), stop=(kc == NTT - 1))
                xsl = pr.xs[:, qt * D + h * 64:qt * D + (h + 1) * 64]
                em.v("scalar_tensor_tensor", xsl, ps_av[j][:, 0:64], rden[j][:], xsl,
                     ALU.mult, ALU.add)

            for i in range(len(iters) + 1):
                if i < len(iters):
                    front(i)
                if i > 0:
                    back(i - 1)


def emit_l0_attn_b(pr):
    em = pr.em
    nc = pr.nc
    with pr.scope() as es2:
        pr.xnT = es2.enter_context(nc.sbuf_tensor("ab_xnT", [128, NDC * T], BF16))[:]
        emit_ln_all(pr)
        scb = es2.enter_context(nc.sbuf_tensor("ab_scb", [128, 16], F32))
        nscb = es2.enter_context(nc.sbuf_tensor("ab_nscb", [128, 16], F32))
        ao = pr.xnT

        with pr.scope() as es3:
            lsc = es3.enter_context(nc.sbuf_tensor("ab_lsc", [16, 1], F32))
            lsr = es3.enter_context(nc.sbuf_tensor("ab_lsr", [1, 16], F32))
            ps16 = es3.enter_context(nc.psum_tensor("ab_ps16", [128, 512], F32))
            em.dma(lsc[:], pr.din["ab_ls"])
            em.v("tensor_scalar", lsc[:], lsc[:], float(np.log(100.0)), None, ALU.min)
            em.a("activation", lsc[:], lsc[:], AF.Exp, bias=0.0, scale=1.0)
            em.t("transpose", ps16[0:1, 0:16], lsc[:], pr.ident32f[0:16, 0:16])
            em.v("tensor_copy", lsr[:], ps16[0:1, 0:16])
            em.mm(ps16[:, 0:16], pr.ones_row_f[:], lsr[:], start=True, stop=True)
            em.v("tensor_copy", scb[:], ps16[:, 0:16])
            em.v("tensor_scalar", nscb[:], scb[:], -1.0, None, ALU.mult)

        with pr.scope() as esA:
            qT = esA.enter_context(nc.sbuf_tensor("ab_qT", [128, NDC * T], BF16))
            kT = esA.enter_context(nc.sbuf_tensor("ab_kT", [128, NDC * T], BF16))
            vbuf = esA.enter_context(nc.sbuf_tensor("ab_v", [128, NTT * 1040], BF16))

            with pr.scope() as es3:
                TB = 3
                ps_q3 = [es3.enter_context(nc.psum_tensor(f"ab_psq{i}", [128, 1024], F32))
                         for i in range(TB)]
                ps_tr = es3.enter_context(nc.psum_tensor("ab_pstr", [128, 1024], BF16))
                qn = es3.enter_context(nc.sbuf_tensor("ab_qn", [128, D], BF16))
                ssq = es3.enter_context(nc.sbuf_tensor("ab_ssq", [128, 16], F32))
                dump = es3.enter_context(nc.sbuf_tensor("ab_dump", [128, D], F32))
                wstg = es3.enter_context(nc.sbuf_tensor("ab_wstg", [128, 512], F32))
                wslb = es3.enter_context(nc.sbuf_tensor("ab_wslb", [128, 512], BF16))
                ones_ap = vbuf[:].rearrange("p (t h w) -> p t h w", h=16, w=65)[:, :, :, 64:65]
                em.v("memset", ones_ap, 1.0)
                wsrc = pr.din["ab_wqkv"].rearrange("(dc p) f -> dc p f", p=128)
                for part, dstT in ((0, qT), (1, kT), (2, None)):
                    for t0 in range(0, NTT, TB):
                        tts = list(range(t0, min(t0 + TB, NTT)))
                        for qs in range(2):
                            for dc in range(NDC):
                                em.dma(wstg[:], wsrc[dc, :, part * 1024 + qs * 512:
                                                       part * 1024 + (qs + 1) * 512])
                                em.v("tensor_copy", wslb[:], wstg[:])
                                for i, tt in enumerate(tts):
                                    em.mm(ps_q3[i][:, qs * 512:(qs + 1) * 512],
                                          pr.xnT[:, dc * T + tt * 128:dc * T + (tt + 1) * 128],
                                          wslb[:],
                                          start=(dc == 0), stop=(dc == NDC - 1))
                        for i, tt in enumerate(tts):
                            ps_q = ps_q3[i]
                            emit_ab_qkv_post(pr, part, tt, ps_q, ps_tr, qn, ssq, dump,
                                             qT, kT, vbuf)

            _ab_attention_and_outproj(pr, qT, kT, vbuf, scb, nscb)
        _ab_outproj(pr)


def emit_ab_qkv_post(pr, part, tt, ps_q, ps_tr, qn, ssq, dump, qT, kT, vbuf):
    em = pr.em
    dstT = (qT, kT, None)[part]
    if part < 2:
        em.a("activation", dump[:, 0:D], ps_q[:], AF.Square, bias=0.0, scale=1.0)
        em.v("tensor_reduce",
             ssq[:].rearrange("p (h one) -> p h one", one=1),
             dump[:, 0:D].rearrange("p (h w) -> p h w", w=64),
             AX.X, ALU.add)
        em.v("tensor_scalar", ssq[:], ssq[:], 1e-24, None, ALU.max)
        em.a("activation", ssq[:], ssq[:], AF.Ln, bias=0.0, scale=1.0)
        em.a("activation", ssq[:], ssq[:], AF.Exp, bias=0.0, scale=-0.5)
        em.v("tensor_tensor", qn[:].rearrange("p (h w) -> p h w", w=64),
             ps_q[:].rearrange("p (h w) -> p h w", w=64),
             ssq[:].unsqueeze(2).broadcast_to([128, 16, 64]), ALU.mult)
        for dc in range(NDC):
            em.t("transpose", ps_tr[:, dc * 128:(dc + 1) * 128],
                 qn[:, dc * 128:(dc + 1) * 128], pr.ident[:])
        dst = dstT[:].rearrange("p (dc t) -> p dc t", t=T)[:, :, tt * 128:(tt + 1) * 128]
        em.v("tensor_copy", dst, ps_tr[:].rearrange("p (dc t) -> p dc t", t=128))
    else:
        dst = vbuf[:, tt * 1040:(tt + 1) * 1040].rearrange(
            "p (h w) -> p h w", w=65)[:, :, 0:64]
        em.v("tensor_copy", dst, ps_q[:].rearrange("p (h w) -> p h w", w=64))


def _ab_attention_and_outproj(pr, qT, kT, vbuf, scb, nscb):
    em = pr.em
    nc = pr.nc
    ao = pr.xnT

    def evac(h, qt, ps_block, rden):
        em.v("tensor_scalar", ao[:, qt * D + h * 64:qt * D + (h + 1) * 64],
             ps_block, rden[:], None, ALU.mult)

    emit_attention_core(pr, qT, kT, vbuf, evac, scale_ap16=scb, bias_ap16=nscb)


def _ab_outproj(pr):
    em = pr.em
    nc = pr.nc
    ao = pr.xnT
    with pr.scope() as es3:
        aT = es3.enter_context(nc.sbuf_tensor("ab_aT", [128, NDC * T], BF16))
        ps_tr = es3.enter_context(nc.psum_tensor("ab_pstr2", [128, 1024], BF16))
        ps_o = es3.enter_context(nc.psum_tensor("ab_pso", [128, 1024], F32))
        bout = es3.enter_context(nc.sbuf_tensor("ab_boutb", [1, D], BF16))
        boutf = es3.enter_context(nc.sbuf_tensor("ab_boutf", [1, D], F32))
        pr.es, saved_es = es3, pr.es
        wo = load_weight_bf16(pr, "ab_wob", "ab_wout", D, D)
        pr.es = saved_es
        em.dma(boutf[:], pr.din["ab_bout"])
        em.v("tensor_copy", bout[:], boutf[:])
        for tt in range(NTT):
            for dc in range(NDC):
                em.t("transpose", ps_tr[:, dc * 128:(dc + 1) * 128],
                     ao[:, tt * D + dc * 128:tt * D + (dc + 1) * 128], pr.ident[:])
            em.v("tensor_copy",
                 aT[:].rearrange("p (dc t) -> p dc t", t=T)[:, :, tt * 128:(tt + 1) * 128],
                 ps_tr[:].rearrange("p (dc t) -> p dc t", t=128))
        for tt in range(NTT):
            for os_ in range(2):
                for dc in range(NDC):
                    em.mm(ps_o[:, os_ * 512:(os_ + 1) * 512],
                          aT[:, dc * T + tt * 128:dc * T + (tt + 1) * 128],
                          wo[:, dc * D + os_ * 512:dc * D + (os_ + 1) * 512],
                          start=(dc == 0), stop=False)
                em.mm(ps_o[:, os_ * 512:(os_ + 1) * 512], pr.ones_row_b[:],
                      bout[:, os_ * 512:(os_ + 1) * 512], start=False, stop=True)
            em.v("tensor_tensor", pr.xs[:, tt * D:(tt + 1) * D], ps_o[:],
                 pr.xs[:, tt * D:(tt + 1) * D], ALU.add)



def bcast_1x1_to_col(pr, src11, dst_col, ps512):
    em = pr.em
    em.mm(ps512[:, 0:1], pr.ones_row_f, src11, start=True, stop=True)
    em.v("tensor_copy", dst_col, ps512[:, 0:1])


def bcast_row_to_tile(pr, src_row, dst, ps512, width):
    em = pr.em
    for off in range(0, width, 512):
        w = min(512, width - off)
        em.mm(ps512[:, 0:w], pr.ones_row_f, src_row[:, off:off + w],
              start=True, stop=True)
        em.v("tensor_copy", dst[:, off:off + w], ps512[:, 0:w])


def emit_neground2(pr, v, c_i32, d, r, out_bf, width):
    em = pr.em
    em.v("tensor_copy", c_i32[:, 0:width], v)
    em.v("tensor_sub", d[:, 0:width], v, c_i32[:, 0:width])
    em.v("scalar_tensor_tensor", r[:, 0:width], d[:, 0:width], 0.5,
         c_i32[:, 0:width], ALU.is_gt, ALU.add)
    em.v("scalar_tensor_tensor", out_bf, d[:, 0:width], -0.5, r[:, 0:width],
         ALU.is_lt, ALU.subtract)


def emit_mean_abs(pr, dram_name, rows, cols, out11, stg, abscols, ps512, onescol):
    em = pr.em
    nch = rows // 128
    src = pr.din[dram_name].rearrange("(c p) f -> c p f", p=128)
    for ci in range(nch):
        em.dma(stg[:, 0:cols], src[ci])
        em.a("activation", stg[:, 0:cols], stg[:, 0:cols], AF.Abs, bias=0.0,
             scale=1.0, accum_out=abscols[:, ci:ci + 1])
    em.v("reduce_sum", pr.col1, abscols[:, 0:nch], AX.X)
    em.mm(ps512[0:1, 0:1], pr.col1, onescol, start=True, stop=True)
    em.v("tensor_scalar", out11, ps512[0:1, 0:1], 1.0 / (rows * cols), None,
         ALU.mult)


def emit_l0_ffn(pr):
    em = pr.em
    nc = pr.nc
    H = 4096
    NHC = H // 128
    gdram = nc.dram_tensor(f"f0_g_{pr.uid}", [H, T], BF16, kind="Internal").ap()
    r2dram = nc.dram_tensor(f"f0_r2_{pr.uid}", [H, D], BF16, kind="Internal").ap()
    pr.uid += 1
    with pr.scope() as es2:
        sb = lambda n, s, d: es2.enter_context(nc.sbuf_tensor(n, s, d))
        ps = lambda n, s, d: es2.enter_context(nc.psum_tensor(n, s, d))
        q1T = sb("f0_q1T", [128, NDC * T], BF16)[:]
        pr.xnT = q1T
        onescol = sb("f0_ones", [128, 1], F32)
        g1 = sb("f0_g1", [1, 1], F32)
        g2 = sb("f0_g2", [1, 1], F32)
        t11 = sb("f0_t11", [1, 1], F32)
        thr1p = sb("f0_thr1p", [128, 1], F32)
        thr1n = sb("f0_thr1n", [128, 1], F32)
        thr2p = sb("f0_thr2p", [128, 1], F32)
        thr2n = sb("f0_thr2n", [128, 1], F32)
        geff1 = sb("f0_geff1", [1, 1], F32)
        geff2 = sb("f0_geff2", [1, 1], F32)
        comb1 = sb("f0_comb1", [128, T], F32)
        s2b = sb("f0_s2b", [128, T], F32)
        fin2 = sb("f0_fin2", [128, T], F32)
        rm2cols = sb("f0_rm2c", [128, NTT * NHC], F32)
        ps512 = ps("f0_ps512", [128, 512], F32)

        em.v("memset", onescol[:], 1.0)
        with pr.scope() as esG:
            stgG = esG.enter_context(nc.sbuf_tensor("f0_stgG", [128, 4096], F32))
            abscols = esG.enter_context(nc.sbuf_tensor("f0_absc", [128, 32], F32))
            emit_mean_abs(pr, "f0_w1T", D, H, g1[:], stgG, abscols, ps512, onescol[:])
            emit_mean_abs(pr, "f0_w2T", H, D, g2[:], stgG, abscols, ps512, onescol[:])
        for g, thrp, thrn, geff in ((g1, thr1p, thr1n, geff1),
                                    (g2, thr2p, thr2n, geff2)):
            em.v("tensor_scalar", g[:], g[:], 1e-5, None, ALU.max)
            em.v("reciprocal", t11[:], g[:])
            em.v("reciprocal", geff[:], t11[:])
            em.v("tensor_scalar", t11[:], g[:], 0.5, None, ALU.mult)
            bcast_1x1_to_col(pr, t11[:], thrp[:], ps512)
            em.v("tensor_scalar", thrn[:], thrp[:], -1.0, None, ALU.mult)

        with pr.scope() as es3:
            R = 2
            xn = [es3.enter_context(nc.sbuf_tensor(f"f0_xn{i}", [128, D], F32)) for i in range(R)]
            vq = [es3.enter_context(nc.sbuf_tensor(f"f0_vq{i}", [128, D], F32)) for i in range(R)]
            dq = [es3.enter_context(nc.sbuf_tensor(f"f0_dq{i}", [128, D], F32)) for i in range(R)]
            rq = [es3.enter_context(nc.sbuf_tensor(f"f0_rq{i}", [128, D], F32)) for i in range(R)]
            ci32 = [es3.enter_context(nc.sbuf_tensor(f"f0_ci{i}", [128, D], I32)) for i in range(R)]
            qb = [es3.enter_context(nc.sbuf_tensor(f"f0_qb{i}", [128, D], BF16)) for i in range(R)]
            cA = [es3.enter_context(nc.sbuf_tensor(f"f0_cA{i}", [128, 1], F32))[:] for i in range(R)]
            cB = [es3.enter_context(nc.sbuf_tensor(f"f0_cB{i}", [128, 1], F32))[:] for i in range(R)]
            cC = [es3.enter_context(nc.sbuf_tensor(f"f0_cC{i}", [128, 1], F32))[:] for i in range(R)]
            row1 = es3.enter_context(nc.sbuf_tensor("f0_row1", [1, T], F32))
            ps_tr = [es3.enter_context(nc.psum_tensor(f"f0_pstr{i}", [128, NDC * 128], BF16)) for i in range(R)]
            ps_th = [es3.enter_context(nc.psum_tensor(f"f0_psth{i}", [128, 512], F32)) for i in range(R)]
            for tt in range(NTT):
                i = tt % R
                emit_ln_tile_f32(pr, tt, xn[i], dq[i], cols=(cA[i], cB[i], cC[i]))
                em.v("reduce_max", cA[i], xn[i][:], AX.X, apply_absolute_value=True)
                em.v("tensor_scalar", cA[i], cA[i], 1e-5, None, ALU.max)
                em.v("reciprocal", cB[i], cA[i])
                em.v("tensor_scalar", cB[i], cB[i], 127.0, None, ALU.mult)
                em.v("tensor_scalar", vq[i][:], xn[i][:], cB[i], None, ALU.mult)
                emit_neground2(pr, vq[i][:], ci32[i], dq[i], rq[i], qb[i][:], D)
                for dc in range(NDC):
                    em.t("transpose", ps_tr[i][:, dc * 128:(dc + 1) * 128],
                         qb[i][:, dc * 128:(dc + 1) * 128], pr.ident[:])
                em.v("tensor_copy",
                     q1T[:].rearrange("p (dc t) -> p dc t", t=T)[:, :, tt * 128:(tt + 1) * 128],
                     ps_tr[i][:].rearrange("p (dc t) -> p dc t", t=128))
                em.v("tensor_scalar", cA[i], cA[i], 1.0 / 127.0, None, ALU.mult)
                em.t("transpose", ps_th[i][0:1, 0:128], cA[i], pr.ident128f)
                em.v("tensor_copy", row1[:, tt * 128:(tt + 1) * 128], ps_th[i][0:1, 0:128])
            em.v("tensor_scalar", row1[:], row1[:], geff1[:], None, ALU.mult)
            bcast_row_to_tile(pr, row1, comb1, ps512, T)

        with pr.scope() as es3:
            R = 2
            stg = [es3.enter_context(nc.sbuf_tensor(f"f0_stgA{i}", [128, 4096], F32)) for i in range(R)]
            r1sl = [es3.enter_context(nc.sbuf_tensor(f"f0_r1sl{i}", [128, NDC * 128], BF16)) for i in range(R)]
            r2sl = [es3.enter_context(nc.sbuf_tensor(f"f0_r2sl{i}", [128, D], BF16)) for i in range(R)]
            hs = [es3.enter_context(nc.sbuf_tensor(f"f0_hs{i}", [128, T], F32)) for i in range(R)]
            acc = [es3.enter_context(nc.sbuf_tensor(f"f0_acc{i}", [128, T], F32)) for i in range(R)]
            gch = [es3.enter_context(nc.sbuf_tensor(f"f0_gch{i}", [128, T], BF16)) for i in range(R)]
            cw = [es3.enter_context(nc.sbuf_tensor(f"f0_cw{i}", [128, 3], F32)) for i in range(R)]
            cb = [es3.enter_context(nc.sbuf_tensor(f"f0_cb{i}", [128, 1], F32)) for i in range(R)]
            ps_h = [es3.enter_context(nc.psum_tensor("f0_psh", [128, 2048], F32))] * R
            ps_t2 = [es3.enter_context(nc.psum_tensor("f0_pst2", [128, 2048], BF16))] * R
            w1src = pr.din["f0_w1T"].rearrange("(dc p) f -> dc p f", p=128)
            w2src = pr.din["f0_w2T"].rearrange("(hc p) f -> hc p f", p=128)
            cwsrc = pr.din["f0_convw"]
            cbsrc = pr.din["f0_convb"]
            for hc in range(NHC):
                i_ = hc % R
                stg_, r1sl_, r2sl_, hs_, acc_, gch_, cw_, cb_ = (
                    stg[i_], r1sl[i_], r2sl[i_], hs[i_], acc[i_], gch[i_], cw[i_], cb[i_])
                ps_h_, ps_t2_ = ps_h[i_], ps_t2[i_]
                for dc in range(NDC):
                    em.dma(stg_[:, dc * 128:(dc + 1) * 128],
                           w1src[dc, :, hc * 128:(hc + 1) * 128])
                em.v("tensor_scalar", stg_[:, 1024:2048], stg_[:, 0:1024], thr1p[:],
                     None, ALU.is_gt)
                em.v("scalar_tensor_tensor", r1sl_[:], stg_[:, 0:1024], thr1n[:],
                     stg_[:, 1024:2048], ALU.is_lt, ALU.subtract)
                em.dma(stg_[:, 2048:3072], w2src[hc])
                em.v("tensor_scalar", stg_[:, 3072:4096], stg_[:, 2048:3072], thr2p[:],
                     None, ALU.is_gt)
                em.v("scalar_tensor_tensor", r2sl_[:], stg_[:, 2048:3072], thr2n[:],
                     stg_[:, 3072:4096], ALU.is_lt, ALU.subtract)
                em.dma(r2dram.rearrange("(hc p) f -> hc p f", p=128)[hc], r2sl_[:])
                for ts4 in range(4):
                    for dc in range(NDC):
                        em.mm(ps_h_[:, ts4 * 512:(ts4 + 1) * 512],
                              r1sl_[:, dc * 128:(dc + 1) * 128],
                              q1T[:, dc * T + ts4 * 512:dc * T + (ts4 + 1) * 512],
                              start=(dc == 0), stop=(dc == NDC - 1))
                em.v("tensor_tensor", hs_[:], ps_h_[:], comb1[:], ALU.mult)
                em.dma(cw_[:], cwsrc[hc * 128:(hc + 1) * 128, :])
                em.dma(cb_[:], cbsrc[hc * 128:(hc + 1) * 128, :])
                em.v("tensor_scalar", acc_[:, 1:T], hs_[:, 0:T - 1], cw_[:, 0:1],
                     cb_[:], ALU.mult, ALU.add)
                em.v("tensor_copy", acc_[:, 0:1], cb_[:])
                em.v("scalar_tensor_tensor", acc_[:], hs_[:], cw_[:, 1:2], acc_[:],
                     ALU.mult, ALU.add)
                em.v("scalar_tensor_tensor", acc_[:, 0:T - 1], hs_[:, 1:T], cw_[:, 2:3],
                     acc_[:, 0:T - 1], ALU.mult, ALU.add)
                em.a("activation", gch_[:], acc_[:], AF.Silu, bias=0.0, scale=1.0)
                em.dma(gdram.rearrange("(hc p) f -> hc p f", p=128)[hc], gch_[:])
                for i in range(NTT):
                    em.t("transpose", ps_t2_[:, (i % 8) * 128 + (i // 8) * 1024:
                                             (i % 8) * 128 + (i // 8) * 1024 + 128],
                         gch_[:, i * 128:(i + 1) * 128], pr.ident[:])
                for i in range(NTT):
                    em.v("reduce_max", rm2cols[:, i * NHC + hc:i * NHC + hc + 1],
                         ps_t2_[:, (i % 8) * 128 + (i // 8) * 1024:
                                  (i % 8) * 128 + (i // 8) * 1024 + 128],
                         AX.X, apply_absolute_value=True)

        with pr.scope() as es3:
            ps_th = es3.enter_context(nc.psum_tensor("f0_psth2", [128, 512], F32))
            row2 = es3.enter_context(nc.sbuf_tensor("f0_row2", [1, T], F32))
            row3 = es3.enter_context(nc.sbuf_tensor("f0_row3", [1, T], F32))
            for tt in range(NTT):
                em.v("reduce_max", pr.col1, rm2cols[:, tt * NHC:(tt + 1) * NHC], AX.X)
                em.v("tensor_scalar", pr.col1, pr.col1, 1e-5, None, ALU.max)
                em.v("reciprocal", pr.col2, pr.col1)
                em.v("tensor_scalar", pr.col2, pr.col2, 127.0, None, ALU.mult)
                em.v("tensor_scalar", pr.col1, pr.col1, 1.0 / 127.0, None, ALU.mult)
                em.t("transpose", ps_th[0:1, 0:128], pr.col1, pr.ident128f)
                em.v("tensor_copy", row2[:, tt * 128:(tt + 1) * 128], ps_th[0:1, 0:128])
                em.t("transpose", ps_th[0:1, 128:256], pr.col2, pr.ident128f)
                em.v("tensor_copy", row3[:, tt * 128:(tt + 1) * 128], ps_th[0:1, 128:256])
            em.v("tensor_scalar", row2[:], row2[:], geff2[:], None, ALU.mult)
            bcast_row_to_tile(pr, row3, s2b, ps512, T)
            bcast_row_to_tile(pr, row2, fin2, ps512, T)

        with pr.scope() as es3:
            R = 3
            gsl = [es3.enter_context(nc.sbuf_tensor(f"f0_gsl{i}", [128, 512], BF16)) for i in range(R)]
            vq = [es3.enter_context(nc.sbuf_tensor(f"f0_vq2{i}", [128, 512], F32)) for i in range(R)]
            dq = [es3.enter_context(nc.sbuf_tensor(f"f0_dq2{i}", [128, 512], F32)) for i in range(R)]
            rq = [es3.enter_context(nc.sbuf_tensor(f"f0_rq2{i}", [128, 512], F32)) for i in range(R)]
            ci32 = [es3.enter_context(nc.sbuf_tensor(f"f0_ci2{i}", [128, 512], I32)) for i in range(R)]
            r2b = [es3.enter_context(nc.sbuf_tensor(f"f0_r2b{i}", [128, D], BF16)) for i in range(2)]
            q2buf = es3.enter_context(nc.sbuf_tensor("f0_q2b", [128, 32 * 512], BF16))
            fout = es3.enter_context(nc.sbuf_tensor("f0_fout", [128, NDC * 512], BF16))
            ps_o = es3.enter_context(nc.psum_tensor("f0_pso", [128, 2048], F32))
            ps_tb = [es3.enter_context(nc.psum_tensor(f"f0_pstb{i}", [128, 1024], BF16)) for i in range(2)]
            gsrc = gdram.rearrange("(hc p) f -> hc p f", p=128)
            r2src = r2dram.rearrange("(hc p) f -> hc p f", p=128)
            for tb in range(4):
                for hc in range(NHC):
                    i_ = hc % R
                    em.dma(gsl[i_][:], gsrc[hc, :, tb * 512:(tb + 1) * 512])
                    em.v("tensor_tensor", vq[i_][:], gsl[i_][:],
                         s2b[:, tb * 512:(tb + 1) * 512], ALU.mult)
                    emit_neground2(pr, vq[i_][:], ci32[i_], dq[i_], rq[i_],
                                   q2buf[:, hc * 512:(hc + 1) * 512], 512)
                for grp in range(2):
                    for hc in range(NHC):
                        em.dma(r2b[hc % 2][:], r2src[hc])
                        for dmi in range(4):
                            dmc = grp * 4 + dmi
                            em.mm(ps_o[:, dmi * 512:(dmi + 1) * 512],
                                  r2b[hc % 2][:, dmc * 128:(dmc + 1) * 128],
                                  q2buf[:, hc * 512:(hc + 1) * 512],
                                  start=(hc == 0), stop=(hc == NHC - 1))
                    for dmi in range(4):
                        dmc = grp * 4 + dmi
                        em.v("tensor_tensor", fout[:, dmc * 512:(dmc + 1) * 512],
                             ps_o[:, dmi * 512:(dmi + 1) * 512],
                             fin2[:, tb * 512:(tb + 1) * 512], ALU.mult)
                for dmc in range(NDC):
                    pstb_ = ps_tb[dmc % 2]
                    for i in range(4):
                        em.t("transpose", pstb_[:, (i % 4) * 256:(i % 4) * 256 + 128],
                             fout[:, dmc * 512 + i * 128:dmc * 512 + (i + 1) * 128],
                             pr.ident[:])
                    for i in range(4):
                        tt = tb * 4 + i
                        em.v("tensor_tensor",
                             pr.xs[:, tt * D + dmc * 128:tt * D + (dmc + 1) * 128],
                             pstb_[:, (i % 4) * 256:(i % 4) * 256 + 128],
                             pr.xs[:, tt * D + dmc * 128:tt * D + (dmc + 1) * 128],
                             ALU.add)


def emit_ln_tile_f32(pr, tt, xn_f32, dump, cols=None):
    em = pr.em
    c1, c2, c3 = cols if cols is not None else (pr.col1, pr.col2, pr.col3)
    xst = pr.xs[:, tt * D:(tt + 1) * D]
    em.v("reduce_sum", c1, xst, AX.X)
    em.v("tensor_scalar", c1, c1, -1.0 / D, None, ALU.mult)
    em.a("activation", dump[:, 0:D], xst, AF.Square, bias=c1, scale=1.0,
         accum_out=c2)
    em.v("tensor_scalar", c2, c2, 1.0 / D, 1e-5, ALU.mult, ALU.add)
    em.a("activation", c3, c2, AF.Ln, bias=0.0, scale=1.0)
    em.a("activation", c3, c3, AF.Exp, bias=0.0, scale=-0.5)
    em.v("tensor_scalar", xn_f32[:], xst, c1, c3, ALU.add, ALU.mult)



def emit_l1_ffn(pr):
    em = pr.em
    nc = pr.nc
    H = 4096
    with pr.scope() as es2:
        sb = lambda n, s, d: es2.enter_context(nc.sbuf_tensor(n, s, d))
        pr.xnT = sb("f1_xnT", [128, NDC * T], BF16)[:]
        emit_ln_all(pr)
        onescol = sb("f1_ones", [128, 1], F32)
        g1 = sb("f1_g1", [1, 1], F32)
        g2 = sb("f1_g2", [1, 1], F32)
        t11 = sb("f1_t11", [1, 1], F32)
        thr1p = sb("f1_thr1p", [128, 1], F32)
        thr1n = sb("f1_thr1n", [128, 1], F32)
        thr2p = sb("f1_thr2p", [128, 1], F32)
        thr2n = sb("f1_thr2n", [128, 1], F32)
        em.v("memset", onescol[:], 1.0)
        with pr.scope() as esG:
            ps512 = esG.enter_context(nc.psum_tensor("f1_ps512", [128, 512], F32))
            stgG = esG.enter_context(nc.sbuf_tensor("f1_stgG", [128, 4096], F32))
            abscols = esG.enter_context(nc.sbuf_tensor("f1_absc", [128, 32], F32))
            emit_mean_abs(pr, "f1_w1T", D, H, g1[:], stgG, abscols, ps512, onescol[:])
            emit_mean_abs(pr, "f1_w2T", H, D, g2[:], stgG, abscols, ps512, onescol[:])
            for g, thrp, thrn in ((g1, thr1p, thr1n), (g2, thr2p, thr2n)):
                em.v("tensor_scalar", t11[:], g[:], 1e-5, 0.5, ALU.add, ALU.mult)
                bcast_1x1_to_col(pr, t11[:], thrp[:], ps512)
                em.v("tensor_scalar", thrn[:], thrp[:], -1.0, None, ALU.mult)

        w1src = pr.din["f1_w1T"].rearrange("(dc p) f -> dc p f", p=128)
        w2src = pr.din["f1_w2T"].rearrange("(hc p) f -> hc p f", p=128)
        with pr.scope() as es3:
            r1h = es3.enter_context(nc.sbuf_tensor("f1_r1h", [128, NDC * 2048], BF16))
            r2h = es3.enter_context(nc.sbuf_tensor("f1_r2h", [128, 16 * D], BF16))
            stgq = [es3.enter_context(nc.sbuf_tensor(f"f1_stgq{i}", [128, 2048], F32))
                    for i in range(2)]
            b1rf = es3.enter_context(nc.sbuf_tensor("f1_b1rf", [1, 512], F32))
            b1rb = es3.enter_context(nc.sbuf_tensor("f1_b1rb", [1, 2048], BF16))
            b2rf = es3.enter_context(nc.sbuf_tensor("f1_b2rf", [1, D], F32))
            b2rb = es3.enter_context(nc.sbuf_tensor("f1_b2rb", [1, D], BF16))
            hneg = [es3.enter_context(nc.sbuf_tensor(f"f1_hneg{i}", [128, 2048], BF16))
                    for i in range(2)]
            hT = [es3.enter_context(nc.sbuf_tensor(f"f1_hT{i}", [128, 2048], BF16))
                  for i in range(2)]
            ps_h = es3.enter_context(nc.psum_tensor("f1_psh", [128, 2048], F32))
            ps_tr = es3.enter_context(nc.psum_tensor("f1_pstr", [128, 1024], BF16))
            ps_o = es3.enter_context(nc.psum_tensor("f1_pso", [128, 1024], F32))
            em.dma(b2rf[:], pr.din["f1_b2"])
            em.v("tensor_copy", b2rb[:], b2rf[:])
            for half in range(2):
                h0 = half * 2048
                for dc in range(NDC):
                    for piece in range(2):
                        sq = stgq[(dc * 2 + piece) % 2]
                        sl = sq[:, 0:1024]
                        em.dma(sl, w1src[dc, :, h0 + piece * 1024:h0 + (piece + 1) * 1024])
                        em.v("tensor_scalar", sq[:, 1024:2048], sl, thr1p[:],
                             None, ALU.is_gt)
                        em.v("scalar_tensor_tensor",
                             r1h[:, dc * 2048 + piece * 1024:dc * 2048 + (piece + 1) * 1024],
                             sl, thr1n[:], sq[:, 1024:2048], ALU.is_lt, ALU.subtract)
                for hc in range(16):
                    sq = stgq[hc % 2]
                    sl = sq[:, 0:1024]
                    em.dma(sl, w2src[half * 16 + hc])
                    em.v("tensor_scalar", sq[:, 1024:2048], sl, thr2p[:],
                         None, ALU.is_gt)
                    em.v("scalar_tensor_tensor", r2h[:, hc * D:(hc + 1) * D],
                         sl, thr2n[:], sq[:, 1024:2048], ALU.is_lt, ALU.subtract)
                for pc in range(4):
                    em.dma(b1rf[:], pr.din["f1_b1"][:, h0 + pc * 512:h0 + (pc + 1) * 512])
                    em.v("tensor_scalar", b1rb[:, pc * 512:(pc + 1) * 512], b1rf[:],
                         -1.0, None, ALU.mult)
                for tt in range(NTT):
                    for hs4 in range(4):
                        for dc in range(NDC):
                            em.mm(ps_h[:, hs4 * 512:(hs4 + 1) * 512],
                                  pr.xnT[:, dc * T + tt * 128:dc * T + (tt + 1) * 128],
                                  r1h[:, dc * 2048 + hs4 * 512:dc * 2048 + (hs4 + 1) * 512],
                                  start=(dc == 0), stop=False)
                        em.mm(ps_h[:, hs4 * 512:(hs4 + 1) * 512], pr.ones_row_b,
                              b1rb[:, hs4 * 512:(hs4 + 1) * 512], start=False, stop=True)
                    hn, hT_ = hneg[tt % 2], hT[tt % 2]
                    em.v("tensor_scalar", hn[:], ps_h[:], 0.0, None, ALU.min)
                    for grp8 in range(2):
                        for i8 in range(8):
                            i = grp8 * 8 + i8
                            em.t("transpose", ps_tr[:, i8 * 128:(i8 + 1) * 128],
                                 hn[:, i * 128:(i + 1) * 128], pr.ident[:])
                        em.v("tensor_copy", hT_[:, grp8 * 1024:(grp8 + 1) * 1024], ps_tr[:])
                    for os_ in range(2):
                        for hc in range(16):
                            em.mm(ps_o[:, os_ * 512:(os_ + 1) * 512],
                                  hT_[:, hc * 128:(hc + 1) * 128],
                                  r2h[:, hc * D + os_ * 512:hc * D + (os_ + 1) * 512],
                                  start=(hc == 0), stop=(half == 1 and hc == 15))
                        if half == 0:
                            em.mm(ps_o[:, os_ * 512:(os_ + 1) * 512], pr.ones_row_b,
                                  b2rb[:, os_ * 512:(os_ + 1) * 512], start=False,
                                  stop=True)
                    em.v("tensor_tensor", pr.xs[:, tt * D:(tt + 1) * D], ps_o[:],
                         pr.xs[:, tt * D:(tt + 1) * D], ALU.add)



import os
_KPHASE = int(os.environ.get("KPHASE", "9"))


def emit_l1_attn_b(pr):
    em = pr.em
    nc = pr.nc
    with pr.scope() as es2:
        sb = lambda n, s, d: es2.enter_context(nc.sbuf_tensor(n, s, d))
        pr.xnT = sb("db_xnT", [128, NDC * T], BF16)[:]
        negm16 = sb("db_negm", [128, 16], F32)
        rstd16 = sb("db_rstd", [128, 16], F32)
        emit_ln_all(pr, stats=(negm16, rstd16))
        gated = sb("db_gated", [128, NDC * T], BF16)
        dec8 = sb("db_dec8", [128, 8], F32)
        nwb = sb("db_nwb", [128, D], F32)
        em.dma(dec8[:], pr.din["db_logdec"])
        em.a("activation", dec8[:], dec8[:], AF.Exp, bias=0.0, scale=1.0)
        with pr.scope() as es3:
            nwrow = es3.enter_context(nc.sbuf_tensor("db_nwrow", [1, D], F32))
            ps512 = es3.enter_context(nc.psum_tensor("db_ps512", [128, 512], F32))
            em.dma(nwrow[:], pr.din["db_normw"])
            bcast_row_to_tile(pr, nwrow, nwb, ps512, D)
        if _KPHASE < 1:
            return
        with pr.scope() as es3:
            acc1 = es3.enter_context(nc.sbuf_tensor("db_acc1", [128, T], F32))
            acc2 = es3.enter_context(nc.sbuf_tensor("db_acc2", [128, T], F32))
            gs = es3.enter_context(nc.sbuf_tensor("db_gs", [128, T], BF16))
            cs = es3.enter_context(nc.sbuf_tensor("db_cs", [128, T], BF16))
            cw7 = es3.enter_context(nc.sbuf_tensor("db_cw7", [128, 7], F32))
            cb1 = es3.enter_context(nc.sbuf_tensor("db_cb1", [128, 1], F32))
            decb = es3.enter_context(nc.sbuf_tensor("db_decb", [128, T], F32))
            for c in range(NDC):
                xc = pr.xnT[:, c * T:(c + 1) * T]
                for (wname, bname, acc, outb, fn) in (
                        ("db_gatew", "db_gateb", acc1, gs, AF.Sigmoid),
                        ("db_candw", "db_candb", acc2, cs, AF.Tanh)):
                    em.dma(cw7[:], pr.din[wname][c * 128:(c + 1) * 128, :])
                    em.dma(cb1[:], pr.din[bname][c * 128:(c + 1) * 128, :])
                    em.v("tensor_scalar", acc[:], xc, cw7[:, 3:4], cb1[:],
                         ALU.mult, ALU.add)
                    for k in (0, 1, 2, 4, 5, 6):
                        sh = k - 3
                        if sh < 0:
                            em.v("scalar_tensor_tensor", acc[:, -sh:T],
                                 xc[:, 0:T + sh], cw7[:, k:k + 1],
                                 acc[:, -sh:T], ALU.mult, ALU.add)
                        else:
                            em.v("scalar_tensor_tensor", acc[:, 0:T - sh],
                                 xc[:, sh:T], cw7[:, k:k + 1],
                                 acc[:, 0:T - sh], ALU.mult, ALU.add)
                    em.a("activation", outb[:], acc[:], fn, bias=0.0, scale=1.0)
                gc = gated[:, c * T:(c + 1) * T]
                em.v("tensor_tensor", gc, gs[:], cs[:], ALU.mult)
                if _KPHASE >= 2:
                    em.v("tensor_scalar", decb[:], acc1[:], 0.0, dec8[:, c:c + 1],
                         ALU.mult, ALU.add)
                    em.v("tensor_tensor_scan", gc, decb[:], gc, 0.0, ALU.mult, ALU.add)
        if _KPHASE < 3:
            return
        with pr.scope() as es3:
            pr.es, saved_es = es3, pr.es
            wo = load_weight_bf16(pr, "db_wob", "db_woutT", D, D)
            pr.es = saved_es
            boutf = es3.enter_context(nc.sbuf_tensor("db_boutf", [1, D], F32))
            boutb = es3.enter_context(nc.sbuf_tensor("db_boutb", [1, D], BF16))
            yb = es3.enter_context(nc.sbuf_tensor("db_yb", [128, D], F32))
            u = es3.enter_context(nc.sbuf_tensor("db_u", [128, D], F32))
            dump = es3.enter_context(nc.sbuf_tensor("db_dump", [128, D], F32))
            ps_o = es3.enter_context(nc.psum_tensor("db_pso", [128, 1024], F32))
            em.dma(boutf[:], pr.din["db_bout"])
            em.v("tensor_copy", boutb[:], boutf[:])
            for tt in range(NTT):
                for os_ in range(2):
                    for c in range(NDC):
                        em.mm(ps_o[:, os_ * 512:(os_ + 1) * 512],
                              gated[:, c * T + tt * 128:c * T + (tt + 1) * 128],
                              wo[:, c * D + os_ * 512:c * D + (os_ + 1) * 512],
                              start=(c == 0), stop=False)
                    em.mm(ps_o[:, os_ * 512:(os_ + 1) * 512], pr.ones_row_b,
                          boutb[:, os_ * 512:(os_ + 1) * 512], start=False, stop=True)
                em.v("tensor_scalar", u[:], pr.xs[:, tt * D:(tt + 1) * D],
                     negm16[:, tt:tt + 1], rstd16[:, tt:tt + 1], ALU.add, ALU.mult)
                em.v("tensor_tensor", yb[:], ps_o[:], u[:], ALU.add)
                em.a("activation", dump[:], yb[:], AF.Square, bias=0.0, scale=1.0,
                     accum_out=pr.col2)
                em.v("tensor_scalar", pr.col2, pr.col2, 1.0 / D, 1.1920929e-07,
                     ALU.mult, ALU.add)
                em.a("activation", pr.col3, pr.col2, AF.Ln, bias=0.0, scale=1.0)
                em.a("activation", pr.col3, pr.col3, AF.Exp, bias=0.0, scale=-0.5)
                em.v("tensor_tensor", u[:], yb[:], nwb[:], ALU.mult)
                em.v("scalar_tensor_tensor", pr.xs[:, tt * D:(tt + 1) * D],
                     u[:], pr.col3, pr.xs[:, tt * D:(tt + 1) * D],
                     ALU.mult, ALU.add)



def emit_l1_attn_a(pr):
    em = pr.em
    nc = pr.nc
    DI = 2048
    NH = 32
    PP = 64
    zdram = nc.dram_tensor(f"sa_z_{pr.uid}", [T, DI], BF16, kind="Internal").ap()
    ydram = nc.dram_tensor(f"sa_y_{pr.uid}", [T, DI], BF16, kind="Internal").ap()
    xdram = nc.dram_tensor(f"sa_x_{pr.uid}", [DI, T], BF16, kind="Internal").ap()
    lscr = nc.dram_tensor(f"sa_l_{pr.uid}", [NH, 128], F32, kind="Internal").ap()
    pr.uid += 1
    with pr.scope() as es2:
        sb = lambda n, s, d: es2.enter_context(nc.sbuf_tensor(n, s, d))
        bc_sb = sb("sa_bc", [128, 2 * T], BF16)
        lam = sb("sa_lam", [32, T], F32)
        U = sb("sa_U", [32, T], F32)
        acol = sb("sa_acol", [32, 1], F32)
        dtb = sb("sa_dtbc", [32, 1], F32)
        ones32 = sb("sa_ones32", [32, 128], F32)
        em.v("memset", ones32[:], 1.0)
        em.dma(dtb[:], pr.din["sa_dtb"])
        em.dma(acol[:], pr.din["sa_alog"])
        em.a("activation", acol[:], acol[:], AF.Exp, bias=0.0, scale=1.0)
        em.v("tensor_scalar", acol[:], acol[:], -1.0, None, ALU.mult)

        with pr.scope() as es3:
            sb3 = lambda n, s, d: es3.enter_context(nc.sbuf_tensor(n, s, d))
            xbc = sb3("sa_xbc", [128, 16 * T], BF16)
            dtR = sb3("sa_dt", [32, T], F32)
            with pr.scope() as es4:
                xnT = es4.enter_context(nc.sbuf_tensor("sa_xnT", [128, NDC * T], BF16))
                pr.xnT = xnT[:]
                emit_ln_all(pr)
                psA = es4.enter_context(nc.psum_tensor("sa_psA", [128, 2048], F32))
                wstg = es4.enter_context(nc.sbuf_tensor("sa_wstg", [128, 512], F32))
                wslb = es4.enter_context(nc.sbuf_tensor("sa_wslb", [128, 512], BF16))
                zb = es4.enter_context(nc.sbuf_tensor("sa_zb", [128, 1024], BF16))
                wsrc = pr.din["sa_winT"].rearrange("(dc p) f -> dc p f", p=128)
                for t0 in range(0, NTT, 2):
                    for zh in range(2):
                        for zs in range(2):
                            for dc in range(NDC):
                                em.dma(wstg[:], wsrc[dc, :, zh * 1024 + zs * 512:
                                                       zh * 1024 + (zs + 1) * 512])
                                em.v("tensor_copy", wslb[:], wstg[:])
                                for i in range(2):
                                    em.mm(psA[:, i * 1024 + zs * 512:
                                              i * 1024 + (zs + 1) * 512],
                                          pr.xnT[:, dc * T + (t0 + i) * 128:
                                                 dc * T + (t0 + i + 1) * 128],
                                          wslb[:], start=(dc == 0), stop=(dc == NDC - 1))
                        for i in range(2):
                            em.v("tensor_copy", zb[:], psA[:, i * 1024:(i + 1) * 1024])
                            em.dma(zdram.rearrange("(tt p) f -> tt p f", p=128)
                                   [t0 + i, :, zh * 1024:(zh + 1) * 1024], zb[:])
                for fc in range(18):
                    for dc in range(NDC):
                        em.dma(wstg[:, 0:128], wsrc[dc, :, DI + fc * 128:DI + (fc + 1) * 128])
                        em.v("tensor_copy", wslb[:, 0:128], wstg[:, 0:128])
                        for ts4 in range(4):
                            em.mm(psA[:, ts4 * 512:(ts4 + 1) * 512],
                                  wslb[:, 0:128],
                                  pr.xnT[:, dc * T + ts4 * 512:dc * T + (ts4 + 1) * 512],
                                  start=(dc == 0), stop=(dc == NDC - 1))
                    dst = (xbc[:, fc * T:(fc + 1) * T] if fc < 16
                           else bc_sb[:, (fc - 16) * T:(fc - 15) * T])
                    em.v("tensor_copy", dst, psA[:])
                for dc in range(NDC):
                    em.dma(wstg[:, 0:32], wsrc[dc, :, DI + 2304:DI + 2304 + 32])
                    em.v("tensor_copy", wslb[:, 0:32], wstg[:, 0:32])
                    for ts4 in range(4):
                        em.mm(psA[0:32, ts4 * 512:(ts4 + 1) * 512],
                              wslb[:, 0:32],
                              pr.xnT[:, dc * T + ts4 * 512:dc * T + (ts4 + 1) * 512],
                              start=(dc == 0), stop=(dc == NDC - 1))
                em.v("tensor_scalar", dtR[:], psA[0:32, :], dtb[:], None, ALU.add)

            with pr.scope() as es4:
                accC = es4.enter_context(nc.sbuf_tensor("sa_accC", [128, T], F32))
                cw4 = es4.enter_context(nc.sbuf_tensor("sa_cw4", [128, 4], F32))
                cb1 = es4.enter_context(nc.sbuf_tensor("sa_cb1", [128, 1], F32))
                for fc in range(18):
                    xc = (xbc[:, fc * T:(fc + 1) * T] if fc < 16
                          else bc_sb[:, (fc - 16) * T:(fc - 15) * T])
                    em.dma(cw4[:], pr.din["sa_convw"][fc * 128:(fc + 1) * 128, :])
                    em.dma(cb1[:], pr.din["sa_convb"][fc * 128:(fc + 1) * 128, :])
                    em.v("tensor_scalar", accC[:], xc, cw4[:, 3:4], cb1[:],
                         ALU.mult, ALU.add)
                    for k in (0, 1, 2):
                        sh = 3 - k
                        em.v("scalar_tensor_tensor", accC[:, sh:T], xc[:, 0:T - sh],
                             cw4[:, k:k + 1], accC[:, sh:T], ALU.mult, ALU.add)
                    if fc < 16:
                        em.a("activation", xc, accC[:], AF.Silu, bias=0.0, scale=1.0)
                        em.dma(xdram.rearrange("(fc p) t -> fc p t", p=128)[fc], xc)
                    else:
                        dst = bc_sb[:, (fc - 16) * T:(fc - 15) * T]
                        em.a("activation", dst, accC[:], AF.Silu, bias=0.0, scale=1.0)

            with pr.scope() as es4:
                t32 = es4.enter_context(nc.sbuf_tensor("sa_t32", [32, T], F32))
                em.a("activation", t32[:], dtR[:], AF.Abs, bias=0.0, scale=1.0)
                em.a("activation", t32[:], t32[:], AF.Exp, bias=0.0, scale=-1.0)
                em.a("activation", t32[:], t32[:], AF.Ln, bias=1.0, scale=1.0)
                em.v("scalar_tensor_tensor", dtR[:], dtR[:], 0.0, t32[:],
                     ALU.max, ALU.add)
                em.v("tensor_scalar", t32[:], dtR[:], 1e-38, None, ALU.max)
                em.a("activation", U[:], t32[:], AF.Ln, bias=0.0, scale=1.0)
                em.v("tensor_scalar", t32[:], dtR[:], acol[:], None, ALU.mult)
                for c in range(NTT):
                    em.v("tensor_tensor_scan", lam[:, c * 128:(c + 1) * 128],
                         ones32[:], t32[:, c * 128:(c + 1) * 128], 0.0,
                         ALU.mult, ALU.add)
                em.v("tensor_sub", U[:], U[:], lam[:])

        with pr.scope() as es3:
            sb3 = lambda n, s, d: es3.enter_context(nc.sbuf_tensor(n, s, d))
            H = sb3("sa_H", [128, DI], F32)
            Hb = sb3("sa_Hb", [128, DI], BF16)
            Mw = sb3("sa_Mw", [128, NH * 128], BF16)
            lamM = sb3("sa_lamM", [128, NH * 128], F32)
            Xstage = sb3("sa_Xstage", [128, DI], BF16)
            Xtok = sb3("sa_Xtok", [128, DI], BF16)
            Xw = sb3("sa_Xw", [128, DI], BF16)
            Btok = sb3("sa_Btok", [128, 128], BF16)
            Gb = sb3("sa_Gb", [128, 128], BF16)
            Utok = sb3("sa_Utok", [128, 32], F32)
            Ltok = sb3("sa_Ltok", [128, 32], F32)
            lend = sb3("sa_lend", [128, 32], F32)
            wS = sb3("sa_wS", [128, 32], F32)
            gam = sb3("sa_gam", [128, 32], F32)
            expl = sb3("sa_expl", [128, 32], F32)
            Db32 = sb3("sa_Db32", [128, 32], F32)
            dcol = sb3("sa_dcol", [32, 1], F32)
            ycbf = sb3("sa_ycbf", [128, DI], BF16)
            lrow = sb3("sa_lrow", [1, 2048], F32)
            r11 = sb3("sa_r11", [1, 32], F32)
            psS = es3.enter_context(nc.psum_tensor("sa_psYS", [128, 2048], F32))
            psY = psS
            psT = es3.enter_context(nc.psum_tensor("sa_psT", [128, 2048], BF16))
            psF = es3.enter_context(nc.psum_tensor("sa_psF", [128, 512], F32))
            em.v("memset", H[:], 0.0)
            em.v("memset", Hb[:], 0.0)
            em.dma(dcol[:], pr.din["sa_d"])
            em.t("transpose", psF[0:1, 0:32], dcol[:], pr.ident32f)
            em.v("tensor_copy", r11[:], psF[0:1, 0:32])
            em.mm(psF[:, 32:64], pr.ones_row_f, r11[:], start=True, stop=True)
            em.v("tensor_copy", Db32[:], psF[:, 32:64])
            lam3 = lamM[:].rearrange("p (h t) -> p h t", t=128)
            Mw3 = Mw[:].rearrange("p (h t) -> p h t", t=128)
            for c in range(NTT):
                cs_ = slice(c * 128, (c + 1) * 128)
                em.t("transpose", psF[:, 32:64], U[:, cs_], pr.ident32f)
                em.v("tensor_copy", Utok[:], psF[:, 32:64])
                em.t("transpose", psF[:, 64:96], lam[:, cs_], pr.ident32f)
                em.v("tensor_copy", Ltok[:], psF[:, 64:96])
                em.t("transpose", psF[0:1, 96:128],
                     lam[:, c * 128 + 127:c * 128 + 128], pr.ident32f)
                em.v("tensor_copy", r11[:], psF[0:1, 96:128])
                em.mm(psF[:, 128:160], pr.ones_row_f, r11[:], start=True, stop=True)
                em.v("tensor_copy", lend[:], psF[:, 128:160])
                em.v("tensor_tensor", wS[:], Utok[:], lend[:], ALU.add)
                em.a("activation", wS[:], wS[:], AF.Exp, bias=0.0, scale=1.0)
                em.a("activation", gam[:], lend[:], AF.Exp, bias=0.0, scale=1.0)
                em.a("activation", expl[:], Ltok[:], AF.Exp, bias=0.0, scale=1.0)
                em.dma(lscr, lam[:, cs_])
                for half in range(2):
                    em.dma(lrow[:], lscr.rearrange("p t -> (p t)").rearrange(
                        "(one f) -> one f", one=1)[:, half * 2048:(half + 1) * 2048])
                    for q4 in range(4):
                        em.mm(psS[:, q4 * 512:(q4 + 1) * 512], pr.ones_row_f,
                              lrow[:, q4 * 512:(q4 + 1) * 512], start=True, stop=True)
                    em.v("tensor_copy", lamM[:, half * 2048:(half + 1) * 2048], psS[:])
                em.v("tensor_tensor", lam3,
                     pr.maskneg.unsqueeze(1).broadcast_to([128, NH, 128]),
                     lam3, ALU.add)
                em.v("tensor_tensor", lam3,
                     Utok[:].unsqueeze(2).broadcast_to([128, NH, 128]),
                     lam3, ALU.add)
                em.a("activation", Mw[:], lamM[:], AF.Exp, bias=0.0, scale=1.0)
                em.mm(psF[:, 0:128], bc_sb[:, cs_],
                      bc_sb[:, T + c * 128:T + (c + 1) * 128], start=True, stop=True)
                em.v("tensor_copy", Gb[:], psF[:, 0:128])
                em.v("tensor_tensor", Mw3, Gb[:].unsqueeze(1).broadcast_to([128, NH, 128]),
                     Mw3, ALU.mult)
                em.dma(Xstage[:].rearrange("p (i t) -> p i t", t=128),
                       xdram.rearrange("(i p) t -> p i t", p=128)[:, :, cs_])
                for i in range(16):
                    em.t("transpose", psT[:, i * 128:(i + 1) * 128],
                         Xstage[:, i * 128:(i + 1) * 128], pr.ident[:])
                em.v("tensor_copy", Xtok[:], psT[:])
                em.t("transpose", psT[:, 0:128], bc_sb[:, cs_], pr.ident[:])
                em.v("tensor_copy", Btok[:], psT[:, 0:128])
                em.v("tensor_tensor", Xw[:].rearrange("p (h w) -> p h w", w=PP),
                     Xtok[:].rearrange("p (h w) -> p h w", w=PP),
                     wS[:].unsqueeze(2).broadcast_to([128, NH, PP]), ALU.mult)
                XDv = lamM[:, 2048:4096].rearrange("p (h w) -> p h w", w=PP)
                em.v("tensor_tensor", XDv,
                     Xtok[:].rearrange("p (h w) -> p h w", w=PP),
                     Db32[:].unsqueeze(2).broadcast_to([128, NH, PP]), ALU.mult)
                for hh in range(2):
                    for hi in range(16):
                        h = hh * 16 + hi
                        em.mm(psY[:, hi * PP:(hi + 1) * PP],
                              Mw[:, h * 128:(h + 1) * 128],
                              Xtok[:, h * PP:(h + 1) * PP],
                              start=(hi % 8 == 0), stop=(hi % 8 == 7))
                        em.mm(psY[:, 1024 + hi * PP:1024 + (hi + 1) * PP],
                              bc_sb[:, T + c * 128:T + (c + 1) * 128],
                              Hb[:, h * PP:(h + 1) * PP],
                              start=(hi % 8 == 0), stop=(hi % 8 == 7))
                    ytmp = lamM[:, 0:1024]
                    em.v("tensor_tensor", ytmp.rearrange("p (h w) -> p h w", w=PP),
                         psY[:, 1024:2048].rearrange("p (h w) -> p h w", w=PP),
                         expl[:, hh * 16:(hh + 1) * 16].unsqueeze(2)
                         .broadcast_to([128, 16, PP]), ALU.mult)
                    em.v("tensor_tensor", ytmp, psY[:, 0:1024], ytmp, ALU.add)
                    em.v("tensor_tensor", ycbf[:, hh * 1024:(hh + 1) * 1024], ytmp,
                         lamM[:, 2048 + hh * 1024:2048 + (hh + 1) * 1024], ALU.add)
                em.dma(ydram.rearrange("(c p) f -> c p f", p=128)[c], ycbf[:])
                for h in range(NH):
                    em.mm(psS[:, h * PP:(h + 1) * PP], Btok[:],
                          Xw[:, h * PP:(h + 1) * PP],
                          start=(h % 8 == 0), stop=(h % 8 == 7))
                em.v("tensor_tensor", H[:].rearrange("p (h w) -> p h w", w=PP),
                     H[:].rearrange("p (h w) -> p h w", w=PP),
                     gam[:].unsqueeze(2).broadcast_to([128, NH, PP]), ALU.mult)
                em.v("tensor_tensor", H[:], psS[:], H[:], ALU.add)
                em.v("tensor_copy", Hb[:], H[:])

        with pr.scope() as es3:
            sb3 = lambda n, s, d: es3.enter_context(nc.sbuf_tensor(n, s, d))
            nw2b = sb3("sa_nw2b", [128, DI], F32)
            ytile = sb3("sa_ytile", [128, DI], BF16)
            ztile = sb3("sa_ztile", [128, DI], BF16)
            zs_ = sb3("sa_zs", [128, DI], F32)
            yz = sb3("sa_yz", [128, DI], F32)
            uu = sb3("sa_uu", [128, DI], F32)
            ynb = sb3("sa_ynb", [128, DI], BF16)
            ynT = sb3("sa_ynT", [128, DI], BF16)
            dump = sb3("sa_dump", [128, DI], F32)
            nwrow = sb3("sa_nwrow", [1, DI], F32)
            psO = es3.enter_context(nc.psum_tensor("sa_psO", [128, 1024], F32))
            psT2 = es3.enter_context(nc.psum_tensor("sa_psT2", [128, 2048], BF16))
            ps512b = es3.enter_context(nc.psum_tensor("sa_ps512b", [128, 512], F32))
            pr.es, saved_es = es3, pr.es
            wo = load_weight_bf16(pr, "sa_wob", "sa_woutT", DI, D)
            pr.es = saved_es
            em.dma(nwrow[:], pr.din["sa_normw"])
            bcast_row_to_tile(pr, nwrow, nw2b, ps512b, DI)
            for tt in range(NTT):
                em.dma(ytile[:], ydram.rearrange("(c p) f -> c p f", p=128)[tt])
                em.dma(ztile[:], zdram.rearrange("(c p) f -> c p f", p=128)[tt])
                em.a("activation", zs_[:], ztile[:], AF.Silu, bias=0.0, scale=1.0)
                em.v("tensor_tensor", yz[:], ytile[:], zs_[:], ALU.mult)
                em.a("activation", dump[:], yz[:], AF.Square, bias=0.0, scale=1.0,
                     accum_out=pr.col2)
                em.v("tensor_scalar", pr.col2, pr.col2, 1.0 / DI, 1e-5,
                     ALU.mult, ALU.add)
                em.a("activation", pr.col3, pr.col2, AF.Ln, bias=0.0, scale=1.0)
                em.a("activation", pr.col3, pr.col3, AF.Exp, bias=0.0, scale=-0.5)
                em.v("tensor_tensor", uu[:], yz[:], nw2b[:], ALU.mult)
                em.v("tensor_scalar", ynb[:], uu[:], pr.col3, None, ALU.mult)
                for i in range(16):
                    em.t("transpose", psT2[:, i * 128:(i + 1) * 128],
                         ynb[:, i * 128:(i + 1) * 128], pr.ident[:])
                em.v("tensor_copy", ynT[:], psT2[:])
                for os_ in range(2):
                    for ic in range(16):
                        em.mm(psO[:, os_ * 512:(os_ + 1) * 512],
                              ynT[:, ic * 128:(ic + 1) * 128],
                              wo[:, ic * D + os_ * 512:ic * D + (os_ + 1) * 512],
                              start=(ic == 0), stop=(ic == 15))
                em.v("tensor_tensor", pr.xs[:, tt * D:(tt + 1) * D], psO[:],
                     pr.xs[:, tt * D:(tt + 1) * D], ALU.add)


EMITTERS = {
    "l0aa": emit_l0_attn_a,
    "l0ab": emit_l0_attn_b,
    "l0ffn1": emit_l0_ffn,
    "l0ffn2": emit_l0_ffn,
    "l1ffn1": emit_l1_ffn,
    "l1ffn2": emit_l1_ffn,
    "l1ab": emit_l1_attn_b,
    "l1aa": emit_l1_attn_a,
}


def build_program(parts):
    nc = bass.Bass("TRN2", target_bir_lowering=False)
    x_in = nc.dram_tensor("x_in", [T, D], F32, kind="ExternalInput").ap()
    x_out = nc.dram_tensor("x_out", [T, D], F32, kind="ExternalOutput").ap()
    names = set(CONST_INPUTS)
    for p_ in parts:
        names.update(PART_INPUTS[p_])
    dins = {}
    for n in sorted(names):
        shp = PREP_SHAPES[n]
        dt = BF16 if n in ("c_ident128", "c_ones_row_b", "c_triu") else F32
        dins[n] = nc.dram_tensor(n, list(shp), dt, kind="ExternalInput").ap()

    em = E(nc)
    global _LAST_EM
    _LAST_EM = em
    with contextlib.ExitStack() as es:
        pr = Prog(NCProxy(nc), em, es, dins)
        setup_shared(pr)
        em.dma(pr.xs[:].rearrange("p (tt d) -> p tt d", d=D),
               x_in.rearrange("(tt p) d -> p tt d", p=128))
        for p_ in parts:
            em.mark(p_)
            EMITTERS[p_](pr)
        em.mark("end")
        em.dma(x_out.rearrange("(tt p) d -> p tt d", p=128),
               pr.xs[:].rearrange("p (tt d) -> p tt d", d=D))
        esem = {e: es.enter_context(nc.semaphore(f"SE_{e}")) for e in E.ENGS}
        dsem = [es.enter_context(nc.semaphore(f"SD_{i}")) for i in range(E.NDMA)]
        block = es.enter_context(nc.Block())
        em.replay(block, (esem, dsem))
    return nc, sorted(names)


PREP_SHAPES = {
    "c_ident128": (128, 128), "c_ident32f": (32, 32), "c_ident128f": (128, 128),
    "c_ones_row_f": (1, 128), "c_maskneg": (128, 128),
    "c_ones_row_b": (1, 128), "c_triu": (128, 128),
    "aa_wqT": (D, D), "aa_wkT": (D, D), "aa_wvT": (D, D),
    "ab_wqkv": (D, 3072), "ab_wout": (D, D), "ab_bout": (1, D), "ab_ls": (16, 1),
    "f0_w1T": (D, 4096), "f0_w2T": (4096, D), "f0_convw": (4096, 3), "f0_convb": (4096, 1),
    "sa_winT": (D, 4384), "sa_convw": (2304, 4), "sa_convb": (2304, 1),
    "sa_dtb": (32, 1), "sa_alog": (32, 1), "sa_d": (32, 1), "sa_normw": (1, 2048),
    "sa_woutT": (2048, D),
    "db_gatew": (D, 7), "db_gateb": (D, 1), "db_candw": (D, 7), "db_candb": (D, 1),
    "db_logdec": (128, 8), "db_woutT": (D, D), "db_bout": (1, D), "db_normw": (1, D),
    "f1_w1T": (D, 4096), "f1_b1": (1, 4096), "f1_w2T": (4096, D), "f1_b2": (1, D),
}


def run_parts(x_full, prepped, parts, sim=False):
    nc, names = build_program(parts)
    in_maps = []
    ncores = 1 if sim else 8
    for b in range(ncores):
        m = {"x_in": np.ascontiguousarray(x_full[b])}
        for n in names:
            arr = prepped[n]
            want_bf = n in ("c_ident128", "c_ones_row_b", "c_triu")
            m[n] = np.ascontiguousarray(arr.astype(bf16 if want_bf else np.float32))
        in_maps.append(m)
    if sim:
        from concourse.bass_interp import MultiCoreSim
        nc.finalize()
        s = MultiCoreSim(nc, num_cores=1, num_workers=1, trace=False)
        for k, v in in_maps[0].items():
            s.cores[0].tensor(k)[:] = v
        s.simulate()
        return np.asarray(s.cores[0].tensor("x_out"))[None]
    res = run_bass_kernel_spmd(nc, in_maps, core_ids=list(range(8)))
    return np.stack([r["x_out"] for r in res.results])


def kernel(**inputs):
    x = np.ascontiguousarray(np.asarray(inputs["L0_attn_a_z"], dtype=np.float32))
    prepped = prep_inputs(inputs)
    return run_parts(x, prepped, ALL_PARTS)


# version 22
# speedup vs baseline: 1.5916x; 1.0054x over previous
import contextlib
import numpy as np
import ml_dtypes
import concourse.bass as bass
import concourse.mybir as mybir
from concourse.bass_utils import run_bass_kernel_spmd

F32 = mybir.dt.float32
BF16 = mybir.dt.bfloat16
I32 = mybir.dt.int32
AF = mybir.ActivationFunctionType
ALU = mybir.AluOpType
AX = mybir.AxisListType
bf16 = ml_dtypes.bfloat16

T = 2048
D = 1024
NTT = T // 128
NDC = D // 128

ALL_PARTS = ["l0aa", "l0ffn1", "l0ab", "l0ffn2", "l1aa", "l1ffn1", "l1ab", "l1ffn2"]
_LAST_EM = None


class E:

    ENGS = ("vector", "scalar", "tensor", "gpsimd", "sync")
    NDMA = 16

    def __init__(self, nc):
        self.nc = nc
        self.ops = {e: [] for e in self.ENGS}
        self.cnt = {e: 0 for e in self.ENGS}
        self.dma_uses = [0] * self.NDMA
        self.dma_next = 0
        self.highwater = {}
        self.last_writer = {}
        self.readers = {}

    @staticmethod
    def _aps(x):
        return [a for a in x if type(a).__name__ == "AP"]

    def _dep_wait(self, waits, dep, eng=None):
        kind = dep[0]
        if kind == "E":
            _, deng, dcnt = dep
            if deng == eng and eng == "tensor":
                return
            key = ("E", deng)
            if waits.get(key, 0) < dcnt:
                waits[key] = dcnt
        else:
            _, slot, val = dep
            key = ("D", slot)
            if waits.get(key, 0) < val:
                waits[key] = val

    def mark(self, label):
        self.marks = getattr(self, "marks", [])
        self.marks.append((label, dict(self.cnt), sum(self.dma_uses)))

    def barrier(self):
        self._barrier_tokens = (
            [("E", e, c) for e, c in self.cnt.items() if c > 0] +
            [("D", s, 16 * u) for s, u in enumerate(self.dma_uses) if u > 0])
        self._barrier_done = set()
        self.last_writer = {}
        self.readers = {}

    def _desc(self, eng, fn, args, kwargs, dma=False):
        aps = self._aps(args) + self._aps(list(kwargs.values()))
        if dma:
            writes = [kwargs["out"]] if "out" in kwargs else [args[0]]
            reads = [a for a in aps if a is not writes[0]]
        else:
            writes = aps[:1]
            if kwargs.get("accum_out") is not None:
                writes = writes + [kwargs["accum_out"]]
            reads = [a for a in aps if all(a is not w for w in writes)]
        waits = {}
        if getattr(self, "_barrier_tokens", None) and eng not in self._barrier_done:
            for dep in self._barrier_tokens:
                self._dep_wait(waits, dep, None)
            self._barrier_done.add(eng)
        rnames = [a.name for a in reads]
        wnames = [a.name for a in writes]
        for t in rnames:
            w = self.last_writer.get(t)
            if w is not None:
                self._dep_wait(waits, w, eng)
        for t in wnames:
            w = self.last_writer.get(t)
            if w is not None:
                self._dep_wait(waits, w, eng)
            for r in self.readers.get(t, ()):
                self._dep_wait(waits, r, eng)
        if dma:
            slot = self.dma_next
            self.dma_next = (self.dma_next + 1) % self.NDMA
            if self.dma_uses[slot] > 0:
                key = ("D", slot)
                if waits.get(key, 0) < 16 * self.dma_uses[slot]:
                    waits[key] = 16 * self.dma_uses[slot]
            self.dma_uses[slot] += 1
            tok = ("D", slot, 16 * self.dma_uses[slot])
            inc = ("D", slot)
        else:
            self.cnt[eng] += 1
            tok = ("E", eng, self.cnt[eng])
            inc = ("E", eng)
        for t in rnames:
            self.readers.setdefault(t, []).append(tok)
        for t in wnames:
            self.last_writer[t] = tok
            self.readers[t] = []
        hw = self.highwater.setdefault(eng, {})
        pruned = {}
        for key, val in waits.items():
            if hw.get(key, -1) < val:
                pruned[key] = val
                hw[key] = val
        self.ops[eng].append((pruned, fn, inc))

    def v(self, meth, *a, **k):
        self._desc("vector", lambda e, a=a, k=k, m=meth: getattr(e, m)(*a, **k), a, k)

    def a(self, meth, *a, **k):
        self._desc("scalar", lambda e, a=a, k=k, m=meth: getattr(e, m)(*a, **k), a, k)

    def t(self, meth, *a, **k):
        self._desc("tensor", lambda e, a=a, k=k, m=meth: getattr(e, m)(*a, **k), a, k)

    def g(self, meth, *a, **k):
        self._desc("gpsimd", lambda e, a=a, k=k, m=meth: getattr(e, m)(*a, **k), a, k)

    def dma(self, out, in_, eng=None):
        if eng is None:
            eng = ("sync", "gpsimd")[self.dma_next % 2]
        self._desc(eng, lambda e, o=out, i=in_: e.dma_start(out=o, in_=i),
                   (), {"out": out, "in_": in_}, dma=True)

    def dma_t(self, out, in_, eng=None):
        if eng is None:
            eng = "sync"
        self._desc(eng, lambda e, o=out, i=in_: e.dma_start(out=o, in_=i, transpose=True),
                   (), {"out": out, "in_": in_}, dma=True)

    def mm(self, out, lhsT, rhs, start, stop, tile_position=None):
        self._desc("tensor",
                   lambda e, o=out, l=lhsT, r=rhs, s1=start, s2=stop, tp=tile_position:
                   e.matmul(o, l, r, start=s1, stop=s2, tile_position=tp),
                   (out, lhsT, rhs), {})

    def replay(self, block, sems):
        esem, dsem = sems
        total_e = dict(self.cnt)
        total_d = list(self.dma_uses)

        for name in self.ENGS:
            ops = self.ops[name]

            def section(eng, ops=ops, name=name):
                for (waits, fn, inc) in ops:
                    for (key, val) in sorted(waits.items(), key=str):
                        if key[0] == "E":
                            eng.wait_ge(esem[key[1]], val)
                        else:
                            eng.wait_ge(dsem[key[1]], val)
                    inst = fn(eng)
                    if inc[0] == "E":
                        inst.then_inc(esem[inc[1]], 1)
                    else:
                        inst.then_inc(dsem[inc[1]], 16)
                if name == "sync":
                    for e2, c in total_e.items():
                        if c > 0:
                            eng.wait_ge(esem[e2], c)
                    for s, u in enumerate(total_d):
                        if u > 0:
                            eng.wait_ge(dsem[s], 16 * u)

            getattr(block, name)(section)


def prep_inputs(inp):
    p = {}
    c = np.ascontiguousarray
    p["c_ident128"] = np.eye(128, dtype=bf16)
    p["c_ident32f"] = np.eye(32, dtype=np.float32)
    p["c_ident128f"] = np.eye(128, dtype=np.float32)
    p["c_ones_row_f"] = np.ones((1, 128), dtype=np.float32)
    p["c_ones_row_b"] = np.ones((1, 128), dtype=bf16)
    p["c_triu"] = np.triu(np.ones((128, 128), dtype=np.float32)).astype(bf16)
    mn = np.zeros((128, 128), dtype=np.float32)
    mn[np.tril_indices(128, -1)] = -1e30
    p["c_maskneg"] = mn

    W = np.asarray(inp["L0_attn_a_W_qkv"]).reshape(16, 3, 64, D)
    p["aa_wqT"] = c(W[:, 0].reshape(D, D).T)
    p["aa_wkT"] = c(W[:, 1].reshape(D, D).T)
    p["aa_wvT"] = c(W[:, 2].reshape(D, D).T)

    p["ab_wqkv"] = c(np.asarray(inp["L0_attn_b_w_qkv"]))
    p["ab_wout"] = c(np.asarray(inp["L0_attn_b_w_out"]))
    p["ab_bout"] = c(np.asarray(inp["L0_attn_b_b_out"]).reshape(1, D))
    p["ab_ls"] = c(np.asarray(inp["L0_attn_b_logit_scale"]).reshape(16, 1))

    p["f0_w1T"] = c(np.asarray(inp["L0_ffn_w1"]).T)
    p["f0_w2T"] = c(np.asarray(inp["L0_ffn_w2"]).T)
    p["f0_convw"] = c(np.asarray(inp["L0_ffn_conv_w"]).reshape(4096, 3))
    p["f0_convb"] = c(np.asarray(inp["L0_ffn_conv_b"]).reshape(4096, 1))

    p["sa_winT"] = c(np.asarray(inp["L1_attn_a_W_in"]).T)
    p["sa_convw"] = c(np.asarray(inp["L1_attn_a_conv_w"]))
    p["sa_convb"] = c(np.asarray(inp["L1_attn_a_conv_b"]).reshape(2304, 1))
    p["sa_dtb"] = c(np.asarray(inp["L1_attn_a_dt_bias"]).reshape(32, 1))
    p["sa_alog"] = c(np.asarray(inp["L1_attn_a_A_log"]).reshape(32, 1))
    p["sa_d"] = c(np.asarray(inp["L1_attn_a_D"]).reshape(32, 1))
    p["sa_normw"] = c(np.asarray(inp["L1_attn_a_norm_w"]).reshape(1, 2048))
    p["sa_woutT"] = c(np.asarray(inp["L1_attn_a_W_out"]).T)

    p["db_gatew"] = c(np.asarray(inp["L1_attn_b_gate_w"]).reshape(D, 7))
    p["db_gateb"] = c(np.asarray(inp["L1_attn_b_gate_b"]).reshape(D, 1))
    p["db_candw"] = c(np.asarray(inp["L1_attn_b_cand_w"]).reshape(D, 7))
    p["db_candb"] = c(np.asarray(inp["L1_attn_b_cand_b"]).reshape(D, 1))
    p["db_logdec"] = c(np.asarray(inp["L1_attn_b_log_decay"]).reshape(8, 128).T)
    p["db_woutT"] = c(np.asarray(inp["L1_attn_b_out_w"]).T)
    p["db_bout"] = c(np.asarray(inp["L1_attn_b_out_b"]).reshape(1, D))
    p["db_normw"] = c(np.asarray(inp["L1_attn_b_norm_w"]).reshape(1, D))

    p["f1_w1T"] = c(np.asarray(inp["L1_ffn_W1"]).T)
    p["f1_b1"] = c(np.asarray(inp["L1_ffn_b1"]).reshape(1, 4096))
    p["f1_w2T"] = c(np.asarray(inp["L1_ffn_W2"]).T)
    p["f1_b2"] = c(np.asarray(inp["L1_ffn_b2"]).reshape(1, D))
    return p


PART_INPUTS = {
    "l0aa": ["aa_wqT", "aa_wkT", "aa_wvT"],
    "l0ab": ["ab_wqkv", "ab_wout", "ab_bout", "ab_ls"],
    "l0ffn1": ["f0_w1T", "f0_w2T", "f0_convw", "f0_convb"],
    "l0ffn2": ["f0_w1T", "f0_w2T", "f0_convw", "f0_convb"],
    "l1aa": ["sa_winT", "sa_convw", "sa_convb", "sa_dtb", "sa_alog", "sa_d",
             "sa_normw", "sa_woutT"],
    "l1ab": ["db_gatew", "db_gateb", "db_candw", "db_candb", "db_logdec",
             "db_woutT", "db_bout", "db_normw"],
    "l1ffn1": ["f1_w1T", "f1_b1", "f1_w2T", "f1_b2"],
    "l1ffn2": ["f1_w1T", "f1_b1", "f1_w2T", "f1_b2"],
}
CONST_INPUTS = ["c_ident128", "c_ident32f", "c_ident128f", "c_ones_row_f",
                "c_ones_row_b", "c_triu", "c_maskneg"]


class NCProxy:

    def __init__(self, nc):
        self._nc = nc
        self._used = {}

    def _uniq(self, name):
        n = self._used.get(name, 0)
        self._used[name] = n + 1
        return name if n == 0 else f"{name}__{n}"

    def sbuf_tensor(self, name, *a, **k):
        return self._nc.sbuf_tensor(self._uniq(name), *a, **k)

    def psum_tensor(self, name, *a, **k):
        return self._nc.psum_tensor(self._uniq(name), *a, **k)

    def dram_tensor(self, name, *a, **k):
        return self._nc.dram_tensor(self._uniq(name), *a, **k)

    def __getattr__(self, a):
        return getattr(self._nc, a)


class Prog:

    def __init__(self, nc, em, es, dins):
        self.nc = nc
        self.em = em
        self.es = es
        self.din = dins
        self.sb = {}
        self.uid = 0

    def sbuf(self, name, shape, dtype):
        t = self.es.enter_context(self.nc.sbuf_tensor(name, shape, dtype))
        return t

    def psum(self, name, shape, dtype):
        return self.es.enter_context(self.nc.psum_tensor(name, shape, dtype))

    @contextlib.contextmanager
    def scope(self):
        with contextlib.ExitStack() as es:
            yield es
        self.em.barrier()


def setup_shared(pr):
    em = pr.em
    pr.xs = pr.sbuf("xs", [128, NTT * D], F32)[:]
    pr.ident = pr.sbuf("ident", [128, 128], BF16)[:]
    pr.ident32f = pr.sbuf("ident32f", [32, 32], F32)[:]
    pr.ident128f = pr.sbuf("ident128f", [128, 128], F32)[:]
    pr.ones_row_f = pr.sbuf("ones_row_f", [1, 128], F32)[:]
    pr.ones_row_b = pr.sbuf("ones_row_b", [1, 128], BF16)[:]
    pr.triu = pr.sbuf("triu", [128, 128], BF16)[:]
    pr.maskneg = pr.sbuf("maskneg", [128, 128], F32)[:]
    pr.col1 = pr.sbuf("col1", [128, 1], F32)[:]
    pr.col2 = pr.sbuf("col2", [128, 1], F32)[:]
    pr.col3 = pr.sbuf("col3", [128, 1], F32)[:]
    em.dma(pr.ident[:], pr.din["c_ident128"])
    em.dma(pr.ident32f[:], pr.din["c_ident32f"])
    em.dma(pr.ident128f[:], pr.din["c_ident128f"])
    em.dma(pr.ones_row_f[:], pr.din["c_ones_row_f"])
    em.dma(pr.ones_row_b[:], pr.din["c_ones_row_b"])
    em.dma(pr.triu[:], pr.din["c_triu"])
    em.dma(pr.maskneg[:], pr.din["c_maskneg"])


def load_weight_bf16(pr, name, dram_name, rows, cols, tag=None):
    em = pr.em
    nch = rows // 128
    wb = pr.sbuf(name, [128, nch * cols], BF16)
    src = pr.din[dram_name].rearrange("(c p) f -> c p f", p=128)
    with pr.scope() as es_w:
        stg = es_w.enter_context(pr.nc.sbuf_tensor(name + "_stg", [128, cols], F32))
        for ci in range(nch):
            em.dma(stg[:], src[ci])
            em.v("tensor_copy", wb[:, ci * cols:(ci + 1) * cols], stg[:])
    return wb


def emit_ln_tile(pr, tt, xn_tile, ps_tr, dump, to_xnT=True, xn_f32=None,
                 cols=None):
    em = pr.em
    c1, c2, c3 = cols if cols is not None else (pr.col1, pr.col2, pr.col3)
    xst = pr.xs[:, tt * D:(tt + 1) * D]
    em.v("reduce_sum", c1, xst, AX.X)
    em.v("tensor_scalar", c1, c1, -1.0 / D, None, ALU.mult)
    em.a("activation", dump[:, 0:D], xst, AF.Square, bias=c1, scale=1.0,
         accum_out=c2)
    em.v("tensor_scalar", c2, c2, 1.0 / D, 1e-5, ALU.mult, ALU.add)
    em.a("activation", c3, c2, AF.Ln, bias=0.0, scale=1.0)
    em.a("activation", c3, c3, AF.Exp, bias=0.0, scale=-0.5)
    em.v("tensor_scalar", xn_tile[:], xst, c1, c3, ALU.add, ALU.mult)
    if xn_f32 is not None:
        em.v("tensor_scalar", xn_f32[:], xst, c1, c3, ALU.add, ALU.mult)
    if to_xnT:
        for dc in range(NDC):
            em.t("transpose", ps_tr[:, dc * 128:(dc + 1) * 128],
                 xn_tile[:, dc * 128:(dc + 1) * 128], pr.ident[:])
        dst = pr.xnT[:].rearrange("p (dc t) -> p dc t", t=T)[:, :, tt * 128:(tt + 1) * 128]
        src = ps_tr[:].rearrange("p (dc t) -> p dc t", t=128)
        em.v("tensor_copy", dst, src)


def emit_ln_all(pr, xn_f32_buf=None, stats=None):
    with pr.scope() as es2:
        R = 2
        xn_tile = [es2.enter_context(pr.nc.sbuf_tensor(f"ln_xnt{i}", [128, D], BF16))
                   for i in range(R)]
        dump = [es2.enter_context(pr.nc.sbuf_tensor(f"ln_dump{i}", [128, D], BF16))
                for i in range(R)]
        ps_tr = [es2.enter_context(pr.nc.psum_tensor(f"ln_pstr{i}", [128, NDC * 128], BF16))
                 for i in range(R)]
        colr = [tuple(es2.enter_context(pr.nc.sbuf_tensor(f"ln_c{i}_{j}", [128, 1], F32))[:]
                      for j in range(3)) for i in range(R)]
        for tt in range(NTT):
            i = tt % R
            xf = None if xn_f32_buf is None else xn_f32_buf[:, tt * D:(tt + 1) * D]
            emit_ln_tile(pr, tt, xn_tile[i], ps_tr[i], dump[i], xn_f32=xf,
                         cols=colr[i])
            if stats is not None:
                pr.em.v("tensor_copy", stats[0][:, tt:tt + 1], colr[i][0])
                pr.em.v("tensor_copy", stats[1][:, tt:tt + 1], colr[i][2])


def emit_attention_core(pr, qT, kT, vbuf, evac_fn, scale_imm=None,
                        scale_ap16=None, bias_ap16=None):
    em = pr.em
    with pr.scope() as es2:
        nc = pr.nc
        NPB = 3
        ps_s = [es2.enter_context(nc.psum_tensor(f"att_s{i}", [128, 1024], F32))
                for i in range(NPB)]
        ps_av = [es2.enter_context(nc.psum_tensor("att_av", [128, 1024], F32))] * 2
        ebuf = [es2.enter_context(nc.sbuf_tensor(f"att_e{i}", [128, 1024], BF16))
                for i in range(NPB)]
        rden8 = es2.enter_context(nc.sbuf_tensor("att_rd8", [128, 8], F32))

        def avoff(j):
            return (j // 4) * 512 + (j % 4) * 65

        for h in range(16):
            bp = 64 * (h % 2)
            cb = (h // 2) * T
            for qh in range(2):
                av = ps_av[(h * 2 + qh) % 2]

                def front(kc, h=h, bp=bp, cb=cb, qh=qh):
                    pb = kc % 3
                    for qs in range(2):
                        em.mm(ps_s[pb][:, qs * 512:(qs + 1) * 512],
                              kT[bp:bp + 64, cb + kc * 128:cb + (kc + 1) * 128],
                              qT[bp:bp + 64, cb + qh * 1024 + qs * 512:cb + qh * 1024 + (qs + 1) * 512],
                              start=True, stop=True, tile_position=(bp, 0))
                    if scale_imm is not None:
                        em.a("activation", ebuf[pb][:], ps_s[pb][:],
                             AF.Exp, bias=0.0, scale=scale_imm)
                    else:
                        em.a("activation", ebuf[pb][:], ps_s[pb][:],
                             AF.Exp, bias=bias_ap16[:, h:h + 1], scale=scale_ap16[:, h:h + 1])

                def back(kc, h=h, av=av):
                    pb = kc % 3
                    for j in range(8):
                        em.mm(av[:, avoff(j):avoff(j) + 65],
                              ebuf[pb][:, j * 128:(j + 1) * 128],
                              vbuf[:, kc * 1040 + h * 65:kc * 1040 + (h + 1) * 65],
                              start=(kc == 0 and j % 4 == 0),
                              stop=(kc == NTT - 1 and j % 4 == 3))

                for kc in range(NTT + 1):
                    if kc < NTT:
                        front(kc)
                    if kc > 0:
                        back(kc - 1)
                av4 = (av[:].rearrange("p (b x) -> p b x", b=2)[:, :, 0:260]
                       .rearrange("p b (g w) -> p b g w", w=65))
                em.v("reciprocal",
                     rden8[:].rearrange("p (b g) -> p b g", b=2),
                     av4[:, :, :, 64])
                evac_fn(h, qh, av, rden8)


def emit_l0_attn_a(pr):
    em = pr.em
    nc = pr.nc
    with pr.scope() as es2:
        qT = es2.enter_context(nc.sbuf_tensor("aa_qT", [128, NDC * T], BF16))
        kT = es2.enter_context(nc.sbuf_tensor("aa_kT", [128, NDC * T], BF16))
        vbuf = es2.enter_context(nc.sbuf_tensor("aa_v", [128, NTT * D], BF16))
        with pr.scope() as es3:
            pr.xnT = es3.enter_context(nc.sbuf_tensor("aa_xnT", [128, NDC * T], BF16))[:]
            emit_ln_all(pr)
            ps_qk = es3.enter_context(nc.psum_tensor("aa_psqk", [128, 2048], F32))
            ps_v = es3.enter_context(nc.psum_tensor("aa_psv", [128, 1024], F32))
            with pr.scope() as es4:
                wstg = es4.enter_context(nc.sbuf_tensor("aa_wstg", [128, 128], F32))
                wsl = es4.enter_context(nc.sbuf_tensor("aa_wsl", [128, 128], BF16))
                for side, wname in ((qT, "aa_wqT"), (kT, "aa_wkT")):
                    wsrc = pr.din[wname].rearrange("(dc p) f -> dc p f", p=128)
                    for fc in range(NDC):
                        for dc in range(NDC):
                            em.dma(wstg[:], wsrc[dc, :, fc * 128:(fc + 1) * 128])
                            em.v("tensor_copy", wsl[:], wstg[:])
                            for qs in range(4):
                                em.mm(ps_qk[:, qs * 512:(qs + 1) * 512],
                                      wsl[:],
                                      pr.xnT[:, dc * T + qs * 512:dc * T + (qs + 1) * 512],
                                      start=(dc == 0), stop=(dc == NDC - 1))
                        em.v("tensor_copy", side[:, fc * T:(fc + 1) * T], ps_qk[:])
            with pr.scope() as es4:
                wvh = es4.enter_context(nc.sbuf_tensor("aa_wvh", [128, NDC * 512], BF16))
                wstg2 = es4.enter_context(nc.sbuf_tensor("aa_wstg2", [128, 512], F32))
                wsrc = pr.din["aa_wvT"].rearrange("(dc p) f -> dc p f", p=128)
                for vs in range(2):
                    for dc in range(NDC):
                        em.dma(wstg2[:], wsrc[dc, :, vs * 512:(vs + 1) * 512])
                        em.v("tensor_copy", wvh[:, dc * 512:(dc + 1) * 512], wstg2[:])
                    for tt in range(NTT):
                        for dc in range(NDC):
                            em.mm(ps_v[:, vs * 512:(vs + 1) * 512],
                                  pr.xnT[:, dc * T + tt * 128:dc * T + (tt + 1) * 128],
                                  wvh[:, dc * 512:(dc + 1) * 512],
                                  start=(dc == 0), stop=(dc == NDC - 1))
                        em.v("tensor_copy",
                             vbuf[:, tt * D + vs * 512:tt * D + (vs + 1) * 512],
                             ps_v[:, vs * 512:(vs + 1) * 512])

        with pr.scope() as es3:
            NSL = 4
            ps_s = es3.enter_context(nc.psum_tensor("aa_pss", [128, 2048], F32))
            ps_av = [es3.enter_context(nc.psum_tensor(f"aa_psav{j}", [128, 512], F32))
                     for j in range(NSL)]
            ebuf = [es3.enter_context(nc.sbuf_tensor(f"aa_e{j}", [128, 2048], BF16))
                    for j in range(NSL)]
            eT = [es3.enter_context(nc.sbuf_tensor(f"aa_eT{j}", [128, 2048], BF16))
                  for j in range(NSL)]
            nmx = [es3.enter_context(nc.sbuf_tensor(f"aa_nmx{j}", [128, 1], F32))
                   for j in range(NSL)]
            den = [es3.enter_context(nc.sbuf_tensor(f"aa_den{j}", [128, 1], F32))
                   for j in range(NSL)]
            rden = [es3.enter_context(nc.sbuf_tensor(f"aa_rden{j}", [128, 1], F32))
                    for j in range(NSL)]
            iters = [(h, qt) for h in range(16) for qt in range(NTT)]

            def front(i):
                h, qt = iters[i]
                j = i % NSL
                bp = 64 * (h % 2)
                cb = (h // 2) * T
                for ks in range(4):
                    em.mm(ps_s[:, ks * 512:(ks + 1) * 512],
                          qT[bp:bp + 64, cb + qt * 128:cb + (qt + 1) * 128],
                          kT[bp:bp + 64, cb + ks * 512:cb + (ks + 1) * 512],
                          start=True, stop=True, tile_position=(bp, 0))
                em.v("reduce_max", nmx[j][:], ps_s[:], AX.X, negate=True)
                em.v("tensor_scalar", nmx[j][:], nmx[j][:], 0.125, None, ALU.mult)
                em.a("activation", ebuf[j][:], ps_s[:], AF.Exp, bias=nmx[j][:],
                     scale=0.125, accum_out=den[j][:])
                em.v("reciprocal", rden[j][:], den[j][:])
                em.dma_t(eT[j][:].rearrange("p (c q) -> p c q", q=128),
                         ebuf[j][:].rearrange("p (c k) -> p c k", k=128),
                         eng="sync")

            def back(i):
                h, qt = iters[i]
                j = i % NSL
                for kc in range(NTT):
                    em.mm(ps_av[j][:, 0:64], eT[j][:, kc * 128:(kc + 1) * 128],
                          vbuf[:, kc * D + h * 64:kc * D + (h + 1) * 64],
                          start=(kc == 0), stop=(kc == NTT - 1))
                xsl = pr.xs[:, qt * D + h * 64:qt * D + (h + 1) * 64]
                em.v("scalar_tensor_tensor", xsl, ps_av[j][:, 0:64], rden[j][:], xsl,
                     ALU.mult, ALU.add)

            for i in range(len(iters) + 1):
                if i < len(iters):
                    front(i)
                if i > 0:
                    back(i - 1)


def emit_l0_attn_b(pr):
    em = pr.em
    nc = pr.nc
    with pr.scope() as es2:
        pr.xnT = es2.enter_context(nc.sbuf_tensor("ab_xnT", [128, NDC * T], BF16))[:]
        emit_ln_all(pr)
        scb = es2.enter_context(nc.sbuf_tensor("ab_scb", [128, 16], F32))
        nscb = es2.enter_context(nc.sbuf_tensor("ab_nscb", [128, 16], F32))
        ao = pr.xnT

        with pr.scope() as es3:
            lsc = es3.enter_context(nc.sbuf_tensor("ab_lsc", [16, 1], F32))
            lsr = es3.enter_context(nc.sbuf_tensor("ab_lsr", [1, 16], F32))
            ps16 = es3.enter_context(nc.psum_tensor("ab_ps16", [128, 512], F32))
            em.dma(lsc[:], pr.din["ab_ls"])
            em.v("tensor_scalar", lsc[:], lsc[:], float(np.log(100.0)), None, ALU.min)
            em.a("activation", lsc[:], lsc[:], AF.Exp, bias=0.0, scale=1.0)
            em.t("transpose", ps16[0:1, 0:16], lsc[:], pr.ident32f[0:16, 0:16])
            em.v("tensor_copy", lsr[:], ps16[0:1, 0:16])
            em.mm(ps16[:, 0:16], pr.ones_row_f[:], lsr[:], start=True, stop=True)
            em.v("tensor_copy", scb[:], ps16[:, 0:16])
            em.v("tensor_scalar", nscb[:], scb[:], -1.0, None, ALU.mult)

        with pr.scope() as esA:
            qT = esA.enter_context(nc.sbuf_tensor("ab_qT", [128, NDC * T], BF16))
            kT = esA.enter_context(nc.sbuf_tensor("ab_kT", [128, NDC * T], BF16))
            vbuf = esA.enter_context(nc.sbuf_tensor("ab_v", [128, NTT * 1040], BF16))

            with pr.scope() as es3:
                TB = 3
                ps_q3 = [es3.enter_context(nc.psum_tensor(f"ab_psq{i}", [128, 1024], F32))
                         for i in range(TB)]
                ps_tr = es3.enter_context(nc.psum_tensor("ab_pstr", [128, 1024], BF16))
                qn = es3.enter_context(nc.sbuf_tensor("ab_qn", [128, D], BF16))
                ssq = es3.enter_context(nc.sbuf_tensor("ab_ssq", [128, 16], F32))
                dump = es3.enter_context(nc.sbuf_tensor("ab_dump", [128, D], F32))
                wstg = es3.enter_context(nc.sbuf_tensor("ab_wstg", [128, 512], F32))
                wslb = es3.enter_context(nc.sbuf_tensor("ab_wslb", [128, 512], BF16))
                ones_ap = vbuf[:].rearrange("p (t h w) -> p t h w", h=16, w=65)[:, :, :, 64:65]
                em.v("memset", ones_ap, 1.0)
                wsrc = pr.din["ab_wqkv"].rearrange("(dc p) f -> dc p f", p=128)
                for part, dstT in ((0, qT), (1, kT), (2, None)):
                    for t0 in range(0, NTT, TB):
                        tts = list(range(t0, min(t0 + TB, NTT)))
                        for qs in range(2):
                            for dc in range(NDC):
                                em.dma(wstg[:], wsrc[dc, :, part * 1024 + qs * 512:
                                                       part * 1024 + (qs + 1) * 512])
                                em.v("tensor_copy", wslb[:], wstg[:])
                                for i, tt in enumerate(tts):
                                    em.mm(ps_q3[i][:, qs * 512:(qs + 1) * 512],
                                          pr.xnT[:, dc * T + tt * 128:dc * T + (tt + 1) * 128],
                                          wslb[:],
                                          start=(dc == 0), stop=(dc == NDC - 1))
                        for i, tt in enumerate(tts):
                            ps_q = ps_q3[i]
                            emit_ab_qkv_post(pr, part, tt, ps_q, ps_tr, qn, ssq, dump,
                                             qT, kT, vbuf)

            _ab_attention_and_outproj(pr, qT, kT, vbuf, scb, nscb)
        _ab_outproj(pr)


def emit_ab_qkv_post(pr, part, tt, ps_q, ps_tr, qn, ssq, dump, qT, kT, vbuf):
    em = pr.em
    dstT = (qT, kT, None)[part]
    if part < 2:
        em.a("activation", dump[:, 0:D], ps_q[:], AF.Square, bias=0.0, scale=1.0)
        em.v("tensor_reduce",
             ssq[:].rearrange("p (h one) -> p h one", one=1),
             dump[:, 0:D].rearrange("p (h w) -> p h w", w=64),
             AX.X, ALU.add)
        em.v("tensor_scalar", ssq[:], ssq[:], 1e-24, None, ALU.max)
        em.a("activation", ssq[:], ssq[:], AF.Ln, bias=0.0, scale=1.0)
        em.a("activation", ssq[:], ssq[:], AF.Exp, bias=0.0, scale=-0.5)
        em.v("tensor_tensor", qn[:].rearrange("p (h w) -> p h w", w=64),
             ps_q[:].rearrange("p (h w) -> p h w", w=64),
             ssq[:].unsqueeze(2).broadcast_to([128, 16, 64]), ALU.mult)
        for dc in range(NDC):
            em.t("transpose", ps_tr[:, dc * 128:(dc + 1) * 128],
                 qn[:, dc * 128:(dc + 1) * 128], pr.ident[:])
        dst = dstT[:].rearrange("p (dc t) -> p dc t", t=T)[:, :, tt * 128:(tt + 1) * 128]
        em.v("tensor_copy", dst, ps_tr[:].rearrange("p (dc t) -> p dc t", t=128))
    else:
        dst = vbuf[:, tt * 1040:(tt + 1) * 1040].rearrange(
            "p (h w) -> p h w", w=65)[:, :, 0:64]
        em.v("tensor_copy", dst, ps_q[:].rearrange("p (h w) -> p h w", w=64))


def _ab_attention_and_outproj(pr, qT, kT, vbuf, scb, nscb):
    em = pr.em
    nc = pr.nc
    ao = pr.xnT

    def evac(h, qh, av, rden8):
        for grp in range(2):
            qt0 = qh * 8 + grp * 4
            dst3 = (ao[:].rearrange("p (q d) -> p q d", d=D)
                    [:, qt0:qt0 + 4, h * 64:(h + 1) * 64])
            src3 = (av[:].rearrange("p (b x) -> p b x", b=2)[:, :, 0:260]
                    .rearrange("p b (g w) -> p b g w", w=65)
                    [:, grp, :, 0:64])
            em.v("tensor_tensor", dst3, src3,
                 rden8[:, grp * 4:(grp + 1) * 4].unsqueeze(2)
                 .broadcast_to([128, 4, 64]), ALU.mult)

    emit_attention_core(pr, qT, kT, vbuf, evac, scale_ap16=scb, bias_ap16=nscb)


def _ab_outproj(pr):
    em = pr.em
    nc = pr.nc
    ao = pr.xnT
    with pr.scope() as es3:
        aT = es3.enter_context(nc.sbuf_tensor("ab_aT", [128, NDC * T], BF16))
        ps_tr = es3.enter_context(nc.psum_tensor("ab_pstr2", [128, 1024], BF16))
        ps_o = es3.enter_context(nc.psum_tensor("ab_pso", [128, 1024], F32))
        bout = es3.enter_context(nc.sbuf_tensor("ab_boutb", [1, D], BF16))
        boutf = es3.enter_context(nc.sbuf_tensor("ab_boutf", [1, D], F32))
        pr.es, saved_es = es3, pr.es
        wo = load_weight_bf16(pr, "ab_wob", "ab_wout", D, D)
        pr.es = saved_es
        em.dma(boutf[:], pr.din["ab_bout"])
        em.v("tensor_copy", bout[:], boutf[:])
        for tt in range(NTT):
            for dc in range(NDC):
                em.t("transpose", ps_tr[:, dc * 128:(dc + 1) * 128],
                     ao[:, tt * D + dc * 128:tt * D + (dc + 1) * 128], pr.ident[:])
            em.v("tensor_copy",
                 aT[:].rearrange("p (dc t) -> p dc t", t=T)[:, :, tt * 128:(tt + 1) * 128],
                 ps_tr[:].rearrange("p (dc t) -> p dc t", t=128))
        for tt in range(NTT):
            for os_ in range(2):
                for dc in range(NDC):
                    em.mm(ps_o[:, os_ * 512:(os_ + 1) * 512],
                          aT[:, dc * T + tt * 128:dc * T + (tt + 1) * 128],
                          wo[:, dc * D + os_ * 512:dc * D + (os_ + 1) * 512],
                          start=(dc == 0), stop=False)
                em.mm(ps_o[:, os_ * 512:(os_ + 1) * 512], pr.ones_row_b[:],
                      bout[:, os_ * 512:(os_ + 1) * 512], start=False, stop=True)
            em.v("tensor_tensor", pr.xs[:, tt * D:(tt + 1) * D], ps_o[:],
                 pr.xs[:, tt * D:(tt + 1) * D], ALU.add)



def bcast_1x1_to_col(pr, src11, dst_col, ps512):
    em = pr.em
    em.mm(ps512[:, 0:1], pr.ones_row_f, src11, start=True, stop=True)
    em.v("tensor_copy", dst_col, ps512[:, 0:1])


def bcast_row_to_tile(pr, src_row, dst, ps512, width):
    em = pr.em
    for off in range(0, width, 512):
        w = min(512, width - off)
        em.mm(ps512[:, 0:w], pr.ones_row_f, src_row[:, off:off + w],
              start=True, stop=True)
        em.v("tensor_copy", dst[:, off:off + w], ps512[:, 0:w])


def emit_neground2(pr, v, c_i32, d, r, out_bf, width):
    em = pr.em
    em.v("tensor_copy", c_i32[:, 0:width], v)
    em.v("tensor_sub", d[:, 0:width], v, c_i32[:, 0:width])
    em.v("scalar_tensor_tensor", r[:, 0:width], d[:, 0:width], 0.5,
         c_i32[:, 0:width], ALU.is_gt, ALU.add)
    em.v("scalar_tensor_tensor", out_bf, d[:, 0:width], -0.5, r[:, 0:width],
         ALU.is_lt, ALU.subtract)


def emit_mean_abs(pr, dram_name, rows, cols, out11, stg, abscols, ps512, onescol):
    em = pr.em
    nch = rows // 128
    src = pr.din[dram_name].rearrange("(c p) f -> c p f", p=128)
    for ci in range(nch):
        em.dma(stg[:, 0:cols], src[ci])
        em.a("activation", stg[:, 0:cols], stg[:, 0:cols], AF.Abs, bias=0.0,
             scale=1.0, accum_out=abscols[:, ci:ci + 1])
    em.v("reduce_sum", pr.col1, abscols[:, 0:nch], AX.X)
    em.mm(ps512[0:1, 0:1], pr.col1, onescol, start=True, stop=True)
    em.v("tensor_scalar", out11, ps512[0:1, 0:1], 1.0 / (rows * cols), None,
         ALU.mult)


def emit_l0_ffn(pr):
    em = pr.em
    nc = pr.nc
    H = 4096
    NHC = H // 128
    gdram = nc.dram_tensor(f"f0_g_{pr.uid}", [H, T], BF16, kind="Internal").ap()
    r2dram = nc.dram_tensor(f"f0_r2_{pr.uid}", [H, D], BF16, kind="Internal").ap()
    pr.uid += 1
    with pr.scope() as es2:
        sb = lambda n, s, d: es2.enter_context(nc.sbuf_tensor(n, s, d))
        ps = lambda n, s, d: es2.enter_context(nc.psum_tensor(n, s, d))
        q1T = sb("f0_q1T", [128, NDC * T], BF16)[:]
        pr.xnT = q1T
        onescol = sb("f0_ones", [128, 1], F32)
        g1 = sb("f0_g1", [1, 1], F32)
        g2 = sb("f0_g2", [1, 1], F32)
        t11 = sb("f0_t11", [1, 1], F32)
        thr1p = sb("f0_thr1p", [128, 1], F32)
        thr1n = sb("f0_thr1n", [128, 1], F32)
        thr2p = sb("f0_thr2p", [128, 1], F32)
        thr2n = sb("f0_thr2n", [128, 1], F32)
        geff1 = sb("f0_geff1", [1, 1], F32)
        geff2 = sb("f0_geff2", [1, 1], F32)
        comb1 = sb("f0_comb1", [128, T], F32)
        s2b = sb("f0_s2b", [128, T], F32)
        fin2 = sb("f0_fin2", [128, T], F32)
        rm2cols = sb("f0_rm2c", [128, NTT * NHC], F32)
        ps512 = ps("f0_ps512", [128, 512], F32)

        em.v("memset", onescol[:], 1.0)
        with pr.scope() as esG:
            stgG = esG.enter_context(nc.sbuf_tensor("f0_stgG", [128, 4096], F32))
            abscols = esG.enter_context(nc.sbuf_tensor("f0_absc", [128, 32], F32))
            emit_mean_abs(pr, "f0_w1T", D, H, g1[:], stgG, abscols, ps512, onescol[:])
            emit_mean_abs(pr, "f0_w2T", H, D, g2[:], stgG, abscols, ps512, onescol[:])
        for g, thrp, thrn, geff in ((g1, thr1p, thr1n, geff1),
                                    (g2, thr2p, thr2n, geff2)):
            em.v("tensor_scalar", g[:], g[:], 1e-5, None, ALU.max)
            em.v("reciprocal", t11[:], g[:])
            em.v("reciprocal", geff[:], t11[:])
            em.v("tensor_scalar", t11[:], g[:], 0.5, None, ALU.mult)
            bcast_1x1_to_col(pr, t11[:], thrp[:], ps512)
            em.v("tensor_scalar", thrn[:], thrp[:], -1.0, None, ALU.mult)

        with pr.scope() as es3:
            R = 2
            xn = [es3.enter_context(nc.sbuf_tensor(f"f0_xn{i}", [128, D], F32)) for i in range(R)]
            vq = [es3.enter_context(nc.sbuf_tensor(f"f0_vq{i}", [128, D], F32)) for i in range(R)]
            dq = [es3.enter_context(nc.sbuf_tensor(f"f0_dq{i}", [128, D], F32)) for i in range(R)]
            rq = [es3.enter_context(nc.sbuf_tensor(f"f0_rq{i}", [128, D], F32)) for i in range(R)]
            ci32 = [es3.enter_context(nc.sbuf_tensor(f"f0_ci{i}", [128, D], I32)) for i in range(R)]
            qb = [es3.enter_context(nc.sbuf_tensor(f"f0_qb{i}", [128, D], BF16)) for i in range(R)]
            cA = [es3.enter_context(nc.sbuf_tensor(f"f0_cA{i}", [128, 1], F32))[:] for i in range(R)]
            cB = [es3.enter_context(nc.sbuf_tensor(f"f0_cB{i}", [128, 1], F32))[:] for i in range(R)]
            cC = [es3.enter_context(nc.sbuf_tensor(f"f0_cC{i}", [128, 1], F32))[:] for i in range(R)]
            row1 = es3.enter_context(nc.sbuf_tensor("f0_row1", [1, T], F32))
            ps_tr = [es3.enter_context(nc.psum_tensor(f"f0_pstr{i}", [128, NDC * 128], BF16)) for i in range(R)]
            ps_th = [es3.enter_context(nc.psum_tensor(f"f0_psth{i}", [128, 512], F32)) for i in range(R)]
            for tt in range(NTT):
                i = tt % R
                emit_ln_tile_f32(pr, tt, xn[i], dq[i], cols=(cA[i], cB[i], cC[i]))
                em.v("reduce_max", cA[i], xn[i][:], AX.X, apply_absolute_value=True)
                em.v("tensor_scalar", cA[i], cA[i], 1e-5, None, ALU.max)
                em.v("reciprocal", cB[i], cA[i])
                em.v("tensor_scalar", cB[i], cB[i], 127.0, None, ALU.mult)
                em.v("tensor_scalar", vq[i][:], xn[i][:], cB[i], None, ALU.mult)
                emit_neground2(pr, vq[i][:], ci32[i], dq[i], rq[i], qb[i][:], D)
                for dc in range(NDC):
                    em.t("transpose", ps_tr[i][:, dc * 128:(dc + 1) * 128],
                         qb[i][:, dc * 128:(dc + 1) * 128], pr.ident[:])
                em.v("tensor_copy",
                     q1T[:].rearrange("p (dc t) -> p dc t", t=T)[:, :, tt * 128:(tt + 1) * 128],
                     ps_tr[i][:].rearrange("p (dc t) -> p dc t", t=128))
                em.v("tensor_scalar", cA[i], cA[i], 1.0 / 127.0, None, ALU.mult)
                em.t("transpose", ps_th[i][0:1, 0:128], cA[i], pr.ident128f)
                em.v("tensor_copy", row1[:, tt * 128:(tt + 1) * 128], ps_th[i][0:1, 0:128])
            em.v("tensor_scalar", row1[:], row1[:], geff1[:], None, ALU.mult)
            bcast_row_to_tile(pr, row1, comb1, ps512, T)

        with pr.scope() as es3:
            R = 2
            stg = [es3.enter_context(nc.sbuf_tensor(f"f0_stgA{i}", [128, 4096], F32)) for i in range(R)]
            r1sl = [es3.enter_context(nc.sbuf_tensor(f"f0_r1sl{i}", [128, NDC * 128], BF16)) for i in range(R)]
            r2sl = [es3.enter_context(nc.sbuf_tensor(f"f0_r2sl{i}", [128, D], BF16)) for i in range(R)]
            hs = [es3.enter_context(nc.sbuf_tensor(f"f0_hs{i}", [128, T], F32)) for i in range(R)]
            acc = [es3.enter_context(nc.sbuf_tensor(f"f0_acc{i}", [128, T], F32)) for i in range(R)]
            gch = [es3.enter_context(nc.sbuf_tensor(f"f0_gch{i}", [128, T], BF16)) for i in range(R)]
            cw = [es3.enter_context(nc.sbuf_tensor(f"f0_cw{i}", [128, 3], F32)) for i in range(R)]
            cb = [es3.enter_context(nc.sbuf_tensor(f"f0_cb{i}", [128, 1], F32)) for i in range(R)]
            ps_h = [es3.enter_context(nc.psum_tensor("f0_psh", [128, 2048], F32))] * R
            ps_t2 = [es3.enter_context(nc.psum_tensor("f0_pst2", [128, 2048], BF16))] * R
            w1src = pr.din["f0_w1T"].rearrange("(dc p) f -> dc p f", p=128)
            w2src = pr.din["f0_w2T"].rearrange("(hc p) f -> hc p f", p=128)
            cwsrc = pr.din["f0_convw"]
            cbsrc = pr.din["f0_convb"]
            for hc in range(NHC):
                i_ = hc % R
                stg_, r1sl_, r2sl_, hs_, acc_, gch_, cw_, cb_ = (
                    stg[i_], r1sl[i_], r2sl[i_], hs[i_], acc[i_], gch[i_], cw[i_], cb[i_])
                ps_h_, ps_t2_ = ps_h[i_], ps_t2[i_]
                for dc in range(NDC):
                    em.dma(stg_[:, dc * 128:(dc + 1) * 128],
                           w1src[dc, :, hc * 128:(hc + 1) * 128])
                em.v("tensor_scalar", stg_[:, 1024:2048], stg_[:, 0:1024], thr1p[:],
                     None, ALU.is_gt)
                em.v("scalar_tensor_tensor", r1sl_[:], stg_[:, 0:1024], thr1n[:],
                     stg_[:, 1024:2048], ALU.is_lt, ALU.subtract)
                em.dma(stg_[:, 2048:3072], w2src[hc])
                em.v("tensor_scalar", stg_[:, 3072:4096], stg_[:, 2048:3072], thr2p[:],
                     None, ALU.is_gt)
                em.v("scalar_tensor_tensor", r2sl_[:], stg_[:, 2048:3072], thr2n[:],
                     stg_[:, 3072:4096], ALU.is_lt, ALU.subtract)
                em.dma(r2dram.rearrange("(hc p) f -> hc p f", p=128)[hc], r2sl_[:])
                for ts4 in range(4):
                    for dc in range(NDC):
                        em.mm(ps_h_[:, ts4 * 512:(ts4 + 1) * 512],
                              r1sl_[:, dc * 128:(dc + 1) * 128],
                              q1T[:, dc * T + ts4 * 512:dc * T + (ts4 + 1) * 512],
                              start=(dc == 0), stop=(dc == NDC - 1))
                em.v("tensor_tensor", hs_[:], ps_h_[:], comb1[:], ALU.mult)
                em.dma(cw_[:], cwsrc[hc * 128:(hc + 1) * 128, :])
                em.dma(cb_[:], cbsrc[hc * 128:(hc + 1) * 128, :])
                em.v("tensor_scalar", acc_[:, 1:T], hs_[:, 0:T - 1], cw_[:, 0:1],
                     cb_[:], ALU.mult, ALU.add)
                em.v("tensor_copy", acc_[:, 0:1], cb_[:])
                em.v("scalar_tensor_tensor", acc_[:], hs_[:], cw_[:, 1:2], acc_[:],
                     ALU.mult, ALU.add)
                em.v("scalar_tensor_tensor", acc_[:, 0:T - 1], hs_[:, 1:T], cw_[:, 2:3],
                     acc_[:, 0:T - 1], ALU.mult, ALU.add)
                em.a("activation", gch_[:], acc_[:], AF.Silu, bias=0.0, scale=1.0)
                em.dma(gdram.rearrange("(hc p) f -> hc p f", p=128)[hc], gch_[:])
                for i in range(NTT):
                    em.t("transpose", ps_t2_[:, (i % 8) * 128 + (i // 8) * 1024:
                                             (i % 8) * 128 + (i // 8) * 1024 + 128],
                         gch_[:, i * 128:(i + 1) * 128], pr.ident[:])
                for i in range(NTT):
                    em.v("reduce_max", rm2cols[:, i * NHC + hc:i * NHC + hc + 1],
                         ps_t2_[:, (i % 8) * 128 + (i // 8) * 1024:
                                  (i % 8) * 128 + (i // 8) * 1024 + 128],
                         AX.X, apply_absolute_value=True)

        with pr.scope() as es3:
            ps_th = es3.enter_context(nc.psum_tensor("f0_psth2", [128, 512], F32))
            row2 = es3.enter_context(nc.sbuf_tensor("f0_row2", [1, T], F32))
            row3 = es3.enter_context(nc.sbuf_tensor("f0_row3", [1, T], F32))
            for tt in range(NTT):
                em.v("reduce_max", pr.col1, rm2cols[:, tt * NHC:(tt + 1) * NHC], AX.X)
                em.v("tensor_scalar", pr.col1, pr.col1, 1e-5, None, ALU.max)
                em.v("reciprocal", pr.col2, pr.col1)
                em.v("tensor_scalar", pr.col2, pr.col2, 127.0, None, ALU.mult)
                em.v("tensor_scalar", pr.col1, pr.col1, 1.0 / 127.0, None, ALU.mult)
                em.t("transpose", ps_th[0:1, 0:128], pr.col1, pr.ident128f)
                em.v("tensor_copy", row2[:, tt * 128:(tt + 1) * 128], ps_th[0:1, 0:128])
                em.t("transpose", ps_th[0:1, 128:256], pr.col2, pr.ident128f)
                em.v("tensor_copy", row3[:, tt * 128:(tt + 1) * 128], ps_th[0:1, 128:256])
            em.v("tensor_scalar", row2[:], row2[:], geff2[:], None, ALU.mult)
            bcast_row_to_tile(pr, row3, s2b, ps512, T)
            bcast_row_to_tile(pr, row2, fin2, ps512, T)

        with pr.scope() as es3:
            R = 3
            gsl = [es3.enter_context(nc.sbuf_tensor(f"f0_gsl{i}", [128, 512], BF16)) for i in range(R)]
            vq = [es3.enter_context(nc.sbuf_tensor(f"f0_vq2{i}", [128, 512], F32)) for i in range(R)]
            dq = [es3.enter_context(nc.sbuf_tensor(f"f0_dq2{i}", [128, 512], F32)) for i in range(R)]
            rq = [es3.enter_context(nc.sbuf_tensor(f"f0_rq2{i}", [128, 512], F32)) for i in range(R)]
            ci32 = [es3.enter_context(nc.sbuf_tensor(f"f0_ci2{i}", [128, 512], I32)) for i in range(R)]
            r2b = [es3.enter_context(nc.sbuf_tensor(f"f0_r2b{i}", [128, D], BF16)) for i in range(2)]
            q2buf = es3.enter_context(nc.sbuf_tensor("f0_q2b", [128, 32 * 512], BF16))
            fout = es3.enter_context(nc.sbuf_tensor("f0_fout", [128, NDC * 512], BF16))
            ps_o = es3.enter_context(nc.psum_tensor("f0_pso", [128, 2048], F32))
            ps_tb = [es3.enter_context(nc.psum_tensor(f"f0_pstb{i}", [128, 1024], BF16)) for i in range(2)]
            gsrc = gdram.rearrange("(hc p) f -> hc p f", p=128)
            r2src = r2dram.rearrange("(hc p) f -> hc p f", p=128)
            for tb in range(4):
                for hc in range(NHC):
                    i_ = hc % R
                    em.dma(gsl[i_][:], gsrc[hc, :, tb * 512:(tb + 1) * 512])
                    em.v("tensor_tensor", vq[i_][:], gsl[i_][:],
                         s2b[:, tb * 512:(tb + 1) * 512], ALU.mult)
                    emit_neground2(pr, vq[i_][:], ci32[i_], dq[i_], rq[i_],
                                   q2buf[:, hc * 512:(hc + 1) * 512], 512)
                for grp in range(2):
                    for hc in range(NHC):
                        em.dma(r2b[hc % 2][:], r2src[hc])
                        for dmi in range(4):
                            dmc = grp * 4 + dmi
                            em.mm(ps_o[:, dmi * 512:(dmi + 1) * 512],
                                  r2b[hc % 2][:, dmc * 128:(dmc + 1) * 128],
                                  q2buf[:, hc * 512:(hc + 1) * 512],
                                  start=(hc == 0), stop=(hc == NHC - 1))
                    for dmi in range(4):
                        dmc = grp * 4 + dmi
                        em.v("tensor_tensor", fout[:, dmc * 512:(dmc + 1) * 512],
                             ps_o[:, dmi * 512:(dmi + 1) * 512],
                             fin2[:, tb * 512:(tb + 1) * 512], ALU.mult)
                for dmc in range(NDC):
                    pstb_ = ps_tb[dmc % 2]
                    for i in range(4):
                        em.t("transpose", pstb_[:, (i % 4) * 256:(i % 4) * 256 + 128],
                             fout[:, dmc * 512 + i * 128:dmc * 512 + (i + 1) * 128],
                             pr.ident[:])
                    for i in range(4):
                        tt = tb * 4 + i
                        em.v("tensor_tensor",
                             pr.xs[:, tt * D + dmc * 128:tt * D + (dmc + 1) * 128],
                             pstb_[:, (i % 4) * 256:(i % 4) * 256 + 128],
                             pr.xs[:, tt * D + dmc * 128:tt * D + (dmc + 1) * 128],
                             ALU.add)


def emit_ln_tile_f32(pr, tt, xn_f32, dump, cols=None):
    em = pr.em
    c1, c2, c3 = cols if cols is not None else (pr.col1, pr.col2, pr.col3)
    xst = pr.xs[:, tt * D:(tt + 1) * D]
    em.v("reduce_sum", c1, xst, AX.X)
    em.v("tensor_scalar", c1, c1, -1.0 / D, None, ALU.mult)
    em.a("activation", dump[:, 0:D], xst, AF.Square, bias=c1, scale=1.0,
         accum_out=c2)
    em.v("tensor_scalar", c2, c2, 1.0 / D, 1e-5, ALU.mult, ALU.add)
    em.a("activation", c3, c2, AF.Ln, bias=0.0, scale=1.0)
    em.a("activation", c3, c3, AF.Exp, bias=0.0, scale=-0.5)
    em.v("tensor_scalar", xn_f32[:], xst, c1, c3, ALU.add, ALU.mult)



def emit_l1_ffn(pr):
    em = pr.em
    nc = pr.nc
    H = 4096
    with pr.scope() as es2:
        sb = lambda n, s, d: es2.enter_context(nc.sbuf_tensor(n, s, d))
        pr.xnT = sb("f1_xnT", [128, NDC * T], BF16)[:]
        emit_ln_all(pr)
        onescol = sb("f1_ones", [128, 1], F32)
        g1 = sb("f1_g1", [1, 1], F32)
        g2 = sb("f1_g2", [1, 1], F32)
        t11 = sb("f1_t11", [1, 1], F32)
        thr1p = sb("f1_thr1p", [128, 1], F32)
        thr1n = sb("f1_thr1n", [128, 1], F32)
        thr2p = sb("f1_thr2p", [128, 1], F32)
        thr2n = sb("f1_thr2n", [128, 1], F32)
        em.v("memset", onescol[:], 1.0)
        with pr.scope() as esG:
            ps512 = esG.enter_context(nc.psum_tensor("f1_ps512", [128, 512], F32))
            stgG = esG.enter_context(nc.sbuf_tensor("f1_stgG", [128, 4096], F32))
            abscols = esG.enter_context(nc.sbuf_tensor("f1_absc", [128, 32], F32))
            emit_mean_abs(pr, "f1_w1T", D, H, g1[:], stgG, abscols, ps512, onescol[:])
            emit_mean_abs(pr, "f1_w2T", H, D, g2[:], stgG, abscols, ps512, onescol[:])
            for g, thrp, thrn in ((g1, thr1p, thr1n), (g2, thr2p, thr2n)):
                em.v("tensor_scalar", t11[:], g[:], 1e-5, 0.5, ALU.add, ALU.mult)
                bcast_1x1_to_col(pr, t11[:], thrp[:], ps512)
                em.v("tensor_scalar", thrn[:], thrp[:], -1.0, None, ALU.mult)

        w1src = pr.din["f1_w1T"].rearrange("(dc p) f -> dc p f", p=128)
        w2src = pr.din["f1_w2T"].rearrange("(hc p) f -> hc p f", p=128)
        with pr.scope() as es3:
            r1h = es3.enter_context(nc.sbuf_tensor("f1_r1h", [128, NDC * 2048], BF16))
            r2h = es3.enter_context(nc.sbuf_tensor("f1_r2h", [128, 16 * D], BF16))
            stgq = [es3.enter_context(nc.sbuf_tensor(f"f1_stgq{i}", [128, 2048], F32))
                    for i in range(2)]
            b1rf = es3.enter_context(nc.sbuf_tensor("f1_b1rf", [1, 512], F32))
            b1rb = es3.enter_context(nc.sbuf_tensor("f1_b1rb", [1, 2048], BF16))
            b2rf = es3.enter_context(nc.sbuf_tensor("f1_b2rf", [1, D], F32))
            b2rb = es3.enter_context(nc.sbuf_tensor("f1_b2rb", [1, D], BF16))
            hneg = [es3.enter_context(nc.sbuf_tensor(f"f1_hneg{i}", [128, 2048], BF16))
                    for i in range(2)]
            hT = [es3.enter_context(nc.sbuf_tensor(f"f1_hT{i}", [128, 2048], BF16))
                  for i in range(2)]
            ps_h = es3.enter_context(nc.psum_tensor("f1_psh", [128, 2048], F32))
            ps_tr = es3.enter_context(nc.psum_tensor("f1_pstr", [128, 1024], BF16))
            ps_o = es3.enter_context(nc.psum_tensor("f1_pso", [128, 1024], F32))
            em.dma(b2rf[:], pr.din["f1_b2"])
            em.v("tensor_copy", b2rb[:], b2rf[:])
            for half in range(2):
                h0 = half * 2048
                for dc in range(NDC):
                    for piece in range(2):
                        sq = stgq[(dc * 2 + piece) % 2]
                        sl = sq[:, 0:1024]
                        em.dma(sl, w1src[dc, :, h0 + piece * 1024:h0 + (piece + 1) * 1024])
                        em.v("tensor_scalar", sq[:, 1024:2048], sl, thr1p[:],
                             None, ALU.is_gt)
                        em.v("scalar_tensor_tensor",
                             r1h[:, dc * 2048 + piece * 1024:dc * 2048 + (piece + 1) * 1024],
                             sl, thr1n[:], sq[:, 1024:2048], ALU.is_lt, ALU.subtract)
                for hc in range(16):
                    sq = stgq[hc % 2]
                    sl = sq[:, 0:1024]
                    em.dma(sl, w2src[half * 16 + hc])
                    em.v("tensor_scalar", sq[:, 1024:2048], sl, thr2p[:],
                         None, ALU.is_gt)
                    em.v("scalar_tensor_tensor", r2h[:, hc * D:(hc + 1) * D],
                         sl, thr2n[:], sq[:, 1024:2048], ALU.is_lt, ALU.subtract)
                for pc in range(4):
                    em.dma(b1rf[:], pr.din["f1_b1"][:, h0 + pc * 512:h0 + (pc + 1) * 512])
                    em.v("tensor_scalar", b1rb[:, pc * 512:(pc + 1) * 512], b1rf[:],
                         -1.0, None, ALU.mult)
                for tt in range(NTT):
                    for hs4 in range(4):
                        for dc in range(NDC):
                            em.mm(ps_h[:, hs4 * 512:(hs4 + 1) * 512],
                                  pr.xnT[:, dc * T + tt * 128:dc * T + (tt + 1) * 128],
                                  r1h[:, dc * 2048 + hs4 * 512:dc * 2048 + (hs4 + 1) * 512],
                                  start=(dc == 0), stop=False)
                        em.mm(ps_h[:, hs4 * 512:(hs4 + 1) * 512], pr.ones_row_b,
                              b1rb[:, hs4 * 512:(hs4 + 1) * 512], start=False, stop=True)
                    hn, hT_ = hneg[tt % 2], hT[tt % 2]
                    em.v("tensor_scalar", hn[:], ps_h[:], 0.0, None, ALU.min)
                    for grp8 in range(2):
                        for i8 in range(8):
                            i = grp8 * 8 + i8
                            em.t("transpose", ps_tr[:, i8 * 128:(i8 + 1) * 128],
                                 hn[:, i * 128:(i + 1) * 128], pr.ident[:])
                        em.v("tensor_copy", hT_[:, grp8 * 1024:(grp8 + 1) * 1024], ps_tr[:])
                    for os_ in range(2):
                        for hc in range(16):
                            em.mm(ps_o[:, os_ * 512:(os_ + 1) * 512],
                                  hT_[:, hc * 128:(hc + 1) * 128],
                                  r2h[:, hc * D + os_ * 512:hc * D + (os_ + 1) * 512],
                                  start=(hc == 0), stop=(half == 1 and hc == 15))
                        if half == 0:
                            em.mm(ps_o[:, os_ * 512:(os_ + 1) * 512], pr.ones_row_b,
                                  b2rb[:, os_ * 512:(os_ + 1) * 512], start=False,
                                  stop=True)
                    em.v("tensor_tensor", pr.xs[:, tt * D:(tt + 1) * D], ps_o[:],
                         pr.xs[:, tt * D:(tt + 1) * D], ALU.add)



import os
_KPHASE = int(os.environ.get("KPHASE", "9"))


def emit_l1_attn_b(pr):
    em = pr.em
    nc = pr.nc
    with pr.scope() as es2:
        sb = lambda n, s, d: es2.enter_context(nc.sbuf_tensor(n, s, d))
        pr.xnT = sb("db_xnT", [128, NDC * T], BF16)[:]
        negm16 = sb("db_negm", [128, 16], F32)
        rstd16 = sb("db_rstd", [128, 16], F32)
        emit_ln_all(pr, stats=(negm16, rstd16))
        gated = sb("db_gated", [128, NDC * T], BF16)
        dec8 = sb("db_dec8", [128, 8], F32)
        nwb = sb("db_nwb", [128, D], F32)
        em.dma(dec8[:], pr.din["db_logdec"])
        em.a("activation", dec8[:], dec8[:], AF.Exp, bias=0.0, scale=1.0)
        with pr.scope() as es3:
            nwrow = es3.enter_context(nc.sbuf_tensor("db_nwrow", [1, D], F32))
            ps512 = es3.enter_context(nc.psum_tensor("db_ps512", [128, 512], F32))
            em.dma(nwrow[:], pr.din["db_normw"])
            bcast_row_to_tile(pr, nwrow, nwb, ps512, D)
        if _KPHASE < 1:
            return
        with pr.scope() as es3:
            acc1 = es3.enter_context(nc.sbuf_tensor("db_acc1", [128, T], F32))
            acc2 = es3.enter_context(nc.sbuf_tensor("db_acc2", [128, T], F32))
            gs = es3.enter_context(nc.sbuf_tensor("db_gs", [128, T], BF16))
            cs = es3.enter_context(nc.sbuf_tensor("db_cs", [128, T], BF16))
            cw7 = es3.enter_context(nc.sbuf_tensor("db_cw7", [128, 7], F32))
            cb1 = es3.enter_context(nc.sbuf_tensor("db_cb1", [128, 1], F32))
            decb = es3.enter_context(nc.sbuf_tensor("db_decb", [128, T], F32))
            for c in range(NDC):
                xc = pr.xnT[:, c * T:(c + 1) * T]
                for (wname, bname, acc, outb, fn) in (
                        ("db_gatew", "db_gateb", acc1, gs, AF.Sigmoid),
                        ("db_candw", "db_candb", acc2, cs, AF.Tanh)):
                    em.dma(cw7[:], pr.din[wname][c * 128:(c + 1) * 128, :])
                    em.dma(cb1[:], pr.din[bname][c * 128:(c + 1) * 128, :])
                    em.v("tensor_scalar", acc[:], xc, cw7[:, 3:4], cb1[:],
                         ALU.mult, ALU.add)
                    for k in (0, 1, 2, 4, 5, 6):
                        sh = k - 3
                        if sh < 0:
                            em.v("scalar_tensor_tensor", acc[:, -sh:T],
                                 xc[:, 0:T + sh], cw7[:, k:k + 1],
                                 acc[:, -sh:T], ALU.mult, ALU.add)
                        else:
                            em.v("scalar_tensor_tensor", acc[:, 0:T - sh],
                                 xc[:, sh:T], cw7[:, k:k + 1],
                                 acc[:, 0:T - sh], ALU.mult, ALU.add)
                    em.a("activation", outb[:], acc[:], fn, bias=0.0, scale=1.0)
                gc = gated[:, c * T:(c + 1) * T]
                em.v("tensor_tensor", gc, gs[:], cs[:], ALU.mult)
                if _KPHASE >= 2:
                    em.v("tensor_scalar", decb[:], acc1[:], 0.0, dec8[:, c:c + 1],
                         ALU.mult, ALU.add)
                    em.v("tensor_tensor_scan", gc, decb[:], gc, 0.0, ALU.mult, ALU.add)
        if _KPHASE < 3:
            return
        with pr.scope() as es3:
            pr.es, saved_es = es3, pr.es
            wo = load_weight_bf16(pr, "db_wob", "db_woutT", D, D)
            pr.es = saved_es
            boutf = es3.enter_context(nc.sbuf_tensor("db_boutf", [1, D], F32))
            boutb = es3.enter_context(nc.sbuf_tensor("db_boutb", [1, D], BF16))
            yb = es3.enter_context(nc.sbuf_tensor("db_yb", [128, D], F32))
            u = es3.enter_context(nc.sbuf_tensor("db_u", [128, D], F32))
            dump = es3.enter_context(nc.sbuf_tensor("db_dump", [128, D], F32))
            ps_o = es3.enter_context(nc.psum_tensor("db_pso", [128, 1024], F32))
            em.dma(boutf[:], pr.din["db_bout"])
            em.v("tensor_copy", boutb[:], boutf[:])
            for tt in range(NTT):
                for os_ in range(2):
                    for c in range(NDC):
                        em.mm(ps_o[:, os_ * 512:(os_ + 1) * 512],
                              gated[:, c * T + tt * 128:c * T + (tt + 1) * 128],
                              wo[:, c * D + os_ * 512:c * D + (os_ + 1) * 512],
                              start=(c == 0), stop=False)
                    em.mm(ps_o[:, os_ * 512:(os_ + 1) * 512], pr.ones_row_b,
                          boutb[:, os_ * 512:(os_ + 1) * 512], start=False, stop=True)
                em.v("tensor_scalar", u[:], pr.xs[:, tt * D:(tt + 1) * D],
                     negm16[:, tt:tt + 1], rstd16[:, tt:tt + 1], ALU.add, ALU.mult)
                em.v("tensor_tensor", yb[:], ps_o[:], u[:], ALU.add)
                em.a("activation", dump[:], yb[:], AF.Square, bias=0.0, scale=1.0,
                     accum_out=pr.col2)
                em.v("tensor_scalar", pr.col2, pr.col2, 1.0 / D, 1.1920929e-07,
                     ALU.mult, ALU.add)
                em.a("activation", pr.col3, pr.col2, AF.Ln, bias=0.0, scale=1.0)
                em.a("activation", pr.col3, pr.col3, AF.Exp, bias=0.0, scale=-0.5)
                em.v("tensor_tensor", u[:], yb[:], nwb[:], ALU.mult)
                em.v("scalar_tensor_tensor", pr.xs[:, tt * D:(tt + 1) * D],
                     u[:], pr.col3, pr.xs[:, tt * D:(tt + 1) * D],
                     ALU.mult, ALU.add)



def emit_l1_attn_a(pr):
    em = pr.em
    nc = pr.nc
    DI = 2048
    NH = 32
    PP = 64
    zdram = nc.dram_tensor(f"sa_z_{pr.uid}", [T, DI], BF16, kind="Internal").ap()
    ydram = nc.dram_tensor(f"sa_y_{pr.uid}", [T, DI], BF16, kind="Internal").ap()
    xdram = nc.dram_tensor(f"sa_x_{pr.uid}", [DI, T], BF16, kind="Internal").ap()
    lscr = nc.dram_tensor(f"sa_l_{pr.uid}", [NH, 128], F32, kind="Internal").ap()
    pr.uid += 1
    with pr.scope() as es2:
        sb = lambda n, s, d: es2.enter_context(nc.sbuf_tensor(n, s, d))
        bc_sb = sb("sa_bc", [128, 2 * T], BF16)
        lam = sb("sa_lam", [32, T], F32)
        U = sb("sa_U", [32, T], F32)
        acol = sb("sa_acol", [32, 1], F32)
        dtb = sb("sa_dtbc", [32, 1], F32)
        ones32 = sb("sa_ones32", [32, 128], F32)
        em.v("memset", ones32[:], 1.0)
        em.dma(dtb[:], pr.din["sa_dtb"])
        em.dma(acol[:], pr.din["sa_alog"])
        em.a("activation", acol[:], acol[:], AF.Exp, bias=0.0, scale=1.0)
        em.v("tensor_scalar", acol[:], acol[:], -1.0, None, ALU.mult)

        with pr.scope() as es3:
            sb3 = lambda n, s, d: es3.enter_context(nc.sbuf_tensor(n, s, d))
            xbc = sb3("sa_xbc", [128, 16 * T], BF16)
            dtR = sb3("sa_dt", [32, T], F32)
            with pr.scope() as es4:
                xnT = es4.enter_context(nc.sbuf_tensor("sa_xnT", [128, NDC * T], BF16))
                pr.xnT = xnT[:]
                emit_ln_all(pr)
                psA = es4.enter_context(nc.psum_tensor("sa_psA", [128, 2048], F32))
                wstg = es4.enter_context(nc.sbuf_tensor("sa_wstg", [128, 512], F32))
                wslb = es4.enter_context(nc.sbuf_tensor("sa_wslb", [128, 512], BF16))
                zb = es4.enter_context(nc.sbuf_tensor("sa_zb", [128, 1024], BF16))
                wsrc = pr.din["sa_winT"].rearrange("(dc p) f -> dc p f", p=128)
                for t0 in range(0, NTT, 2):
                    for zh in range(2):
                        for zs in range(2):
                            for dc in range(NDC):
                                em.dma(wstg[:], wsrc[dc, :, zh * 1024 + zs * 512:
                                                       zh * 1024 + (zs + 1) * 512])
                                em.v("tensor_copy", wslb[:], wstg[:])
                                for i in range(2):
                                    em.mm(psA[:, i * 1024 + zs * 512:
                                              i * 1024 + (zs + 1) * 512],
                                          pr.xnT[:, dc * T + (t0 + i) * 128:
                                                 dc * T + (t0 + i + 1) * 128],
                                          wslb[:], start=(dc == 0), stop=(dc == NDC - 1))
                        for i in range(2):
                            em.v("tensor_copy", zb[:], psA[:, i * 1024:(i + 1) * 1024])
                            em.dma(zdram.rearrange("(tt p) f -> tt p f", p=128)
                                   [t0 + i, :, zh * 1024:(zh + 1) * 1024], zb[:])
                for fc in range(18):
                    for dc in range(NDC):
                        em.dma(wstg[:, 0:128], wsrc[dc, :, DI + fc * 128:DI + (fc + 1) * 128])
                        em.v("tensor_copy", wslb[:, 0:128], wstg[:, 0:128])
                        for ts4 in range(4):
                            em.mm(psA[:, ts4 * 512:(ts4 + 1) * 512],
                                  wslb[:, 0:128],
                                  pr.xnT[:, dc * T + ts4 * 512:dc * T + (ts4 + 1) * 512],
                                  start=(dc == 0), stop=(dc == NDC - 1))
                    dst = (xbc[:, fc * T:(fc + 1) * T] if fc < 16
                           else bc_sb[:, (fc - 16) * T:(fc - 15) * T])
                    em.v("tensor_copy", dst, psA[:])
                for dc in range(NDC):
                    em.dma(wstg[:, 0:32], wsrc[dc, :, DI + 2304:DI + 2304 + 32])
                    em.v("tensor_copy", wslb[:, 0:32], wstg[:, 0:32])
                    for ts4 in range(4):
                        em.mm(psA[0:32, ts4 * 512:(ts4 + 1) * 512],
                              wslb[:, 0:32],
                              pr.xnT[:, dc * T + ts4 * 512:dc * T + (ts4 + 1) * 512],
                              start=(dc == 0), stop=(dc == NDC - 1))
                em.v("tensor_scalar", dtR[:], psA[0:32, :], dtb[:], None, ALU.add)

            with pr.scope() as es4:
                accC = es4.enter_context(nc.sbuf_tensor("sa_accC", [128, T], F32))
                cw4 = es4.enter_context(nc.sbuf_tensor("sa_cw4", [128, 4], F32))
                cb1 = es4.enter_context(nc.sbuf_tensor("sa_cb1", [128, 1], F32))
                for fc in range(18):
                    xc = (xbc[:, fc * T:(fc + 1) * T] if fc < 16
                          else bc_sb[:, (fc - 16) * T:(fc - 15) * T])
                    em.dma(cw4[:], pr.din["sa_convw"][fc * 128:(fc + 1) * 128, :])
                    em.dma(cb1[:], pr.din["sa_convb"][fc * 128:(fc + 1) * 128, :])
                    em.v("tensor_scalar", accC[:], xc, cw4[:, 3:4], cb1[:],
                         ALU.mult, ALU.add)
                    for k in (0, 1, 2):
                        sh = 3 - k
                        em.v("scalar_tensor_tensor", accC[:, sh:T], xc[:, 0:T - sh],
                             cw4[:, k:k + 1], accC[:, sh:T], ALU.mult, ALU.add)
                    if fc < 16:
                        em.a("activation", xc, accC[:], AF.Silu, bias=0.0, scale=1.0)
                        em.dma(xdram.rearrange("(fc p) t -> fc p t", p=128)[fc], xc)
                    else:
                        dst = bc_sb[:, (fc - 16) * T:(fc - 15) * T]
                        em.a("activation", dst, accC[:], AF.Silu, bias=0.0, scale=1.0)

            with pr.scope() as es4:
                t32 = es4.enter_context(nc.sbuf_tensor("sa_t32", [32, T], F32))
                em.a("activation", t32[:], dtR[:], AF.Abs, bias=0.0, scale=1.0)
                em.a("activation", t32[:], t32[:], AF.Exp, bias=0.0, scale=-1.0)
                em.a("activation", t32[:], t32[:], AF.Ln, bias=1.0, scale=1.0)
                em.v("scalar_tensor_tensor", dtR[:], dtR[:], 0.0, t32[:],
                     ALU.max, ALU.add)
                em.v("tensor_scalar", t32[:], dtR[:], 1e-38, None, ALU.max)
                em.a("activation", U[:], t32[:], AF.Ln, bias=0.0, scale=1.0)
                em.v("tensor_scalar", t32[:], dtR[:], acol[:], None, ALU.mult)
                for c in range(NTT):
                    em.v("tensor_tensor_scan", lam[:, c * 128:(c + 1) * 128],
                         ones32[:], t32[:, c * 128:(c + 1) * 128], 0.0,
                         ALU.mult, ALU.add)
                em.v("tensor_sub", U[:], U[:], lam[:])

        with pr.scope() as es3:
            sb3 = lambda n, s, d: es3.enter_context(nc.sbuf_tensor(n, s, d))
            H = sb3("sa_H", [128, DI], F32)
            Hb = sb3("sa_Hb", [128, DI], BF16)
            Mw = sb3("sa_Mw", [128, NH * 128], BF16)
            lamM = sb3("sa_lamM", [128, NH * 128], F32)
            Xstage = sb3("sa_Xstage", [128, DI], BF16)
            Xtok = sb3("sa_Xtok", [128, DI], BF16)
            Xw = sb3("sa_Xw", [128, DI], BF16)
            Btok = sb3("sa_Btok", [128, 128], BF16)
            Gb = sb3("sa_Gb", [128, 128], BF16)
            Utok = sb3("sa_Utok", [128, 32], F32)
            Ltok = sb3("sa_Ltok", [128, 32], F32)
            lend = sb3("sa_lend", [128, 32], F32)
            wS = sb3("sa_wS", [128, 32], F32)
            gam = sb3("sa_gam", [128, 32], F32)
            expl = sb3("sa_expl", [128, 32], F32)
            Db32 = sb3("sa_Db32", [128, 32], F32)
            dcol = sb3("sa_dcol", [32, 1], F32)
            ycbf = sb3("sa_ycbf", [128, DI], BF16)
            lrow = sb3("sa_lrow", [1, 2048], F32)
            r11 = sb3("sa_r11", [1, 32], F32)
            psS = es3.enter_context(nc.psum_tensor("sa_psYS", [128, 2048], F32))
            psY = psS
            psT = es3.enter_context(nc.psum_tensor("sa_psT", [128, 2048], BF16))
            psF = es3.enter_context(nc.psum_tensor("sa_psF", [128, 512], F32))
            em.v("memset", H[:], 0.0)
            em.v("memset", Hb[:], 0.0)
            em.dma(dcol[:], pr.din["sa_d"])
            em.t("transpose", psF[0:1, 0:32], dcol[:], pr.ident32f)
            em.v("tensor_copy", r11[:], psF[0:1, 0:32])
            em.mm(psF[:, 32:64], pr.ones_row_f, r11[:], start=True, stop=True)
            em.v("tensor_copy", Db32[:], psF[:, 32:64])
            lam3 = lamM[:].rearrange("p (h t) -> p h t", t=128)
            Mw3 = Mw[:].rearrange("p (h t) -> p h t", t=128)
            for c in range(NTT):
                cs_ = slice(c * 128, (c + 1) * 128)
                em.t("transpose", psF[:, 32:64], U[:, cs_], pr.ident32f)
                em.v("tensor_copy", Utok[:], psF[:, 32:64])
                em.t("transpose", psF[:, 64:96], lam[:, cs_], pr.ident32f)
                em.v("tensor_copy", Ltok[:], psF[:, 64:96])
                em.t("transpose", psF[0:1, 96:128],
                     lam[:, c * 128 + 127:c * 128 + 128], pr.ident32f)
                em.v("tensor_copy", r11[:], psF[0:1, 96:128])
                em.mm(psF[:, 128:160], pr.ones_row_f, r11[:], start=True, stop=True)
                em.v("tensor_copy", lend[:], psF[:, 128:160])
                em.v("tensor_tensor", wS[:], Utok[:], lend[:], ALU.add)
                em.a("activation", wS[:], wS[:], AF.Exp, bias=0.0, scale=1.0)
                em.a("activation", gam[:], lend[:], AF.Exp, bias=0.0, scale=1.0)
                em.a("activation", expl[:], Ltok[:], AF.Exp, bias=0.0, scale=1.0)
                em.dma(lscr, lam[:, cs_])
                for half in range(2):
                    em.dma(lrow[:], lscr.rearrange("p t -> (p t)").rearrange(
                        "(one f) -> one f", one=1)[:, half * 2048:(half + 1) * 2048])
                    for q4 in range(4):
                        em.mm(psS[:, q4 * 512:(q4 + 1) * 512], pr.ones_row_f,
                              lrow[:, q4 * 512:(q4 + 1) * 512], start=True, stop=True)
                    em.v("tensor_copy", lamM[:, half * 2048:(half + 1) * 2048], psS[:])
                em.v("tensor_tensor", lam3,
                     pr.maskneg.unsqueeze(1).broadcast_to([128, NH, 128]),
                     lam3, ALU.add)
                em.v("tensor_tensor", lam3,
                     Utok[:].unsqueeze(2).broadcast_to([128, NH, 128]),
                     lam3, ALU.add)
                em.a("activation", Mw[:], lamM[:], AF.Exp, bias=0.0, scale=1.0)
                em.mm(psF[:, 0:128], bc_sb[:, cs_],
                      bc_sb[:, T + c * 128:T + (c + 1) * 128], start=True, stop=True)
                em.v("tensor_copy", Gb[:], psF[:, 0:128])
                em.v("tensor_tensor", Mw3, Gb[:].unsqueeze(1).broadcast_to([128, NH, 128]),
                     Mw3, ALU.mult)
                em.dma(Xstage[:].rearrange("p (i t) -> p i t", t=128),
                       xdram.rearrange("(i p) t -> p i t", p=128)[:, :, cs_])
                for i in range(16):
                    em.t("transpose", psT[:, i * 128:(i + 1) * 128],
                         Xstage[:, i * 128:(i + 1) * 128], pr.ident[:])
                em.v("tensor_copy", Xtok[:], psT[:])
                em.t("transpose", psT[:, 0:128], bc_sb[:, cs_], pr.ident[:])
                em.v("tensor_copy", Btok[:], psT[:, 0:128])
                em.v("tensor_tensor", Xw[:].rearrange("p (h w) -> p h w", w=PP),
                     Xtok[:].rearrange("p (h w) -> p h w", w=PP),
                     wS[:].unsqueeze(2).broadcast_to([128, NH, PP]), ALU.mult)
                XDv = lamM[:, 2048:4096].rearrange("p (h w) -> p h w", w=PP)
                em.v("tensor_tensor", XDv,
                     Xtok[:].rearrange("p (h w) -> p h w", w=PP),
                     Db32[:].unsqueeze(2).broadcast_to([128, NH, PP]), ALU.mult)
                for hh in range(2):
                    for hi in range(16):
                        h = hh * 16 + hi
                        em.mm(psY[:, hi * PP:(hi + 1) * PP],
                              Mw[:, h * 128:(h + 1) * 128],
                              Xtok[:, h * PP:(h + 1) * PP],
                              start=(hi % 8 == 0), stop=(hi % 8 == 7))
                        em.mm(psY[:, 1024 + hi * PP:1024 + (hi + 1) * PP],
                              bc_sb[:, T + c * 128:T + (c + 1) * 128],
                              Hb[:, h * PP:(h + 1) * PP],
                              start=(hi % 8 == 0), stop=(hi % 8 == 7))
                    ytmp = lamM[:, 0:1024]
                    em.v("tensor_tensor", ytmp.rearrange("p (h w) -> p h w", w=PP),
                         psY[:, 1024:2048].rearrange("p (h w) -> p h w", w=PP),
                         expl[:, hh * 16:(hh + 1) * 16].unsqueeze(2)
                         .broadcast_to([128, 16, PP]), ALU.mult)
                    em.v("tensor_tensor", ytmp, psY[:, 0:1024], ytmp, ALU.add)
                    em.v("tensor_tensor", ycbf[:, hh * 1024:(hh + 1) * 1024], ytmp,
                         lamM[:, 2048 + hh * 1024:2048 + (hh + 1) * 1024], ALU.add)
                em.dma(ydram.rearrange("(c p) f -> c p f", p=128)[c], ycbf[:])
                for h in range(NH):
                    em.mm(psS[:, h * PP:(h + 1) * PP], Btok[:],
                          Xw[:, h * PP:(h + 1) * PP],
                          start=(h % 8 == 0), stop=(h % 8 == 7))
                em.v("tensor_tensor", H[:].rearrange("p (h w) -> p h w", w=PP),
                     H[:].rearrange("p (h w) -> p h w", w=PP),
                     gam[:].unsqueeze(2).broadcast_to([128, NH, PP]), ALU.mult)
                em.v("tensor_tensor", H[:], psS[:], H[:], ALU.add)
                em.v("tensor_copy", Hb[:], H[:])

        with pr.scope() as es3:
            sb3 = lambda n, s, d: es3.enter_context(nc.sbuf_tensor(n, s, d))
            nw2b = sb3("sa_nw2b", [128, DI], F32)
            ytile = sb3("sa_ytile", [128, DI], BF16)
            ztile = sb3("sa_ztile", [128, DI], BF16)
            zs_ = sb3("sa_zs", [128, DI], F32)
            yz = sb3("sa_yz", [128, DI], F32)
            uu = sb3("sa_uu", [128, DI], F32)
            ynb = sb3("sa_ynb", [128, DI], BF16)
            ynT = sb3("sa_ynT", [128, DI], BF16)
            dump = sb3("sa_dump", [128, DI], F32)
            nwrow = sb3("sa_nwrow", [1, DI], F32)
            psO = es3.enter_context(nc.psum_tensor("sa_psO", [128, 1024], F32))
            psT2 = es3.enter_context(nc.psum_tensor("sa_psT2", [128, 2048], BF16))
            ps512b = es3.enter_context(nc.psum_tensor("sa_ps512b", [128, 512], F32))
            pr.es, saved_es = es3, pr.es
            wo = load_weight_bf16(pr, "sa_wob", "sa_woutT", DI, D)
            pr.es = saved_es
            em.dma(nwrow[:], pr.din["sa_normw"])
            bcast_row_to_tile(pr, nwrow, nw2b, ps512b, DI)
            for tt in range(NTT):
                em.dma(ytile[:], ydram.rearrange("(c p) f -> c p f", p=128)[tt])
                em.dma(ztile[:], zdram.rearrange("(c p) f -> c p f", p=128)[tt])
                em.a("activation", zs_[:], ztile[:], AF.Silu, bias=0.0, scale=1.0)
                em.v("tensor_tensor", yz[:], ytile[:], zs_[:], ALU.mult)
                em.a("activation", dump[:], yz[:], AF.Square, bias=0.0, scale=1.0,
                     accum_out=pr.col2)
                em.v("tensor_scalar", pr.col2, pr.col2, 1.0 / DI, 1e-5,
                     ALU.mult, ALU.add)
                em.a("activation", pr.col3, pr.col2, AF.Ln, bias=0.0, scale=1.0)
                em.a("activation", pr.col3, pr.col3, AF.Exp, bias=0.0, scale=-0.5)
                em.v("tensor_tensor", uu[:], yz[:], nw2b[:], ALU.mult)
                em.v("tensor_scalar", ynb[:], uu[:], pr.col3, None, ALU.mult)
                for i in range(16):
                    em.t("transpose", psT2[:, i * 128:(i + 1) * 128],
                         ynb[:, i * 128:(i + 1) * 128], pr.ident[:])
                em.v("tensor_copy", ynT[:], psT2[:])
                for os_ in range(2):
                    for ic in range(16):
                        em.mm(psO[:, os_ * 512:(os_ + 1) * 512],
                              ynT[:, ic * 128:(ic + 1) * 128],
                              wo[:, ic * D + os_ * 512:ic * D + (os_ + 1) * 512],
                              start=(ic == 0), stop=(ic == 15))
                em.v("tensor_tensor", pr.xs[:, tt * D:(tt + 1) * D], psO[:],
                     pr.xs[:, tt * D:(tt + 1) * D], ALU.add)


EMITTERS = {
    "l0aa": emit_l0_attn_a,
    "l0ab": emit_l0_attn_b,
    "l0ffn1": emit_l0_ffn,
    "l0ffn2": emit_l0_ffn,
    "l1ffn1": emit_l1_ffn,
    "l1ffn2": emit_l1_ffn,
    "l1ab": emit_l1_attn_b,
    "l1aa": emit_l1_attn_a,
}


def build_program(parts):
    nc = bass.Bass("TRN2", target_bir_lowering=False)
    x_in = nc.dram_tensor("x_in", [T, D], F32, kind="ExternalInput").ap()
    x_out = nc.dram_tensor("x_out", [T, D], F32, kind="ExternalOutput").ap()
    names = set(CONST_INPUTS)
    for p_ in parts:
        names.update(PART_INPUTS[p_])
    dins = {}
    for n in sorted(names):
        shp = PREP_SHAPES[n]
        dt = BF16 if n in ("c_ident128", "c_ones_row_b", "c_triu") else F32
        dins[n] = nc.dram_tensor(n, list(shp), dt, kind="ExternalInput").ap()

    em = E(nc)
    global _LAST_EM
    _LAST_EM = em
    with contextlib.ExitStack() as es:
        pr = Prog(NCProxy(nc), em, es, dins)
        setup_shared(pr)
        em.dma(pr.xs[:].rearrange("p (tt d) -> p tt d", d=D),
               x_in.rearrange("(tt p) d -> p tt d", p=128))
        for p_ in parts:
            em.mark(p_)
            EMITTERS[p_](pr)
        em.mark("end")
        em.dma(x_out.rearrange("(tt p) d -> p tt d", p=128),
               pr.xs[:].rearrange("p (tt d) -> p tt d", d=D))
        esem = {e: es.enter_context(nc.semaphore(f"SE_{e}")) for e in E.ENGS}
        dsem = [es.enter_context(nc.semaphore(f"SD_{i}")) for i in range(E.NDMA)]
        block = es.enter_context(nc.Block())
        em.replay(block, (esem, dsem))
    return nc, sorted(names)


PREP_SHAPES = {
    "c_ident128": (128, 128), "c_ident32f": (32, 32), "c_ident128f": (128, 128),
    "c_ones_row_f": (1, 128), "c_maskneg": (128, 128),
    "c_ones_row_b": (1, 128), "c_triu": (128, 128),
    "aa_wqT": (D, D), "aa_wkT": (D, D), "aa_wvT": (D, D),
    "ab_wqkv": (D, 3072), "ab_wout": (D, D), "ab_bout": (1, D), "ab_ls": (16, 1),
    "f0_w1T": (D, 4096), "f0_w2T": (4096, D), "f0_convw": (4096, 3), "f0_convb": (4096, 1),
    "sa_winT": (D, 4384), "sa_convw": (2304, 4), "sa_convb": (2304, 1),
    "sa_dtb": (32, 1), "sa_alog": (32, 1), "sa_d": (32, 1), "sa_normw": (1, 2048),
    "sa_woutT": (2048, D),
    "db_gatew": (D, 7), "db_gateb": (D, 1), "db_candw": (D, 7), "db_candb": (D, 1),
    "db_logdec": (128, 8), "db_woutT": (D, D), "db_bout": (1, D), "db_normw": (1, D),
    "f1_w1T": (D, 4096), "f1_b1": (1, 4096), "f1_w2T": (4096, D), "f1_b2": (1, D),
}


def run_parts(x_full, prepped, parts, sim=False):
    nc, names = build_program(parts)
    in_maps = []
    ncores = 1 if sim else 8
    for b in range(ncores):
        m = {"x_in": np.ascontiguousarray(x_full[b])}
        for n in names:
            arr = prepped[n]
            want_bf = n in ("c_ident128", "c_ones_row_b", "c_triu")
            m[n] = np.ascontiguousarray(arr.astype(bf16 if want_bf else np.float32))
        in_maps.append(m)
    if sim:
        from concourse.bass_interp import MultiCoreSim
        nc.finalize()
        s = MultiCoreSim(nc, num_cores=1, num_workers=1, trace=False)
        for k, v in in_maps[0].items():
            s.cores[0].tensor(k)[:] = v
        s.simulate()
        return np.asarray(s.cores[0].tensor("x_out"))[None]
    res = run_bass_kernel_spmd(nc, in_maps, core_ids=list(range(8)))
    return np.stack([r["x_out"] for r in res.results])


def kernel(**inputs):
    x = np.ascontiguousarray(np.asarray(inputs["L0_attn_a_z"], dtype=np.float32))
    prepped = prep_inputs(inputs)
    return run_parts(x, prepped, ALL_PARTS)
